# version 05
# baseline (speedup 1.0000x reference)
import numpy as np
import ml_dtypes

import concourse.bass as bass
import concourse.mybir as mybir
from concourse.bass_utils import run_bass_kernel_spmd

F32 = mybir.dt.float32
BF16 = mybir.dt.bfloat16
AF = mybir.ActivationFunctionType
ALU = mybir.AluOpType
AX = mybir.AxisListType
BF = ml_dtypes.bfloat16

D = 1024
DC = 8
T = 2048
NSEQ = 2
NTOK = NSEQ * T
EPS = 1e-5
MAGIC = 1.5 * (2.0 ** 23)

ENGS = ("sync", "gpsimd", "tensor", "vector", "scalar")

N_LD = 8
N_GL = 4
N_ST = 4


class Sched:
    def __init__(self, nc):
        self.nc = nc
        self.q = {e: [] for e in ENGS}
        self.sem = {}
        self.cnt = {}
        self.last_wait = {e: {} for e in ENGS}

    def new_sem(self, name):
        self.sem[name] = self.nc.alloc_semaphore(name=name)
        self.cnt[name] = 0
        return name

    def emit(self, eng, fn, waits=(), inc=None, inc_by=1):
        lw = self.last_wait[eng]
        for w in waits:
            if w is None:
                continue
            sname, v = w
            if v > 0 and lw.get(sname, -1) < v:
                self.q[eng].append(("w", sname, v))
                lw[sname] = v
        self.q[eng].append(("o", fn, inc, inc_by))
        if inc is not None:
            self.cnt[inc] += inc_by
            return (inc, self.cnt[inc])
        return None

    def now(self, sem):
        return (sem, self.cnt[sem])

    def barrier(self):
        toks = [(n, c) for n, c in self.cnt.items() if c > 0]
        for e in ENGS:
            lw = self.last_wait[e]
            for sname, v in toks:
                if lw.get(sname, -1) < v:
                    self.q[e].append(("w", sname, v))
                    lw[sname] = v

    def build(self):
        with self.nc.Block() as block:
            for eng in ENGS:
                stream = self.q[eng]
                sem = self.sem

                def body(engine, stream=stream):
                    for item in stream:
                        if item[0] == "w":
                            engine.wait_ge(sem[item[1]], item[2])
                        else:
                            _, fn, inc, n = item
                            inst = fn(engine)
                            if inc is not None:
                                inst.then_inc(sem[inc], n)
                getattr(block, eng)(body)


class SA:
    def __init__(self, nc):
        self.nc = nc
        self.base = (nc.sbuf_base + 31) & ~31
        self.off = self.base
        self.top = nc.sbuf_top
        self.uid = 0

    def alloc(self, name, shape, dtype):
        per = int(np.prod(shape[1:])) * mybir.dt.size(dtype)
        per = (per + 31) & ~31
        assert self.off + per <= self.top, (
            f"SBUF overflow at {name}: {self.off}+{per} > {self.top}")
        self.uid += 1
        t = self.nc.alloc_sbuf_tensor_at(f"{name}_{self.uid}", list(shape), dtype,
                                         offset=self.off)
        self.off += per
        return t

    def mark(self):
        return self.off

    def reset(self, m):
        self.off = m


class Ctx:
    def __init__(self, inputs, scratch, outputs):
        nc = bass.Bass("TRN2", target_bir_lowering=False, debug=False)
        self.nc = nc
        self.s = Sched(nc)
        self.sa = SA(nc)
        s = self.s
        self.ld_sems = [s.new_sem(f"ld{i}") for i in range(N_LD)]
        self.gl_sems = [s.new_sem(f"gl{i}") for i in range(N_GL)]
        self.st_sems = [s.new_sem(f"st{i}") for i in range(N_ST)]
        self._ld_i = 0
        self._gl_i = 0
        self._st_i = 0
        for n in ("Tn", "Vn", "Cn", "Pn"):
            s.new_sem(n)
        self.dram = {}
        for name, shape, dt in inputs:
            self.dram[name] = nc.dram_tensor(name, list(shape), dt,
                                             kind="ExternalInput").ap()
        for name, shape, dt in scratch:
            self.dram[name] = nc.dram_tensor(name, list(shape), dt,
                                             kind="Internal").ap()
        for name, shape, dt in outputs:
            self.dram[name] = nc.dram_tensor(name, list(shape), dt,
                                             kind="ExternalOutput").ap()
        self.st = [nc.alloc_psum_tensor(f"ps_st{i}", [128, 512], F32)
                   for i in range(4)]
        self.ps_g = [nc.alloc_psum_tensor(f"ps_g{i}", [128, 512], F32)
                     for i in range(3)]
        self.ps_bf = nc.alloc_psum_tensor("ps_bf", [128, 1024], BF16)
        self.war = {}
        self._make_consts()
        self.s.barrier()

    def _make_consts(self):
        nc, s, sa = self.nc, self.s, self.sa
        tri = np.triu(np.ones((128, 128), np.float32)).astype(BF)
        RD = 32
        inv_freq = 1.0 / (10000.0 ** (np.arange(0, RD, 2, dtype=np.float64) / RD))
        tpos = np.arange(T, dtype=np.float64)
        freqs = tpos[:, None] * inv_freq[None, :]
        emb = np.concatenate([freqs, freqs], axis=-1)
        cos = np.cos(emb).astype(np.float32)
        sin = np.sin(emb).astype(np.float32)
        sgn = np.ones(32, np.float32)
        sgn[:16] = -1.0
        consts = {
            "ident": np.eye(128, dtype=BF),
            "triu": tri,
            "ones_b": np.ones((128, 1), BF),
            "ones_f": np.ones((128, 1), np.float32),
            "ones_r": np.ones((1, 128), np.float32),
            "eps_c": np.full((128, 1), EPS, np.float32),
        }
        self.rope_cos = np.ascontiguousarray(np.tile(cos.T, (4, 1)))
        self.rope_sin = np.ascontiguousarray(np.tile((sin * sgn[None, :]).T, (4, 1)))
        self.c = {}
        for name, arr in consts.items():
            dt = BF16 if arr.dtype == BF else F32
            dr = nc.inline_tensor(np.ascontiguousarray(arr), name=f"c_{name}").ap()
            sb = sa.alloc(f"c_{name}", list(arr.shape), dt)
            self.load(sb[:, :], dr[:, :])
            self.c[name] = sb

    def _dma(self, eng, sems, idx_attr, dst_ap, src_ap, waits):
        i = getattr(self, idx_attr)
        setattr(self, idx_attr, i + 1)
        sem = sems[i % len(sems)]
        pre = self.s.now(sem)
        return self.s.emit(eng, lambda e: e.dma_start(out=dst_ap, in_=src_ap),
                           waits=list(waits) + [pre], inc=sem, inc_by=16)

    def load(self, dst_ap, src_ap, waits=()):
        return self._dma("sync", self.ld_sems, "_ld_i", dst_ap, src_ap, waits)

    def cast_load(self, dst_ap, src_ap, waits=()):
        return self._dma("gpsimd", self.gl_sems, "_gl_i", dst_ap, src_ap, waits)

    def store(self, dst_ap, src_ap, waits=()):
        return self._dma("gpsimd", self.st_sems, "_st_i", dst_ap, src_ap, waits)

    def mm(self, out_ap, lhsT, rhs, start, stop, waits=(), force_inc=False):
        return self.s.emit(
            "tensor",
            lambda e: e.matmul(out_ap, lhsT, rhs, start=start, stop=stop),
            waits=waits, inc=("Tn" if (stop or force_inc) else None))

    def tr(self, out_ap, in_ap, waits=()):
        ident = self.c["ident"]
        return self.s.emit(
            "tensor", lambda e: e.transpose(out_ap, in_ap, ident[:, :]),
            waits=waits, inc="Tn")

    def act(self, fn, waits=()):
        return self.s.emit("scalar", fn, waits=waits, inc="Cn")

    def dve(self, fn, waits=()):
        return self.s.emit("vector", fn, waits=waits, inc="Vn")

    def pool(self, fn, waits=()):
        return self.s.emit("gpsimd", fn, waits=waits, inc="Pn")



def emit_ln(ctx, src, col0, n, xbuf, sqbuf, rowbuf, tmpbuf, dst_fn,
            gain_col=None, dst_war=None):
    s, c = ctx.s, ctx.c
    ps_sum, ps_sq = ctx.ps_g[0], ctx.ps_g[1]
    x_toks = []
    sum_tok = None
    sq_act_tok = None
    sq_tok = ctx.war.get("lnsq")
    for cc in range(DC):
        xa = xbuf[:, cc * n:(cc + 1) * n]
        xt = ctx.load(xa, src[cc * 128:(cc + 1) * 128, col0:col0 + n],
                      waits=[ctx.war.get(f"lnx{cc}")])
        x_toks.append(xt)
        sum_tok = ctx.mm(ps_sum[0:1, 0:n], c["ones_f"][:, :], xa,
                         start=(cc == 0), stop=(cc == DC - 1),
                         waits=[xt, ctx.war.get("g0")] if cc == 0 else [xt])
        sq_act_tok = ctx.act(
            lambda e, o=sqbuf[:, 0:n], i=xa: e.activation(o, i, AF.Square),
            waits=[xt, sq_tok])
        sq_tok = ctx.mm(ps_sq[0:1, 0:n], c["ones_b"][:, :], sqbuf[:, 0:n],
                        start=(cc == 0), stop=(cc == DC - 1), force_inc=True,
                        waits=([sq_act_tok, ctx.war.get("g1")]
                               if cc == 0 else [sq_act_tok]))
    ctx.war["lnsq"] = sq_tok
    A = rowbuf[0:1, 0:n]
    B = rowbuf[0:1, n:2 * n]
    M = rowbuf[0:1, 2 * n:3 * n]
    T1 = rowbuf[0:1, 3 * n:4 * n]
    T2 = rowbuf[0:1, 4 * n:5 * n]
    t1 = ctx.dve(lambda e: e.tensor_scalar_mul(M, ps_sum[0:1, 0:n], 1.0 / D),
                 waits=[sum_tok])
    t2 = ctx.dve(lambda e: e.tensor_tensor(T1, M, M, ALU.mult), waits=[t1])
    t3 = ctx.dve(lambda e: e.scalar_tensor_tensor(T2, ps_sq[0:1, 0:n], 1.0 / D,
                                                  T1, ALU.mult, ALU.subtract),
                 waits=[t2, sq_tok])
    t4 = ctx.act(lambda e: e.activation(T1, T2, AF.Sqrt, bias=c["eps_c"][0:1, :]),
                 waits=[t3])
    t5 = ctx.dve(lambda e: e.reciprocal(A, T1),
                 waits=[t4, ctx.war.get("rowA")])
    t6 = ctx.dve(lambda e: e.scalar_tensor_tensor(B, M, -1.0, A, ALU.mult, ALU.mult),
                 waits=[t5, ctx.war.get("rowB")])
    bA = ctx.mm(ps_sum[:, 0:n], c["ones_r"][:, :], A, start=True, stop=True,
                waits=[t5, t1])
    bB = ctx.mm(ps_sq[:, 0:n], c["ones_r"][:, :], B, start=True, stop=True,
                waits=[t6, t3])
    ctx.war["rowA"] = bA
    ctx.war["rowB"] = bB
    xhat_toks = []
    tp = None
    for cc in range(DC):
        xa = xbuf[:, cc * n:(cc + 1) * n]
        tp = ctx.dve(lambda e, o=tmpbuf[:, 0:n], i=xa:
                     e.tensor_tensor(o, i, ps_sum[:, 0:n], ALU.mult),
                     waits=[bA, tp])
        if gain_col is None:
            tw = ctx.dve(lambda e, o=dst_fn(cc), i=tmpbuf[:, 0:n]:
                         e.tensor_tensor(o, i, ps_sq[:, 0:n], ALU.add),
                         waits=[bB, tp, dst_war(cc) if dst_war else None])
        else:
            tw = ctx.dve(lambda e, o=tmpbuf[:, 0:n], i=tmpbuf[:, 0:n]:
                         e.tensor_tensor(o, i, ps_sq[:, 0:n], ALU.add),
                         waits=[bB, tp])
            tw = ctx.dve(lambda e, o=dst_fn(cc), i=tmpbuf[:, 0:n], g=gain_col(cc):
                         e.tensor_scalar_mul(o, i, g),
                         waits=[tw, dst_war(cc) if dst_war else None])
        tp = tw
        ctx.war[f"lnx{cc}"] = tw
        xhat_toks.append(tw)
    ctx.war["g0"] = xhat_toks[-1]
    ctx.war["g1"] = xhat_toks[-1]
    return xhat_toks, x_toks



def emit_attn_core(ctx, nh, dh, Q_sb, K_sb, V_sb, scale, PT_sb, dmask_sb,
                   otmn_sb, recip_sb, ofm4_sb, qk_tok, v_tok, oproj_fn):
    s, c = ctx.s, ctx.c
    hpc = 128 // dh
    DCk = nh // hpc
    ng64 = dh // 64
    hw = ng64 * 65
    gw = nh * hw
    nh64 = nh * ng64
    exp_tok = [None] * 4
    pv_par = [None, None]
    mask_tok = None
    tr_slot_tok = [None] * 8
    otmn_tok = None
    tr_i = 0
    hh = 0
    for j in range(16):
        obank_used = []
        for h in range(nh):
            rb = (h % hpc) * dh
            cc = h // hpc
            par = hh % 2
            hh += 1
            PT = PT_sb[par]
            dmask = dmask_sb[par]
            for b in range(0, j + 1, 4):
                mm_t = None
                bank = ctx.st[b // 4]
                for c4 in range(b, min(b + 4, j + 1)):
                    w = [qk_tok, v_tok, exp_tok[c4 // 4]] if c4 == b else []
                    mm_t = ctx.mm(
                        bank[:, (c4 - b) * 128:(c4 - b + 1) * 128],
                        K_sb[rb:rb + dh, cc * T + c4 * 128: cc * T + (c4 + 1) * 128],
                        Q_sb[rb:rb + dh, cc * T + j * 128: cc * T + (j + 1) * 128],
                        start=True, stop=True, waits=w)
                hi = min(b + 4, j + 1) * 128
                o_ap = PT[:, b * 128:hi]
                i_ap = bank[:, 0:hi - b * 128]
                exp_tok[b // 4] = ctx.act(
                    lambda e, o=o_ap, i=i_ap: e.activation(o, i, AF.Exp, scale=scale),
                    waits=[mm_t, pv_par[par]])
            dm_in = PT[:, j * 128:(j + 1) * 128]
            mask_tok = ctx.dve(
                lambda e, o=dmask[:, 0:128], i=dm_in:
                e.tensor_tensor(o, i, c["triu"][:, :], ALU.mult),
                waits=[exp_tok[j // 4], pv_par[par]])
            pv_tok = None
            for g in range(ng64):
                h64 = h * ng64 + g
                bk = h64 // 7
                ocol = (h64 % 7) * 65
                if bk not in obank_used:
                    obank_used.append(bk)
                for c4 in range(j + 1):
                    lt = (dmask[:, 0:128] if c4 == j
                          else PT[:, c4 * 128:(c4 + 1) * 128])
                    if c4 == 0:
                        w = [v_tok, mask_tok, ctx.war.get(f"ob{bk}")]
                    elif c4 == j:
                        w = [mask_tok]
                    else:
                        w = [exp_tok[c4 // 4]]
                    vs = V_sb[:, c4 * gw + h * hw + g * 65:
                              c4 * gw + h * hw + g * 65 + 65]
                    pv_tok = ctx.mm(ctx.ps_g[bk][:, ocol:ocol + 65], lt, vs,
                                    start=(c4 == 0), stop=(c4 == j), waits=w)
            pv_par[par] = pv_tok
        rc_tok = {}
        for bk in obank_used:
            ngrp = min(7, nh64 - bk * 7)
            src = ctx.ps_g[bk][:, 0:ngrp * 65].rearrange(
                "p (g k) -> p g k", k=65)[:, :, 64]
            dst = recip_sb[:, bk * 7:bk * 7 + ngrp]
            rc_tok[bk] = ctx.dve(lambda e, o=dst, i=src: e.reciprocal(o, i),
                                 waits=[pv_par[0], pv_par[1]])
        for h in range(nh):
            nt = None
            for g in range(ng64):
                h64 = h * ng64 + g
                bk = h64 // 7
                ocol = (h64 % 7) * 65
                o_ap = otmn_sb[:, g * 64:(g + 1) * 64]
                i_ap = ctx.ps_g[bk][:, ocol:ocol + 64]
                r_ap = recip_sb[:, h64:h64 + 1]
                nt = ctx.dve(
                    lambda e, o=o_ap, i=i_ap, r=r_ap: e.tensor_scalar_mul(o, i, r),
                    waits=[rc_tok[bk], otmn_tok])
            slot = tr_i % 8
            tr_i += 1
            tr_out = ctx.ps_bf[0:dh, slot * 128:(slot + 1) * 128]
            otmn_tok = ctx.tr(tr_out, otmn_sb[:, 0:dh],
                              waits=[nt, tr_slot_tok[slot]])
            if dh == 128:
                dst = ofm4_sb[:, (j % 4) * 128:(j % 4) * 128 + 128]
            else:
                cc = h // hpc
                rb2 = (h % hpc) * dh
                dst = ofm4_sb[rb2:rb2 + dh,
                              cc * 512 + (j % 4) * 128: cc * 512 + (j % 4) * 128 + 128]
            tr_slot_tok[slot] = ctx.dve(
                lambda e, o=dst, i=tr_out: e.tensor_copy(o, i),
                waits=[otmn_tok, ctx.war.get("ofm4")])
        for bk in obank_used:
            ctx.war[f"ob{bk}"] = s.now("Vn")
        if j % 4 == 3:
            ctx.war["opx0"] = exp_tok[0]
            ctx.war["opx1"] = exp_tok[1]
            op_tok = oproj_fn(j // 4, s.now("Vn"))
            exp_tok[0] = op_tok
            exp_tok[1] = op_tok
            ctx.war["ofm4"] = op_tok


def emit_proj_to(ctx, w_sb, x_sb, nchunk, x_off, n, mcols, out_fn, waits=()):
    for m in range(mcols // 128):
        bank = ctx.ps_g[2] if (m % 2 == 0) else ctx.ps_st[:, 0:512]
        ps = bank[:, 0:n] if (m % 2 == 0) else ctx.ps_st[:, 0:n]
        t = None
        for cc in range(nchunk):
            w = list(waits) + [ctx.war.get("proj%d" % (m % 2))] if cc == 0 else []
            t = ctx.mm(ps, w_sb[:, cc * mcols + m * 128: cc * mcols + (m + 1) * 128],
                       x_sb[:, cc * n + x_off: cc * n + x_off + n],
                       start=(cc == 0), stop=(cc == nchunk - 1), waits=w)
        ev = out_fn(m, ps, t)
        ctx.war["proj%d" % (m % 2)] = ev


def emit_l0_attn(ctx, src_name, dst_name):
    s, c, sa = ctx.s, ctx.c, ctx.sa
    src = ctx.dram[src_name]
    dst = ctx.dram[dst_name]
    mark = sa.mark()
    xbuf = sa.alloc("p1_x", [128, DC * 512], F32)
    sqbuf = sa.alloc("p1_sq", [128, 512], BF16)
    rowbuf = sa.alloc("p1_row", [1, 5 * 512], F32)
    tmpbuf = sa.alloc("p1_tmp", [128, 512], F32)
    xhat = sa.alloc("p1_xhat", [128, DC * T], BF16)
    wq_sb = sa.alloc("p1_wq", [128, DC * 128], BF16)
    wk_sb = sa.alloc("p1_wk", [128, DC * 128], BF16)
    wv_sb = sa.alloc("p1_wv", [128, DC * 128], BF16)
    wo_sb = sa.alloc("p1_wo", [128, 1024], BF16)
    Q_sb = sa.alloc("p1_Q", [128, T], BF16)
    K_sb = sa.alloc("p1_K", [128, T], BF16)
    V_sb = sa.alloc("p1_V", [128, 16 * 130], BF16)
    PT_sb = [sa.alloc("p1_PT%d" % i, [128, 2048], BF16) for i in range(2)]
    dmask = [sa.alloc("p1_dm%d" % i, [128, 128], BF16) for i in range(2)]
    otmn = sa.alloc("p1_otmn", [128, 128], BF16)
    recip = sa.alloc("p1_rc", [128, 16], F32)
    ofm4 = sa.alloc("p1_ofm4", [128, 512], BF16)
    xres = [sa.alloc(f"p1_xres{i}", [128, 512], F32) for i in range(2)]
    ostg = [sa.alloc(f"p1_ostg{i}", [128, 512], F32) for i in range(4)]

    wq_t = ctx.dram["wq0"].rearrange("(c p) m -> p c m", p=128)
    wk_t = ctx.dram["wk0"].rearrange("(c p) m -> p c m", p=128)
    wv_t = ctx.dram["wv0"].rearrange("(c p) m -> p c m", p=128)
    wtok = [ctx.cast_load(wq_sb[:, :].rearrange("p (c m) -> p c m", m=128), wq_t),
            ctx.cast_load(wk_sb[:, :].rearrange("p (c m) -> p c m", m=128), wk_t),
            ctx.cast_load(wv_sb[:, :].rearrange("p (c m) -> p c m", m=128), wv_t),
            ctx.cast_load(wo_sb[:, :], ctx.dram["wo0"][:, :])]
    ones_tok = ctx.dve(lambda e: e.memset(V_sb[:, :], 1.0))

    for sq in range(NSEQ):
        base = sq * T
        xh_toks = []
        for tile in range(4):
            def dst_fn(cc, tile=tile):
                return xhat[:, cc * T + tile * 512: cc * T + tile * 512 + 512]
            ts_, _ = emit_ln(ctx, src, base + tile * 512, 512,
                             xbuf, sqbuf, rowbuf, tmpbuf, dst_fn)
            xh_toks += ts_
        xh_tok = s.now("Vn")
        qk_evs = []
        for (w_sb, dst_sb) in ((wq_sb, Q_sb), (wk_sb, K_sb)):
            for tile in range(4):
                ps = ctx.ps_g[2][:, 0:512] if tile % 2 == 0 else ctx.st[0][:, 0:512]
                t = None
                for cc in range(DC):
                    w = ([xh_tok, wtok[0], wtok[1], wtok[2],
                          ctx.war.get("qk%d" % (tile % 2))] if cc == 0 else [])
                    t = ctx.mm(ps, w_sb[:, cc * 128:(cc + 1) * 128],
                               xhat[:, cc * T + tile * 512: cc * T + tile * 512 + 512],
                               start=(cc == 0), stop=(cc == DC - 1), waits=w)
                o_ap = dst_sb[:, tile * 512: tile * 512 + 512]
                ev = ctx.act(lambda e, o=o_ap, i=ps: e.activation(o, i, AF.Copy),
                             waits=[t])
                ctx.war["qk%d" % (tile % 2)] = ev
                qk_evs.append(ev)
        v_evs = []
        for tc in range(16):
            ps = ctx.st[1 + (tc % 3)][:, 0:128]
            t = None
            for cc in range(DC):
                w = ([xh_tok, ctx.war.get("v%d" % (tc % 3))] if cc == 0 else [])
                t = ctx.mm(ps, xhat[:, cc * T + tc * 128: cc * T + (tc + 1) * 128],
                           wv_sb[:, cc * 128:(cc + 1) * 128],
                           start=(cc == 0), stop=(cc == DC - 1), waits=w)
            o_ap = V_sb[:, tc * 130: tc * 130 + 130].rearrange(
                "p (g k) -> p g k", k=65)[:, :, 0:64]
            i_ap = ps.rearrange("p (g k) -> p g k", k=64)
            ev = ctx.dve(lambda e, o=o_ap, i=i_ap: e.tensor_copy(o, i),
                         waits=[t, ones_tok])
            ctx.war["v%d" % (tc % 3)] = ev
            v_evs.append(ev)
        qk_tok = s.now("Cn")
        v_tok = s.now("Vn")

        def oproj(jg, ofm_tok, base=base):
            last = None
            for m in range(8):
                ps = ctx.st[m % 2][:, 0:512]
                t = ctx.mm(ps, wo_sb[:, m * 128:(m + 1) * 128], ofm4[:, 0:512],
                           start=True, stop=True,
                           waits=[ofm_tok, wtok[3], ctx.war.get("op%d" % (m % 2)),
                                  ctx.war.get("opx%d" % (m % 2))])
                xr = xres[m % 2]
                xt = ctx.load(xr[:, :], src[m * 128:(m + 1) * 128,
                                            base + jg * 512: base + jg * 512 + 512],
                              waits=[ctx.war.get("xres%d" % (m % 2))])
                og = ostg[(jg * 8 + m) % 4]
                ev = ctx.dve(lambda e, o=og[:, :], i=ps, x=xr[:, :]:
                             e.tensor_tensor(o, i, x, ALU.add),
                             waits=[t, xt, ctx.war.get("ostg%d" % ((jg * 8 + m) % 4))])
                st = ctx.store(dst[m * 128:(m + 1) * 128,
                                   base + jg * 512: base + jg * 512 + 512],
                               og[:, :], waits=[ev])
                ctx.war["ostg%d" % ((jg * 8 + m) % 4)] = st
                ctx.war["op%d" % (m % 2)] = ev
                ctx.war["xres%d" % (m % 2)] = ev
                last = ev
            return last

        emit_attn_core(ctx, 1, 128, Q_sb, K_sb, V_sb, 128.0 ** -0.5,
                       PT_sb, dmask, otmn, recip, ofm4, qk_tok, v_tok, oproj)
        s.barrier()
        ctx.war = {}
    sa.reset(mark)



def _prep_weights(inp):
    w = {}
    f32c = lambda a: np.ascontiguousarray(np.asarray(a), dtype=np.float32)
    w["wq0"] = f32c(inp["L0_attn_Wq"])
    w["wk0"] = f32c(inp["L0_attn_Wk"])
    w["wv0"] = f32c(inp["L0_attn_Wv"])
    w["wo0"] = f32c(inp["L0_attn_Wo"])
    w["w1t"] = f32c(np.asarray(inp["L0_ffn_W1"]).T)
    w["b1v"] = f32c(np.asarray(inp["L0_ffn_b1"]).reshape(32, 128).T)
    w["w2t"] = f32c(np.asarray(inp["L0_ffn_W2"]).T)
    w["b2v"] = f32c(np.asarray(inp["L0_ffn_b2"]).reshape(8, 128).T)
    HEADS = 16
    hd = D // HEADS
    perm = np.arange(D).reshape(hd, HEADS).T.reshape(-1)
    w["qw1"] = f32c(np.asarray(inp["L1_attn_qw"])[:, perm])
    w["kw1"] = f32c(np.asarray(inp["L1_attn_kw"])[:, perm])
    w["vw1"] = f32c(np.asarray(inp["L1_attn_vw"])[:, perm])
    w["ow1"] = f32c(np.asarray(inp["L1_attn_ow"])[perm, :])
    w["ln1v"] = f32c(np.asarray(inp["L1_ffn_ln1_w"]).reshape(8, 128).T)
    w["ln2v"] = f32c(np.asarray(inp["L1_ffn_ln2_w"]).reshape(8, 128).T)
    w["kvdt"] = f32c(np.asarray(inp["L1_ffn_kv_d"]).T)
    w["qdt"] = f32c(np.asarray(inp["L1_ffn_q_d"]).T)
    w["vut"] = f32c(np.asarray(inp["L1_ffn_v_u"]).T)
    rq = np.asarray(inp["L1_ffn_rope_q_w"], np.float32)
    rk = np.asarray(inp["L1_ffn_rope_k_w"], np.float32)
    qu_full = np.asarray(inp["L1_ffn_q_u"], np.float32)
    ku_full = np.asarray(inp["L1_ffn_k_u"], np.float32)
    pj = (np.arange(32) + 16) % 32
    qmix = np.zeros((1024, 256), np.float32)
    qmixp = np.zeros((1024, 256), np.float32)
    kmixn = np.zeros((1024, 256), np.float32)
    kmixr = np.zeros((1024, 1024), np.float32)
    kmixrp = np.zeros((1024, 1024), np.float32)
    for h in range(16):
        rows = slice(h * 64, h * 64 + 32)
        qmix[h * 64:h * 64 + 32] = rq[h * 32:h * 32 + 32]
        qmix[h * 64 + 32:h * 64 + 64] = qu_full[h * 64 + 32:h * 64 + 64]
        qmixp[h * 64:h * 64 + 32] = rq[h * 32 + pj]
        kmixn[h * 64 + 32:h * 64 + 64] = ku_full[h * 64 + 32:h * 64 + 64]
        kmixr[h * 64:h * 64 + 32] = rk[h * 32:h * 32 + 32]
        kmixrp[h * 64:h * 64 + 32] = rk[h * 32 + pj]
    w["qmix"] = f32c(qmix.T)
    w["qmixp"] = f32c(qmixp.T)
    w["kmixn"] = f32c(kmixn.T)
    w["kmixr"] = f32c(kmixr.T)
    w["kmixrp"] = f32c(kmixrp.T)
    w["owft"] = f32c(np.asarray(inp["L1_ffn_o_w"]).T)
    w["rwt"] = f32c(np.asarray(inp["L1_ffn_router_w"]).T)
    w["rbias"] = f32c(np.asarray(inp["L1_ffn_routing_bias"]).reshape(1, 7))
    ga = np.concatenate([np.asarray(inp["L1_ffn_shared_gate"]),
                         np.asarray(inp["L1_ffn_routed_gate"])], 0)
    ua = np.concatenate([np.asarray(inp["L1_ffn_shared_up"]),
                         np.asarray(inp["L1_ffn_routed_up"])], 0)
    da = np.concatenate([np.asarray(inp["L1_ffn_shared_down"]),
                         np.asarray(inp["L1_ffn_routed_down"])], 0)
    w["gall"] = f32c(ga.reshape(8, 16, 128, 8, 128).transpose(0, 1, 4, 3, 2))
    w["uall"] = f32c(ua.reshape(8, 16, 128, 8, 128).transpose(0, 1, 4, 3, 2))
    w["dall"] = f32c(da.reshape(8, 8, 128, 16, 128).transpose(0, 1, 4, 3, 2))

    def tile4(mat):
        return f32c(np.ascontiguousarray(
            np.asarray(mat, np.float32).reshape(8, 128, 8, 128).transpose(2, 1, 0, 3)))
    w["qw1"] = tile4(w["qw1"])
    w["kw1"] = tile4(w["kw1"])
    w["ow1"] = tile4(w["ow1"])
    w["kmixr"] = tile4(w["kmixr"])
    w["kmixrp"] = tile4(w["kmixrp"])
    w["owft"] = tile4(w["owft"])
    w["vw1"] = f32c(np.asarray(w["vw1"]).reshape(8, 128, 2, 512).transpose(2, 1, 0, 3))
    return w


WSHAPES = {
    "wq0": ([1024, 128], F32), "wk0": ([1024, 128], F32), "wv0": ([1024, 128], F32),
    "wo0": ([128, 1024], F32), "w1t": ([1024, 4096], F32), "b1v": ([128, 32], F32),
    "w2t": ([4096, 1024], F32), "b2v": ([128, 8], F32),
    "qw1": ([8, 128, 8, 128], F32), "kw1": ([8, 128, 8, 128], F32),
    "vw1": ([2, 128, 8, 512], F32), "ow1": ([8, 128, 8, 128], F32),
    "ln1v": ([128, 8], F32), "ln2v": ([128, 8], F32),
    "kvdt": ([1024, 256], F32), "qdt": ([1024, 256], F32),
    "vut": ([256, 1024], F32),
    "qmix": ([256, 1024], F32), "qmixp": ([256, 1024], F32),
    "kmixn": ([256, 1024], F32),
    "kmixr": ([8, 128, 8, 128], F32), "kmixrp": ([8, 128, 8, 128], F32),
    "owft": ([8, 128, 8, 128], F32), "rwt": ([1024, 7], F32), "rbias": ([1, 7], F32),
    "gall": ([8, 16, 128, 8, 128], F32), "uall": ([8, 16, 128, 8, 128], F32),
    "dall": ([8, 8, 128, 16, 128], F32),
}

ACT_BUFS = {
    "XT": None, "RX1": None, "RX2": None, "RX3": None, "Y": None, "Y1": None,
    "OUT": None,
}

PHASES = {
    "P0c": (lambda ctx: emit_precast(ctx),
            ["gall", "uall", "dall", "qw1", "kw1", "vw1", "ow1",
             "kmixr", "kmixrp", "owft"], [], []),
    "P1": (lambda ctx: emit_l0_attn(ctx, "XT", "RX1"),
           ["wq0", "wk0", "wv0", "wo0"], ["XT"], ["RX1"]),
    "P2": (lambda ctx: emit_l0_ffn(ctx, "RX1", "RX2"),
           ["w1t", "b1v", "w2t", "b2v"], ["RX1"], ["RX2"]),
    "P3": (lambda ctx: emit_l1_attn(ctx, "RX2", "RX3"),
           [], ["RX2"], ["RX3"]),
    "P4": (lambda ctx: emit_outer_ln(ctx, "RX3", "Y"),
           [], ["RX3"], ["Y"]),
    "P5": (lambda ctx: emit_mla(ctx, "Y", "Y1"),
           ["ln1v", "qmix", "qmixp", "kmixn", "kvdt", "qdt", "vut"],
           ["Y"], ["Y1"]),
    "P6": (lambda ctx: emit_moe(ctx, "RX3", "Y1", "OUT"),
           ["ln2v", "rwt", "rbias"], ["RX3", "Y1"], ["OUT"]),
}

SCRATCH_BUFS = {
    "WQ1B": ([1024, 4096], BF16),
    "WQ2B": ([8, 32, 128, 128], BF16),
    "GB": ([8, 16, 128, 1024], BF16), "UB": ([8, 16, 128, 1024], BF16),
    "DB": ([8, 8, 128, 2048], BF16),
    "QWB": ([8, 128, 1024], BF16), "KWB": ([8, 128, 1024], BF16),
    "VWB": ([2, 128, 4096], BF16), "OWB": ([8, 128, 1024], BF16),
    "KRB": ([8, 128, 1024], BF16), "KRPB": ([8, 128, 1024], BF16),
    "OWFB": ([8, 128, 1024], BF16),
}
SCRATCH_OF = {"P2": ["WQ1B", "WQ2B"],
              "P0c": ["GB", "UB", "DB", "QWB", "KWB", "VWB", "OWB",
                      "KRB", "KRPB", "OWFB"],
              "P3": ["QWB", "KWB", "VWB", "OWB"],
              "P5": ["KRB", "KRPB", "OWFB"],
              "P6": ["GB", "UB", "DB"]}


def build_launch(phases, ext_in_bufs, ext_out_bufs):
    wnames = sorted({w for p in phases for w in PHASES[p][1]})
    bufs = sorted({b for p in phases for b in PHASES[p][2] + PHASES[p][3]})
    inputs = [(n,) + tuple(WSHAPES[n]) for n in wnames]
    inputs = [(n, WSHAPES[n][0], WSHAPES[n][1]) for n in wnames]
    for b in bufs:
        if b in ext_in_bufs:
            inputs.append((b, [D, NTOK], F32))
    outputs = [(b, [D, NTOK], F32) for b in bufs if b in ext_out_bufs]
    scratch = [(b, [D, NTOK], F32) for b in bufs
               if b not in ext_in_bufs and b not in ext_out_bufs]
    seen = set()
    for p in phases:
        for sb in SCRATCH_OF.get(p, []):
            if sb not in seen:
                seen.add(sb)
                scratch.append((sb,) + SCRATCH_BUFS[sb])
    ctx = Ctx(inputs, scratch, outputs)
    for p in phases:
        PHASES[p][0](ctx)
        ctx.s.barrier()
        ctx.war = {}
    ctx.s.build()
    return ctx, wnames


def _run_launch(phases, ext_inputs_per_core, out_names):
    ctx, wnames = build_launch(
        phases,
        ext_in_bufs={n for n in ext_inputs_per_core[0]
                     if n in ACT_BUFS},
        ext_out_bufs=set(out_names))
    res = run_bass_kernel_spmd(ctx.nc, ext_inputs_per_core,
                               core_ids=list(range(len(ext_inputs_per_core))))
    return res.results


ALL_PHASES = ["P0c", "P1", "P2", "P3", "P4", "P5", "P6"]


def kernel(**inputs):
    w = _prep_weights(inputs)
    x = np.asarray(inputs["L0_attn_x"], dtype=np.float32)
    n_cores = 8
    in_maps = []
    for cc in range(n_cores):
        shard = x[cc * NSEQ:(cc + 1) * NSEQ]
        xt = np.ascontiguousarray(shard.reshape(NTOK, D).T)
        m = dict(w)
        m["XT"] = xt
        in_maps.append(m)
    results = _run_launch(ALL_PHASES, in_maps, ["OUT"])
    outs = []
    for cc in range(n_cores):
        o = results[cc]["OUT"]
        outs.append(np.ascontiguousarray(o.T).reshape(NSEQ, T, D))
    return np.concatenate(outs, 0)



def emit_quantize(ctx, w_name, rows, cols, store_fn):
    s, c, sa = ctx.s, ctx.c, ctx.sa
    src = ctx.dram[w_name]
    nch = rows // 128
    mark = sa.mark()
    wbuf = sa.alloc("qz_w", [128, cols], F32)
    abuf = sa.alloc("qz_a", [128, cols], F32)
    grow = sa.alloc("qz_g", [1, 8], F32)
    gvec = sa.alloc("qz_gv", [128, 1], F32)
    qbuf = sa.alloc("qz_q", [128, cols], BF16)
    ps = ctx.ps_g[0]
    mm_tok = None
    ld_toks = []
    for ch in range(nch):
        lt = ctx.load(wbuf[:, :], src[ch * 128:(ch + 1) * 128, :],
                      waits=[ctx.war.get("qzw")])
        ab = ctx.act(lambda e, o=abuf[:, :], i=wbuf[:, :]:
                     e.activation(o, i, AF.Abs), waits=[lt, mm_tok])
        ctx.war["qzw"] = ab
        for sub in range(cols // 512):
            mm_tok = ctx.mm(ps[0:1, 0:512], c["ones_f"][:, :],
                            abuf[:, sub * 512:(sub + 1) * 512],
                            start=(ch == 0 and sub == 0),
                            stop=(ch == nch - 1 and sub == cols // 512 - 1),
                            force_inc=True, waits=[ab])
    t1 = ctx.dve(lambda e: e.reduce_sum(grow[0:1, 0:1], ps[0:1, 0:512], AX.X),
                 waits=[mm_tok])
    t2 = ctx.dve(lambda e: e.tensor_scalar(grow[0:1, 1:2], grow[0:1, 0:1],
                                           1.0 / (rows * cols), EPS,
                                           ALU.mult, ALU.add), waits=[t1])
    t3 = ctx.dve(lambda e: e.reciprocal(grow[0:1, 2:3], grow[0:1, 1:2]),
                 waits=[t2])
    bc = ctx.mm(ctx.ps_g[1][0:128, 0:1], c["ones_r"][:, :], grow[0:1, 2:3],
                start=True, stop=True, waits=[t3, ctx.war.get("g1")])
    t4 = ctx.dve(lambda e: e.tensor_copy(gvec[:, :], ctx.ps_g[1][:, 0:1]),
                 waits=[bc])
    prev_q = None
    for ch in range(nch):
        lt = ctx.load(wbuf[:, :], src[ch * 128:(ch + 1) * 128, :],
                      waits=[ctx.war.get("qzw")])
        a1 = ctx.dve(lambda e, o=abuf[:, :], i=wbuf[:, :]:
                     e.tensor_scalar(o, i, gvec[:, :], MAGIC, ALU.mult, ALU.add),
                     waits=[lt, t4, prev_q])
        ctx.war["qzw"] = a1
        a2 = ctx.dve(lambda e, o=abuf[:, :], i=abuf[:, :]:
                     e.tensor_scalar(o, i, MAGIC, None, ALU.subtract),
                     waits=[a1])
        a3 = ctx.dve(lambda e, o=qbuf[:, :], i=abuf[:, :]:
                     e.tensor_scalar(o, i, 1.0, -1.0, ALU.min, ALU.max),
                     waits=[a2, prev_q])
        prev_q = store_fn(ch, qbuf, a3)
    s.barrier()
    ctx.war = {}
    sa.reset(mark)


def emit_l0_ffn(ctx, src_name, dst_name):
    s, c, sa = ctx.s, ctx.c, ctx.sa
    src = ctx.dram[src_name]
    dst = ctx.dram[dst_name]
    wq1d = ctx.dram["WQ1B"]
    wq2d = ctx.dram["WQ2B"]

    def store1(ch, qb, tok):
        return ctx.store(wq1d[ch * 128:(ch + 1) * 128, :], qb[:, :], waits=[tok])

    def store2(ch, qb, tok):
        d = wq2d[:, ch, :, :].rearrange("m p j -> p m j")
        sp = qb[:, :].rearrange("p (m j) -> p m j", j=128)
        return ctx.store(d, sp, waits=[tok])

    emit_quantize(ctx, "w1t", 1024, 4096, store1)
    emit_quantize(ctx, "w2t", 4096, 1024, store2)

    mark = sa.mark()
    xbuf = sa.alloc("p2_x", [128, DC * 512], F32)
    sqbuf = sa.alloc("p2_sq", [128, 512], BF16)
    rowbuf = sa.alloc("p2_row", [1, 5 * 512], F32)
    tmpbuf = sa.alloc("p2_tmp", [128, 512], F32)
    xhat = sa.alloc("p2_xhat", [128, DC * 512], BF16)
    wq1_sb = sa.alloc("p2_wq1", [128, DC * 4096], BF16)
    h_sb = sa.alloc("p2_h", [128, 32 * 512], BF16)
    wq2_sb = [sa.alloc(f"p2_wq2_{i}", [128, 32 * 128], BF16) for i in range(2)]
    b1_sb = sa.alloc("p2_b1", [128, 32], F32)
    b2_sb = sa.alloc("p2_b2", [128, 8], F32)
    ostg = [sa.alloc(f"p2_ostg{i}", [128, 512], F32) for i in range(4)]

    for ch in range(DC):
        ctx.load(wq1_sb[:, ch * 4096:(ch + 1) * 4096],
                 wq1d[ch * 128:(ch + 1) * 128, :])
    ctx.load(b1_sb[:, :], ctx.dram["b1v"][:, :])
    ctx.load(b2_sb[:, :], ctx.dram["b2v"][:, :])
    s.barrier()

    for tile in range(NTOK // 512):
        col0 = tile * 512

        def dst_fn(cc):
            return xhat[:, cc * 512:(cc + 1) * 512]
        emit_ln(ctx, src, col0, 512, xbuf, sqbuf, rowbuf, tmpbuf, dst_fn)
        xh_tok = s.now("Vn")
        for mf in range(32):
            ps = ctx.st[mf % 2]
            t = None
            for cc in range(DC):
                w = ([xh_tok, ctx.war.get("h%d" % (mf % 2))] if cc == 0 else [])
                t = ctx.mm(ps[:, 0:512],
                           wq1_sb[:, cc * 4096 + mf * 128: cc * 4096 + (mf + 1) * 128],
                           xhat[:, cc * 512:(cc + 1) * 512],
                           start=(cc == 0), stop=(cc == DC - 1), waits=w)
            o_ap = h_sb[:, mf * 512:(mf + 1) * 512]
            ev = ctx.act(lambda e, o=o_ap, i=ps[:, 0:512], b=b1_sb[:, mf:mf + 1]:
                         e.activation(o, i, AF.Relu, bias=b), waits=[t])
            ctx.war["h%d" % (mf % 2)] = ev
        h_tok = s.now("Cn")
        for m in range(8):
            slot = wq2_sb[m % 2]
            d = wq2d[m].rearrange("k p j -> p k j")
            wt = ctx.load(slot[:, :].rearrange("p (k j) -> p k j", j=128), d,
                          waits=[ctx.war.get("wq2%d" % (m % 2))])
            ps = ctx.st[2 + m % 2]
            t = None
            for k in range(32):
                w = ([h_tok, wt, ctx.war.get("o%d" % (m % 2))] if k == 0 else [])
                t = ctx.mm(ps[:, 0:512], slot[:, k * 128:(k + 1) * 128],
                           h_sb[:, k * 512:(k + 1) * 512],
                           start=(k == 0), stop=(k == 31), waits=w)
            ctx.war["wq2%d" % (m % 2)] = t
            og = ostg[m % 4]
            ev = ctx.dve(lambda e, o=og[:, :], i=ps[:, 0:512],
                         b=b2_sb[:, m:m + 1], x=xbuf[:, m * 512:(m + 1) * 512]:
                         e.scalar_tensor_tensor(o, i, b, x, ALU.add, ALU.add),
                         waits=[t, ctx.war.get("ostg%d" % (m % 4))])
            ctx.war["o%d" % (m % 2)] = ev
            ctx.war[f"lnx{m}"] = ev
            st = ctx.store(dst[m * 128:(m + 1) * 128, col0:col0 + 512],
                           og[:, :], waits=[ev])
            ctx.war["ostg%d" % (m % 4)] = st
        ctx.war["h0"] = s.now("Tn")
        ctx.war["h1"] = s.now("Tn")
    s.barrier()
    ctx.war = {}
    sa.reset(mark)



def emit_l1_attn(ctx, src_name, dst_name):
    s, c, sa = ctx.s, ctx.c, ctx.sa
    src = ctx.dram[src_name]
    dst = ctx.dram[dst_name]
    mark = sa.mark()
    xbuf = sa.alloc("p3_x", [128, DC * 512], F32)
    sqbuf = sa.alloc("p3_sq", [128, 512], BF16)
    rowbuf = sa.alloc("p3_row", [1, 5 * 512], F32)
    tmpbuf = sa.alloc("p3_tmp", [128, 512], F32)
    xhat = sa.alloc("p3_xhat", [128, DC * T], BF16)
    Q_sb = sa.alloc("p3_Q", [128, DC * T], BF16)
    K_sb = sa.alloc("p3_K", [128, DC * T], BF16)
    V_sb = sa.alloc("p3_V", [128, 16 * 1040], BF16)
    vw_sb = sa.alloc("p3_vw", [128, DC * 512], BF16)
    qkw_sb = [sa.alloc(f"p3_qkw{i}", [128, DC * 128], BF16) for i in range(2)]
    ow_sb = [sa.alloc(f"p3_ow{i}", [128, DC * 128], BF16) for i in range(2)]
    ofm4 = sa.alloc("p3_ofm4", [128, DC * 512], BF16)
    PT_sb = [sa.alloc("p3_PT%d" % i, [128, 2048], BF16) for i in range(2)]
    dmask = [sa.alloc("p3_dm%d" % i, [128, 128], BF16) for i in range(2)]
    otmn = sa.alloc("p3_otmn", [128, 128], BF16)
    recip = sa.alloc("p3_rc", [128, 16], F32)
    xres = [sa.alloc(f"p3_xres{i}", [128, 512], F32) for i in range(2)]
    ostg = [sa.alloc(f"p3_ostg{i}", [128, 512], F32) for i in range(2)]

    qw_d = ctx.dram["QWB"]
    kw_d = ctx.dram["KWB"]
    vw_d = ctx.dram["VWB"]
    ow_d = ctx.dram["OWB"]
    ones_tok = ctx.dve(lambda e: e.memset(V_sb[:, :], 1.0))

    for sq in range(NSEQ):
        base = sq * T
        for tile in range(4):
            def dst_fn(cc, tile=tile):
                return xhat[:, cc * T + tile * 512: cc * T + tile * 512 + 512]
            emit_ln(ctx, src, base + tile * 512, 512,
                    xbuf, sqbuf, rowbuf, tmpbuf, dst_fn)
        xh_tok = s.now("Vn")
        for wi, (wd, dst_sb) in enumerate(((qw_d, Q_sb), (kw_d, K_sb))):
            for mc in range(DC):
                slot = qkw_sb[mc % 2]
                wt = ctx.load(slot[:, :], wd[mc],
                              waits=[ctx.war.get("qkw%d" % (mc % 2))])
                for tile in range(4):
                    pi = (mc * 4 + tile) % 2
                    ps = ctx.st[pi]
                    t = None
                    for cc in range(DC):
                        w = ([xh_tok, wt, ctx.war.get("qkps%d" % pi)]
                             if cc == 0 else [])
                        t = ctx.mm(ps[:, 0:512], slot[:, cc * 128:(cc + 1) * 128],
                                   xhat[:, cc * T + tile * 512: cc * T + tile * 512 + 512],
                                   start=(cc == 0), stop=(cc == DC - 1), waits=w)
                    o_ap = dst_sb[:, mc * T + tile * 512: mc * T + tile * 512 + 512]
                    ev = ctx.act(lambda e, o=o_ap, i=ps[:, 0:512]:
                                 e.activation(o, i, AF.Copy), waits=[t])
                    ctx.war["qkps%d" % pi] = ev
                ctx.war["qkw%d" % (mc % 2)] = s.now("Tn")
        for dvs in range(2):
            vt = ctx.load(vw_sb[:, :], vw_d[dvs], waits=[ctx.war.get("vw")])
            for tc in range(16):
                ps = ctx.st[2 + tc % 2]
                t = None
                for cc in range(DC):
                    w = ([xh_tok, vt, ctx.war.get("vps%d" % (tc % 2))]
                         if cc == 0 else [])
                    t = ctx.mm(ps[:, 0:512],
                               xhat[:, cc * T + tc * 128: cc * T + (tc + 1) * 128],
                               vw_sb[:, cc * 512:(cc + 1) * 512],
                               start=(cc == 0), stop=(cc == DC - 1), waits=w)
                o_ap = V_sb[:, tc * 1040 + dvs * 520: tc * 1040 + dvs * 520 + 520]
                o_ap = o_ap.rearrange("p (g k) -> p g k", k=65)[:, :, 0:64]
                i_ap = ps[:, 0:512].rearrange("p (g k) -> p g k", k=64)
                ev = ctx.dve(lambda e, o=o_ap, i=i_ap: e.tensor_copy(o, i),
                             waits=[t, ones_tok])
                ctx.war["vps%d" % (tc % 2)] = ev
            ctx.war["vw"] = s.now("Tn")
        qk_tok = s.now("Cn")
        v_tok = s.now("Vn")

        def oproj(jg, ofm_tok, base=base):
            last = None
            for m in range(8):
                slot = ow_sb[m % 2]
                wt = ctx.load(slot[:, :], ow_d[m],
                              waits=[ctx.war.get("oww%d" % (m % 2))])
                ps = ctx.st[m % 2]
                t = None
                for cc in range(DC):
                    w = ([ofm_tok, wt, ctx.war.get("op%d" % (m % 2)),
                          ctx.war.get("opx%d" % (m % 2))] if cc == 0 else [])
                    t = ctx.mm(ps[:, 0:512], slot[:, cc * 128:(cc + 1) * 128],
                               ofm4[:, cc * 512:(cc + 1) * 512],
                               start=(cc == 0), stop=(cc == DC - 1), waits=w)
                ctx.war["oww%d" % (m % 2)] = t
                xr = xres[m % 2]
                xt = ctx.load(xr[:, :], src[m * 128:(m + 1) * 128,
                                            base + jg * 512: base + jg * 512 + 512],
                              waits=[ctx.war.get("xres%d" % (m % 2))])
                og = ostg[m % 2]
                ev = ctx.dve(lambda e, o=og[:, :], i=ps[:, 0:512], x=xr[:, :]:
                             e.tensor_tensor(o, i, x, ALU.add),
                             waits=[t, xt, ctx.war.get("ostg%d" % (m % 2))])
                st = ctx.store(dst[m * 128:(m + 1) * 128,
                                   base + jg * 512: base + jg * 512 + 512],
                               og[:, :], waits=[ev])
                ctx.war["ostg%d" % (m % 2)] = st
                ctx.war["op%d" % (m % 2)] = ev
                ctx.war["xres%d" % (m % 2)] = ev
                last = ev
            return last

        emit_attn_core(ctx, 16, 64, Q_sb, K_sb, V_sb, float(D) ** -0.5,
                       PT_sb, dmask, otmn, recip, ofm4, qk_tok, v_tok, oproj)
        s.barrier()
        ctx.war = {}
    sa.reset(mark)



def emit_outer_ln(ctx, src_name, dst_name):
    s, sa = ctx.s, ctx.sa
    src = ctx.dram[src_name]
    dst = ctx.dram[dst_name]
    mark = sa.mark()
    xbuf = sa.alloc("p4_x", [128, DC * 512], F32)
    sqbuf = sa.alloc("p4_sq", [128, 512], BF16)
    rowbuf = sa.alloc("p4_row", [1, 5 * 512], F32)
    tmpbuf = sa.alloc("p4_tmp", [128, 512], F32)
    ybuf = sa.alloc("p4_y", [128, DC * 512], F32)
    for tile in range(NTOK // 512):
        col0 = tile * 512

        def dst_fn(cc):
            return ybuf[:, cc * 512:(cc + 1) * 512]
        toks, _ = emit_ln(ctx, src, col0, 512, xbuf, sqbuf, rowbuf, tmpbuf,
                          dst_fn, dst_war=lambda cc: ctx.war.get(f"p4y{cc}"))
        for cc in range(DC):
            st = ctx.store(dst[cc * 128:(cc + 1) * 128, col0:col0 + 512],
                           ybuf[:, cc * 512:(cc + 1) * 512], waits=[toks[cc]])
            ctx.war[f"p4y{cc}"] = st
    s.barrier()
    ctx.war = {}
    sa.reset(mark)



def emit_mla(ctx, src_name, dst_name):
    s, c, sa = ctx.s, ctx.c, ctx.sa
    src = ctx.dram[src_name]
    dst = ctx.dram[dst_name]
    mark = sa.mark()
    K_sb = sa.alloc("p5_K", [128, DC * T], BF16)
    Q_sb = sa.alloc("p5_Q", [128, DC * T], BF16)
    V_sb = sa.alloc("p5_V", [128, 16 * 1040], BF16)
    qmix_sb = sa.alloc("p5_qmix", [128, 2 * 1024], BF16)
    qmixp_sb = sa.alloc("p5_qmixp", [128, 2 * 1024], BF16)
    kmixn_sb = sa.alloc("p5_kmixn", [128, 2 * 1024], BF16)
    kr_sb = [sa.alloc(f"p5_kr{i}", [128, DC * 128], BF16) for i in range(2)]
    krp_sb = [sa.alloc(f"p5_krp{i}", [128, DC * 128], BF16) for i in range(2)]
    vut_sb = sa.alloc("p5_vut", [128, 2 * 512], BF16)
    cmask = sa.alloc("p5_cm", [128, T], BF16)
    smask = sa.alloc("p5_sm", [128, T], BF16)
    ln1v_sb = sa.alloc("p5_ln1", [128, 8], F32)
    seq_mark = sa.mark()

    cm = np.ones((128, T), np.float32)
    sm = np.zeros((128, T), np.float32)
    cm[0:32] = ctx.rope_cos[0:32]
    cm[64:96] = ctx.rope_cos[0:32]
    sm[0:32] = ctx.rope_sin[0:32]
    sm[64:96] = ctx.rope_sin[0:32]
    cm_d = ctx.nc.inline_tensor(cm.astype(BF), name="p5cm").ap()
    sm_d = ctx.nc.inline_tensor(sm.astype(BF), name="p5sm").ap()
    ctx.load(cmask[:, :], cm_d[:, :])
    ctx.load(smask[:, :], sm_d[:, :])
    ctx.load(ln1v_sb[:, :], ctx.dram["ln1v"][:, :])
    for (sb, nm, m) in ((qmix_sb, "qmix", 1024), (qmixp_sb, "qmixp", 1024),
                        (kmixn_sb, "kmixn", 1024)):
        ctx.cast_load(sb[:, :].rearrange("p (cc m) -> p cc m", m=m),
                      ctx.dram[nm].rearrange("(cc p) m -> p cc m", p=128))
    kvd_d = ctx.dram["kvdt"].rearrange("(cc p) m -> p cc m", p=128)
    qd_d = ctx.dram["qdt"].rearrange("(cc p) m -> p cc m", p=128)
    kr_d = ctx.dram["KRB"]
    krp_d = ctx.dram["KRPB"]
    vut_d = ctx.dram["vut"].rearrange("(cc p) m -> p cc m", p=128)
    ow_d = ctx.dram["OWFB"]
    s.barrier()

    yh = sa.alloc("p5_yh", [128, DC * T], BF16)
    kvlat = sa.alloc("p5_kvl", [128, 2 * T], BF16)
    qlat = sa.alloc("p5_ql", [128, 2 * T], BF16)
    ln_mark = sa.mark()
    xbuf = sa.alloc("p5_x", [128, DC * 512], F32)
    sqbuf = sa.alloc("p5_sq", [128, 512], BF16)
    rowbuf = sa.alloc("p5_row", [1, 5 * 512], F32)
    tmpbuf = sa.alloc("p5_tmp", [128, 512], F32)
    sa.reset(ln_mark)
    rt1 = sa.alloc("p5_rt1", [128, 512], F32)
    rt2 = sa.alloc("p5_rt2", [128, 512], F32)
    sa.reset(ln_mark)
    PT_sb = [sa.alloc("p5_PT%d" % i, [128, 2048], BF16) for i in range(2)]
    ofm4 = sa.alloc("p5_ofm4", [128, DC * 512], BF16)
    dmask = [sa.alloc("p5_dm%d" % i, [128, 128], BF16) for i in range(2)]
    otmn = sa.alloc("p5_otmn", [128, 128], BF16)
    recip = sa.alloc("p5_rc", [128, 16], F32)
    xres = [sa.alloc(f"p5_xres{i}", [128, 512], F32) for i in range(2)]
    ostg = [sa.alloc(f"p5_ostg{i}", [128, 512], F32) for i in range(2)]

    for sq in range(NSEQ):
        base = sq * T
        for tile in range(4):
            def dst_fn(cc, tile=tile):
                return yh[:, cc * T + tile * 512: cc * T + tile * 512 + 512]
            emit_ln(ctx, src, base + tile * 512, 512, xbuf, sqbuf, rowbuf,
                    tmpbuf, dst_fn,
                    gain_col=lambda cc: ln1v_sb[:, cc:cc + 1])
        yh_tok = s.now("Vn")
        for wi, (wd, dst_sb) in enumerate(((kvd_d, kvlat), (qd_d, qlat))):
            for m2 in range(2):
                slot = kr_sb[(wi * 2 + m2) % 2]
                wt = ctx.cast_load(
                    slot[:, :].rearrange("p (cc m) -> p cc m", m=128),
                    wd[:, :, m2 * 128:(m2 + 1) * 128],
                    waits=[ctx.war.get("krw%d" % ((wi * 2 + m2) % 2))])
                for tile in range(4):
                    pi = tile % 2
                    ps = ctx.st[pi]
                    t = None
                    for cc in range(DC):
                        w = ([yh_tok, wt, ctx.war.get("lat%d" % pi)]
                             if cc == 0 else [])
                        t = ctx.mm(ps[:, 0:512],
                                   slot[:, cc * 128:(cc + 1) * 128],
                                   yh[:, cc * T + tile * 512: cc * T + tile * 512 + 512],
                                   start=(cc == 0), stop=(cc == DC - 1), waits=w)
                    o_ap = dst_sb[:, m2 * T + tile * 512: m2 * T + tile * 512 + 512]
                    ev = ctx.act(lambda e, o=o_ap, i=ps[:, 0:512]:
                                 e.activation(o, i, AF.Copy), waits=[t])
                    ctx.war["lat%d" % pi] = ev
                ctx.war["krw%d" % ((wi * 2 + m2) % 2)] = s.now("Tn")
        lat_tok = s.now("Cn")
        asm_guard = s.now("Vn")
        for cc in range(DC):
            krt = ctx.load(kr_sb[cc % 2][:, :], kr_d[cc],
                           waits=[ctx.war.get("krw%d" % (cc % 2))])
            krpt = ctx.load(krp_sb[cc % 2][:, :], krp_d[cc],
                            waits=[ctx.war.get("krpw%d" % (cc % 2))])
            for tile in range(4):
                ps1 = ctx.st[tile % 2]
                ps2 = ctx.st[2 + tile % 2]
                t1 = None
                for k8 in range(DC):
                    w = ([yh_tok, lat_tok, krt, ctx.war.get("kps%d" % (tile % 2))]
                         if k8 == 0 else [])
                    t1 = ctx.mm(ps1[:, 0:512],
                                kr_sb[cc % 2][:, k8 * 128:(k8 + 1) * 128],
                                yh[:, k8 * T + tile * 512: k8 * T + tile * 512 + 512],
                                start=(k8 == 0), stop=False, waits=w)
                for lc in range(2):
                    t1 = ctx.mm(ps1[:, 0:512],
                                kmixn_sb[:, lc * 1024 + cc * 128: lc * 1024 + (cc + 1) * 128],
                                kvlat[:, lc * T + tile * 512: lc * T + tile * 512 + 512],
                                start=False, stop=(lc == 1), waits=[])
                t2 = None
                for k8 in range(DC):
                    w = ([krpt, ctx.war.get("kps2%d" % (tile % 2))] if k8 == 0 else [])
                    t2 = ctx.mm(ps2[:, 0:512],
                                krp_sb[cc % 2][:, k8 * 128:(k8 + 1) * 128],
                                yh[:, k8 * T + tile * 512: k8 * T + tile * 512 + 512],
                                start=(k8 == 0), stop=(k8 == DC - 1), waits=w)
                m1 = ctx.dve(lambda e, o=rt1[:, :], i=ps1[:, 0:512], t=tile:
                             e.tensor_tensor(o, i, cmask[:, t * 512:(t + 1) * 512],
                                             ALU.mult),
                             waits=[t1, asm_guard, ctx.war.get("rt1")])
                m2_ = ctx.dve(lambda e, o=rt2[:, :], i=ps2[:, 0:512], t=tile:
                              e.tensor_tensor(o, i, smask[:, t * 512:(t + 1) * 512],
                                              ALU.mult),
                              waits=[t2, asm_guard, ctx.war.get("rt2")])
                kd = K_sb[:, cc * T + tile * 512: cc * T + tile * 512 + 512]
                m3 = ctx.dve(lambda e, o=kd: e.tensor_tensor(o, rt1[:, :], rt2[:, :],
                                                             ALU.add),
                             waits=[m1, m2_])
                ctx.war["kps%d" % (tile % 2)] = m1
                ctx.war["kps2%d" % (tile % 2)] = m2_
                t1 = None
                t2 = None
                for lc in range(2):
                    w = ([lat_tok, m1] if lc == 0 else [])
                    t1 = ctx.mm(ps1[:, 0:512],
                                qmix_sb[:, lc * 1024 + cc * 128: lc * 1024 + (cc + 1) * 128],
                                qlat[:, lc * T + tile * 512: lc * T + tile * 512 + 512],
                                start=(lc == 0), stop=(lc == 1), waits=w)
                    w2 = ([m2_] if lc == 0 else [])
                    t2 = ctx.mm(ps2[:, 0:512],
                                qmixp_sb[:, lc * 1024 + cc * 128: lc * 1024 + (cc + 1) * 128],
                                qlat[:, lc * T + tile * 512: lc * T + tile * 512 + 512],
                                start=(lc == 0), stop=(lc == 1), waits=w2)
                m1 = ctx.dve(lambda e, o=rt1[:, :], i=ps1[:, 0:512], t=tile:
                             e.tensor_tensor(o, i, cmask[:, t * 512:(t + 1) * 512],
                                             ALU.mult),
                             waits=[t1, m3])
                m2_ = ctx.dve(lambda e, o=rt2[:, :], i=ps2[:, 0:512], t=tile:
                              e.tensor_tensor(o, i, smask[:, t * 512:(t + 1) * 512],
                                              ALU.mult),
                              waits=[t2])
                qd = Q_sb[:, cc * T + tile * 512: cc * T + tile * 512 + 512]
                m3 = ctx.dve(lambda e, o=qd: e.tensor_tensor(o, rt1[:, :], rt2[:, :],
                                                             ALU.add),
                             waits=[m1, m2_])
                ctx.war["rt1"] = m3
                ctx.war["rt2"] = m3
                ctx.war["kps%d" % (tile % 2)] = m1
                ctx.war["kps2%d" % (tile % 2)] = m2_
            ctx.war["krw%d" % (cc % 2)] = s.now("Tn")
            ctx.war["krpw%d" % (cc % 2)] = s.now("Tn")
        ones_tok = ctx.dve(lambda e: e.memset(V_sb[:, :], 1.0),
                           waits=[ctx.war.get("vmem")])
        for dvs in range(2):
            vt = ctx.cast_load(vut_sb[:, :].rearrange("p (k m) -> p k m", m=512),
                               vut_d[:, :, dvs * 512:(dvs + 1) * 512],
                               waits=[ctx.war.get("vw")])
            for tc in range(16):
                ps = ctx.st[tc % 2]
                t = None
                for lc in range(2):
                    w = ([lat_tok, vt, ctx.war.get("kps%d" % (tc % 2)),
                          ctx.war.get("vps%d" % (tc % 2))] if lc == 0 else [])
                    t = ctx.mm(ps[:, 0:512],
                               kvlat[:, lc * T + tc * 128: lc * T + (tc + 1) * 128],
                               vut_sb[:, lc * 512:(lc + 1) * 512],
                               start=(lc == 0), stop=(lc == 1), waits=w)
                o_ap = V_sb[:, tc * 1040 + dvs * 520: tc * 1040 + dvs * 520 + 520]
                o_ap = o_ap.rearrange("p (g k) -> p g k", k=65)[:, :, 0:64]
                i_ap = ps[:, 0:512].rearrange("p (g k) -> p g k", k=64)
                ev = ctx.dve(lambda e, o=o_ap, i=i_ap: e.tensor_copy(o, i),
                             waits=[t, ones_tok])
                ctx.war["vps%d" % (tc % 2)] = ev
            ctx.war["vw"] = s.now("Tn")
        ctx.war["vmem"] = s.now("Tn")
        qk_tok = s.now("Vn")
        v_tok = s.now("Vn")
        ctx.war["oww0"] = s.now("Tn")
        ctx.war["oww1"] = s.now("Tn")
        ctx.war["xres0"] = qk_tok
        ctx.war["xres1"] = qk_tok

        def oproj(jg, ofm_tok, base=base):
            last = None
            for m in range(8):
                slot = krp_sb[m % 2]
                wt = ctx.load(slot[:, :], ow_d[m],
                              waits=[ctx.war.get("oww%d" % (m % 2))])
                ps = ctx.st[m % 2]
                t = None
                for cc in range(DC):
                    w = ([ofm_tok, wt, ctx.war.get("op%d" % (m % 2)),
                          ctx.war.get("opx%d" % (m % 2))] if cc == 0 else [])
                    t = ctx.mm(ps[:, 0:512], slot[:, cc * 128:(cc + 1) * 128],
                               ofm4[:, cc * 512:(cc + 1) * 512],
                               start=(cc == 0), stop=(cc == DC - 1), waits=w)
                ctx.war["oww%d" % (m % 2)] = t
                xr = xres[m % 2]
                xt = ctx.load(xr[:, :], src[m * 128:(m + 1) * 128,
                                            base + jg * 512: base + jg * 512 + 512],
                              waits=[ctx.war.get("xres%d" % (m % 2))])
                og = ostg[m % 2]
                ev = ctx.dve(lambda e, o=og[:, :], i=ps[:, 0:512], x=xr[:, :]:
                             e.tensor_tensor(o, i, x, ALU.add),
                             waits=[t, xt, ctx.war.get("ostg%d" % (m % 2))])
                st = ctx.store(dst[m * 128:(m + 1) * 128,
                                   base + jg * 512: base + jg * 512 + 512],
                               og[:, :], waits=[ev])
                ctx.war["ostg%d" % (m % 2)] = st
                ctx.war["op%d" % (m % 2)] = ev
                ctx.war["xres%d" % (m % 2)] = ev
                last = ev
            return last

        emit_attn_core(ctx, 16, 64, Q_sb, K_sb, V_sb, 64.0 ** -0.5,
                       PT_sb, dmask, otmn, recip, ofm4, qk_tok, v_tok, oproj)
        s.barrier()
        ctx.war = {}
    sa.reset(mark)



def emit_precast(ctx):
    s = ctx.s
    pairs = [
        ("gall", "GB", 8), ("uall", "UB", 8), ("dall", "DB", 8),
        ("qw1", "QWB", 1), ("kw1", "KWB", 1), ("vw1", "VWB", 1),
        ("ow1", "OWB", 1), ("kmixr", "KRB", 1), ("kmixrp", "KRPB", 1),
        ("owft", "OWFB", 1),
    ]
    for src, dstn, nsplit in pairs:
        sd = ctx.dram[src]
        dd = ctx.dram[dstn]
        if nsplit == 1:
            ctx.cast_load(dd.rearrange("a p c -> (a p) c"),
                          sd.rearrange("a p b c -> (a p) (b c)"))
        else:
            for e in range(nsplit):
                ctx.cast_load(dd[e].rearrange("m p c -> (m p) c"),
                              sd[e].rearrange("m p b c -> (m p) (b c)"))
    s.barrier()
    ctx.war = {}



def emit_moe(ctx, x_name, y1_name, dst_name):
    s, c, sa = ctx.s, ctx.c, ctx.sa
    xsrc = ctx.dram[x_name]
    ysrc = ctx.dram[y1_name]
    dst = ctx.dram[dst_name]
    NTILE = 1024
    mark = sa.mark()
    y1 = sa.alloc("p6_y1", [128, DC * NTILE], F32)
    yh2 = sa.alloc("p6_yh2", [128, DC * NTILE], BF16)
    h_sb = sa.alloc("p6_h", [128, 16 * NTILE], BF16)
    acc = [sa.alloc(f"p6_acc{i}", [128, DC * NTILE], F32) for i in range(2)]
    wbc = sa.alloc("p6_wbc", [128, NTILE], F32)
    gstage = [sa.alloc(f"p6_gs{i}", [128, DC * 128], BF16) for i in range(2)]
    dstage = [sa.alloc(f"p6_ds{i}", [128, 16 * 128], BF16) for i in range(2)]
    glbuf = sa.alloc("p6_gl", [128, 2 * 512], BF16)
    ubuf = sa.alloc("p6_ub", [128, 2 * 512], BF16)
    tmpo = sa.alloc("p6_tmpo", [128, 512], F32)
    rp_sb = sa.alloc("p6_rp", [128, DC * 7], F32)
    ln2v_sb = sa.alloc("p6_ln2", [128, 8], F32)
    rbias_sb = sa.alloc("p6_rb", [1, 7], F32)
    rsum_sb = sa.alloc("p6_rs", [1, 7], F32)
    rsbc = sa.alloc("p6_rsbc", [128, 7], F32)
    rbbc = sa.alloc("p6_rbbc", [128, 7], F32)
    arow = sa.alloc("p6_ar", [1, NTILE], F32)
    brow = sa.alloc("p6_br", [1, NTILE], F32)
    raw_sb = sa.alloc("p6_raw", [7, NTILE], F32)
    ltm = sa.alloc("p6_ltm", [128, 8 * 7], F32)
    ab_sb = sa.alloc("p6_ab", [128, 2 * 8], F32)
    prob = sa.alloc("p6_prob", [128, 8 * 7], F32)
    rsc = sa.alloc("p6_rsc", [128, 8 * 7], F32)
    m12 = sa.alloc("p6_m12", [128, 3 * 8], F32)
    wgt_tm = sa.alloc("p6_wgt", [128, 8 * 7], F32)
    identf = sa.alloc("p6_idf", [128, 128], F32)
    diag = sa.alloc("p6_diag", [128, 128], F32)
    onesq = sa.alloc("p6_onsq", [128, 128], F32)
    xbuf_alias = sa.mark()
    sqbuf = sa.alloc("p6_sq", [128, 512], BF16)
    rowbuf = sa.alloc("p6_row", [1, 5 * 512], F32)
    tmpbuf = sa.alloc("p6_tmp", [128, 512], F32)
    xres = [sa.alloc(f"p6_xres{i}", [128, 512], F32) for i in range(2)]
    ostg = [sa.alloc(f"p6_ostg{i}", [128, 512], F32) for i in range(2)]

    idf_d = ctx.nc.inline_tensor(np.eye(128, dtype=np.float32), name="p6idf").ap()
    onsq_d = ctx.nc.inline_tensor(np.ones((128, 128), np.float32), name="p6ones").ap()
    ctx.load(identf[:, :], idf_d[:, :])
    ctx.load(onesq[:, :], onsq_d[:, :])
    ctx.load(ln2v_sb[:, :], ctx.dram["ln2v"][:, :])
    ctx.load(rbias_sb[:, :], ctx.dram["rbias"][:, :])
    rw_tok = ctx.load(rp_sb[:, :].rearrange("p (cc m) -> p cc m", m=7),
                      ctx.dram["rwt"].rearrange("(cc p) m -> p cc m", p=128))
    s.barrier()
    t = None
    for cc in range(DC):
        t = ctx.dve(lambda e, o=rp_sb[:, cc * 7:(cc + 1) * 7], g=ln2v_sb[:, cc:cc + 1]:
                    e.tensor_scalar_mul(o, o, g), waits=[])
    rs_t = None
    for cc in range(DC):
        rs_t = ctx.mm(ctx.ps_g[2][0:1, 0:7], c["ones_f"][:, :],
                      rp_sb[:, cc * 7:(cc + 1) * 7],
                      start=(cc == 0), stop=(cc == DC - 1), waits=[t])
    t = ctx.dve(lambda e: e.tensor_copy(rsum_sb[0:1, :], ctx.ps_g[2][0:1, 0:7]),
                waits=[rs_t])
    b1 = ctx.mm(ctx.ps_g[2][:, 0:7], c["ones_r"][:, :], rsum_sb[0:1, :],
                start=True, stop=True, waits=[t])
    t = ctx.dve(lambda e: e.tensor_copy(rsbc[:, :], ctx.ps_g[2][:, 0:7]), waits=[b1])
    b2 = ctx.mm(ctx.ps_g[2][:, 0:7], c["ones_r"][:, :], rbias_sb[0:1, :],
                start=True, stop=True, waits=[t])
    t = ctx.dve(lambda e: e.tensor_copy(rbbc[:, :], ctx.ps_g[2][:, 0:7]), waits=[b2])
    s.barrier()
    ctx.war = {}

    for tile in range(NTOK // NTILE):
        col0 = tile * NTILE
        for sub in range(2):
            def dst_fn(cc, sub=sub):
                return yh2[:, cc * NTILE + sub * 512: cc * NTILE + sub * 512 + 512]
            emit_ln(ctx, ysrc, col0 + sub * 512, 512,
                    _Slice1024(y1, sub), sqbuf, rowbuf, tmpbuf, dst_fn,
                    gain_col=lambda cc: ln2v_sb[:, cc:cc + 1])
            a_t = ctx.dve(lambda e, o=arow[0:1, sub * 512:(sub + 1) * 512]:
                          e.tensor_copy(o, rowbuf[0:1, 0:512]))
            b_t = ctx.dve(lambda e, o=brow[0:1, sub * 512:(sub + 1) * 512]:
                          e.tensor_copy(o, rowbuf[0:1, 512:1024]))
            ctx.war["rowA"] = a_t
            ctx.war["rowB"] = b_t
        yh_tok = s.now("Vn")
        for sub in range(2):
            t = None
            for cc in range(DC):
                w = ([yh_tok, ctx.war.get("rps")] if cc == 0 else [])
                t = ctx.mm(ctx.ps_g[2][0:7, 0:512], rp_sb[:, cc * 7:(cc + 1) * 7],
                           y1[:, cc * NTILE + sub * 512: cc * NTILE + sub * 512 + 512],
                           start=(cc == 0), stop=(cc == DC - 1), waits=w)
            ev = ctx.dve(lambda e, o=raw_sb[0:7, sub * 512:(sub + 1) * 512],
                         i=ctx.ps_g[2][0:7, 0:512]: e.tensor_copy(o, i), waits=[t])
            ctx.war["rps"] = ev
        for bb in range(8):
            t1 = ctx.s.emit(
                "tensor",
                lambda e, o=ctx.ps_g[2][0:128, 0:7],
                i=raw_sb[0:7, bb * 128:(bb + 1) * 128]:
                e.transpose(o, i, identf[0:7, 0:7]),
                waits=[ctx.war.get("rps"), ctx.war.get("rtr")], inc="Tn")
            ta = ctx.mm(ctx.ps_g[0][0:128, 0:1], arow[0:1, bb * 128:(bb + 1) * 128],
                        c["ones_r"][0:1, 0:1], start=True, stop=True,
                        waits=[ctx.war.get("rta")])
            tb = ctx.mm(ctx.ps_g[0][0:128, 1:2], brow[0:1, bb * 128:(bb + 1) * 128],
                        c["ones_r"][0:1, 0:1], start=True, stop=True)
            e1 = ctx.dve(lambda e, o=ltm[:, bb * 7:(bb + 1) * 7],
                         i=ctx.ps_g[2][0:128, 0:7]: e.tensor_copy(o, i), waits=[t1])
            e2 = ctx.dve(lambda e, o=ab_sb[:, bb * 2:bb * 2 + 2],
                         i=ctx.ps_g[0][0:128, 0:2]: e.tensor_copy(o, i), waits=[tb, ta])
            ctx.war["rtr"] = e1
            ctx.war["rta"] = e2
            l1 = ctx.dve(lambda e, o=rsc[:, bb * 7:(bb + 1) * 7],
                         b_=ab_sb[:, bb * 2 + 1:bb * 2 + 2]:
                         e.scalar_tensor_tensor(o, rsbc[:, :], b_, rbbc[:, :],
                                                ALU.mult, ALU.add), waits=[e2])
            l2 = ctx.dve(lambda e, o=ltm[:, bb * 7:(bb + 1) * 7],
                         a_=ab_sb[:, bb * 2:bb * 2 + 1],
                         t2=rsc[:, bb * 7:(bb + 1) * 7]:
                         e.scalar_tensor_tensor(o, o, a_, t2, ALU.mult, ALU.add),
                         waits=[l1, e1])
            sg = ctx.act(lambda e, o=prob[:, bb * 7:(bb + 1) * 7],
                         i=ltm[:, bb * 7:(bb + 1) * 7]:
                         e.activation(o, i, AF.Sigmoid), waits=[l2])
            bm = bb * 7
            m1c = m12[:, bb * 3:bb * 3 + 1]
            m2c = m12[:, bb * 3 + 1:bb * 3 + 2]
            sc_ = m12[:, bb * 3 + 2:bb * 3 + 3]
            r1 = ctx.dve(lambda e, o=m1c, i=prob[:, bm:bm + 7]:
                         e.reduce_max(o, i, AX.X), waits=[sg])
            r2 = ctx.dve(lambda e, o=rsc[:, bm:bm + 7], i=prob[:, bm:bm + 7], m=m1c:
                         e.tensor_scalar(o, i, m, None, ALU.is_ge), waits=[r1])
            r3 = ctx.dve(lambda e, o=rsc[:, bm:bm + 7], i=rsc[:, bm:bm + 7],
                         p=prob[:, bm:bm + 7]:
                         e.scalar_tensor_tensor(o, i, -1e30, p, ALU.mult, ALU.add),
                         waits=[r2])
            r4 = ctx.dve(lambda e, o=m2c, i=rsc[:, bm:bm + 7]:
                         e.reduce_max(o, i, AX.X), waits=[r3])
            r5 = ctx.dve(lambda e, o=rsc[:, bm:bm + 7], i=prob[:, bm:bm + 7], m=m2c:
                         e.tensor_scalar(o, i, m, None, ALU.is_ge), waits=[r4])
            r6 = ctx.dve(lambda e, o=sc_, i=m1c, i2=m2c:
                         e.tensor_tensor(o, i, i2, ALU.add), waits=[r4])
            r7 = ctx.dve(lambda e, o=sc_, i=sc_: e.reciprocal(o, i), waits=[r6])
            r8 = ctx.dve(lambda e, o=wgt_tm[:, bm:bm + 7], i=prob[:, bm:bm + 7],
                         i2=rsc[:, bm:bm + 7]:
                         e.tensor_tensor(o, i, i2, ALU.mult), waits=[r5])
            r9 = ctx.dve(lambda e, o=wgt_tm[:, bm:bm + 7], i=wgt_tm[:, bm:bm + 7],
                         m=sc_: e.tensor_scalar_mul(o, i, m), waits=[r7, r8])
        wgt_tok = s.now("Vn")
        ctx.war["dps0"] = wgt_tok
        ctx.war["dps1"] = wgt_tok
        ctx.war["wbcp"] = wgt_tok
        for ei in range(8):
            if ei > 0:
                wtoks = []
                for bb in range(8):
                    dg = ctx.dve(lambda e, o=diag[:, :],
                                 m=wgt_tm[:, bb * 7 + ei - 1: bb * 7 + ei]:
                                 e.tensor_scalar_mul(o, identf[:, :], m),
                                 waits=[wgt_tok, ctx.war.get("diag")])
                    bt = ctx.mm(ctx.ps_g[2][:, 0:128], onesq[:, :], diag[:, :],
                                start=True, stop=True,
                                waits=[dg, ctx.war.get("wbcp")])
                    ctx.war["diag"] = bt
                    ev = ctx.dve(lambda e, o=wbc[:, bb * 128:(bb + 1) * 128],
                                 i=ctx.ps_g[2][:, 0:128]: e.tensor_copy(o, i),
                                 waits=[bt, ctx.war.get("wbcr")])
                    ctx.war["wbcp"] = ev
                    wtoks.append(ev)
                wbc_tok = wtoks[-1]
            else:
                wbc_tok = None
            for mf in range(16):
                gslot = gstage[mf % 2]
                gt = ctx.load(gslot[:, :], ctx.dram["GB"][ei, mf],
                              waits=[ctx.war.get("gsl%d" % (mf % 2))])
                for nt in range(2):
                    ps_gm = ctx.st[nt]
                    t = None
                    for cc in range(DC):
                        w = ([yh_tok, gt, ctx.war.get("gps%d" % nt)]
                             if cc == 0 else [])
                        t = ctx.mm(ps_gm[:, 0:512],
                                   gslot[:, cc * 128:(cc + 1) * 128],
                                   yh2[:, cc * NTILE + nt * 512: cc * NTILE + nt * 512 + 512],
                                   start=(cc == 0), stop=(cc == DC - 1), waits=w)
                    sl = ctx.act(lambda e, o=glbuf[:, nt * 512:(nt + 1) * 512],
                                 i=ps_gm[:, 0:512]: e.activation(o, i, AF.Silu),
                                 waits=[t, ctx.war.get("glb%d" % nt)])
                    ctx.war["gps%d" % nt] = sl
                uslot = gstage[(mf + 1) % 2]
                ut = ctx.load(uslot[:, :], ctx.dram["UB"][ei, mf],
                              waits=[ctx.war.get("gsl%d" % ((mf + 1) % 2))])
                for nt in range(2):
                    ps_um = ctx.st[2 + nt]
                    t = None
                    for cc in range(DC):
                        w = ([ut, ctx.war.get("ups%d" % nt)] if cc == 0 else [])
                        t = ctx.mm(ps_um[:, 0:512],
                                   uslot[:, cc * 128:(cc + 1) * 128],
                                   yh2[:, cc * NTILE + nt * 512: cc * NTILE + nt * 512 + 512],
                                   start=(cc == 0), stop=(cc == DC - 1), waits=w)
                    ue = ctx.act(lambda e, o=ubuf[:, nt * 512:(nt + 1) * 512],
                                 i=ps_um[:, 0:512]: e.activation(o, i, AF.Copy),
                                 waits=[t, ctx.war.get("ubf%d" % nt)])
                    ctx.war["ups%d" % nt] = ue
                    hm = ctx.pool(lambda e, o=h_sb[:, mf * NTILE + nt * 512:
                                                   mf * NTILE + nt * 512 + 512],
                                  i=glbuf[:, nt * 512:(nt + 1) * 512],
                                  i2=ubuf[:, nt * 512:(nt + 1) * 512]:
                                  e.tensor_tensor(o, i, i2, ALU.mult),
                                  waits=[ue, ctx.war.get("gps%d" % nt),
                                         ctx.war.get("hsb%d" % nt)])
                    ctx.war["glb%d" % nt] = hm
                    ctx.war["ubf%d" % nt] = hm
                ctx.war["gsl%d" % (mf % 2)] = s.now("Tn")
                ctx.war["gsl%d" % ((mf + 1) % 2)] = s.now("Tn")
            h_tok = s.now("Pn")
            for m2 in range(8):
                dslot = dstage[m2 % 2]
                dt = ctx.load(dslot[:, :], ctx.dram["DB"][ei, m2],
                              waits=[ctx.war.get("dsl%d" % (m2 % 2))])
                for nt in range(2):
                    ps_d = ctx.ps_g[nt]
                    t = None
                    for k in range(16):
                        w = ([h_tok, dt, ctx.war.get("dps%d" % nt)]
                             if k == 0 else [])
                        t = ctx.mm(ps_d[:, 0:512],
                                   dslot[:, k * 128:(k + 1) * 128],
                                   h_sb[:, k * NTILE + nt * 512: k * NTILE + nt * 512 + 512],
                                   start=(k == 0), stop=(k == 15), waits=w)
                    col = m2 * NTILE + nt * 512
                    if ei == 0:
                        ev = ctx.dve(lambda e, o=acc[1][:, col:col + 512],
                                     i=ps_d[:, 0:512]: e.tensor_copy(o, i),
                                     waits=[t, ctx.war.get("accw")])
                    else:
                        ev = ctx.dve(lambda e, o=tmpo[:, :], i=ps_d[:, 0:512],
                                     wb=wbc[:, nt * 512:(nt + 1) * 512]:
                                     e.tensor_tensor(o, i, wb, ALU.mult),
                                     waits=[t, wbc_tok, ctx.war.get("tmpo")])
                        ctx.war["wbcr"] = ev
                        ev = ctx.dve(lambda e, o=acc[(ei + 1) % 2][:, col:col + 512],
                                     i=acc[ei % 2][:, col:col + 512], t2=tmpo[:, :]:
                                     e.tensor_tensor(o, i, t2, ALU.add),
                                     waits=[ev, ctx.war.get("accw")])
                        ctx.war["tmpo"] = ev
                    ctx.war["dps%d" % nt] = ev
                ctx.war["dsl%d" % (m2 % 2)] = s.now("Tn")
            ctx.war["hsb0"] = s.now("Tn")
            ctx.war["hsb1"] = s.now("Tn")
        acc_fin = acc[0]
        acc_tok = s.now("Vn")
        for m2 in range(8):
            for nt in range(2):
                col = m2 * NTILE + nt * 512
                xr = xres[nt]
                xt = ctx.load(xr[:, :],
                              xsrc[m2 * 128:(m2 + 1) * 128,
                                   col0 + nt * 512: col0 + nt * 512 + 512],
                              waits=[ctx.war.get("xres%d" % nt)])
                o1 = ctx.dve(lambda e, o=ostg[nt][:, :], i=acc_fin[:, col:col + 512],
                             y=y1[:, col:col + 512]:
                             e.tensor_tensor(o, i, y, ALU.add),
                             waits=[acc_tok, ctx.war.get("ost%d" % nt)])
                o2 = ctx.dve(lambda e, o=ostg[nt][:, :], i=ostg[nt][:, :], x=xr[:, :]:
                             e.tensor_tensor(o, i, x, ALU.add), waits=[o1, xt])
                ctx.war["xres%d" % nt] = o2
                st_ = ctx.store(dst[m2 * 128:(m2 + 1) * 128,
                                    col0 + nt * 512: col0 + nt * 512 + 512],
                                ostg[nt][:, :], waits=[o2])
                ctx.war["ost%d" % nt] = st_
        ctx.war["accw"] = s.now("Vn")
        ctx.war["lnW"] = s.now("Tn")
        s.barrier()
        ctx.war = {}
    sa.reset(mark)


class _Slice1024:
    def __init__(self, y1, sub):
        self.y1 = y1
        self.sub = sub

    def __getitem__(self, key):
        p, f = key
        lo, hi = f.start, f.stop
        cc = lo // 512
        return self.y1[p, cc * 1024 + self.sub * 512:
                       cc * 1024 + self.sub * 512 + (hi - lo)]


# version 09
# speedup vs baseline: 1.0688x; 1.0688x over previous
import numpy as np
import ml_dtypes

import concourse.bass as bass
import concourse.mybir as mybir
from concourse.bass_utils import run_bass_kernel_spmd

F32 = mybir.dt.float32
BF16 = mybir.dt.bfloat16
AF = mybir.ActivationFunctionType
ALU = mybir.AluOpType
AX = mybir.AxisListType
BF = ml_dtypes.bfloat16

D = 1024
DC = 8
T = 2048
NSEQ = 2
NTOK = NSEQ * T
EPS = 1e-5
MAGIC = 1.5 * (2.0 ** 23)

ENGS = ("sync", "gpsimd", "tensor", "vector", "scalar")

N_LD = 8
N_GL = 4
N_ST = 4


class Sched:
    def __init__(self, nc):
        self.nc = nc
        self.q = {e: [] for e in ENGS}
        self.sem = {}
        self.cnt = {}
        self.last_wait = {e: {} for e in ENGS}

    def new_sem(self, name):
        self.sem[name] = self.nc.alloc_semaphore(name=name)
        self.cnt[name] = 0
        return name

    def emit(self, eng, fn, waits=(), inc=None, inc_by=1):
        lw = self.last_wait[eng]
        for w in waits:
            if w is None:
                continue
            sname, v = w
            if v > 0 and lw.get(sname, -1) < v:
                self.q[eng].append(("w", sname, v))
                lw[sname] = v
        self.q[eng].append(("o", fn, inc, inc_by))
        if inc is not None:
            self.cnt[inc] += inc_by
            return (inc, self.cnt[inc])
        return None

    def now(self, sem):
        return (sem, self.cnt[sem])

    def barrier(self):
        toks = [(n, c) for n, c in self.cnt.items() if c > 0]
        for e in ENGS:
            lw = self.last_wait[e]
            for sname, v in toks:
                if lw.get(sname, -1) < v:
                    self.q[e].append(("w", sname, v))
                    lw[sname] = v

    def build(self):
        with self.nc.Block() as block:
            for eng in ENGS:
                stream = self.q[eng]
                sem = self.sem

                def body(engine, stream=stream):
                    for item in stream:
                        if item[0] == "w":
                            engine.wait_ge(sem[item[1]], item[2])
                        else:
                            _, fn, inc, n = item
                            inst = fn(engine)
                            if inc is not None:
                                inst.then_inc(sem[inc], n)
                getattr(block, eng)(body)


class SA:
    def __init__(self, nc):
        self.nc = nc
        self.base = (nc.sbuf_base + 31) & ~31
        self.off = self.base
        self.top = nc.sbuf_top
        self.uid = 0

    def alloc(self, name, shape, dtype):
        per = int(np.prod(shape[1:])) * mybir.dt.size(dtype)
        per = (per + 31) & ~31
        assert self.off + per <= self.top, (
            f"SBUF overflow at {name}: {self.off}+{per} > {self.top}")
        self.uid += 1
        t = self.nc.alloc_sbuf_tensor_at(f"{name}_{self.uid}", list(shape), dtype,
                                         offset=self.off)
        self.off += per
        return t

    def mark(self):
        return self.off

    def reset(self, m):
        self.off = m


class Ctx:
    def __init__(self, inputs, scratch, outputs):
        nc = bass.Bass("TRN2", target_bir_lowering=False, debug=False)
        self.nc = nc
        self.s = Sched(nc)
        self.sa = SA(nc)
        s = self.s
        self.ld_sems = [s.new_sem(f"ld{i}") for i in range(N_LD)]
        self.gl_sems = [s.new_sem(f"gl{i}") for i in range(N_GL)]
        self.st_sems = [s.new_sem(f"st{i}") for i in range(N_ST)]
        self._ld_i = 0
        self._gl_i = 0
        self._st_i = 0
        for n in ("Tn", "Vn", "Cn", "Pn"):
            s.new_sem(n)
        self.dram = {}
        for name, shape, dt in inputs:
            self.dram[name] = nc.dram_tensor(name, list(shape), dt,
                                             kind="ExternalInput").ap()
        for name, shape, dt in scratch:
            self.dram[name] = nc.dram_tensor(name, list(shape), dt,
                                             kind="Internal").ap()
        for name, shape, dt in outputs:
            self.dram[name] = nc.dram_tensor(name, list(shape), dt,
                                             kind="ExternalOutput").ap()
        self.st = [nc.alloc_psum_tensor(f"ps_st{i}", [128, 512], F32)
                   for i in range(4)]
        self.ps_g = [nc.alloc_psum_tensor(f"ps_g{i}", [128, 512], F32)
                     for i in range(3)]
        self.ps_bf = nc.alloc_psum_tensor("ps_bf", [128, 1024], BF16)
        self.war = {}
        self._make_consts()
        self.s.barrier()

    def _make_consts(self):
        nc, s, sa = self.nc, self.s, self.sa
        tri = np.triu(np.ones((128, 128), np.float32)).astype(BF)
        RD = 32
        inv_freq = 1.0 / (10000.0 ** (np.arange(0, RD, 2, dtype=np.float64) / RD))
        tpos = np.arange(T, dtype=np.float64)
        freqs = tpos[:, None] * inv_freq[None, :]
        emb = np.concatenate([freqs, freqs], axis=-1)
        cos = np.cos(emb).astype(np.float32)
        sin = np.sin(emb).astype(np.float32)
        sgn = np.ones(32, np.float32)
        sgn[:16] = -1.0
        consts = {
            "ident": np.eye(128, dtype=BF),
            "triu": tri,
            "ones_b": np.ones((128, 1), BF),
            "ones_f": np.ones((128, 1), np.float32),
            "ones_r": np.ones((1, 128), np.float32),
            "eps_c": np.full((128, 1), EPS, np.float32),
        }
        self.rope_cos = np.ascontiguousarray(np.tile(cos.T, (4, 1)))
        self.rope_sin = np.ascontiguousarray(np.tile((sin * sgn[None, :]).T, (4, 1)))
        self.c = {}
        for name, arr in consts.items():
            dt = BF16 if arr.dtype == BF else F32
            dr = nc.inline_tensor(np.ascontiguousarray(arr), name=f"c_{name}").ap()
            sb = sa.alloc(f"c_{name}", list(arr.shape), dt)
            self.load(sb[:, :], dr[:, :])
            self.c[name] = sb

    def _dma(self, eng, sems, idx_attr, dst_ap, src_ap, waits):
        i = getattr(self, idx_attr)
        setattr(self, idx_attr, i + 1)
        sem = sems[i % len(sems)]
        pre = self.s.now(sem)
        return self.s.emit(eng, lambda e: e.dma_start(out=dst_ap, in_=src_ap),
                           waits=list(waits) + [pre], inc=sem, inc_by=16)

    def load(self, dst_ap, src_ap, waits=()):
        return self._dma("sync", self.ld_sems, "_ld_i", dst_ap, src_ap, waits)

    def cast_load(self, dst_ap, src_ap, waits=()):
        return self._dma("gpsimd", self.gl_sems, "_gl_i", dst_ap, src_ap, waits)

    def store(self, dst_ap, src_ap, waits=()):
        return self._dma("gpsimd", self.st_sems, "_st_i", dst_ap, src_ap, waits)

    def dma_tr(self, dst_ap, src_ap, waits=()):
        i = self._ld_i
        self._ld_i = i + 1
        sem = self.ld_sems[i % N_LD]
        pre = self.s.now(sem)
        return self.s.emit("sync",
                           lambda e: e.dma_start_transpose(dst_ap, src_ap),
                           waits=list(waits) + [pre], inc=sem, inc_by=16)

    def mm(self, out_ap, lhsT, rhs, start, stop, waits=(), force_inc=False):
        return self.s.emit(
            "tensor",
            lambda e: e.matmul(out_ap, lhsT, rhs, start=start, stop=stop),
            waits=waits, inc=("Tn" if (stop or force_inc) else None))

    def tr(self, out_ap, in_ap, waits=()):
        ident = self.c["ident"]
        return self.s.emit(
            "tensor", lambda e: e.transpose(out_ap, in_ap, ident[:, :]),
            waits=waits, inc="Tn")

    def act(self, fn, waits=()):
        return self.s.emit("scalar", fn, waits=waits, inc="Cn")

    def dve(self, fn, waits=()):
        return self.s.emit("vector", fn, waits=waits, inc="Vn")

    def pool(self, fn, waits=()):
        return self.s.emit("gpsimd", fn, waits=waits, inc="Pn")



def emit_ln(ctx, src, col0, n, xbuf, sqbuf, rowbuf, tmpbuf, dst_fn,
            gain_col=None, dst_war=None):
    s, c = ctx.s, ctx.c
    ps_sum, ps_sq = ctx.ps_g[0], ctx.ps_g[1]
    x_toks = []
    sum_tok = None
    sq_act_tok = None
    sq_tok = ctx.war.get("lnsq")
    for cc in range(DC):
        xa = xbuf[:, cc * n:(cc + 1) * n]
        xt = ctx.load(xa, src[cc * 128:(cc + 1) * 128, col0:col0 + n],
                      waits=[ctx.war.get(f"lnx{cc}")])
        x_toks.append(xt)
        sum_tok = ctx.mm(ps_sum[0:1, 0:n], c["ones_f"][:, :], xa,
                         start=(cc == 0), stop=(cc == DC - 1),
                         waits=[xt, ctx.war.get("g0")] if cc == 0 else [xt])
        sq_act_tok = ctx.act(
            lambda e, o=sqbuf[:, 0:n], i=xa: e.activation(o, i, AF.Square),
            waits=[xt, sq_tok])
        sq_tok = ctx.mm(ps_sq[0:1, 0:n], c["ones_b"][:, :], sqbuf[:, 0:n],
                        start=(cc == 0), stop=(cc == DC - 1), force_inc=True,
                        waits=([sq_act_tok, ctx.war.get("g1")]
                               if cc == 0 else [sq_act_tok]))
    ctx.war["lnsq"] = sq_tok
    A = rowbuf[0:1, 0:n]
    B = rowbuf[0:1, n:2 * n]
    M = rowbuf[0:1, 2 * n:3 * n]
    T1 = rowbuf[0:1, 3 * n:4 * n]
    T2 = rowbuf[0:1, 4 * n:5 * n]
    t1 = ctx.dve(lambda e: e.tensor_scalar_mul(M, ps_sum[0:1, 0:n], 1.0 / D),
                 waits=[sum_tok])
    t2 = ctx.dve(lambda e: e.tensor_tensor(T1, M, M, ALU.mult), waits=[t1])
    t3 = ctx.dve(lambda e: e.scalar_tensor_tensor(T2, ps_sq[0:1, 0:n], 1.0 / D,
                                                  T1, ALU.mult, ALU.subtract),
                 waits=[t2, sq_tok])
    t4 = ctx.act(lambda e: e.activation(T1, T2, AF.Sqrt, bias=c["eps_c"][0:1, :]),
                 waits=[t3])
    t5 = ctx.dve(lambda e: e.reciprocal(A, T1),
                 waits=[t4, ctx.war.get("rowA")])
    t6 = ctx.dve(lambda e: e.scalar_tensor_tensor(B, M, -1.0, A, ALU.mult, ALU.mult),
                 waits=[t5, ctx.war.get("rowB")])
    bA = ctx.mm(ps_sum[:, 0:n], c["ones_r"][:, :], A, start=True, stop=True,
                waits=[t5, t1])
    bB = ctx.mm(ps_sq[:, 0:n], c["ones_r"][:, :], B, start=True, stop=True,
                waits=[t6, t3])
    ctx.war["rowA"] = bA
    ctx.war["rowB"] = bB
    xhat_toks = []
    tp = None
    for cc in range(DC):
        xa = xbuf[:, cc * n:(cc + 1) * n]
        tp = ctx.dve(lambda e, o=tmpbuf[:, 0:n], i=xa:
                     e.tensor_tensor(o, i, ps_sum[:, 0:n], ALU.mult),
                     waits=[bA, tp])
        if gain_col is None:
            tw = ctx.dve(lambda e, o=dst_fn(cc), i=tmpbuf[:, 0:n]:
                         e.tensor_tensor(o, i, ps_sq[:, 0:n], ALU.add),
                         waits=[bB, tp, dst_war(cc) if dst_war else None])
        else:
            tw = ctx.dve(lambda e, o=tmpbuf[:, 0:n], i=tmpbuf[:, 0:n]:
                         e.tensor_tensor(o, i, ps_sq[:, 0:n], ALU.add),
                         waits=[bB, tp])
            tw = ctx.dve(lambda e, o=dst_fn(cc), i=tmpbuf[:, 0:n], g=gain_col(cc):
                         e.tensor_scalar_mul(o, i, g),
                         waits=[tw, dst_war(cc) if dst_war else None])
        tp = tw
        ctx.war[f"lnx{cc}"] = tw
        xhat_toks.append(tw)
    ctx.war["g0"] = xhat_toks[-1]
    ctx.war["g1"] = xhat_toks[-1]
    return xhat_toks, x_toks



def emit_attn_core(ctx, nh, dh, Q_sb, K_sb, V_sb, scale, PT_sb, dmask_sb,
                   otmn_sb, recip_sb, ofm4_sb, qk_tok, v_tok, oproj_fn):
    s, c = ctx.s, ctx.c
    hpc = 128 // dh
    DCk = nh // hpc
    ng64 = dh // 64
    hw = ng64 * 65
    gw = nh * hw
    nh64 = nh * ng64
    exp_tok = [None] * 4
    pv_par = [None, None]
    mask_tok = None
    tr_toks = []
    hh = 0
    for j in range(16):
        obank_used = []
        for h in range(nh):
            rb = (h % hpc) * dh
            cc = h // hpc
            par = hh % 2
            hh += 1
            PT = PT_sb[par]
            dmask = dmask_sb[par]
            for b in range(0, j + 1, 4):
                mm_t = None
                bank = ctx.st[b // 4]
                for c4 in range(b, min(b + 4, j + 1)):
                    w = [qk_tok, v_tok, exp_tok[c4 // 4]] if c4 == b else []
                    mm_t = ctx.mm(
                        bank[:, (c4 - b) * 128:(c4 - b + 1) * 128],
                        K_sb[rb:rb + dh, cc * T + c4 * 128: cc * T + (c4 + 1) * 128],
                        Q_sb[rb:rb + dh, cc * T + j * 128: cc * T + (j + 1) * 128],
                        start=True, stop=True, waits=w)
                hi = min(b + 4, j + 1) * 128
                o_ap = PT[:, b * 128:hi]
                i_ap = bank[:, 0:hi - b * 128]
                exp_tok[b // 4] = ctx.act(
                    lambda e, o=o_ap, i=i_ap: e.activation(o, i, AF.Exp, scale=scale),
                    waits=[mm_t, pv_par[par]])
            dm_in = PT[:, j * 128:(j + 1) * 128]
            mask_tok = ctx.dve(
                lambda e, o=dmask[:, 0:128], i=dm_in:
                e.tensor_tensor(o, i, c["triu"][:, :], ALU.mult),
                waits=[exp_tok[j // 4], pv_par[par]])
            pv_tok = None
            for g in range(ng64):
                h64 = h * ng64 + g
                bk = h64 // 7
                ocol = (h64 % 7) * 65
                if bk not in obank_used:
                    obank_used.append(bk)
                for c4 in range(j + 1):
                    lt = (dmask[:, 0:128] if c4 == j
                          else PT[:, c4 * 128:(c4 + 1) * 128])
                    if c4 == 0:
                        w = [v_tok, mask_tok, ctx.war.get(f"ob{bk}")]
                    elif c4 == j:
                        w = [mask_tok]
                    else:
                        w = [exp_tok[c4 // 4]]
                    vs = V_sb[:, c4 * gw + h * hw + g * 65:
                              c4 * gw + h * hw + g * 65 + 65]
                    pv_tok = ctx.mm(ctx.ps_g[bk][:, ocol:ocol + 65], lt, vs,
                                    start=(c4 == 0), stop=(c4 == j), waits=w)
            pv_par[par] = pv_tok
        rc_tok = {}
        for bk in obank_used:
            ngrp = min(7, nh64 - bk * 7)
            src = ctx.ps_g[bk][:, 0:ngrp * 65].rearrange(
                "p (g k) -> p g k", k=65)[:, :, 64]
            dst = recip_sb[:, bk * 7:bk * 7 + ngrp]
            rc_tok[bk] = ctx.dve(lambda e, o=dst, i=src: e.reciprocal(o, i),
                                 waits=[pv_par[0], pv_par[1]])
        nt = None
        for h in range(nh):
            for g in range(ng64):
                h64 = h * ng64 + g
                bk = h64 // 7
                ocol = (h64 % 7) * 65
                o_ap = otmn_sb[:, h64 * 64:(h64 + 1) * 64]
                i_ap = ctx.ps_g[bk][:, ocol:ocol + 64]
                r_ap = recip_sb[:, h64:h64 + 1]
                nt = ctx.dve(
                    lambda e, o=o_ap, i=i_ap, r=r_ap: e.tensor_scalar_mul(o, i, r),
                    waits=[rc_tok[bk], ctx.war.get(f"otm{h64 // 2}")])
            if (h % hpc == hpc - 1) or dh == 128:
                cc = h // hpc
                src = otmn_sb[:, cc * 128:(cc + 1) * 128]
                dst = ofm4_sb[:, cc * 512 + (j % 4) * 128:
                              cc * 512 + (j % 4) * 128 + 128]
                tt = ctx.dma_tr(dst, src, waits=[nt, ctx.war.get("ofm4")])
                ctx.war[f"otm{cc}"] = tt
                tr_toks.append(tt)
        for bk in obank_used:
            ctx.war[f"ob{bk}"] = s.now("Vn")
        if j % 4 == 3:
            ctx.war["opx0"] = exp_tok[0]
            ctx.war["opx1"] = exp_tok[1]
            op_tok = oproj_fn(j // 4, tr_toks + [s.now("Vn")])
            exp_tok[0] = op_tok
            exp_tok[1] = op_tok
            ctx.war["ofm4"] = op_tok
            tr_toks = []


def emit_proj_to(ctx, w_sb, x_sb, nchunk, x_off, n, mcols, out_fn, waits=()):
    for m in range(mcols // 128):
        bank = ctx.ps_g[2] if (m % 2 == 0) else ctx.ps_st[:, 0:512]
        ps = bank[:, 0:n] if (m % 2 == 0) else ctx.ps_st[:, 0:n]
        t = None
        for cc in range(nchunk):
            w = list(waits) + [ctx.war.get("proj%d" % (m % 2))] if cc == 0 else []
            t = ctx.mm(ps, w_sb[:, cc * mcols + m * 128: cc * mcols + (m + 1) * 128],
                       x_sb[:, cc * n + x_off: cc * n + x_off + n],
                       start=(cc == 0), stop=(cc == nchunk - 1), waits=w)
        ev = out_fn(m, ps, t)
        ctx.war["proj%d" % (m % 2)] = ev


def emit_l0_attn(ctx, src_name, dst_name):
    s, c, sa = ctx.s, ctx.c, ctx.sa
    src = ctx.dram[src_name]
    dst = ctx.dram[dst_name]
    mark = sa.mark()
    xbuf = sa.alloc("p1_x", [128, DC * 512], F32)
    sqbuf = sa.alloc("p1_sq", [128, 512], BF16)
    rowbuf = sa.alloc("p1_row", [1, 5 * 512], F32)
    tmpbuf = sa.alloc("p1_tmp", [128, 512], F32)
    xhat = sa.alloc("p1_xhat", [128, DC * T], BF16)
    wq_sb = sa.alloc("p1_wq", [128, DC * 128], BF16)
    wk_sb = sa.alloc("p1_wk", [128, DC * 128], BF16)
    wv_sb = sa.alloc("p1_wv", [128, DC * 128], BF16)
    wo_sb = sa.alloc("p1_wo", [128, 1024], BF16)
    Q_sb = sa.alloc("p1_Q", [128, T], BF16)
    K_sb = sa.alloc("p1_K", [128, T], BF16)
    V_sb = sa.alloc("p1_V", [128, 16 * 130], BF16)
    PT_sb = [sa.alloc("p1_PT%d" % i, [128, 2048], BF16) for i in range(2)]
    dmask = [sa.alloc("p1_dm%d" % i, [128, 128], BF16) for i in range(2)]
    otmn = sa.alloc("p1_otmn", [128, 128], BF16)
    recip = sa.alloc("p1_rc", [128, 16], F32)
    ofm4 = sa.alloc("p1_ofm4", [128, 512], BF16)
    xres = [sa.alloc(f"p1_xres{i}", [128, 512], F32) for i in range(2)]
    ostg = [sa.alloc(f"p1_ostg{i}", [128, 512], F32) for i in range(4)]

    wq_t = ctx.dram["wq0"].rearrange("(c p) m -> p c m", p=128)
    wk_t = ctx.dram["wk0"].rearrange("(c p) m -> p c m", p=128)
    wv_t = ctx.dram["wv0"].rearrange("(c p) m -> p c m", p=128)
    wtok = [ctx.cast_load(wq_sb[:, :].rearrange("p (c m) -> p c m", m=128), wq_t),
            ctx.cast_load(wk_sb[:, :].rearrange("p (c m) -> p c m", m=128), wk_t),
            ctx.cast_load(wv_sb[:, :].rearrange("p (c m) -> p c m", m=128), wv_t),
            ctx.cast_load(wo_sb[:, :], ctx.dram["wo0"][:, :])]
    ones_tok = ctx.dve(lambda e: e.memset(V_sb[:, :], 1.0))

    for sq in range(NSEQ):
        base = sq * T
        xh_toks = []
        for tile in range(4):
            def dst_fn(cc, tile=tile):
                return xhat[:, cc * T + tile * 512: cc * T + tile * 512 + 512]
            ts_, _ = emit_ln(ctx, src, base + tile * 512, 512,
                             xbuf, sqbuf, rowbuf, tmpbuf, dst_fn)
            xh_toks += ts_
        xh_tok = s.now("Vn")
        qk_evs = []
        for (w_sb, dst_sb) in ((wq_sb, Q_sb), (wk_sb, K_sb)):
            for tile in range(4):
                ps = ctx.ps_g[2][:, 0:512] if tile % 2 == 0 else ctx.st[0][:, 0:512]
                t = None
                for cc in range(DC):
                    w = ([xh_tok, wtok[0], wtok[1], wtok[2],
                          ctx.war.get("qk%d" % (tile % 2))] if cc == 0 else [])
                    t = ctx.mm(ps, w_sb[:, cc * 128:(cc + 1) * 128],
                               xhat[:, cc * T + tile * 512: cc * T + tile * 512 + 512],
                               start=(cc == 0), stop=(cc == DC - 1), waits=w)
                o_ap = dst_sb[:, tile * 512: tile * 512 + 512]
                ev = ctx.act(lambda e, o=o_ap, i=ps: e.activation(o, i, AF.Copy),
                             waits=[t])
                ctx.war["qk%d" % (tile % 2)] = ev
                qk_evs.append(ev)
        v_evs = []
        for tc in range(16):
            ps = ctx.st[1 + (tc % 3)][:, 0:128]
            t = None
            for cc in range(DC):
                w = ([xh_tok, ctx.war.get("v%d" % (tc % 3))] if cc == 0 else [])
                t = ctx.mm(ps, xhat[:, cc * T + tc * 128: cc * T + (tc + 1) * 128],
                           wv_sb[:, cc * 128:(cc + 1) * 128],
                           start=(cc == 0), stop=(cc == DC - 1), waits=w)
            o_ap = V_sb[:, tc * 130: tc * 130 + 130].rearrange(
                "p (g k) -> p g k", k=65)[:, :, 0:64]
            i_ap = ps.rearrange("p (g k) -> p g k", k=64)
            ev = ctx.dve(lambda e, o=o_ap, i=i_ap: e.tensor_copy(o, i),
                         waits=[t, ones_tok])
            ctx.war["v%d" % (tc % 3)] = ev
            v_evs.append(ev)
        qk_tok = s.now("Cn")
        v_tok = s.now("Vn")

        def oproj(jg, ofm_toks, base=base):
            last = None
            for m in range(8):
                ps = ctx.st[m % 2][:, 0:512]
                t = ctx.mm(ps, wo_sb[:, m * 128:(m + 1) * 128], ofm4[:, 0:512],
                           start=True, stop=True,
                           waits=list(ofm_toks) + [wtok[3],
                                  ctx.war.get("op%d" % (m % 2)),
                                  ctx.war.get("opx%d" % (m % 2))])
                xr = xres[m % 2]
                xt = ctx.load(xr[:, :], src[m * 128:(m + 1) * 128,
                                            base + jg * 512: base + jg * 512 + 512],
                              waits=[ctx.war.get("xres%d" % (m % 2))])
                og = ostg[(jg * 8 + m) % 4]
                ev = ctx.dve(lambda e, o=og[:, :], i=ps, x=xr[:, :]:
                             e.tensor_tensor(o, i, x, ALU.add),
                             waits=[t, xt, ctx.war.get("ostg%d" % ((jg * 8 + m) % 4))])
                st = ctx.store(dst[m * 128:(m + 1) * 128,
                                   base + jg * 512: base + jg * 512 + 512],
                               og[:, :], waits=[ev])
                ctx.war["ostg%d" % ((jg * 8 + m) % 4)] = st
                ctx.war["op%d" % (m % 2)] = ev
                ctx.war["xres%d" % (m % 2)] = ev
                last = ev
            return last

        emit_attn_core(ctx, 1, 128, Q_sb, K_sb, V_sb, 128.0 ** -0.5,
                       PT_sb, dmask, otmn, recip, ofm4, qk_tok, v_tok, oproj)
        s.barrier()
        ctx.war = {}
    sa.reset(mark)



def _prep_weights(inp):
    w = {}
    f32c = lambda a: np.ascontiguousarray(np.asarray(a), dtype=np.float32)
    w["wq0"] = f32c(inp["L0_attn_Wq"])
    w["wk0"] = f32c(inp["L0_attn_Wk"])
    w["wv0"] = f32c(inp["L0_attn_Wv"])
    w["wo0"] = f32c(inp["L0_attn_Wo"])
    w["w1t"] = f32c(np.asarray(inp["L0_ffn_W1"]).T)
    w["b1v"] = f32c(np.asarray(inp["L0_ffn_b1"]).reshape(32, 128).T)
    w["w2t"] = f32c(np.asarray(inp["L0_ffn_W2"]).T)
    w["b2v"] = f32c(np.asarray(inp["L0_ffn_b2"]).reshape(8, 128).T)
    HEADS = 16
    hd = D // HEADS
    perm = np.arange(D).reshape(hd, HEADS).T.reshape(-1)
    w["qw1"] = f32c(np.asarray(inp["L1_attn_qw"])[:, perm])
    w["kw1"] = f32c(np.asarray(inp["L1_attn_kw"])[:, perm])
    w["vw1"] = f32c(np.asarray(inp["L1_attn_vw"])[:, perm])
    w["ow1"] = f32c(np.asarray(inp["L1_attn_ow"])[perm, :])
    w["ln1v"] = f32c(np.asarray(inp["L1_ffn_ln1_w"]).reshape(8, 128).T)
    w["ln2v"] = f32c(np.asarray(inp["L1_ffn_ln2_w"]).reshape(8, 128).T)
    w["kvdt"] = f32c(np.asarray(inp["L1_ffn_kv_d"]).T)
    w["qdt"] = f32c(np.asarray(inp["L1_ffn_q_d"]).T)
    w["vut"] = f32c(np.asarray(inp["L1_ffn_v_u"]).T)
    rq = np.asarray(inp["L1_ffn_rope_q_w"], np.float32)
    rk = np.asarray(inp["L1_ffn_rope_k_w"], np.float32)
    qu_full = np.asarray(inp["L1_ffn_q_u"], np.float32)
    ku_full = np.asarray(inp["L1_ffn_k_u"], np.float32)
    pj = (np.arange(32) + 16) % 32
    qmix = np.zeros((1024, 256), np.float32)
    qmixp = np.zeros((1024, 256), np.float32)
    kmixn = np.zeros((1024, 256), np.float32)
    kmixr = np.zeros((1024, 1024), np.float32)
    kmixrp = np.zeros((1024, 1024), np.float32)
    for h in range(16):
        rows = slice(h * 64, h * 64 + 32)
        qmix[h * 64:h * 64 + 32] = rq[h * 32:h * 32 + 32]
        qmix[h * 64 + 32:h * 64 + 64] = qu_full[h * 64 + 32:h * 64 + 64]
        qmixp[h * 64:h * 64 + 32] = rq[h * 32 + pj]
        kmixn[h * 64 + 32:h * 64 + 64] = ku_full[h * 64 + 32:h * 64 + 64]
        kmixr[h * 64:h * 64 + 32] = rk[h * 32:h * 32 + 32]
        kmixrp[h * 64:h * 64 + 32] = rk[h * 32 + pj]
    w["qmix"] = f32c(qmix.T)
    w["qmixp"] = f32c(qmixp.T)
    w["kmixn"] = f32c(kmixn.T)
    w["kmixr"] = f32c(kmixr.T)
    w["kmixrp"] = f32c(kmixrp.T)
    w["owft"] = f32c(np.asarray(inp["L1_ffn_o_w"]).T)
    w["rwt"] = f32c(np.asarray(inp["L1_ffn_router_w"]).T)
    w["rbias"] = f32c(np.asarray(inp["L1_ffn_routing_bias"]).reshape(1, 7))
    ga = np.concatenate([np.asarray(inp["L1_ffn_shared_gate"]),
                         np.asarray(inp["L1_ffn_routed_gate"])], 0)
    ua = np.concatenate([np.asarray(inp["L1_ffn_shared_up"]),
                         np.asarray(inp["L1_ffn_routed_up"])], 0)
    da = np.concatenate([np.asarray(inp["L1_ffn_shared_down"]),
                         np.asarray(inp["L1_ffn_routed_down"])], 0)
    w["gall"] = f32c(ga.reshape(8, 16, 128, 8, 128).transpose(0, 1, 4, 3, 2))
    w["uall"] = f32c(ua.reshape(8, 16, 128, 8, 128).transpose(0, 1, 4, 3, 2))
    w["dall"] = f32c(da.reshape(8, 8, 128, 16, 128).transpose(0, 1, 4, 3, 2))

    def tile4(mat):
        return f32c(np.ascontiguousarray(
            np.asarray(mat, np.float32).reshape(8, 128, 8, 128).transpose(2, 1, 0, 3)))
    w["qw1"] = tile4(w["qw1"])
    w["kw1"] = tile4(w["kw1"])
    w["ow1"] = tile4(w["ow1"])
    w["kmixr"] = tile4(w["kmixr"])
    w["kmixrp"] = tile4(w["kmixrp"])
    w["owft"] = tile4(w["owft"])
    w["vw1"] = f32c(np.asarray(w["vw1"]).reshape(8, 128, 2, 512).transpose(2, 1, 0, 3))
    return w


WSHAPES = {
    "wq0": ([1024, 128], F32), "wk0": ([1024, 128], F32), "wv0": ([1024, 128], F32),
    "wo0": ([128, 1024], F32), "w1t": ([1024, 4096], F32), "b1v": ([128, 32], F32),
    "w2t": ([4096, 1024], F32), "b2v": ([128, 8], F32),
    "qw1": ([8, 128, 8, 128], F32), "kw1": ([8, 128, 8, 128], F32),
    "vw1": ([2, 128, 8, 512], F32), "ow1": ([8, 128, 8, 128], F32),
    "ln1v": ([128, 8], F32), "ln2v": ([128, 8], F32),
    "kvdt": ([1024, 256], F32), "qdt": ([1024, 256], F32),
    "vut": ([256, 1024], F32),
    "qmix": ([256, 1024], F32), "qmixp": ([256, 1024], F32),
    "kmixn": ([256, 1024], F32),
    "kmixr": ([8, 128, 8, 128], F32), "kmixrp": ([8, 128, 8, 128], F32),
    "owft": ([8, 128, 8, 128], F32), "rwt": ([1024, 7], F32), "rbias": ([1, 7], F32),
    "gall": ([8, 16, 128, 8, 128], F32), "uall": ([8, 16, 128, 8, 128], F32),
    "dall": ([8, 8, 128, 16, 128], F32),
}

ACT_BUFS = {
    "XT": None, "RX1": None, "RX2": None, "RX3": None, "Y": None, "Y1": None,
    "OUT": None,
}

PHASES = {
    "P0c": (lambda ctx: emit_precast(ctx),
            ["gall", "uall", "dall", "qw1", "kw1", "vw1", "ow1",
             "kmixr", "kmixrp", "owft"], [], []),
    "P1": (lambda ctx: emit_l0_attn(ctx, "XT", "RX1"),
           ["wq0", "wk0", "wv0", "wo0"], ["XT"], ["RX1"]),
    "P2": (lambda ctx: emit_l0_ffn(ctx, "RX1", "RX2"),
           ["w1t", "b1v", "w2t", "b2v"], ["RX1"], ["RX2"]),
    "P3": (lambda ctx: emit_l1_attn(ctx, "RX2", "RX3"),
           [], ["RX2"], ["RX3"]),
    "P4": (lambda ctx: emit_outer_ln(ctx, "RX3", "Y"),
           [], ["RX3"], ["Y"]),
    "P5": (lambda ctx: emit_mla(ctx, "Y", "Y1"),
           ["ln1v", "qmix", "qmixp", "kmixn", "kvdt", "qdt", "vut"],
           ["Y"], ["Y1"]),
    "P6": (lambda ctx: emit_moe(ctx, "RX3", "Y1", "OUT"),
           ["ln2v", "rwt", "rbias"], ["RX3", "Y1"], ["OUT"]),
}

SCRATCH_BUFS = {
    "WQ1B": ([1024, 4096], BF16),
    "WQ2B": ([8, 32, 128, 128], BF16),
    "GB": ([8, 16, 128, 1024], BF16), "UB": ([8, 16, 128, 1024], BF16),
    "DB": ([8, 8, 128, 2048], BF16),
    "QWB": ([8, 128, 1024], BF16), "KWB": ([8, 128, 1024], BF16),
    "VWB": ([2, 128, 4096], BF16), "OWB": ([8, 128, 1024], BF16),
    "KRB": ([8, 128, 1024], BF16), "KRPB": ([8, 128, 1024], BF16),
    "OWFB": ([8, 128, 1024], BF16),
}
SCRATCH_OF = {"P2": ["WQ1B", "WQ2B"],
              "P0c": ["GB", "UB", "DB", "QWB", "KWB", "VWB", "OWB",
                      "KRB", "KRPB", "OWFB"],
              "P3": ["QWB", "KWB", "VWB", "OWB"],
              "P5": ["KRB", "KRPB", "OWFB"],
              "P6": ["GB", "UB", "DB"]}


def build_launch(phases, ext_in_bufs, ext_out_bufs):
    wnames = sorted({w for p in phases for w in PHASES[p][1]})
    bufs = sorted({b for p in phases for b in PHASES[p][2] + PHASES[p][3]})
    inputs = [(n,) + tuple(WSHAPES[n]) for n in wnames]
    inputs = [(n, WSHAPES[n][0], WSHAPES[n][1]) for n in wnames]
    for b in bufs:
        if b in ext_in_bufs:
            inputs.append((b, [D, NTOK], F32))
    outputs = [(b, [D, NTOK], F32) for b in bufs if b in ext_out_bufs]
    scratch = [(b, [D, NTOK], F32) for b in bufs
               if b not in ext_in_bufs and b not in ext_out_bufs]
    seen = set()
    for p in phases:
        for sb in SCRATCH_OF.get(p, []):
            if sb not in seen:
                seen.add(sb)
                scratch.append((sb,) + SCRATCH_BUFS[sb])
    ctx = Ctx(inputs, scratch, outputs)
    for p in phases:
        PHASES[p][0](ctx)
        ctx.s.barrier()
        ctx.war = {}
    ctx.s.build()
    return ctx, wnames


def _run_launch(phases, ext_inputs_per_core, out_names):
    ctx, wnames = build_launch(
        phases,
        ext_in_bufs={n for n in ext_inputs_per_core[0]
                     if n in ACT_BUFS},
        ext_out_bufs=set(out_names))
    res = run_bass_kernel_spmd(ctx.nc, ext_inputs_per_core,
                               core_ids=list(range(len(ext_inputs_per_core))))
    return res.results


ALL_PHASES = ["P0c", "P1", "P2", "P3", "P4", "P5", "P6"]


def kernel(**inputs):
    w = _prep_weights(inputs)
    x = np.asarray(inputs["L0_attn_x"], dtype=np.float32)
    n_cores = 8
    in_maps = []
    for cc in range(n_cores):
        shard = x[cc * NSEQ:(cc + 1) * NSEQ]
        xt = np.ascontiguousarray(shard.reshape(NTOK, D).T)
        m = dict(w)
        m["XT"] = xt
        in_maps.append(m)
    results = _run_launch(ALL_PHASES, in_maps, ["OUT"])
    outs = []
    for cc in range(n_cores):
        o = results[cc]["OUT"]
        outs.append(np.ascontiguousarray(o.T).reshape(NSEQ, T, D))
    return np.concatenate(outs, 0)



def emit_quantize(ctx, w_name, rows, cols, store_fn):
    s, c, sa = ctx.s, ctx.c, ctx.sa
    src = ctx.dram[w_name]
    nch = rows // 128
    mark = sa.mark()
    wbuf = sa.alloc("qz_w", [128, cols], F32)
    abuf = sa.alloc("qz_a", [128, cols], F32)
    grow = sa.alloc("qz_g", [1, 8], F32)
    gvec = sa.alloc("qz_gv", [128, 1], F32)
    qbuf = sa.alloc("qz_q", [128, cols], BF16)
    ps = ctx.ps_g[0]
    mm_tok = None
    ld_toks = []
    for ch in range(nch):
        lt = ctx.load(wbuf[:, :], src[ch * 128:(ch + 1) * 128, :],
                      waits=[ctx.war.get("qzw")])
        ab = ctx.act(lambda e, o=abuf[:, :], i=wbuf[:, :]:
                     e.activation(o, i, AF.Abs), waits=[lt, mm_tok])
        ctx.war["qzw"] = ab
        for sub in range(cols // 512):
            mm_tok = ctx.mm(ps[0:1, 0:512], c["ones_f"][:, :],
                            abuf[:, sub * 512:(sub + 1) * 512],
                            start=(ch == 0 and sub == 0),
                            stop=(ch == nch - 1 and sub == cols // 512 - 1),
                            force_inc=True, waits=[ab])
    t1 = ctx.dve(lambda e: e.reduce_sum(grow[0:1, 0:1], ps[0:1, 0:512], AX.X),
                 waits=[mm_tok])
    t2 = ctx.dve(lambda e: e.tensor_scalar(grow[0:1, 1:2], grow[0:1, 0:1],
                                           1.0 / (rows * cols), EPS,
                                           ALU.mult, ALU.add), waits=[t1])
    t3 = ctx.dve(lambda e: e.reciprocal(grow[0:1, 2:3], grow[0:1, 1:2]),
                 waits=[t2])
    bc = ctx.mm(ctx.ps_g[1][0:128, 0:1], c["ones_r"][:, :], grow[0:1, 2:3],
                start=True, stop=True, waits=[t3, ctx.war.get("g1")])
    t4 = ctx.dve(lambda e: e.tensor_copy(gvec[:, :], ctx.ps_g[1][:, 0:1]),
                 waits=[bc])
    prev_q = None
    for ch in range(nch):
        lt = ctx.load(wbuf[:, :], src[ch * 128:(ch + 1) * 128, :],
                      waits=[ctx.war.get("qzw")])
        a1 = ctx.dve(lambda e, o=abuf[:, :], i=wbuf[:, :]:
                     e.tensor_scalar(o, i, gvec[:, :], MAGIC, ALU.mult, ALU.add),
                     waits=[lt, t4, prev_q])
        ctx.war["qzw"] = a1
        a2 = ctx.dve(lambda e, o=abuf[:, :], i=abuf[:, :]:
                     e.tensor_scalar(o, i, MAGIC, None, ALU.subtract),
                     waits=[a1])
        a3 = ctx.dve(lambda e, o=qbuf[:, :], i=abuf[:, :]:
                     e.tensor_scalar(o, i, 1.0, -1.0, ALU.min, ALU.max),
                     waits=[a2, prev_q])
        prev_q = store_fn(ch, qbuf, a3)
    s.barrier()
    ctx.war = {}
    sa.reset(mark)


def emit_l0_ffn(ctx, src_name, dst_name):
    s, c, sa = ctx.s, ctx.c, ctx.sa
    src = ctx.dram[src_name]
    dst = ctx.dram[dst_name]
    wq1d = ctx.dram["WQ1B"]
    wq2d = ctx.dram["WQ2B"]

    def store1(ch, qb, tok):
        return ctx.store(wq1d[ch * 128:(ch + 1) * 128, :], qb[:, :], waits=[tok])

    def store2(ch, qb, tok):
        d = wq2d[:, ch, :, :].rearrange("m p j -> p m j")
        sp = qb[:, :].rearrange("p (m j) -> p m j", j=128)
        return ctx.store(d, sp, waits=[tok])

    emit_quantize(ctx, "w1t", 1024, 4096, store1)
    emit_quantize(ctx, "w2t", 4096, 1024, store2)

    mark = sa.mark()
    xbuf = sa.alloc("p2_x", [128, DC * 512], F32)
    sqbuf = sa.alloc("p2_sq", [128, 512], BF16)
    rowbuf = sa.alloc("p2_row", [1, 5 * 512], F32)
    tmpbuf = sa.alloc("p2_tmp", [128, 512], F32)
    xhat = sa.alloc("p2_xhat", [128, DC * 512], BF16)
    wq1_sb = sa.alloc("p2_wq1", [128, DC * 4096], BF16)
    h_sb = sa.alloc("p2_h", [128, 32 * 512], BF16)
    wq2_sb = [sa.alloc(f"p2_wq2_{i}", [128, 32 * 128], BF16) for i in range(2)]
    b1_sb = sa.alloc("p2_b1", [128, 32], F32)
    b2_sb = sa.alloc("p2_b2", [128, 8], F32)
    ostg = [sa.alloc(f"p2_ostg{i}", [128, 512], F32) for i in range(4)]

    for ch in range(DC):
        ctx.load(wq1_sb[:, ch * 4096:(ch + 1) * 4096],
                 wq1d[ch * 128:(ch + 1) * 128, :])
    ctx.load(b1_sb[:, :], ctx.dram["b1v"][:, :])
    ctx.load(b2_sb[:, :], ctx.dram["b2v"][:, :])
    s.barrier()

    for tile in range(NTOK // 512):
        col0 = tile * 512

        def dst_fn(cc):
            return xhat[:, cc * 512:(cc + 1) * 512]
        emit_ln(ctx, src, col0, 512, xbuf, sqbuf, rowbuf, tmpbuf, dst_fn)
        xh_tok = s.now("Vn")
        for mf in range(32):
            ps = ctx.st[mf % 2]
            t = None
            for cc in range(DC):
                w = ([xh_tok, ctx.war.get("h%d" % (mf % 2))] if cc == 0 else [])
                t = ctx.mm(ps[:, 0:512],
                           wq1_sb[:, cc * 4096 + mf * 128: cc * 4096 + (mf + 1) * 128],
                           xhat[:, cc * 512:(cc + 1) * 512],
                           start=(cc == 0), stop=(cc == DC - 1), waits=w)
            o_ap = h_sb[:, mf * 512:(mf + 1) * 512]
            ev = ctx.act(lambda e, o=o_ap, i=ps[:, 0:512], b=b1_sb[:, mf:mf + 1]:
                         e.activation(o, i, AF.Relu, bias=b), waits=[t])
            ctx.war["h%d" % (mf % 2)] = ev
        h_tok = s.now("Cn")
        for m in range(8):
            slot = wq2_sb[m % 2]
            d = wq2d[m].rearrange("k p j -> p k j")
            wt = ctx.load(slot[:, :].rearrange("p (k j) -> p k j", j=128), d,
                          waits=[ctx.war.get("wq2%d" % (m % 2))])
            ps = ctx.st[2 + m % 2]
            t = None
            for k in range(32):
                w = ([h_tok, wt, ctx.war.get("o%d" % (m % 2))] if k == 0 else [])
                t = ctx.mm(ps[:, 0:512], slot[:, k * 128:(k + 1) * 128],
                           h_sb[:, k * 512:(k + 1) * 512],
                           start=(k == 0), stop=(k == 31), waits=w)
            ctx.war["wq2%d" % (m % 2)] = t
            og = ostg[m % 4]
            ev = ctx.dve(lambda e, o=og[:, :], i=ps[:, 0:512],
                         b=b2_sb[:, m:m + 1], x=xbuf[:, m * 512:(m + 1) * 512]:
                         e.scalar_tensor_tensor(o, i, b, x, ALU.add, ALU.add),
                         waits=[t, ctx.war.get("ostg%d" % (m % 4))])
            ctx.war["o%d" % (m % 2)] = ev
            ctx.war[f"lnx{m}"] = ev
            st = ctx.store(dst[m * 128:(m + 1) * 128, col0:col0 + 512],
                           og[:, :], waits=[ev])
            ctx.war["ostg%d" % (m % 4)] = st
        ctx.war["h0"] = s.now("Tn")
        ctx.war["h1"] = s.now("Tn")
    s.barrier()
    ctx.war = {}
    sa.reset(mark)



def emit_l1_attn(ctx, src_name, dst_name):
    s, c, sa = ctx.s, ctx.c, ctx.sa
    src = ctx.dram[src_name]
    dst = ctx.dram[dst_name]
    mark = sa.mark()
    xbuf = sa.alloc("p3_x", [128, DC * 512], F32)
    sqbuf = sa.alloc("p3_sq", [128, 512], BF16)
    rowbuf = sa.alloc("p3_row", [1, 5 * 512], F32)
    tmpbuf = sa.alloc("p3_tmp", [128, 512], F32)
    xhat = sa.alloc("p3_xhat", [128, DC * T], BF16)
    Q_sb = sa.alloc("p3_Q", [128, DC * T], BF16)
    K_sb = sa.alloc("p3_K", [128, DC * T], BF16)
    V_sb = sa.alloc("p3_V", [128, 16 * 1040], BF16)
    vw_sb = sa.alloc("p3_vw", [128, DC * 512], BF16)
    qkw_sb = [sa.alloc(f"p3_qkw{i}", [128, DC * 128], BF16) for i in range(2)]
    ow_sb = [sa.alloc(f"p3_ow{i}", [128, DC * 128], BF16) for i in range(2)]
    ofm4 = sa.alloc("p3_ofm4", [128, DC * 512], BF16)
    PT_sb = [sa.alloc("p3_PT%d" % i, [128, 2048], BF16) for i in range(2)]
    dmask = [sa.alloc("p3_dm%d" % i, [128, 128], BF16) for i in range(2)]
    otmn = sa.alloc("p3_otmn", [128, 1024], BF16)
    recip = sa.alloc("p3_rc", [128, 16], F32)
    xres = [sa.alloc(f"p3_xres{i}", [128, 512], F32) for i in range(2)]
    ostg = [sa.alloc(f"p3_ostg{i}", [128, 512], F32) for i in range(2)]

    qw_d = ctx.dram["QWB"]
    kw_d = ctx.dram["KWB"]
    vw_d = ctx.dram["VWB"]
    ow_d = ctx.dram["OWB"]
    ones_tok = ctx.dve(lambda e: e.memset(V_sb[:, :], 1.0))

    for sq in range(NSEQ):
        base = sq * T
        for tile in range(4):
            def dst_fn(cc, tile=tile):
                return xhat[:, cc * T + tile * 512: cc * T + tile * 512 + 512]
            emit_ln(ctx, src, base + tile * 512, 512,
                    xbuf, sqbuf, rowbuf, tmpbuf, dst_fn)
        xh_tok = s.now("Vn")
        for wi, (wd, dst_sb) in enumerate(((qw_d, Q_sb), (kw_d, K_sb))):
            for mc in range(DC):
                slot = qkw_sb[mc % 2]
                wt = ctx.load(slot[:, :], wd[mc],
                              waits=[ctx.war.get("qkw%d" % (mc % 2))])
                for tile in range(4):
                    pi = (mc * 4 + tile) % 2
                    ps = ctx.st[pi]
                    t = None
                    for cc in range(DC):
                        w = ([xh_tok, wt, ctx.war.get("qkps%d" % pi)]
                             if cc == 0 else [])
                        t = ctx.mm(ps[:, 0:512], slot[:, cc * 128:(cc + 1) * 128],
                                   xhat[:, cc * T + tile * 512: cc * T + tile * 512 + 512],
                                   start=(cc == 0), stop=(cc == DC - 1), waits=w)
                    o_ap = dst_sb[:, mc * T + tile * 512: mc * T + tile * 512 + 512]
                    ev = ctx.act(lambda e, o=o_ap, i=ps[:, 0:512]:
                                 e.activation(o, i, AF.Copy), waits=[t])
                    ctx.war["qkps%d" % pi] = ev
                ctx.war["qkw%d" % (mc % 2)] = s.now("Tn")
        for dvs in range(2):
            vt = ctx.load(vw_sb[:, :], vw_d[dvs], waits=[ctx.war.get("vw")])
            for tc in range(16):
                ps = ctx.st[2 + tc % 2]
                t = None
                for cc in range(DC):
                    w = ([xh_tok, vt, ctx.war.get("vps%d" % (tc % 2))]
                         if cc == 0 else [])
                    t = ctx.mm(ps[:, 0:512],
                               xhat[:, cc * T + tc * 128: cc * T + (tc + 1) * 128],
                               vw_sb[:, cc * 512:(cc + 1) * 512],
                               start=(cc == 0), stop=(cc == DC - 1), waits=w)
                o_ap = V_sb[:, tc * 1040 + dvs * 520: tc * 1040 + dvs * 520 + 520]
                o_ap = o_ap.rearrange("p (g k) -> p g k", k=65)[:, :, 0:64]
                i_ap = ps[:, 0:512].rearrange("p (g k) -> p g k", k=64)
                ev = ctx.dve(lambda e, o=o_ap, i=i_ap: e.tensor_copy(o, i),
                             waits=[t, ones_tok])
                ctx.war["vps%d" % (tc % 2)] = ev
            ctx.war["vw"] = s.now("Tn")
        qk_tok = s.now("Cn")
        v_tok = s.now("Vn")

        def oproj(jg, ofm_toks, base=base):
            last = None
            for m in range(8):
                slot = ow_sb[m % 2]
                wt = ctx.load(slot[:, :], ow_d[m],
                              waits=[ctx.war.get("oww%d" % (m % 2))])
                ps = ctx.st[m % 2]
                t = None
                for cc in range(DC):
                    w = (list(ofm_toks) + [wt, ctx.war.get("op%d" % (m % 2)),
                          ctx.war.get("opx%d" % (m % 2))] if cc == 0 else [])
                    t = ctx.mm(ps[:, 0:512], slot[:, cc * 128:(cc + 1) * 128],
                               ofm4[:, cc * 512:(cc + 1) * 512],
                               start=(cc == 0), stop=(cc == DC - 1), waits=w)
                ctx.war["oww%d" % (m % 2)] = t
                xr = xres[m % 2]
                xt = ctx.load(xr[:, :], src[m * 128:(m + 1) * 128,
                                            base + jg * 512: base + jg * 512 + 512],
                              waits=[ctx.war.get("xres%d" % (m % 2))])
                og = ostg[m % 2]
                ev = ctx.dve(lambda e, o=og[:, :], i=ps[:, 0:512], x=xr[:, :]:
                             e.tensor_tensor(o, i, x, ALU.add),
                             waits=[t, xt, ctx.war.get("ostg%d" % (m % 2))])
                st = ctx.store(dst[m * 128:(m + 1) * 128,
                                   base + jg * 512: base + jg * 512 + 512],
                               og[:, :], waits=[ev])
                ctx.war["ostg%d" % (m % 2)] = st
                ctx.war["op%d" % (m % 2)] = ev
                ctx.war["xres%d" % (m % 2)] = ev
                last = ev
            return last

        emit_attn_core(ctx, 16, 64, Q_sb, K_sb, V_sb, float(D) ** -0.5,
                       PT_sb, dmask, otmn, recip, ofm4, qk_tok, v_tok, oproj)
        s.barrier()
        ctx.war = {}
    sa.reset(mark)



def emit_outer_ln(ctx, src_name, dst_name):
    s, sa = ctx.s, ctx.sa
    src = ctx.dram[src_name]
    dst = ctx.dram[dst_name]
    mark = sa.mark()
    xbuf = sa.alloc("p4_x", [128, DC * 512], F32)
    sqbuf = sa.alloc("p4_sq", [128, 512], BF16)
    rowbuf = sa.alloc("p4_row", [1, 5 * 512], F32)
    tmpbuf = sa.alloc("p4_tmp", [128, 512], F32)
    ybuf = sa.alloc("p4_y", [128, DC * 512], F32)
    for tile in range(NTOK // 512):
        col0 = tile * 512

        def dst_fn(cc):
            return ybuf[:, cc * 512:(cc + 1) * 512]
        toks, _ = emit_ln(ctx, src, col0, 512, xbuf, sqbuf, rowbuf, tmpbuf,
                          dst_fn, dst_war=lambda cc: ctx.war.get(f"p4y{cc}"))
        for cc in range(DC):
            st = ctx.store(dst[cc * 128:(cc + 1) * 128, col0:col0 + 512],
                           ybuf[:, cc * 512:(cc + 1) * 512], waits=[toks[cc]])
            ctx.war[f"p4y{cc}"] = st
    s.barrier()
    ctx.war = {}
    sa.reset(mark)



def emit_mla(ctx, src_name, dst_name):
    s, c, sa = ctx.s, ctx.c, ctx.sa
    src = ctx.dram[src_name]
    dst = ctx.dram[dst_name]
    mark = sa.mark()
    K_sb = sa.alloc("p5_K", [128, DC * T], BF16)
    Q_sb = sa.alloc("p5_Q", [128, DC * T], BF16)
    V_sb = sa.alloc("p5_V", [128, 16 * 1040], BF16)
    qmix_sb = sa.alloc("p5_qmix", [128, 2 * 1024], BF16)
    qmixp_sb = sa.alloc("p5_qmixp", [128, 2 * 1024], BF16)
    kmixn_sb = sa.alloc("p5_kmixn", [128, 2 * 1024], BF16)
    kr_sb = [sa.alloc(f"p5_kr{i}", [128, DC * 128], BF16) for i in range(2)]
    krp_sb = [sa.alloc(f"p5_krp{i}", [128, DC * 128], BF16) for i in range(2)]
    vut_sb = sa.alloc("p5_vut", [128, 2 * 512], BF16)
    cmask = sa.alloc("p5_cm", [128, T], BF16)
    smask = sa.alloc("p5_sm", [128, T], BF16)
    ln1v_sb = sa.alloc("p5_ln1", [128, 8], F32)
    seq_mark = sa.mark()

    cm = np.ones((128, T), np.float32)
    sm = np.zeros((128, T), np.float32)
    cm[0:32] = ctx.rope_cos[0:32]
    cm[64:96] = ctx.rope_cos[0:32]
    sm[0:32] = ctx.rope_sin[0:32]
    sm[64:96] = ctx.rope_sin[0:32]
    cm_d = ctx.nc.inline_tensor(cm.astype(BF), name="p5cm").ap()
    sm_d = ctx.nc.inline_tensor(sm.astype(BF), name="p5sm").ap()
    ctx.load(cmask[:, :], cm_d[:, :])
    ctx.load(smask[:, :], sm_d[:, :])
    ctx.load(ln1v_sb[:, :], ctx.dram["ln1v"][:, :])
    for (sb, nm, m) in ((qmix_sb, "qmix", 1024), (qmixp_sb, "qmixp", 1024),
                        (kmixn_sb, "kmixn", 1024)):
        ctx.cast_load(sb[:, :].rearrange("p (cc m) -> p cc m", m=m),
                      ctx.dram[nm].rearrange("(cc p) m -> p cc m", p=128))
    kvd_d = ctx.dram["kvdt"].rearrange("(cc p) m -> p cc m", p=128)
    qd_d = ctx.dram["qdt"].rearrange("(cc p) m -> p cc m", p=128)
    kr_d = ctx.dram["KRB"]
    krp_d = ctx.dram["KRPB"]
    vut_d = ctx.dram["vut"].rearrange("(cc p) m -> p cc m", p=128)
    ow_d = ctx.dram["OWFB"]
    s.barrier()

    yh = sa.alloc("p5_yh", [128, DC * T], BF16)
    kvlat = sa.alloc("p5_kvl", [128, 2 * T], BF16)
    qlat = sa.alloc("p5_ql", [128, 2 * T], BF16)
    ln_mark = sa.mark()
    xbuf = sa.alloc("p5_x", [128, DC * 512], F32)
    sqbuf = sa.alloc("p5_sq", [128, 512], BF16)
    rowbuf = sa.alloc("p5_row", [1, 5 * 512], F32)
    tmpbuf = sa.alloc("p5_tmp", [128, 512], F32)
    sa.reset(ln_mark)
    rt1 = sa.alloc("p5_rt1", [128, 512], F32)
    rt2 = sa.alloc("p5_rt2", [128, 512], F32)
    sa.reset(ln_mark)
    PT_sb = [sa.alloc("p5_PT%d" % i, [128, 2048], BF16) for i in range(2)]
    ofm4 = sa.alloc("p5_ofm4", [128, DC * 512], BF16)
    dmask = [sa.alloc("p5_dm%d" % i, [128, 128], BF16) for i in range(2)]
    otmn = sa.alloc("p5_otmn", [128, 1024], BF16)
    recip = sa.alloc("p5_rc", [128, 16], F32)
    xres = [sa.alloc(f"p5_xres{i}", [128, 512], F32) for i in range(2)]
    ostg = [sa.alloc(f"p5_ostg{i}", [128, 512], F32) for i in range(2)]

    for sq in range(NSEQ):
        base = sq * T
        for tile in range(4):
            def dst_fn(cc, tile=tile):
                return yh[:, cc * T + tile * 512: cc * T + tile * 512 + 512]
            emit_ln(ctx, src, base + tile * 512, 512, xbuf, sqbuf, rowbuf,
                    tmpbuf, dst_fn,
                    gain_col=lambda cc: ln1v_sb[:, cc:cc + 1])
        yh_tok = s.now("Vn")
        for wi, (wd, dst_sb) in enumerate(((kvd_d, kvlat), (qd_d, qlat))):
            for m2 in range(2):
                slot = kr_sb[(wi * 2 + m2) % 2]
                wt = ctx.cast_load(
                    slot[:, :].rearrange("p (cc m) -> p cc m", m=128),
                    wd[:, :, m2 * 128:(m2 + 1) * 128],
                    waits=[ctx.war.get("krw%d" % ((wi * 2 + m2) % 2))])
                for tile in range(4):
                    pi = tile % 2
                    ps = ctx.st[pi]
                    t = None
                    for cc in range(DC):
                        w = ([yh_tok, wt, ctx.war.get("lat%d" % pi)]
                             if cc == 0 else [])
                        t = ctx.mm(ps[:, 0:512],
                                   slot[:, cc * 128:(cc + 1) * 128],
                                   yh[:, cc * T + tile * 512: cc * T + tile * 512 + 512],
                                   start=(cc == 0), stop=(cc == DC - 1), waits=w)
                    o_ap = dst_sb[:, m2 * T + tile * 512: m2 * T + tile * 512 + 512]
                    ev = ctx.act(lambda e, o=o_ap, i=ps[:, 0:512]:
                                 e.activation(o, i, AF.Copy), waits=[t])
                    ctx.war["lat%d" % pi] = ev
                ctx.war["krw%d" % ((wi * 2 + m2) % 2)] = s.now("Tn")
        lat_tok = s.now("Cn")
        asm_guard = s.now("Vn")
        for cc in range(DC):
            krt = ctx.load(kr_sb[cc % 2][:, :], kr_d[cc],
                           waits=[ctx.war.get("krw%d" % (cc % 2))])
            krpt = ctx.load(krp_sb[cc % 2][:, :], krp_d[cc],
                            waits=[ctx.war.get("krpw%d" % (cc % 2))])
            for tile in range(4):
                ps1 = ctx.st[tile % 2]
                ps2 = ctx.st[2 + tile % 2]
                t1 = None
                for k8 in range(DC):
                    w = ([yh_tok, lat_tok, krt, ctx.war.get("kps%d" % (tile % 2))]
                         if k8 == 0 else [])
                    t1 = ctx.mm(ps1[:, 0:512],
                                kr_sb[cc % 2][:, k8 * 128:(k8 + 1) * 128],
                                yh[:, k8 * T + tile * 512: k8 * T + tile * 512 + 512],
                                start=(k8 == 0), stop=False, waits=w)
                for lc in range(2):
                    t1 = ctx.mm(ps1[:, 0:512],
                                kmixn_sb[:, lc * 1024 + cc * 128: lc * 1024 + (cc + 1) * 128],
                                kvlat[:, lc * T + tile * 512: lc * T + tile * 512 + 512],
                                start=False, stop=(lc == 1), waits=[])
                t2 = None
                for k8 in range(DC):
                    w = ([krpt, ctx.war.get("kps2%d" % (tile % 2))] if k8 == 0 else [])
                    t2 = ctx.mm(ps2[:, 0:512],
                                krp_sb[cc % 2][:, k8 * 128:(k8 + 1) * 128],
                                yh[:, k8 * T + tile * 512: k8 * T + tile * 512 + 512],
                                start=(k8 == 0), stop=(k8 == DC - 1), waits=w)
                m1 = ctx.dve(lambda e, o=rt1[:, :], i=ps1[:, 0:512], t=tile:
                             e.tensor_tensor(o, i, cmask[:, t * 512:(t + 1) * 512],
                                             ALU.mult),
                             waits=[t1, asm_guard, ctx.war.get("rt1")])
                m2_ = ctx.dve(lambda e, o=rt2[:, :], i=ps2[:, 0:512], t=tile:
                              e.tensor_tensor(o, i, smask[:, t * 512:(t + 1) * 512],
                                              ALU.mult),
                              waits=[t2, asm_guard, ctx.war.get("rt2")])
                kd = K_sb[:, cc * T + tile * 512: cc * T + tile * 512 + 512]
                m3 = ctx.dve(lambda e, o=kd: e.tensor_tensor(o, rt1[:, :], rt2[:, :],
                                                             ALU.add),
                             waits=[m1, m2_])
                ctx.war["kps%d" % (tile % 2)] = m1
                ctx.war["kps2%d" % (tile % 2)] = m2_
                t1 = None
                t2 = None
                for lc in range(2):
                    w = ([lat_tok, m1] if lc == 0 else [])
                    t1 = ctx.mm(ps1[:, 0:512],
                                qmix_sb[:, lc * 1024 + cc * 128: lc * 1024 + (cc + 1) * 128],
                                qlat[:, lc * T + tile * 512: lc * T + tile * 512 + 512],
                                start=(lc == 0), stop=(lc == 1), waits=w)
                    w2 = ([m2_] if lc == 0 else [])
                    t2 = ctx.mm(ps2[:, 0:512],
                                qmixp_sb[:, lc * 1024 + cc * 128: lc * 1024 + (cc + 1) * 128],
                                qlat[:, lc * T + tile * 512: lc * T + tile * 512 + 512],
                                start=(lc == 0), stop=(lc == 1), waits=w2)
                m1 = ctx.dve(lambda e, o=rt1[:, :], i=ps1[:, 0:512], t=tile:
                             e.tensor_tensor(o, i, cmask[:, t * 512:(t + 1) * 512],
                                             ALU.mult),
                             waits=[t1, m3])
                m2_ = ctx.dve(lambda e, o=rt2[:, :], i=ps2[:, 0:512], t=tile:
                              e.tensor_tensor(o, i, smask[:, t * 512:(t + 1) * 512],
                                              ALU.mult),
                              waits=[t2])
                qd = Q_sb[:, cc * T + tile * 512: cc * T + tile * 512 + 512]
                m3 = ctx.dve(lambda e, o=qd: e.tensor_tensor(o, rt1[:, :], rt2[:, :],
                                                             ALU.add),
                             waits=[m1, m2_])
                ctx.war["rt1"] = m3
                ctx.war["rt2"] = m3
                ctx.war["kps%d" % (tile % 2)] = m1
                ctx.war["kps2%d" % (tile % 2)] = m2_
            ctx.war["krw%d" % (cc % 2)] = s.now("Tn")
            ctx.war["krpw%d" % (cc % 2)] = s.now("Tn")
        ones_tok = ctx.dve(lambda e: e.memset(V_sb[:, :], 1.0),
                           waits=[ctx.war.get("vmem")])
        for dvs in range(2):
            vt = ctx.cast_load(vut_sb[:, :].rearrange("p (k m) -> p k m", m=512),
                               vut_d[:, :, dvs * 512:(dvs + 1) * 512],
                               waits=[ctx.war.get("vw")])
            for tc in range(16):
                ps = ctx.st[tc % 2]
                t = None
                for lc in range(2):
                    w = ([lat_tok, vt, ctx.war.get("kps%d" % (tc % 2)),
                          ctx.war.get("vps%d" % (tc % 2))] if lc == 0 else [])
                    t = ctx.mm(ps[:, 0:512],
                               kvlat[:, lc * T + tc * 128: lc * T + (tc + 1) * 128],
                               vut_sb[:, lc * 512:(lc + 1) * 512],
                               start=(lc == 0), stop=(lc == 1), waits=w)
                o_ap = V_sb[:, tc * 1040 + dvs * 520: tc * 1040 + dvs * 520 + 520]
                o_ap = o_ap.rearrange("p (g k) -> p g k", k=65)[:, :, 0:64]
                i_ap = ps[:, 0:512].rearrange("p (g k) -> p g k", k=64)
                ev = ctx.dve(lambda e, o=o_ap, i=i_ap: e.tensor_copy(o, i),
                             waits=[t, ones_tok])
                ctx.war["vps%d" % (tc % 2)] = ev
            ctx.war["vw"] = s.now("Tn")
        ctx.war["vmem"] = s.now("Tn")
        qk_tok = s.now("Vn")
        v_tok = s.now("Vn")
        ctx.war["oww0"] = s.now("Tn")
        ctx.war["oww1"] = s.now("Tn")
        ctx.war["xres0"] = qk_tok
        ctx.war["xres1"] = qk_tok

        def oproj(jg, ofm_toks, base=base):
            last = None
            for m in range(8):
                slot = krp_sb[m % 2]
                wt = ctx.load(slot[:, :], ow_d[m],
                              waits=[ctx.war.get("oww%d" % (m % 2))])
                ps = ctx.st[m % 2]
                t = None
                for cc in range(DC):
                    w = (list(ofm_toks) + [wt, ctx.war.get("op%d" % (m % 2)),
                          ctx.war.get("opx%d" % (m % 2))] if cc == 0 else [])
                    t = ctx.mm(ps[:, 0:512], slot[:, cc * 128:(cc + 1) * 128],
                               ofm4[:, cc * 512:(cc + 1) * 512],
                               start=(cc == 0), stop=(cc == DC - 1), waits=w)
                ctx.war["oww%d" % (m % 2)] = t
                xr = xres[m % 2]
                xt = ctx.load(xr[:, :], src[m * 128:(m + 1) * 128,
                                            base + jg * 512: base + jg * 512 + 512],
                              waits=[ctx.war.get("xres%d" % (m % 2))])
                og = ostg[m % 2]
                ev = ctx.dve(lambda e, o=og[:, :], i=ps[:, 0:512], x=xr[:, :]:
                             e.tensor_tensor(o, i, x, ALU.add),
                             waits=[t, xt, ctx.war.get("ostg%d" % (m % 2))])
                st = ctx.store(dst[m * 128:(m + 1) * 128,
                                   base + jg * 512: base + jg * 512 + 512],
                               og[:, :], waits=[ev])
                ctx.war["ostg%d" % (m % 2)] = st
                ctx.war["op%d" % (m % 2)] = ev
                ctx.war["xres%d" % (m % 2)] = ev
                last = ev
            return last

        emit_attn_core(ctx, 16, 64, Q_sb, K_sb, V_sb, 64.0 ** -0.5,
                       PT_sb, dmask, otmn, recip, ofm4, qk_tok, v_tok, oproj)
        s.barrier()
        ctx.war = {}
    sa.reset(mark)



def emit_precast(ctx):
    s = ctx.s
    pairs = [
        ("gall", "GB", 8), ("uall", "UB", 8), ("dall", "DB", 8),
        ("qw1", "QWB", 1), ("kw1", "KWB", 1), ("vw1", "VWB", 1),
        ("ow1", "OWB", 1), ("kmixr", "KRB", 1), ("kmixrp", "KRPB", 1),
        ("owft", "OWFB", 1),
    ]
    for src, dstn, nsplit in pairs:
        sd = ctx.dram[src]
        dd = ctx.dram[dstn]
        if nsplit == 1:
            ctx.cast_load(dd.rearrange("a p c -> (a p) c"),
                          sd.rearrange("a p b c -> (a p) (b c)"))
        else:
            for e in range(nsplit):
                ctx.cast_load(dd[e].rearrange("m p c -> (m p) c"),
                              sd[e].rearrange("m p b c -> (m p) (b c)"))
    s.barrier()
    ctx.war = {}



def emit_moe(ctx, x_name, y1_name, dst_name):
    s, c, sa = ctx.s, ctx.c, ctx.sa
    xsrc = ctx.dram[x_name]
    ysrc = ctx.dram[y1_name]
    dst = ctx.dram[dst_name]
    NTILE = 1024
    mark = sa.mark()
    y1 = sa.alloc("p6_y1", [128, DC * NTILE], F32)
    yh2 = sa.alloc("p6_yh2", [128, DC * NTILE], BF16)
    h_sb = sa.alloc("p6_h", [128, 16 * NTILE], BF16)
    acc = [sa.alloc(f"p6_acc{i}", [128, DC * NTILE], F32) for i in range(2)]
    wbc = sa.alloc("p6_wbc", [128, NTILE], F32)
    gstage = [sa.alloc(f"p6_gs{i}", [128, DC * 128], BF16) for i in range(2)]
    dstage = [sa.alloc(f"p6_ds{i}", [128, 16 * 128], BF16) for i in range(2)]
    glbuf = sa.alloc("p6_gl", [128, 2 * 512], BF16)
    ubuf = sa.alloc("p6_ub", [128, 2 * 512], BF16)
    tmpo = sa.alloc("p6_tmpo", [128, 512], F32)
    rp_sb = sa.alloc("p6_rp", [128, DC * 7], F32)
    ln2v_sb = sa.alloc("p6_ln2", [128, 8], F32)
    rbias_sb = sa.alloc("p6_rb", [1, 7], F32)
    rsum_sb = sa.alloc("p6_rs", [1, 7], F32)
    rsbc = sa.alloc("p6_rsbc", [128, 7], F32)
    rbbc = sa.alloc("p6_rbbc", [128, 7], F32)
    arow = sa.alloc("p6_ar", [1, NTILE], F32)
    brow = sa.alloc("p6_br", [1, NTILE], F32)
    raw_sb = sa.alloc("p6_raw", [7, NTILE], F32)
    ltm = sa.alloc("p6_ltm", [128, 8 * 7], F32)
    ab_sb = sa.alloc("p6_ab", [128, 2 * 8], F32)
    prob = sa.alloc("p6_prob", [128, 8 * 7], F32)
    rsc = sa.alloc("p6_rsc", [128, 8 * 7], F32)
    m12 = sa.alloc("p6_m12", [128, 3 * 8], F32)
    wgt_tm = sa.alloc("p6_wgt", [128, 8 * 7], F32)
    identf = sa.alloc("p6_idf", [128, 128], F32)
    diag = sa.alloc("p6_diag", [128, 128], F32)
    onesq = sa.alloc("p6_onsq", [128, 128], F32)
    xbuf_alias = sa.mark()
    sqbuf = sa.alloc("p6_sq", [128, 512], BF16)
    rowbuf = sa.alloc("p6_row", [1, 5 * 512], F32)
    tmpbuf = sa.alloc("p6_tmp", [128, 512], F32)
    xres = [sa.alloc(f"p6_xres{i}", [128, 512], F32) for i in range(2)]
    ostg = [sa.alloc(f"p6_ostg{i}", [128, 512], F32) for i in range(2)]

    idf_d = ctx.nc.inline_tensor(np.eye(128, dtype=np.float32), name="p6idf").ap()
    onsq_d = ctx.nc.inline_tensor(np.ones((128, 128), np.float32), name="p6ones").ap()
    ctx.load(identf[:, :], idf_d[:, :])
    ctx.load(onesq[:, :], onsq_d[:, :])
    ctx.load(ln2v_sb[:, :], ctx.dram["ln2v"][:, :])
    ctx.load(rbias_sb[:, :], ctx.dram["rbias"][:, :])
    rw_tok = ctx.load(rp_sb[:, :].rearrange("p (cc m) -> p cc m", m=7),
                      ctx.dram["rwt"].rearrange("(cc p) m -> p cc m", p=128))
    s.barrier()
    t = None
    for cc in range(DC):
        t = ctx.dve(lambda e, o=rp_sb[:, cc * 7:(cc + 1) * 7], g=ln2v_sb[:, cc:cc + 1]:
                    e.tensor_scalar_mul(o, o, g), waits=[])
    rs_t = None
    for cc in range(DC):
        rs_t = ctx.mm(ctx.ps_g[2][0:1, 0:7], c["ones_f"][:, :],
                      rp_sb[:, cc * 7:(cc + 1) * 7],
                      start=(cc == 0), stop=(cc == DC - 1), waits=[t])
    t = ctx.dve(lambda e: e.tensor_copy(rsum_sb[0:1, :], ctx.ps_g[2][0:1, 0:7]),
                waits=[rs_t])
    b1 = ctx.mm(ctx.ps_g[2][:, 0:7], c["ones_r"][:, :], rsum_sb[0:1, :],
                start=True, stop=True, waits=[t])
    t = ctx.dve(lambda e: e.tensor_copy(rsbc[:, :], ctx.ps_g[2][:, 0:7]), waits=[b1])
    b2 = ctx.mm(ctx.ps_g[2][:, 0:7], c["ones_r"][:, :], rbias_sb[0:1, :],
                start=True, stop=True, waits=[t])
    t = ctx.dve(lambda e: e.tensor_copy(rbbc[:, :], ctx.ps_g[2][:, 0:7]), waits=[b2])
    s.barrier()
    ctx.war = {}

    for tile in range(NTOK // NTILE):
        col0 = tile * NTILE
        for sub in range(2):
            def dst_fn(cc, sub=sub):
                return yh2[:, cc * NTILE + sub * 512: cc * NTILE + sub * 512 + 512]
            emit_ln(ctx, ysrc, col0 + sub * 512, 512,
                    _Slice1024(y1, sub), sqbuf, rowbuf, tmpbuf, dst_fn,
                    gain_col=lambda cc: ln2v_sb[:, cc:cc + 1])
            a_t = ctx.dve(lambda e, o=arow[0:1, sub * 512:(sub + 1) * 512]:
                          e.tensor_copy(o, rowbuf[0:1, 0:512]))
            b_t = ctx.dve(lambda e, o=brow[0:1, sub * 512:(sub + 1) * 512]:
                          e.tensor_copy(o, rowbuf[0:1, 512:1024]))
            ctx.war["rowA"] = a_t
            ctx.war["rowB"] = b_t
        yh_tok = s.now("Vn")
        for sub in range(2):
            t = None
            for cc in range(DC):
                w = ([yh_tok, ctx.war.get("rps")] if cc == 0 else [])
                t = ctx.mm(ctx.ps_g[2][0:7, 0:512], rp_sb[:, cc * 7:(cc + 1) * 7],
                           y1[:, cc * NTILE + sub * 512: cc * NTILE + sub * 512 + 512],
                           start=(cc == 0), stop=(cc == DC - 1), waits=w)
            ev = ctx.dve(lambda e, o=raw_sb[0:7, sub * 512:(sub + 1) * 512],
                         i=ctx.ps_g[2][0:7, 0:512]: e.tensor_copy(o, i), waits=[t])
            ctx.war["rps"] = ev
        for bb in range(8):
            t1 = ctx.s.emit(
                "tensor",
                lambda e, o=ctx.ps_g[2][0:128, 0:7],
                i=raw_sb[0:7, bb * 128:(bb + 1) * 128]:
                e.transpose(o, i, identf[0:7, 0:7]),
                waits=[ctx.war.get("rps"), ctx.war.get("rtr")], inc="Tn")
            ta = ctx.mm(ctx.ps_g[0][0:128, 0:1], arow[0:1, bb * 128:(bb + 1) * 128],
                        c["ones_r"][0:1, 0:1], start=True, stop=True,
                        waits=[ctx.war.get("rta")])
            tb = ctx.mm(ctx.ps_g[0][0:128, 1:2], brow[0:1, bb * 128:(bb + 1) * 128],
                        c["ones_r"][0:1, 0:1], start=True, stop=True)
            e1 = ctx.dve(lambda e, o=ltm[:, bb * 7:(bb + 1) * 7],
                         i=ctx.ps_g[2][0:128, 0:7]: e.tensor_copy(o, i), waits=[t1])
            e2 = ctx.dve(lambda e, o=ab_sb[:, bb * 2:bb * 2 + 2],
                         i=ctx.ps_g[0][0:128, 0:2]: e.tensor_copy(o, i), waits=[tb, ta])
            ctx.war["rtr"] = e1
            ctx.war["rta"] = e2
            l1 = ctx.dve(lambda e, o=rsc[:, bb * 7:(bb + 1) * 7],
                         b_=ab_sb[:, bb * 2 + 1:bb * 2 + 2]:
                         e.scalar_tensor_tensor(o, rsbc[:, :], b_, rbbc[:, :],
                                                ALU.mult, ALU.add), waits=[e2])
            l2 = ctx.dve(lambda e, o=ltm[:, bb * 7:(bb + 1) * 7],
                         a_=ab_sb[:, bb * 2:bb * 2 + 1],
                         t2=rsc[:, bb * 7:(bb + 1) * 7]:
                         e.scalar_tensor_tensor(o, o, a_, t2, ALU.mult, ALU.add),
                         waits=[l1, e1])
            sg = ctx.act(lambda e, o=prob[:, bb * 7:(bb + 1) * 7],
                         i=ltm[:, bb * 7:(bb + 1) * 7]:
                         e.activation(o, i, AF.Sigmoid), waits=[l2])
            bm = bb * 7
            m1c = m12[:, bb * 3:bb * 3 + 1]
            m2c = m12[:, bb * 3 + 1:bb * 3 + 2]
            sc_ = m12[:, bb * 3 + 2:bb * 3 + 3]
            r1 = ctx.dve(lambda e, o=m1c, i=prob[:, bm:bm + 7]:
                         e.reduce_max(o, i, AX.X), waits=[sg])
            r2 = ctx.dve(lambda e, o=rsc[:, bm:bm + 7], i=prob[:, bm:bm + 7], m=m1c:
                         e.tensor_scalar(o, i, m, None, ALU.is_ge), waits=[r1])
            r3 = ctx.dve(lambda e, o=rsc[:, bm:bm + 7], i=rsc[:, bm:bm + 7],
                         p=prob[:, bm:bm + 7]:
                         e.scalar_tensor_tensor(o, i, -1e30, p, ALU.mult, ALU.add),
                         waits=[r2])
            r4 = ctx.dve(lambda e, o=m2c, i=rsc[:, bm:bm + 7]:
                         e.reduce_max(o, i, AX.X), waits=[r3])
            r5 = ctx.dve(lambda e, o=rsc[:, bm:bm + 7], i=prob[:, bm:bm + 7], m=m2c:
                         e.tensor_scalar(o, i, m, None, ALU.is_ge), waits=[r4])
            r6 = ctx.dve(lambda e, o=sc_, i=m1c, i2=m2c:
                         e.tensor_tensor(o, i, i2, ALU.add), waits=[r4])
            r7 = ctx.dve(lambda e, o=sc_, i=sc_: e.reciprocal(o, i), waits=[r6])
            r8 = ctx.dve(lambda e, o=wgt_tm[:, bm:bm + 7], i=prob[:, bm:bm + 7],
                         i2=rsc[:, bm:bm + 7]:
                         e.tensor_tensor(o, i, i2, ALU.mult), waits=[r5])
            r9 = ctx.dve(lambda e, o=wgt_tm[:, bm:bm + 7], i=wgt_tm[:, bm:bm + 7],
                         m=sc_: e.tensor_scalar_mul(o, i, m), waits=[r7, r8])
        wgt_tok = s.now("Vn")
        ctx.war["dps0"] = wgt_tok
        ctx.war["dps1"] = wgt_tok
        ctx.war["wbcp"] = wgt_tok
        for ei in range(8):
            if ei > 0:
                wtoks = []
                for bb in range(8):
                    dg = ctx.dve(lambda e, o=diag[:, :],
                                 m=wgt_tm[:, bb * 7 + ei - 1: bb * 7 + ei]:
                                 e.tensor_scalar_mul(o, identf[:, :], m),
                                 waits=[wgt_tok, ctx.war.get("diag")])
                    bt = ctx.mm(ctx.ps_g[2][:, 0:128], onesq[:, :], diag[:, :],
                                start=True, stop=True,
                                waits=[dg, ctx.war.get("wbcp")])
                    ctx.war["diag"] = bt
                    ev = ctx.dve(lambda e, o=wbc[:, bb * 128:(bb + 1) * 128],
                                 i=ctx.ps_g[2][:, 0:128]: e.tensor_copy(o, i),
                                 waits=[bt, ctx.war.get("wbcr")])
                    ctx.war["wbcp"] = ev
                    wtoks.append(ev)
                wbc_tok = wtoks[-1]
            else:
                wbc_tok = None
            for mf in range(16):
                gslot = gstage[mf % 2]
                gt = ctx.load(gslot[:, :], ctx.dram["GB"][ei, mf],
                              waits=[ctx.war.get("gsl%d" % (mf % 2))])
                for nt in range(2):
                    ps_gm = ctx.st[nt]
                    t = None
                    for cc in range(DC):
                        w = ([yh_tok, gt, ctx.war.get("gps%d" % nt)]
                             if cc == 0 else [])
                        t = ctx.mm(ps_gm[:, 0:512],
                                   gslot[:, cc * 128:(cc + 1) * 128],
                                   yh2[:, cc * NTILE + nt * 512: cc * NTILE + nt * 512 + 512],
                                   start=(cc == 0), stop=(cc == DC - 1), waits=w)
                    sl = ctx.act(lambda e, o=glbuf[:, nt * 512:(nt + 1) * 512],
                                 i=ps_gm[:, 0:512]: e.activation(o, i, AF.Silu),
                                 waits=[t, ctx.war.get("glb%d" % nt)])
                    ctx.war["gps%d" % nt] = sl
                uslot = gstage[(mf + 1) % 2]
                ut = ctx.load(uslot[:, :], ctx.dram["UB"][ei, mf],
                              waits=[ctx.war.get("gsl%d" % ((mf + 1) % 2))])
                for nt in range(2):
                    ps_um = ctx.st[2 + nt]
                    t = None
                    for cc in range(DC):
                        w = ([ut, ctx.war.get("ups%d" % nt)] if cc == 0 else [])
                        t = ctx.mm(ps_um[:, 0:512],
                                   uslot[:, cc * 128:(cc + 1) * 128],
                                   yh2[:, cc * NTILE + nt * 512: cc * NTILE + nt * 512 + 512],
                                   start=(cc == 0), stop=(cc == DC - 1), waits=w)
                    ue = ctx.act(lambda e, o=ubuf[:, nt * 512:(nt + 1) * 512],
                                 i=ps_um[:, 0:512]: e.activation(o, i, AF.Copy),
                                 waits=[t, ctx.war.get("ubf%d" % nt)])
                    ctx.war["ups%d" % nt] = ue
                    hm = ctx.pool(lambda e, o=h_sb[:, mf * NTILE + nt * 512:
                                                   mf * NTILE + nt * 512 + 512],
                                  i=glbuf[:, nt * 512:(nt + 1) * 512],
                                  i2=ubuf[:, nt * 512:(nt + 1) * 512]:
                                  e.tensor_tensor(o, i, i2, ALU.mult),
                                  waits=[ue, ctx.war.get("gps%d" % nt),
                                         ctx.war.get("hsb%d" % nt)])
                    ctx.war["glb%d" % nt] = hm
                    ctx.war["ubf%d" % nt] = hm
                ctx.war["gsl%d" % (mf % 2)] = s.now("Tn")
                ctx.war["gsl%d" % ((mf + 1) % 2)] = s.now("Tn")
            h_tok = s.now("Pn")
            for m2 in range(8):
                dslot = dstage[m2 % 2]
                dt = ctx.load(dslot[:, :], ctx.dram["DB"][ei, m2],
                              waits=[ctx.war.get("dsl%d" % (m2 % 2))])
                for nt in range(2):
                    ps_d = ctx.ps_g[nt]
                    t = None
                    for k in range(16):
                        w = ([h_tok, dt, ctx.war.get("dps%d" % nt)]
                             if k == 0 else [])
                        t = ctx.mm(ps_d[:, 0:512],
                                   dslot[:, k * 128:(k + 1) * 128],
                                   h_sb[:, k * NTILE + nt * 512: k * NTILE + nt * 512 + 512],
                                   start=(k == 0), stop=(k == 15), waits=w)
                    col = m2 * NTILE + nt * 512
                    if ei == 0:
                        ev = ctx.dve(lambda e, o=acc[1][:, col:col + 512],
                                     i=ps_d[:, 0:512]: e.tensor_copy(o, i),
                                     waits=[t, ctx.war.get("accw")])
                    else:
                        ev = ctx.dve(lambda e, o=tmpo[:, :], i=ps_d[:, 0:512],
                                     wb=wbc[:, nt * 512:(nt + 1) * 512]:
                                     e.tensor_tensor(o, i, wb, ALU.mult),
                                     waits=[t, wbc_tok, ctx.war.get("tmpo")])
                        ctx.war["wbcr"] = ev
                        ev = ctx.dve(lambda e, o=acc[(ei + 1) % 2][:, col:col + 512],
                                     i=acc[ei % 2][:, col:col + 512], t2=tmpo[:, :]:
                                     e.tensor_tensor(o, i, t2, ALU.add),
                                     waits=[ev, ctx.war.get("accw")])
                        ctx.war["tmpo"] = ev
                    ctx.war["dps%d" % nt] = ev
                ctx.war["dsl%d" % (m2 % 2)] = s.now("Tn")
            ctx.war["hsb0"] = s.now("Tn")
            ctx.war["hsb1"] = s.now("Tn")
        acc_fin = acc[0]
        acc_tok = s.now("Vn")
        for m2 in range(8):
            for nt in range(2):
                col = m2 * NTILE + nt * 512
                xr = xres[nt]
                xt = ctx.load(xr[:, :],
                              xsrc[m2 * 128:(m2 + 1) * 128,
                                   col0 + nt * 512: col0 + nt * 512 + 512],
                              waits=[ctx.war.get("xres%d" % nt)])
                o1 = ctx.dve(lambda e, o=ostg[nt][:, :], i=acc_fin[:, col:col + 512],
                             y=y1[:, col:col + 512]:
                             e.tensor_tensor(o, i, y, ALU.add),
                             waits=[acc_tok, ctx.war.get("ost%d" % nt)])
                o2 = ctx.dve(lambda e, o=ostg[nt][:, :], i=ostg[nt][:, :], x=xr[:, :]:
                             e.tensor_tensor(o, i, x, ALU.add), waits=[o1, xt])
                ctx.war["xres%d" % nt] = o2
                st_ = ctx.store(dst[m2 * 128:(m2 + 1) * 128,
                                    col0 + nt * 512: col0 + nt * 512 + 512],
                                ostg[nt][:, :], waits=[o2])
                ctx.war["ost%d" % nt] = st_
        ctx.war["accw"] = s.now("Vn")
        ctx.war["lnW"] = s.now("Tn")
        s.barrier()
        ctx.war = {}
    sa.reset(mark)


class _Slice1024:
    def __init__(self, y1, sub):
        self.y1 = y1
        self.sub = sub

    def __getitem__(self, key):
        p, f = key
        lo, hi = f.start, f.stop
        cc = lo // 512
        return self.y1[p, cc * 1024 + self.sub * 512:
                       cc * 1024 + self.sub * 512 + (hi - lo)]


# version 10
# speedup vs baseline: 1.1608x; 1.0861x over previous
import numpy as np
import ml_dtypes

import concourse.bass as bass
import concourse.mybir as mybir
from concourse.bass_utils import run_bass_kernel_spmd

F32 = mybir.dt.float32
BF16 = mybir.dt.bfloat16
AF = mybir.ActivationFunctionType
ALU = mybir.AluOpType
AX = mybir.AxisListType
BF = ml_dtypes.bfloat16

D = 1024
DC = 8
T = 2048
NSEQ = 2
NTOK = NSEQ * T
EPS = 1e-5
MAGIC = 1.5 * (2.0 ** 23)

ENGS = ("sync", "gpsimd", "tensor", "vector", "scalar")

N_LD = 8
N_GL = 4
N_ST = 4


class Sched:
    def __init__(self, nc):
        self.nc = nc
        self.q = {e: [] for e in ENGS}
        self.sem = {}
        self.cnt = {}
        self.last_wait = {e: {} for e in ENGS}

    def new_sem(self, name):
        self.sem[name] = self.nc.alloc_semaphore(name=name)
        self.cnt[name] = 0
        return name

    def emit(self, eng, fn, waits=(), inc=None, inc_by=1):
        lw = self.last_wait[eng]
        for w in waits:
            if w is None:
                continue
            sname, v = w
            if v > 0 and lw.get(sname, -1) < v:
                self.q[eng].append(("w", sname, v))
                lw[sname] = v
        self.q[eng].append(("o", fn, inc, inc_by))
        if inc is not None:
            self.cnt[inc] += inc_by
            return (inc, self.cnt[inc])
        return None

    def now(self, sem):
        return (sem, self.cnt[sem])

    def barrier(self):
        toks = [(n, c) for n, c in self.cnt.items() if c > 0]
        for e in ENGS:
            lw = self.last_wait[e]
            for sname, v in toks:
                if lw.get(sname, -1) < v:
                    self.q[e].append(("w", sname, v))
                    lw[sname] = v

    def build(self):
        with self.nc.Block() as block:
            for eng in ENGS:
                stream = self.q[eng]
                sem = self.sem

                def body(engine, stream=stream):
                    for item in stream:
                        if item[0] == "w":
                            engine.wait_ge(sem[item[1]], item[2])
                        else:
                            _, fn, inc, n = item
                            inst = fn(engine)
                            if inc is not None:
                                inst.then_inc(sem[inc], n)
                getattr(block, eng)(body)


class SA:
    def __init__(self, nc):
        self.nc = nc
        self.base = (nc.sbuf_base + 31) & ~31
        self.off = self.base
        self.top = nc.sbuf_top
        self.uid = 0

    def alloc(self, name, shape, dtype):
        per = int(np.prod(shape[1:])) * mybir.dt.size(dtype)
        per = (per + 31) & ~31
        assert self.off + per <= self.top, (
            f"SBUF overflow at {name}: {self.off}+{per} > {self.top}")
        self.uid += 1
        t = self.nc.alloc_sbuf_tensor_at(f"{name}_{self.uid}", list(shape), dtype,
                                         offset=self.off)
        self.off += per
        return t

    def mark(self):
        return self.off

    def reset(self, m):
        self.off = m


class Ctx:
    def __init__(self, inputs, scratch, outputs):
        nc = bass.Bass("TRN2", target_bir_lowering=False, debug=False)
        self.nc = nc
        self.s = Sched(nc)
        self.sa = SA(nc)
        s = self.s
        self.ld_sems = [s.new_sem(f"ld{i}") for i in range(N_LD)]
        self.gl_sems = [s.new_sem(f"gl{i}") for i in range(N_GL)]
        self.st_sems = [s.new_sem(f"st{i}") for i in range(N_ST)]
        self._ld_i = 0
        self._gl_i = 0
        self._st_i = 0
        for n in ("Tn", "Vn", "Cn", "Pn"):
            s.new_sem(n)
        self.dram = {}
        for name, shape, dt in inputs:
            self.dram[name] = nc.dram_tensor(name, list(shape), dt,
                                             kind="ExternalInput").ap()
        for name, shape, dt in scratch:
            self.dram[name] = nc.dram_tensor(name, list(shape), dt,
                                             kind="Internal").ap()
        for name, shape, dt in outputs:
            self.dram[name] = nc.dram_tensor(name, list(shape), dt,
                                             kind="ExternalOutput").ap()
        self.st = [nc.alloc_psum_tensor(f"ps_st{i}", [128, 512], F32)
                   for i in range(4)]
        self.ps_g = [nc.alloc_psum_tensor(f"ps_g{i}", [128, 512], F32)
                     for i in range(3)]
        self.ps_bf = nc.alloc_psum_tensor("ps_bf", [128, 1024], BF16)
        self.war = {}
        self._make_consts()
        self.s.barrier()

    def _make_consts(self):
        nc, s, sa = self.nc, self.s, self.sa
        tri = np.triu(np.ones((128, 128), np.float32)).astype(BF)
        RD = 32
        inv_freq = 1.0 / (10000.0 ** (np.arange(0, RD, 2, dtype=np.float64) / RD))
        tpos = np.arange(T, dtype=np.float64)
        freqs = tpos[:, None] * inv_freq[None, :]
        emb = np.concatenate([freqs, freqs], axis=-1)
        cos = np.cos(emb).astype(np.float32)
        sin = np.sin(emb).astype(np.float32)
        sgn = np.ones(32, np.float32)
        sgn[:16] = -1.0
        consts = {
            "ident": np.eye(128, dtype=BF),
            "triu": tri,
            "ones_b": np.ones((128, 1), BF),
            "ones_f": np.ones((128, 1), np.float32),
            "ones_r": np.ones((1, 128), np.float32),
            "eps_c": np.full((128, 1), EPS, np.float32),
        }
        self.rope_cos = np.ascontiguousarray(np.tile(cos.T, (4, 1)))
        self.rope_sin = np.ascontiguousarray(np.tile((sin * sgn[None, :]).T, (4, 1)))
        self.c = {}
        for name, arr in consts.items():
            dt = BF16 if arr.dtype == BF else F32
            dr = nc.inline_tensor(np.ascontiguousarray(arr), name=f"c_{name}").ap()
            sb = sa.alloc(f"c_{name}", list(arr.shape), dt)
            self.load(sb[:, :], dr[:, :])
            self.c[name] = sb

    def _dma(self, eng, sems, idx_attr, dst_ap, src_ap, waits):
        i = getattr(self, idx_attr)
        setattr(self, idx_attr, i + 1)
        sem = sems[i % len(sems)]
        pre = self.s.now(sem)
        return self.s.emit(eng, lambda e: e.dma_start(out=dst_ap, in_=src_ap),
                           waits=list(waits) + [pre], inc=sem, inc_by=16)

    def load(self, dst_ap, src_ap, waits=()):
        return self._dma("sync", self.ld_sems, "_ld_i", dst_ap, src_ap, waits)

    def cast_load(self, dst_ap, src_ap, waits=()):
        return self._dma("gpsimd", self.gl_sems, "_gl_i", dst_ap, src_ap, waits)

    def store(self, dst_ap, src_ap, waits=()):
        return self._dma("gpsimd", self.st_sems, "_st_i", dst_ap, src_ap, waits)

    def dma_tr(self, dst_ap, src_ap, waits=()):
        i = self._ld_i
        self._ld_i = i + 1
        sem = self.ld_sems[i % N_LD]
        pre = self.s.now(sem)
        return self.s.emit("sync",
                           lambda e: e.dma_start_transpose(dst_ap, src_ap),
                           waits=list(waits) + [pre], inc=sem, inc_by=16)

    def mm(self, out_ap, lhsT, rhs, start, stop, waits=(), force_inc=False):
        return self.s.emit(
            "tensor",
            lambda e: e.matmul(out_ap, lhsT, rhs, start=start, stop=stop),
            waits=waits, inc=("Tn" if (stop or force_inc) else None))

    def tr(self, out_ap, in_ap, waits=()):
        ident = self.c["ident"]
        return self.s.emit(
            "tensor", lambda e: e.transpose(out_ap, in_ap, ident[:, :]),
            waits=waits, inc="Tn")

    def act(self, fn, waits=()):
        return self.s.emit("scalar", fn, waits=waits, inc="Cn")

    def dve(self, fn, waits=()):
        return self.s.emit("vector", fn, waits=waits, inc="Vn")

    def pool(self, fn, waits=()):
        return self.s.emit("gpsimd", fn, waits=waits, inc="Pn")



def emit_ln(ctx, src, col0, n, xbuf, sqbuf, rowbuf, tmpbuf, dst_fn,
            gain_col=None, dst_war=None):
    s, c = ctx.s, ctx.c
    ps_sum, ps_sq = ctx.ps_g[0], ctx.ps_g[1]
    x_toks = []
    sum_tok = None
    sq_act_tok = None
    sq_tok = ctx.war.get("lnsq")
    for cc in range(DC):
        xa = xbuf[:, cc * n:(cc + 1) * n]
        xt = ctx.load(xa, src[cc * 128:(cc + 1) * 128, col0:col0 + n],
                      waits=[ctx.war.get(f"lnx{cc}")])
        x_toks.append(xt)
        sum_tok = ctx.mm(ps_sum[0:1, 0:n], c["ones_f"][:, :], xa,
                         start=(cc == 0), stop=(cc == DC - 1),
                         waits=[xt, ctx.war.get("g0")] if cc == 0 else [xt])
        sq_act_tok = ctx.act(
            lambda e, o=sqbuf[:, 0:n], i=xa: e.activation(o, i, AF.Square),
            waits=[xt, sq_tok])
        sq_tok = ctx.mm(ps_sq[0:1, 0:n], c["ones_b"][:, :], sqbuf[:, 0:n],
                        start=(cc == 0), stop=(cc == DC - 1), force_inc=True,
                        waits=([sq_act_tok, ctx.war.get("g1")]
                               if cc == 0 else [sq_act_tok]))
    ctx.war["lnsq"] = sq_tok
    A = rowbuf[0:1, 0:n]
    B = rowbuf[0:1, n:2 * n]
    M = rowbuf[0:1, 2 * n:3 * n]
    T1 = rowbuf[0:1, 3 * n:4 * n]
    T2 = rowbuf[0:1, 4 * n:5 * n]
    t1 = ctx.dve(lambda e: e.tensor_scalar_mul(M, ps_sum[0:1, 0:n], 1.0 / D),
                 waits=[sum_tok])
    t2 = ctx.dve(lambda e: e.tensor_tensor(T1, M, M, ALU.mult), waits=[t1])
    t3 = ctx.dve(lambda e: e.scalar_tensor_tensor(T2, ps_sq[0:1, 0:n], 1.0 / D,
                                                  T1, ALU.mult, ALU.subtract),
                 waits=[t2, sq_tok])
    t4 = ctx.act(lambda e: e.activation(T1, T2, AF.Sqrt, bias=c["eps_c"][0:1, :]),
                 waits=[t3])
    t5 = ctx.dve(lambda e: e.reciprocal(A, T1),
                 waits=[t4, ctx.war.get("rowA")])
    t6 = ctx.dve(lambda e: e.scalar_tensor_tensor(B, M, -1.0, A, ALU.mult, ALU.mult),
                 waits=[t5, ctx.war.get("rowB")])
    bA = ctx.mm(ps_sum[:, 0:n], c["ones_r"][:, :], A, start=True, stop=True,
                waits=[t5, t1])
    bB = ctx.mm(ps_sq[:, 0:n], c["ones_r"][:, :], B, start=True, stop=True,
                waits=[t6, t3])
    ctx.war["rowA"] = bA
    ctx.war["rowB"] = bB
    xhat_toks = []
    tp = None
    for cc in range(DC):
        xa = xbuf[:, cc * n:(cc + 1) * n]
        tp = ctx.dve(lambda e, o=tmpbuf[:, 0:n], i=xa:
                     e.tensor_tensor(o, i, ps_sum[:, 0:n], ALU.mult),
                     waits=[bA, tp])
        if gain_col is None:
            tw = ctx.dve(lambda e, o=dst_fn(cc), i=tmpbuf[:, 0:n]:
                         e.tensor_tensor(o, i, ps_sq[:, 0:n], ALU.add),
                         waits=[bB, tp, dst_war(cc) if dst_war else None])
        else:
            tw = ctx.dve(lambda e, o=tmpbuf[:, 0:n], i=tmpbuf[:, 0:n]:
                         e.tensor_tensor(o, i, ps_sq[:, 0:n], ALU.add),
                         waits=[bB, tp])
            tw = ctx.dve(lambda e, o=dst_fn(cc), i=tmpbuf[:, 0:n], g=gain_col(cc):
                         e.tensor_scalar_mul(o, i, g),
                         waits=[tw, dst_war(cc) if dst_war else None])
        tp = tw
        ctx.war[f"lnx{cc}"] = tw
        xhat_toks.append(tw)
    ctx.war["g0"] = xhat_toks[-1]
    ctx.war["g1"] = xhat_toks[-1]
    return xhat_toks, x_toks



def emit_attn_core(ctx, nh, dh, Q_sb, K_sb, V_sb, scale, PT_sb, dmask_sb,
                   otmn_sb, recip_sb, ofm4_sb, qk_tok, v_tok, oproj_fn):
    s, c = ctx.s, ctx.c
    hpc = 128 // dh
    DCk = nh // hpc
    ng64 = dh // 64
    hw = ng64 * 65
    gw = nh * hw
    nh64 = nh * ng64
    exp_tok = [None] * 4
    pv_par = [None, None]
    mask_tok = None
    tr_toks = []
    hh = 0
    for j in range(16):
        obank_used = []
        for h in range(nh):
            rb = (h % hpc) * dh
            cc = h // hpc
            par = hh % 2
            hh += 1
            PT = PT_sb[par]
            dmask = dmask_sb[par]
            for b in range(0, j + 1, 4):
                mm_t = None
                bank = ctx.st[b // 4]
                for c4 in range(b, min(b + 4, j + 1)):
                    w = [qk_tok, v_tok, exp_tok[c4 // 4]] if c4 == b else []
                    mm_t = ctx.mm(
                        bank[:, (c4 - b) * 128:(c4 - b + 1) * 128],
                        K_sb[rb:rb + dh, cc * T + c4 * 128: cc * T + (c4 + 1) * 128],
                        Q_sb[rb:rb + dh, cc * T + j * 128: cc * T + (j + 1) * 128],
                        start=True, stop=True, waits=w)
                hi = min(b + 4, j + 1) * 128
                o_ap = PT[:, b * 128:hi]
                i_ap = bank[:, 0:hi - b * 128]
                exp_tok[b // 4] = ctx.act(
                    lambda e, o=o_ap, i=i_ap: e.activation(o, i, AF.Exp, scale=scale),
                    waits=[mm_t, pv_par[par]])
            dm_in = PT[:, j * 128:(j + 1) * 128]
            mask_tok = ctx.dve(
                lambda e, o=dmask[:, 0:128], i=dm_in:
                e.tensor_tensor(o, i, c["triu"][:, :], ALU.mult),
                waits=[exp_tok[j // 4], pv_par[par]])
            pv_tok = None
            for g in range(ng64):
                h64 = h * ng64 + g
                bk = h64 // 7
                ocol = (h64 % 7) * 65
                if bk not in obank_used:
                    obank_used.append(bk)
                for c4 in range(j + 1):
                    lt = (dmask[:, 0:128] if c4 == j
                          else PT[:, c4 * 128:(c4 + 1) * 128])
                    if c4 == 0:
                        w = [v_tok, mask_tok, ctx.war.get(f"ob{bk}")]
                    elif c4 == j:
                        w = [mask_tok]
                    else:
                        w = [exp_tok[c4 // 4]]
                    vs = V_sb[:, c4 * gw + h * hw + g * 65:
                              c4 * gw + h * hw + g * 65 + 65]
                    pv_tok = ctx.mm(ctx.ps_g[bk][:, ocol:ocol + 65], lt, vs,
                                    start=(c4 == 0), stop=(c4 == j), waits=w)
            pv_par[par] = pv_tok
        rc_tok = {}
        for bk in obank_used:
            ngrp = min(7, nh64 - bk * 7)
            src = ctx.ps_g[bk][:, 0:ngrp * 65].rearrange(
                "p (g k) -> p g k", k=65)[:, :, 64]
            dst = recip_sb[:, bk * 7:bk * 7 + ngrp]
            rc_tok[bk] = ctx.dve(lambda e, o=dst, i=src: e.reciprocal(o, i),
                                 waits=[pv_par[0], pv_par[1]])
        nt = None
        for h in range(nh):
            for g in range(ng64):
                h64 = h * ng64 + g
                bk = h64 // 7
                ocol = (h64 % 7) * 65
                o_ap = otmn_sb[:, h64 * 64:(h64 + 1) * 64]
                i_ap = ctx.ps_g[bk][:, ocol:ocol + 64]
                r_ap = recip_sb[:, h64:h64 + 1]
                nt = ctx.dve(
                    lambda e, o=o_ap, i=i_ap, r=r_ap: e.tensor_scalar_mul(o, i, r),
                    waits=[rc_tok[bk], ctx.war.get(f"otm{h64 // 2}")])
            if (h % hpc == hpc - 1) or dh == 128:
                cc = h // hpc
                src = otmn_sb[:, cc * 128:(cc + 1) * 128]
                dst = ofm4_sb[:, cc * 512 + (j % 4) * 128:
                              cc * 512 + (j % 4) * 128 + 128]
                tt = ctx.dma_tr(dst, src, waits=[nt, ctx.war.get("ofm4")])
                ctx.war[f"otm{cc}"] = tt
                tr_toks.append(tt)
        for bk in obank_used:
            ctx.war[f"ob{bk}"] = s.now("Vn")
        if j % 4 == 3:
            ctx.war["opx0"] = exp_tok[0]
            ctx.war["opx1"] = exp_tok[1]
            op_tok = oproj_fn(j // 4, tr_toks + [s.now("Vn")])
            exp_tok[0] = op_tok
            exp_tok[1] = op_tok
            ctx.war["ofm4"] = op_tok
            tr_toks = []


def emit_proj_to(ctx, w_sb, x_sb, nchunk, x_off, n, mcols, out_fn, waits=()):
    for m in range(mcols // 128):
        bank = ctx.ps_g[2] if (m % 2 == 0) else ctx.ps_st[:, 0:512]
        ps = bank[:, 0:n] if (m % 2 == 0) else ctx.ps_st[:, 0:n]
        t = None
        for cc in range(nchunk):
            w = list(waits) + [ctx.war.get("proj%d" % (m % 2))] if cc == 0 else []
            t = ctx.mm(ps, w_sb[:, cc * mcols + m * 128: cc * mcols + (m + 1) * 128],
                       x_sb[:, cc * n + x_off: cc * n + x_off + n],
                       start=(cc == 0), stop=(cc == nchunk - 1), waits=w)
        ev = out_fn(m, ps, t)
        ctx.war["proj%d" % (m % 2)] = ev


def emit_l0_attn(ctx, src_name, dst_name):
    s, c, sa = ctx.s, ctx.c, ctx.sa
    src = ctx.dram[src_name]
    dst = ctx.dram[dst_name]
    mark = sa.mark()
    xbuf = sa.alloc("p1_x", [128, DC * 512], F32)
    sqbuf = sa.alloc("p1_sq", [128, 512], BF16)
    rowbuf = sa.alloc("p1_row", [1, 5 * 512], F32)
    tmpbuf = sa.alloc("p1_tmp", [128, 512], F32)
    xhat = sa.alloc("p1_xhat", [128, DC * T], BF16)
    wq_sb = sa.alloc("p1_wq", [128, DC * 128], BF16)
    wk_sb = sa.alloc("p1_wk", [128, DC * 128], BF16)
    wv_sb = sa.alloc("p1_wv", [128, DC * 128], BF16)
    wo_sb = sa.alloc("p1_wo", [128, 1024], BF16)
    Q_sb = sa.alloc("p1_Q", [128, T], BF16)
    K_sb = sa.alloc("p1_K", [128, T], BF16)
    V_sb = sa.alloc("p1_V", [128, 16 * 130], BF16)
    PT_sb = [sa.alloc("p1_PT%d" % i, [128, 2048], BF16) for i in range(2)]
    dmask = [sa.alloc("p1_dm%d" % i, [128, 128], BF16) for i in range(2)]
    otmn = sa.alloc("p1_otmn", [128, 128], BF16)
    recip = sa.alloc("p1_rc", [128, 16], F32)
    ofm4 = sa.alloc("p1_ofm4", [128, 512], BF16)
    xres = [sa.alloc(f"p1_xres{i}", [128, 512], F32) for i in range(2)]
    ostg = [sa.alloc(f"p1_ostg{i}", [128, 512], F32) for i in range(4)]

    wq_t = ctx.dram["wq0"].rearrange("(c p) m -> p c m", p=128)
    wk_t = ctx.dram["wk0"].rearrange("(c p) m -> p c m", p=128)
    wv_t = ctx.dram["wv0"].rearrange("(c p) m -> p c m", p=128)
    wtok = [ctx.cast_load(wq_sb[:, :].rearrange("p (c m) -> p c m", m=128), wq_t),
            ctx.cast_load(wk_sb[:, :].rearrange("p (c m) -> p c m", m=128), wk_t),
            ctx.cast_load(wv_sb[:, :].rearrange("p (c m) -> p c m", m=128), wv_t),
            ctx.cast_load(wo_sb[:, :], ctx.dram["wo0"][:, :])]
    ones_tok = ctx.dve(lambda e: e.memset(V_sb[:, :], 1.0))

    for sq in range(NSEQ):
        base = sq * T
        xh_toks = []
        for tile in range(4):
            def dst_fn(cc, tile=tile):
                return xhat[:, cc * T + tile * 512: cc * T + tile * 512 + 512]
            ts_, _ = emit_ln(ctx, src, base + tile * 512, 512,
                             xbuf, sqbuf, rowbuf, tmpbuf, dst_fn)
            xh_toks += ts_
        xh_tok = s.now("Vn")
        qk_evs = []
        for (w_sb, dst_sb) in ((wq_sb, Q_sb), (wk_sb, K_sb)):
            for tile in range(4):
                ps = ctx.ps_g[2][:, 0:512] if tile % 2 == 0 else ctx.st[0][:, 0:512]
                t = None
                for cc in range(DC):
                    w = ([xh_tok, wtok[0], wtok[1], wtok[2],
                          ctx.war.get("qk%d" % (tile % 2))] if cc == 0 else [])
                    t = ctx.mm(ps, w_sb[:, cc * 128:(cc + 1) * 128],
                               xhat[:, cc * T + tile * 512: cc * T + tile * 512 + 512],
                               start=(cc == 0), stop=(cc == DC - 1), waits=w)
                o_ap = dst_sb[:, tile * 512: tile * 512 + 512]
                ev = ctx.act(lambda e, o=o_ap, i=ps: e.activation(o, i, AF.Copy),
                             waits=[t])
                ctx.war["qk%d" % (tile % 2)] = ev
                qk_evs.append(ev)
        v_evs = []
        for tc in range(16):
            ps = ctx.st[1 + (tc % 3)][:, 0:128]
            t = None
            for cc in range(DC):
                w = ([xh_tok, ctx.war.get("v%d" % (tc % 3))] if cc == 0 else [])
                t = ctx.mm(ps, xhat[:, cc * T + tc * 128: cc * T + (tc + 1) * 128],
                           wv_sb[:, cc * 128:(cc + 1) * 128],
                           start=(cc == 0), stop=(cc == DC - 1), waits=w)
            o_ap = V_sb[:, tc * 130: tc * 130 + 130].rearrange(
                "p (g k) -> p g k", k=65)[:, :, 0:64]
            i_ap = ps.rearrange("p (g k) -> p g k", k=64)
            ev = ctx.dve(lambda e, o=o_ap, i=i_ap: e.tensor_copy(o, i),
                         waits=[t, ones_tok])
            ctx.war["v%d" % (tc % 3)] = ev
            v_evs.append(ev)
        qk_tok = s.now("Cn")
        v_tok = s.now("Vn")

        def oproj(jg, ofm_toks, base=base):
            last = None
            for m in range(8):
                ps = ctx.st[m % 2][:, 0:512]
                t = ctx.mm(ps, wo_sb[:, m * 128:(m + 1) * 128], ofm4[:, 0:512],
                           start=True, stop=True,
                           waits=list(ofm_toks) + [wtok[3],
                                  ctx.war.get("op%d" % (m % 2)),
                                  ctx.war.get("opx%d" % (m % 2))])
                xr = xres[m % 2]
                xt = ctx.load(xr[:, :], src[m * 128:(m + 1) * 128,
                                            base + jg * 512: base + jg * 512 + 512],
                              waits=[ctx.war.get("xres%d" % (m % 2))])
                og = ostg[(jg * 8 + m) % 4]
                ev = ctx.dve(lambda e, o=og[:, :], i=ps, x=xr[:, :]:
                             e.tensor_tensor(o, i, x, ALU.add),
                             waits=[t, xt, ctx.war.get("ostg%d" % ((jg * 8 + m) % 4))])
                st = ctx.store(dst[m * 128:(m + 1) * 128,
                                   base + jg * 512: base + jg * 512 + 512],
                               og[:, :], waits=[ev])
                ctx.war["ostg%d" % ((jg * 8 + m) % 4)] = st
                ctx.war["op%d" % (m % 2)] = ev
                ctx.war["xres%d" % (m % 2)] = ev
                last = ev
            return last

        emit_attn_core(ctx, 1, 128, Q_sb, K_sb, V_sb, 128.0 ** -0.5,
                       PT_sb, dmask, otmn, recip, ofm4, qk_tok, v_tok, oproj)
        s.barrier()
        ctx.war = {}
    sa.reset(mark)



def _prep_weights(inp):
    w = {}
    f32c = lambda a: np.ascontiguousarray(np.asarray(a), dtype=np.float32)
    w["wq0"] = f32c(inp["L0_attn_Wq"])
    w["wk0"] = f32c(inp["L0_attn_Wk"])
    w["wv0"] = f32c(inp["L0_attn_Wv"])
    w["wo0"] = f32c(inp["L0_attn_Wo"])
    w["w1t"] = f32c(np.asarray(inp["L0_ffn_W1"]).T)
    w["b1v"] = f32c(np.asarray(inp["L0_ffn_b1"]).reshape(32, 128).T)
    w["w2t"] = f32c(np.asarray(inp["L0_ffn_W2"]).T)
    w["b2v"] = f32c(np.asarray(inp["L0_ffn_b2"]).reshape(8, 128).T)
    HEADS = 16
    hd = D // HEADS
    perm = np.arange(D).reshape(hd, HEADS).T.reshape(-1)
    w["qw1"] = f32c(np.asarray(inp["L1_attn_qw"])[:, perm])
    w["kw1"] = f32c(np.asarray(inp["L1_attn_kw"])[:, perm])
    w["vw1"] = f32c(np.asarray(inp["L1_attn_vw"])[:, perm])
    w["ow1"] = f32c(np.asarray(inp["L1_attn_ow"])[perm, :])
    w["ln1v"] = f32c(np.asarray(inp["L1_ffn_ln1_w"]).reshape(8, 128).T)
    w["ln2v"] = f32c(np.asarray(inp["L1_ffn_ln2_w"]).reshape(8, 128).T)
    w["kvdt"] = f32c(np.asarray(inp["L1_ffn_kv_d"]).T)
    w["qdt"] = f32c(np.asarray(inp["L1_ffn_q_d"]).T)
    w["vut"] = f32c(np.asarray(inp["L1_ffn_v_u"]).T)
    rq = np.asarray(inp["L1_ffn_rope_q_w"], np.float32)
    rk = np.asarray(inp["L1_ffn_rope_k_w"], np.float32)
    qu_full = np.asarray(inp["L1_ffn_q_u"], np.float32)
    ku_full = np.asarray(inp["L1_ffn_k_u"], np.float32)
    pj = (np.arange(32) + 16) % 32
    qmix = np.zeros((1024, 256), np.float32)
    qmixp = np.zeros((1024, 256), np.float32)
    kmixn = np.zeros((1024, 256), np.float32)
    kmixr = np.zeros((1024, 1024), np.float32)
    kmixrp = np.zeros((1024, 1024), np.float32)
    for h in range(16):
        rows = slice(h * 64, h * 64 + 32)
        qmix[h * 64:h * 64 + 32] = rq[h * 32:h * 32 + 32]
        qmix[h * 64 + 32:h * 64 + 64] = qu_full[h * 64 + 32:h * 64 + 64]
        qmixp[h * 64:h * 64 + 32] = rq[h * 32 + pj]
        kmixn[h * 64 + 32:h * 64 + 64] = ku_full[h * 64 + 32:h * 64 + 64]
        kmixr[h * 64:h * 64 + 32] = rk[h * 32:h * 32 + 32]
        kmixrp[h * 64:h * 64 + 32] = rk[h * 32 + pj]
    w["qmix"] = f32c(qmix.T)
    w["qmixp"] = f32c(qmixp.T)
    w["kmixn"] = f32c(kmixn.T)
    w["kmixr"] = f32c(kmixr.T)
    w["kmixrp"] = f32c(kmixrp.T)
    w["owft"] = f32c(np.asarray(inp["L1_ffn_o_w"]).T)
    w["rwt"] = f32c(np.asarray(inp["L1_ffn_router_w"]).T)
    w["rbias"] = f32c(np.asarray(inp["L1_ffn_routing_bias"]).reshape(1, 7))
    ga = np.concatenate([np.asarray(inp["L1_ffn_shared_gate"]),
                         np.asarray(inp["L1_ffn_routed_gate"])], 0)
    ua = np.concatenate([np.asarray(inp["L1_ffn_shared_up"]),
                         np.asarray(inp["L1_ffn_routed_up"])], 0)
    da = np.concatenate([np.asarray(inp["L1_ffn_shared_down"]),
                         np.asarray(inp["L1_ffn_routed_down"])], 0)
    w["gall"] = f32c(ga.reshape(8, 16, 128, 8, 128).transpose(0, 1, 4, 3, 2))
    w["uall"] = f32c(ua.reshape(8, 16, 128, 8, 128).transpose(0, 1, 4, 3, 2))
    w["dall"] = f32c(da.reshape(8, 8, 128, 16, 128).transpose(0, 1, 4, 3, 2))

    def tile4(mat):
        return f32c(np.ascontiguousarray(
            np.asarray(mat, np.float32).reshape(8, 128, 8, 128).transpose(2, 1, 0, 3)))
    w["qw1"] = tile4(w["qw1"])
    w["kw1"] = tile4(w["kw1"])
    w["ow1"] = tile4(w["ow1"])
    w["kmixr"] = tile4(w["kmixr"])
    w["kmixrp"] = tile4(w["kmixrp"])
    w["owft"] = tile4(w["owft"])
    w["vw1"] = f32c(np.asarray(w["vw1"]).reshape(8, 128, 2, 512).transpose(2, 1, 0, 3))
    return w


WSHAPES = {
    "wq0": ([1024, 128], F32), "wk0": ([1024, 128], F32), "wv0": ([1024, 128], F32),
    "wo0": ([128, 1024], F32), "w1t": ([1024, 4096], F32), "b1v": ([128, 32], F32),
    "w2t": ([4096, 1024], F32), "b2v": ([128, 8], F32),
    "qw1": ([8, 128, 8, 128], F32), "kw1": ([8, 128, 8, 128], F32),
    "vw1": ([2, 128, 8, 512], F32), "ow1": ([8, 128, 8, 128], F32),
    "ln1v": ([128, 8], F32), "ln2v": ([128, 8], F32),
    "kvdt": ([1024, 256], F32), "qdt": ([1024, 256], F32),
    "vut": ([256, 1024], F32),
    "qmix": ([256, 1024], F32), "qmixp": ([256, 1024], F32),
    "kmixn": ([256, 1024], F32),
    "kmixr": ([8, 128, 8, 128], F32), "kmixrp": ([8, 128, 8, 128], F32),
    "owft": ([8, 128, 8, 128], F32), "rwt": ([1024, 7], F32), "rbias": ([1, 7], F32),
    "gall": ([8, 16, 128, 8, 128], F32), "uall": ([8, 16, 128, 8, 128], F32),
    "dall": ([8, 8, 128, 16, 128], F32),
}

ACT_BUFS = {
    "XT": None, "RX1": None, "RX2": None, "RX3": None, "Y": None, "Y1": None,
    "OUT": None,
}

PHASES = {
    "P0c": (lambda ctx: emit_precast(ctx),
            ["gall", "uall", "dall", "qw1", "kw1", "vw1", "ow1",
             "kmixr", "kmixrp", "owft"], [], []),
    "P1": (lambda ctx: emit_l0_attn(ctx, "XT", "RX1"),
           ["wq0", "wk0", "wv0", "wo0"], ["XT"], ["RX1"]),
    "P2": (lambda ctx: emit_l0_ffn(ctx, "RX1", "RX2"),
           ["w1t", "b1v", "w2t", "b2v"], ["RX1"], ["RX2"]),
    "P3": (lambda ctx: emit_l1_attn(ctx, "RX2", "RX3"),
           [], ["RX2"], ["RX3"]),
    "P4": (lambda ctx: emit_outer_ln(ctx, "RX3", "Y"),
           [], ["RX3"], ["Y"]),
    "P5": (lambda ctx: emit_mla(ctx, "Y", "Y1"),
           ["ln1v", "qmix", "qmixp", "kmixn", "kvdt", "qdt", "vut"],
           ["Y"], ["Y1"]),
    "P6": (lambda ctx: emit_moe(ctx, "RX3", "Y1", "OUT"),
           ["ln2v", "rwt", "rbias"], ["RX3", "Y1"], ["OUT"]),
}

SCRATCH_BUFS = {
    "WQ1B": ([1024, 4096], BF16),
    "WQ2B": ([8, 32, 128, 128], BF16),
    "GB": ([8, 16, 128, 1024], BF16), "UB": ([8, 16, 128, 1024], BF16),
    "DB": ([8, 8, 128, 2048], BF16),
    "QWB": ([8, 128, 1024], BF16), "KWB": ([8, 128, 1024], BF16),
    "VWB": ([2, 128, 4096], BF16), "OWB": ([8, 128, 1024], BF16),
    "KRB": ([8, 128, 1024], BF16), "KRPB": ([8, 128, 1024], BF16),
    "OWFB": ([8, 128, 1024], BF16),
}
SCRATCH_OF = {"P2": ["WQ1B", "WQ2B"],
              "P0c": ["GB", "UB", "DB", "QWB", "KWB", "VWB", "OWB",
                      "KRB", "KRPB", "OWFB"],
              "P3": ["QWB", "KWB", "VWB", "OWB"],
              "P5": ["KRB", "KRPB", "OWFB"],
              "P6": ["GB", "UB", "DB"]}


def build_launch(phases, ext_in_bufs, ext_out_bufs):
    wnames = sorted({w for p in phases for w in PHASES[p][1]})
    bufs = sorted({b for p in phases for b in PHASES[p][2] + PHASES[p][3]})
    inputs = [(n,) + tuple(WSHAPES[n]) for n in wnames]
    inputs = [(n, WSHAPES[n][0], WSHAPES[n][1]) for n in wnames]
    for b in bufs:
        if b in ext_in_bufs:
            inputs.append((b, [D, NTOK], F32))
    outputs = [(b, [D, NTOK], F32) for b in bufs if b in ext_out_bufs]
    scratch = [(b, [D, NTOK], F32) for b in bufs
               if b not in ext_in_bufs and b not in ext_out_bufs]
    seen = set()
    for p in phases:
        for sb in SCRATCH_OF.get(p, []):
            if sb not in seen:
                seen.add(sb)
                scratch.append((sb,) + SCRATCH_BUFS[sb])
    ctx = Ctx(inputs, scratch, outputs)
    for p in phases:
        PHASES[p][0](ctx)
        ctx.s.barrier()
        ctx.war = {}
    ctx.s.build()
    return ctx, wnames


def _run_launch(phases, ext_inputs_per_core, out_names):
    ctx, wnames = build_launch(
        phases,
        ext_in_bufs={n for n in ext_inputs_per_core[0]
                     if n in ACT_BUFS},
        ext_out_bufs=set(out_names))
    res = run_bass_kernel_spmd(ctx.nc, ext_inputs_per_core,
                               core_ids=list(range(len(ext_inputs_per_core))))
    return res.results


ALL_PHASES = ["P0c", "P1", "P2", "P3", "P4", "P5", "P6"]


def kernel(**inputs):
    w = _prep_weights(inputs)
    x = np.asarray(inputs["L0_attn_x"], dtype=np.float32)
    n_cores = 8
    in_maps = []
    for cc in range(n_cores):
        shard = x[cc * NSEQ:(cc + 1) * NSEQ]
        xt = np.ascontiguousarray(shard.reshape(NTOK, D).T)
        m = dict(w)
        m["XT"] = xt
        in_maps.append(m)
    results = _run_launch(ALL_PHASES, in_maps, ["OUT"])
    outs = []
    for cc in range(n_cores):
        o = results[cc]["OUT"]
        outs.append(np.ascontiguousarray(o.T).reshape(NSEQ, T, D))
    return np.concatenate(outs, 0)



def emit_quantize(ctx, w_name, rows, cols, store_fn):
    s, c, sa = ctx.s, ctx.c, ctx.sa
    src = ctx.dram[w_name]
    nch = rows // 128
    mark = sa.mark()
    wbuf = sa.alloc("qz_w", [128, cols], F32)
    abuf = sa.alloc("qz_a", [128, cols], F32)
    grow = sa.alloc("qz_g", [1, 8], F32)
    gvec = sa.alloc("qz_gv", [128, 1], F32)
    qbuf = sa.alloc("qz_q", [128, cols], BF16)
    ps = ctx.ps_g[0]
    mm_tok = None
    ld_toks = []
    for ch in range(nch):
        lt = ctx.load(wbuf[:, :], src[ch * 128:(ch + 1) * 128, :],
                      waits=[ctx.war.get("qzw")])
        ab = ctx.act(lambda e, o=abuf[:, :], i=wbuf[:, :]:
                     e.activation(o, i, AF.Abs), waits=[lt, mm_tok])
        ctx.war["qzw"] = ab
        for sub in range(cols // 512):
            mm_tok = ctx.mm(ps[0:1, 0:512], c["ones_f"][:, :],
                            abuf[:, sub * 512:(sub + 1) * 512],
                            start=(ch == 0 and sub == 0),
                            stop=(ch == nch - 1 and sub == cols // 512 - 1),
                            force_inc=True, waits=[ab])
    t1 = ctx.dve(lambda e: e.reduce_sum(grow[0:1, 0:1], ps[0:1, 0:512], AX.X),
                 waits=[mm_tok])
    t2 = ctx.dve(lambda e: e.tensor_scalar(grow[0:1, 1:2], grow[0:1, 0:1],
                                           1.0 / (rows * cols), EPS,
                                           ALU.mult, ALU.add), waits=[t1])
    t3 = ctx.dve(lambda e: e.reciprocal(grow[0:1, 2:3], grow[0:1, 1:2]),
                 waits=[t2])
    bc = ctx.mm(ctx.ps_g[1][0:128, 0:1], c["ones_r"][:, :], grow[0:1, 2:3],
                start=True, stop=True, waits=[t3, ctx.war.get("g1")])
    t4 = ctx.dve(lambda e: e.tensor_copy(gvec[:, :], ctx.ps_g[1][:, 0:1]),
                 waits=[bc])
    prev_q = None
    for ch in range(nch):
        lt = ctx.load(wbuf[:, :], src[ch * 128:(ch + 1) * 128, :],
                      waits=[ctx.war.get("qzw")])
        a1 = ctx.dve(lambda e, o=abuf[:, :], i=wbuf[:, :]:
                     e.tensor_scalar(o, i, gvec[:, :], MAGIC, ALU.mult, ALU.add),
                     waits=[lt, t4, prev_q])
        ctx.war["qzw"] = a1
        a2 = ctx.dve(lambda e, o=abuf[:, :], i=abuf[:, :]:
                     e.tensor_scalar(o, i, MAGIC, None, ALU.subtract),
                     waits=[a1])
        a3 = ctx.dve(lambda e, o=qbuf[:, :], i=abuf[:, :]:
                     e.tensor_scalar(o, i, 1.0, -1.0, ALU.min, ALU.max),
                     waits=[a2, prev_q])
        prev_q = store_fn(ch, qbuf, a3)
    s.barrier()
    ctx.war = {}
    sa.reset(mark)


def emit_l0_ffn(ctx, src_name, dst_name):
    s, c, sa = ctx.s, ctx.c, ctx.sa
    src = ctx.dram[src_name]
    dst = ctx.dram[dst_name]
    wq1d = ctx.dram["WQ1B"]
    wq2d = ctx.dram["WQ2B"]

    def store1(ch, qb, tok):
        return ctx.store(wq1d[ch * 128:(ch + 1) * 128, :], qb[:, :], waits=[tok])

    def store2(ch, qb, tok):
        d = wq2d[:, ch, :, :].rearrange("m p j -> p m j")
        sp = qb[:, :].rearrange("p (m j) -> p m j", j=128)
        return ctx.store(d, sp, waits=[tok])

    emit_quantize(ctx, "w1t", 1024, 4096, store1)
    emit_quantize(ctx, "w2t", 4096, 1024, store2)

    mark = sa.mark()
    xbuf = sa.alloc("p2_x", [128, DC * 512], F32)
    sqbuf = sa.alloc("p2_sq", [128, 512], BF16)
    rowbuf = sa.alloc("p2_row", [1, 5 * 512], F32)
    tmpbuf = sa.alloc("p2_tmp", [128, 512], F32)
    xhat = sa.alloc("p2_xhat", [128, DC * 512], BF16)
    wq1_sb = sa.alloc("p2_wq1", [128, DC * 4096], BF16)
    h_sb = sa.alloc("p2_h", [128, 32 * 512], BF16)
    wq2_sb = [sa.alloc(f"p2_wq2_{i}", [128, 32 * 128], BF16) for i in range(2)]
    b1_sb = sa.alloc("p2_b1", [128, 32], F32)
    b2_sb = sa.alloc("p2_b2", [128, 8], F32)
    ostg = [sa.alloc(f"p2_ostg{i}", [128, 512], F32) for i in range(4)]

    for ch in range(DC):
        ctx.load(wq1_sb[:, ch * 4096:(ch + 1) * 4096],
                 wq1d[ch * 128:(ch + 1) * 128, :])
    ctx.load(b1_sb[:, :], ctx.dram["b1v"][:, :])
    ctx.load(b2_sb[:, :], ctx.dram["b2v"][:, :])
    s.barrier()

    for tile in range(NTOK // 512):
        col0 = tile * 512

        def dst_fn(cc):
            return xhat[:, cc * 512:(cc + 1) * 512]
        emit_ln(ctx, src, col0, 512, xbuf, sqbuf, rowbuf, tmpbuf, dst_fn)
        xh_tok = s.now("Vn")
        for mf in range(32):
            ps = ctx.st[mf % 2]
            t = None
            for cc in range(DC):
                w = ([xh_tok, ctx.war.get("h%d" % (mf % 2))] if cc == 0 else [])
                t = ctx.mm(ps[:, 0:512],
                           wq1_sb[:, cc * 4096 + mf * 128: cc * 4096 + (mf + 1) * 128],
                           xhat[:, cc * 512:(cc + 1) * 512],
                           start=(cc == 0), stop=(cc == DC - 1), waits=w)
            o_ap = h_sb[:, mf * 512:(mf + 1) * 512]
            ev = ctx.act(lambda e, o=o_ap, i=ps[:, 0:512], b=b1_sb[:, mf:mf + 1]:
                         e.activation(o, i, AF.Relu, bias=b), waits=[t])
            ctx.war["h%d" % (mf % 2)] = ev
        h_tok = s.now("Cn")
        for m in range(8):
            slot = wq2_sb[m % 2]
            d = wq2d[m].rearrange("k p j -> p k j")
            wt = ctx.load(slot[:, :].rearrange("p (k j) -> p k j", j=128), d,
                          waits=[ctx.war.get("wq2%d" % (m % 2))])
            ps = ctx.st[2 + m % 2]
            t = None
            for k in range(32):
                w = ([h_tok, wt, ctx.war.get("o%d" % (m % 2))] if k == 0 else [])
                t = ctx.mm(ps[:, 0:512], slot[:, k * 128:(k + 1) * 128],
                           h_sb[:, k * 512:(k + 1) * 512],
                           start=(k == 0), stop=(k == 31), waits=w)
            ctx.war["wq2%d" % (m % 2)] = t
            og = ostg[m % 4]
            ev = ctx.dve(lambda e, o=og[:, :], i=ps[:, 0:512],
                         b=b2_sb[:, m:m + 1], x=xbuf[:, m * 512:(m + 1) * 512]:
                         e.scalar_tensor_tensor(o, i, b, x, ALU.add, ALU.add),
                         waits=[t, ctx.war.get("ostg%d" % (m % 4))])
            ctx.war["o%d" % (m % 2)] = ev
            ctx.war[f"lnx{m}"] = ev
            st = ctx.store(dst[m * 128:(m + 1) * 128, col0:col0 + 512],
                           og[:, :], waits=[ev])
            ctx.war["ostg%d" % (m % 4)] = st
        ctx.war["h0"] = s.now("Tn")
        ctx.war["h1"] = s.now("Tn")
    s.barrier()
    ctx.war = {}
    sa.reset(mark)



def emit_l1_attn(ctx, src_name, dst_name):
    s, c, sa = ctx.s, ctx.c, ctx.sa
    src = ctx.dram[src_name]
    dst = ctx.dram[dst_name]
    mark = sa.mark()
    xbuf = sa.alloc("p3_x", [128, DC * 512], F32)
    sqbuf = sa.alloc("p3_sq", [128, 512], BF16)
    rowbuf = sa.alloc("p3_row", [1, 5 * 512], F32)
    tmpbuf = sa.alloc("p3_tmp", [128, 512], F32)
    xhat = sa.alloc("p3_xhat", [128, DC * T], BF16)
    Q_sb = sa.alloc("p3_Q", [128, DC * T], BF16)
    K_sb = sa.alloc("p3_K", [128, DC * T], BF16)
    V_sb = sa.alloc("p3_V", [128, 16 * 1040], BF16)
    vw_sb = sa.alloc("p3_vw", [128, DC * 512], BF16)
    qkw_sb = [sa.alloc(f"p3_qkw{i}", [128, DC * 128], BF16) for i in range(2)]
    ow_sb = [sa.alloc(f"p3_ow{i}", [128, DC * 128], BF16) for i in range(2)]
    ofm4 = sa.alloc("p3_ofm4", [128, DC * 512], BF16)
    PT_sb = [sa.alloc("p3_PT%d" % i, [128, 2048], BF16) for i in range(2)]
    dmask = [sa.alloc("p3_dm%d" % i, [128, 128], BF16) for i in range(2)]
    otmn = sa.alloc("p3_otmn", [128, 1024], BF16)
    recip = sa.alloc("p3_rc", [128, 16], F32)
    xres = [sa.alloc(f"p3_xres{i}", [128, 512], F32) for i in range(2)]
    ostg = [sa.alloc(f"p3_ostg{i}", [128, 512], F32) for i in range(2)]

    qw_d = ctx.dram["QWB"]
    kw_d = ctx.dram["KWB"]
    vw_d = ctx.dram["VWB"]
    ow_d = ctx.dram["OWB"]
    ones_tok = ctx.dve(lambda e: e.memset(V_sb[:, :], 1.0))

    for sq in range(NSEQ):
        base = sq * T
        for tile in range(4):
            def dst_fn(cc, tile=tile):
                return xhat[:, cc * T + tile * 512: cc * T + tile * 512 + 512]
            emit_ln(ctx, src, base + tile * 512, 512,
                    xbuf, sqbuf, rowbuf, tmpbuf, dst_fn)
        xh_tok = s.now("Vn")
        for wi, (wd, dst_sb) in enumerate(((qw_d, Q_sb), (kw_d, K_sb))):
            for mc in range(DC):
                slot = qkw_sb[mc % 2]
                wt = ctx.load(slot[:, :], wd[mc],
                              waits=[ctx.war.get("qkw%d" % (mc % 2))])
                for tile in range(4):
                    pi = (mc * 4 + tile) % 2
                    ps = ctx.st[pi]
                    t = None
                    for cc in range(DC):
                        w = ([xh_tok, wt, ctx.war.get("qkps%d" % pi)]
                             if cc == 0 else [])
                        t = ctx.mm(ps[:, 0:512], slot[:, cc * 128:(cc + 1) * 128],
                                   xhat[:, cc * T + tile * 512: cc * T + tile * 512 + 512],
                                   start=(cc == 0), stop=(cc == DC - 1), waits=w)
                    o_ap = dst_sb[:, mc * T + tile * 512: mc * T + tile * 512 + 512]
                    ev = ctx.act(lambda e, o=o_ap, i=ps[:, 0:512]:
                                 e.activation(o, i, AF.Copy), waits=[t])
                    ctx.war["qkps%d" % pi] = ev
                ctx.war["qkw%d" % (mc % 2)] = s.now("Tn")
        for dvs in range(2):
            vt = ctx.load(vw_sb[:, :], vw_d[dvs], waits=[ctx.war.get("vw")])
            for tc in range(16):
                ps = ctx.st[2 + tc % 2]
                t = None
                for cc in range(DC):
                    w = ([xh_tok, vt, ctx.war.get("vps%d" % (tc % 2))]
                         if cc == 0 else [])
                    t = ctx.mm(ps[:, 0:512],
                               xhat[:, cc * T + tc * 128: cc * T + (tc + 1) * 128],
                               vw_sb[:, cc * 512:(cc + 1) * 512],
                               start=(cc == 0), stop=(cc == DC - 1), waits=w)
                o_ap = V_sb[:, tc * 1040 + dvs * 520: tc * 1040 + dvs * 520 + 520]
                o_ap = o_ap.rearrange("p (g k) -> p g k", k=65)[:, :, 0:64]
                i_ap = ps[:, 0:512].rearrange("p (g k) -> p g k", k=64)
                ev = ctx.dve(lambda e, o=o_ap, i=i_ap: e.tensor_copy(o, i),
                             waits=[t, ones_tok])
                ctx.war["vps%d" % (tc % 2)] = ev
            ctx.war["vw"] = s.now("Tn")
        qk_tok = s.now("Cn")
        v_tok = s.now("Vn")

        def oproj(jg, ofm_toks, base=base):
            last = None
            for m in range(8):
                slot = ow_sb[m % 2]
                wt = ctx.load(slot[:, :], ow_d[m],
                              waits=[ctx.war.get("oww%d" % (m % 2))])
                ps = ctx.st[m % 2]
                t = None
                for cc in range(DC):
                    w = (list(ofm_toks) + [wt, ctx.war.get("op%d" % (m % 2)),
                          ctx.war.get("opx%d" % (m % 2))] if cc == 0 else [])
                    t = ctx.mm(ps[:, 0:512], slot[:, cc * 128:(cc + 1) * 128],
                               ofm4[:, cc * 512:(cc + 1) * 512],
                               start=(cc == 0), stop=(cc == DC - 1), waits=w)
                ctx.war["oww%d" % (m % 2)] = t
                xr = xres[m % 2]
                xt = ctx.load(xr[:, :], src[m * 128:(m + 1) * 128,
                                            base + jg * 512: base + jg * 512 + 512],
                              waits=[ctx.war.get("xres%d" % (m % 2))])
                og = ostg[m % 2]
                ev = ctx.dve(lambda e, o=og[:, :], i=ps[:, 0:512], x=xr[:, :]:
                             e.tensor_tensor(o, i, x, ALU.add),
                             waits=[t, xt, ctx.war.get("ostg%d" % (m % 2))])
                st = ctx.store(dst[m * 128:(m + 1) * 128,
                                   base + jg * 512: base + jg * 512 + 512],
                               og[:, :], waits=[ev])
                ctx.war["ostg%d" % (m % 2)] = st
                ctx.war["op%d" % (m % 2)] = ev
                ctx.war["xres%d" % (m % 2)] = ev
                last = ev
            return last

        emit_attn_core(ctx, 16, 64, Q_sb, K_sb, V_sb, float(D) ** -0.5,
                       PT_sb, dmask, otmn, recip, ofm4, qk_tok, v_tok, oproj)
        s.barrier()
        ctx.war = {}
    sa.reset(mark)



def emit_outer_ln(ctx, src_name, dst_name):
    s, sa = ctx.s, ctx.sa
    src = ctx.dram[src_name]
    dst = ctx.dram[dst_name]
    mark = sa.mark()
    xbuf = sa.alloc("p4_x", [128, DC * 512], F32)
    sqbuf = sa.alloc("p4_sq", [128, 512], BF16)
    rowbuf = sa.alloc("p4_row", [1, 5 * 512], F32)
    tmpbuf = sa.alloc("p4_tmp", [128, 512], F32)
    ybuf = sa.alloc("p4_y", [128, DC * 512], F32)
    for tile in range(NTOK // 512):
        col0 = tile * 512

        def dst_fn(cc):
            return ybuf[:, cc * 512:(cc + 1) * 512]
        toks, _ = emit_ln(ctx, src, col0, 512, xbuf, sqbuf, rowbuf, tmpbuf,
                          dst_fn, dst_war=lambda cc: ctx.war.get(f"p4y{cc}"))
        for cc in range(DC):
            st = ctx.store(dst[cc * 128:(cc + 1) * 128, col0:col0 + 512],
                           ybuf[:, cc * 512:(cc + 1) * 512], waits=[toks[cc]])
            ctx.war[f"p4y{cc}"] = st
    s.barrier()
    ctx.war = {}
    sa.reset(mark)



def emit_mla(ctx, src_name, dst_name):
    s, c, sa = ctx.s, ctx.c, ctx.sa
    src = ctx.dram[src_name]
    dst = ctx.dram[dst_name]
    mark = sa.mark()
    K_sb = sa.alloc("p5_K", [128, DC * T], BF16)
    Q_sb = sa.alloc("p5_Q", [128, DC * T], BF16)
    V_sb = sa.alloc("p5_V", [128, 16 * 1040], BF16)
    qmix_sb = sa.alloc("p5_qmix", [128, 2 * 1024], BF16)
    qmixp_sb = sa.alloc("p5_qmixp", [128, 2 * 1024], BF16)
    kmixn_sb = sa.alloc("p5_kmixn", [128, 2 * 1024], BF16)
    kr_sb = [sa.alloc(f"p5_kr{i}", [128, DC * 128], BF16) for i in range(2)]
    krp_sb = [sa.alloc(f"p5_krp{i}", [128, DC * 128], BF16) for i in range(2)]
    vut_sb = sa.alloc("p5_vut", [128, 2 * 512], BF16)
    cmask = sa.alloc("p5_cm", [128, T], BF16)
    smask = sa.alloc("p5_sm", [128, T], BF16)
    ln1v_sb = sa.alloc("p5_ln1", [128, 8], F32)
    seq_mark = sa.mark()

    cm = np.ones((128, T), np.float32)
    sm = np.zeros((128, T), np.float32)
    cm[0:32] = ctx.rope_cos[0:32]
    cm[64:96] = ctx.rope_cos[0:32]
    sm[0:32] = ctx.rope_sin[0:32]
    sm[64:96] = ctx.rope_sin[0:32]
    cm_d = ctx.nc.inline_tensor(cm.astype(BF), name="p5cm").ap()
    sm_d = ctx.nc.inline_tensor(sm.astype(BF), name="p5sm").ap()
    ctx.load(cmask[:, :], cm_d[:, :])
    ctx.load(smask[:, :], sm_d[:, :])
    ctx.load(ln1v_sb[:, :], ctx.dram["ln1v"][:, :])
    for (sb, nm, m) in ((qmix_sb, "qmix", 1024), (qmixp_sb, "qmixp", 1024),
                        (kmixn_sb, "kmixn", 1024)):
        ctx.cast_load(sb[:, :].rearrange("p (cc m) -> p cc m", m=m),
                      ctx.dram[nm].rearrange("(cc p) m -> p cc m", p=128))
    kvd_d = ctx.dram["kvdt"].rearrange("(cc p) m -> p cc m", p=128)
    qd_d = ctx.dram["qdt"].rearrange("(cc p) m -> p cc m", p=128)
    kr_d = ctx.dram["KRB"]
    krp_d = ctx.dram["KRPB"]
    vut_d = ctx.dram["vut"].rearrange("(cc p) m -> p cc m", p=128)
    ow_d = ctx.dram["OWFB"]
    s.barrier()

    yh = sa.alloc("p5_yh", [128, DC * T], BF16)
    kvlat = sa.alloc("p5_kvl", [128, 2 * T], BF16)
    qlat = sa.alloc("p5_ql", [128, 2 * T], BF16)
    ln_mark = sa.mark()
    xbuf = sa.alloc("p5_x", [128, DC * 512], F32)
    sqbuf = sa.alloc("p5_sq", [128, 512], BF16)
    rowbuf = sa.alloc("p5_row", [1, 5 * 512], F32)
    tmpbuf = sa.alloc("p5_tmp", [128, 512], F32)
    sa.reset(ln_mark)
    rt1 = sa.alloc("p5_rt1", [128, 512], F32)
    rt2 = sa.alloc("p5_rt2", [128, 512], F32)
    rt3 = sa.alloc("p5_rt3", [128, 512], F32)
    rt4 = sa.alloc("p5_rt4", [128, 512], F32)
    sa.reset(ln_mark)
    PT_sb = [sa.alloc("p5_PT%d" % i, [128, 2048], BF16) for i in range(2)]
    ofm4 = sa.alloc("p5_ofm4", [128, DC * 512], BF16)
    dmask = [sa.alloc("p5_dm%d" % i, [128, 128], BF16) for i in range(2)]
    otmn = sa.alloc("p5_otmn", [128, 1024], BF16)
    recip = sa.alloc("p5_rc", [128, 16], F32)
    xres = [sa.alloc(f"p5_xres{i}", [128, 512], F32) for i in range(2)]
    ostg = [sa.alloc(f"p5_ostg{i}", [128, 512], F32) for i in range(2)]

    for sq in range(NSEQ):
        base = sq * T
        for tile in range(4):
            def dst_fn(cc, tile=tile):
                return yh[:, cc * T + tile * 512: cc * T + tile * 512 + 512]
            emit_ln(ctx, src, base + tile * 512, 512, xbuf, sqbuf, rowbuf,
                    tmpbuf, dst_fn,
                    gain_col=lambda cc: ln1v_sb[:, cc:cc + 1])
        yh_tok = s.now("Vn")
        for wi, (wd, dst_sb) in enumerate(((kvd_d, kvlat), (qd_d, qlat))):
            for m2 in range(2):
                slot = kr_sb[(wi * 2 + m2) % 2]
                wt = ctx.cast_load(
                    slot[:, :].rearrange("p (cc m) -> p cc m", m=128),
                    wd[:, :, m2 * 128:(m2 + 1) * 128],
                    waits=[ctx.war.get("krw%d" % ((wi * 2 + m2) % 2))])
                for tile in range(4):
                    pi = tile % 2
                    ps = ctx.st[pi]
                    t = None
                    for cc in range(DC):
                        w = ([yh_tok, wt, ctx.war.get("lat%d" % pi)]
                             if cc == 0 else [])
                        t = ctx.mm(ps[:, 0:512],
                                   slot[:, cc * 128:(cc + 1) * 128],
                                   yh[:, cc * T + tile * 512: cc * T + tile * 512 + 512],
                                   start=(cc == 0), stop=(cc == DC - 1), waits=w)
                    o_ap = dst_sb[:, m2 * T + tile * 512: m2 * T + tile * 512 + 512]
                    ev = ctx.act(lambda e, o=o_ap, i=ps[:, 0:512]:
                                 e.activation(o, i, AF.Copy), waits=[t])
                    ctx.war["lat%d" % pi] = ev
                ctx.war["krw%d" % ((wi * 2 + m2) % 2)] = s.now("Tn")
        lat_tok = s.now("Cn")
        asm_guard = s.now("Vn")
        for cc in range(DC):
            krt = ctx.load(kr_sb[cc % 2][:, :], kr_d[cc],
                           waits=[ctx.war.get("krw%d" % (cc % 2))])
            krpt = ctx.load(krp_sb[cc % 2][:, :], krp_d[cc],
                            waits=[ctx.war.get("krpw%d" % (cc % 2))])
            for tile in range(4):
                ps1 = ctx.st[0]
                ps2 = ctx.st[1]
                qs1 = ctx.st[2]
                qs2 = ctx.st[3]
                t1 = None
                for k8 in range(DC):
                    w = ([yh_tok, lat_tok, krt, ctx.war.get("kps")]
                         if k8 == 0 else [])
                    t1 = ctx.mm(ps1[:, 0:512],
                                kr_sb[cc % 2][:, k8 * 128:(k8 + 1) * 128],
                                yh[:, k8 * T + tile * 512: k8 * T + tile * 512 + 512],
                                start=(k8 == 0), stop=False, waits=w)
                for lc in range(2):
                    t1 = ctx.mm(ps1[:, 0:512],
                                kmixn_sb[:, lc * 1024 + cc * 128: lc * 1024 + (cc + 1) * 128],
                                kvlat[:, lc * T + tile * 512: lc * T + tile * 512 + 512],
                                start=False, stop=(lc == 1), waits=[])
                t2 = None
                for k8 in range(DC):
                    w = ([krpt, ctx.war.get("kps2")] if k8 == 0 else [])
                    t2 = ctx.mm(ps2[:, 0:512],
                                krp_sb[cc % 2][:, k8 * 128:(k8 + 1) * 128],
                                yh[:, k8 * T + tile * 512: k8 * T + tile * 512 + 512],
                                start=(k8 == 0), stop=(k8 == DC - 1), waits=w)
                m1 = ctx.dve(lambda e, o=rt1[:, :], i=ps1[:, 0:512], t=tile:
                             e.tensor_tensor(o, i, cmask[:, t * 512:(t + 1) * 512],
                                             ALU.mult),
                             waits=[t1, asm_guard, ctx.war.get("rt1")])
                m2_ = ctx.dve(lambda e, o=rt2[:, :], i=ps2[:, 0:512], t=tile:
                              e.tensor_tensor(o, i, smask[:, t * 512:(t + 1) * 512],
                                              ALU.mult),
                              waits=[t2, asm_guard, ctx.war.get("rt2")])
                kd = K_sb[:, cc * T + tile * 512: cc * T + tile * 512 + 512]
                m3 = ctx.dve(lambda e, o=kd: e.tensor_tensor(o, rt1[:, :], rt2[:, :],
                                                             ALU.add),
                             waits=[m1, m2_])
                ctx.war["kps"] = m1
                ctx.war["kps2"] = m2_
                ctx.war["rt1"] = m3
                ctx.war["rt2"] = m3
                t1 = None
                t2 = None
                for lc in range(2):
                    w = ([lat_tok, ctx.war.get("qps")] if lc == 0 else [])
                    t1 = ctx.mm(qs1[:, 0:512],
                                qmix_sb[:, lc * 1024 + cc * 128: lc * 1024 + (cc + 1) * 128],
                                qlat[:, lc * T + tile * 512: lc * T + tile * 512 + 512],
                                start=(lc == 0), stop=(lc == 1), waits=w)
                    w2 = ([ctx.war.get("qps2")] if lc == 0 else [])
                    t2 = ctx.mm(qs2[:, 0:512],
                                qmixp_sb[:, lc * 1024 + cc * 128: lc * 1024 + (cc + 1) * 128],
                                qlat[:, lc * T + tile * 512: lc * T + tile * 512 + 512],
                                start=(lc == 0), stop=(lc == 1), waits=w2)
                q1 = ctx.dve(lambda e, o=rt3[:, :], i=qs1[:, 0:512], t=tile:
                             e.tensor_tensor(o, i, cmask[:, t * 512:(t + 1) * 512],
                                             ALU.mult),
                             waits=[t1, asm_guard, ctx.war.get("rt3")])
                q2 = ctx.dve(lambda e, o=rt4[:, :], i=qs2[:, 0:512], t=tile:
                              e.tensor_tensor(o, i, smask[:, t * 512:(t + 1) * 512],
                                              ALU.mult),
                              waits=[t2, asm_guard, ctx.war.get("rt4")])
                qd = Q_sb[:, cc * T + tile * 512: cc * T + tile * 512 + 512]
                q3 = ctx.dve(lambda e, o=qd: e.tensor_tensor(o, rt3[:, :], rt4[:, :],
                                                             ALU.add),
                             waits=[q1, q2])
                ctx.war["qps"] = q1
                ctx.war["qps2"] = q2
                ctx.war["rt3"] = q3
                ctx.war["rt4"] = q3
            ctx.war["krw%d" % (cc % 2)] = s.now("Tn")
            ctx.war["krpw%d" % (cc % 2)] = s.now("Tn")
        ones_tok = ctx.dve(lambda e: e.memset(V_sb[:, :], 1.0),
                           waits=[ctx.war.get("vmem")])
        for dvs in range(2):
            vt = ctx.cast_load(vut_sb[:, :].rearrange("p (k m) -> p k m", m=512),
                               vut_d[:, :, dvs * 512:(dvs + 1) * 512],
                               waits=[ctx.war.get("vw")])
            for tc in range(16):
                ps = ctx.st[tc % 2]
                t = None
                for lc in range(2):
                    w = ([lat_tok, vt, ctx.war.get("kps"), ctx.war.get("kps2"),
                          ctx.war.get("qps"), ctx.war.get("qps2"),
                          ctx.war.get("vps%d" % (tc % 2))] if lc == 0 else [])
                    t = ctx.mm(ps[:, 0:512],
                               kvlat[:, lc * T + tc * 128: lc * T + (tc + 1) * 128],
                               vut_sb[:, lc * 512:(lc + 1) * 512],
                               start=(lc == 0), stop=(lc == 1), waits=w)
                o_ap = V_sb[:, tc * 1040 + dvs * 520: tc * 1040 + dvs * 520 + 520]
                o_ap = o_ap.rearrange("p (g k) -> p g k", k=65)[:, :, 0:64]
                i_ap = ps[:, 0:512].rearrange("p (g k) -> p g k", k=64)
                ev = ctx.dve(lambda e, o=o_ap, i=i_ap: e.tensor_copy(o, i),
                             waits=[t, ones_tok])
                ctx.war["vps%d" % (tc % 2)] = ev
            ctx.war["vw"] = s.now("Tn")
        ctx.war["vmem"] = s.now("Tn")
        qk_tok = s.now("Vn")
        v_tok = s.now("Vn")
        ctx.war["oww0"] = s.now("Tn")
        ctx.war["oww1"] = s.now("Tn")
        ctx.war["xres0"] = qk_tok
        ctx.war["xres1"] = qk_tok

        def oproj(jg, ofm_toks, base=base):
            last = None
            for m in range(8):
                slot = krp_sb[m % 2]
                wt = ctx.load(slot[:, :], ow_d[m],
                              waits=[ctx.war.get("oww%d" % (m % 2))])
                ps = ctx.st[m % 2]
                t = None
                for cc in range(DC):
                    w = (list(ofm_toks) + [wt, ctx.war.get("op%d" % (m % 2)),
                          ctx.war.get("opx%d" % (m % 2))] if cc == 0 else [])
                    t = ctx.mm(ps[:, 0:512], slot[:, cc * 128:(cc + 1) * 128],
                               ofm4[:, cc * 512:(cc + 1) * 512],
                               start=(cc == 0), stop=(cc == DC - 1), waits=w)
                ctx.war["oww%d" % (m % 2)] = t
                xr = xres[m % 2]
                xt = ctx.load(xr[:, :], src[m * 128:(m + 1) * 128,
                                            base + jg * 512: base + jg * 512 + 512],
                              waits=[ctx.war.get("xres%d" % (m % 2))])
                og = ostg[m % 2]
                ev = ctx.dve(lambda e, o=og[:, :], i=ps[:, 0:512], x=xr[:, :]:
                             e.tensor_tensor(o, i, x, ALU.add),
                             waits=[t, xt, ctx.war.get("ostg%d" % (m % 2))])
                st = ctx.store(dst[m * 128:(m + 1) * 128,
                                   base + jg * 512: base + jg * 512 + 512],
                               og[:, :], waits=[ev])
                ctx.war["ostg%d" % (m % 2)] = st
                ctx.war["op%d" % (m % 2)] = ev
                ctx.war["xres%d" % (m % 2)] = ev
                last = ev
            return last

        emit_attn_core(ctx, 16, 64, Q_sb, K_sb, V_sb, 64.0 ** -0.5,
                       PT_sb, dmask, otmn, recip, ofm4, qk_tok, v_tok, oproj)
        s.barrier()
        ctx.war = {}
    sa.reset(mark)



def emit_precast(ctx):
    s = ctx.s
    pairs = [
        ("gall", "GB", 8), ("uall", "UB", 8), ("dall", "DB", 8),
        ("qw1", "QWB", 1), ("kw1", "KWB", 1), ("vw1", "VWB", 1),
        ("ow1", "OWB", 1), ("kmixr", "KRB", 1), ("kmixrp", "KRPB", 1),
        ("owft", "OWFB", 1),
    ]
    for src, dstn, nsplit in pairs:
        sd = ctx.dram[src]
        dd = ctx.dram[dstn]
        if nsplit == 1:
            ctx.cast_load(dd.rearrange("a p c -> (a p) c"),
                          sd.rearrange("a p b c -> (a p) (b c)"))
        else:
            for e in range(nsplit):
                ctx.cast_load(dd[e].rearrange("m p c -> (m p) c"),
                              sd[e].rearrange("m p b c -> (m p) (b c)"))
    s.barrier()
    ctx.war = {}



def emit_moe(ctx, x_name, y1_name, dst_name):
    s, c, sa = ctx.s, ctx.c, ctx.sa
    xsrc = ctx.dram[x_name]
    ysrc = ctx.dram[y1_name]
    dst = ctx.dram[dst_name]
    NTILE = 1024
    mark = sa.mark()
    y1 = sa.alloc("p6_y1", [128, DC * NTILE], F32)
    yh2 = sa.alloc("p6_yh2", [128, DC * NTILE], BF16)
    h_sb = sa.alloc("p6_h", [128, 16 * NTILE], BF16)
    acc = [sa.alloc(f"p6_acc{i}", [128, DC * NTILE], F32) for i in range(2)]
    wbc = sa.alloc("p6_wbc", [128, NTILE], F32)
    gstage = [sa.alloc(f"p6_gs{i}", [128, DC * 128], BF16) for i in range(4)]
    dstage = [sa.alloc(f"p6_ds{i}", [128, 16 * 128], BF16) for i in range(2)]
    glbuf = sa.alloc("p6_gl", [128, 2 * 512], BF16)
    ubuf = sa.alloc("p6_ub", [128, 2 * 512], BF16)
    tmpo = sa.alloc("p6_tmpo", [128, 512], F32)
    rp_sb = sa.alloc("p6_rp", [128, DC * 7], F32)
    ln2v_sb = sa.alloc("p6_ln2", [128, 8], F32)
    rbias_sb = sa.alloc("p6_rb", [1, 7], F32)
    rsum_sb = sa.alloc("p6_rs", [1, 7], F32)
    rsbc = sa.alloc("p6_rsbc", [128, 7], F32)
    rbbc = sa.alloc("p6_rbbc", [128, 7], F32)
    arow = sa.alloc("p6_ar", [1, NTILE], F32)
    brow = sa.alloc("p6_br", [1, NTILE], F32)
    raw_sb = sa.alloc("p6_raw", [7, NTILE], F32)
    ltm = sa.alloc("p6_ltm", [128, 8 * 7], F32)
    ab_sb = sa.alloc("p6_ab", [128, 2 * 8], F32)
    prob = sa.alloc("p6_prob", [128, 8 * 7], F32)
    rsc = sa.alloc("p6_rsc", [128, 8 * 7], F32)
    m12 = sa.alloc("p6_m12", [128, 3 * 8], F32)
    wgt_tm = sa.alloc("p6_wgt", [128, 8 * 7], F32)
    identf = sa.alloc("p6_idf", [128, 128], F32)
    diag = sa.alloc("p6_diag", [128, 128], F32)
    onesq = sa.alloc("p6_onsq", [128, 128], F32)
    xbuf_alias = sa.mark()
    sqbuf = sa.alloc("p6_sq", [128, 512], BF16)
    rowbuf = sa.alloc("p6_row", [1, 5 * 512], F32)
    tmpbuf = sa.alloc("p6_tmp", [128, 512], F32)
    xres = [sa.alloc(f"p6_xres{i}", [128, 512], F32) for i in range(2)]
    ostg = [sa.alloc(f"p6_ostg{i}", [128, 512], F32) for i in range(2)]

    idf_d = ctx.nc.inline_tensor(np.eye(128, dtype=np.float32), name="p6idf").ap()
    onsq_d = ctx.nc.inline_tensor(np.ones((128, 128), np.float32), name="p6ones").ap()
    ctx.load(identf[:, :], idf_d[:, :])
    ctx.load(onesq[:, :], onsq_d[:, :])
    ctx.load(ln2v_sb[:, :], ctx.dram["ln2v"][:, :])
    ctx.load(rbias_sb[:, :], ctx.dram["rbias"][:, :])
    rw_tok = ctx.load(rp_sb[:, :].rearrange("p (cc m) -> p cc m", m=7),
                      ctx.dram["rwt"].rearrange("(cc p) m -> p cc m", p=128))
    s.barrier()
    t = None
    for cc in range(DC):
        t = ctx.dve(lambda e, o=rp_sb[:, cc * 7:(cc + 1) * 7], g=ln2v_sb[:, cc:cc + 1]:
                    e.tensor_scalar_mul(o, o, g), waits=[])
    rs_t = None
    for cc in range(DC):
        rs_t = ctx.mm(ctx.ps_g[2][0:1, 0:7], c["ones_f"][:, :],
                      rp_sb[:, cc * 7:(cc + 1) * 7],
                      start=(cc == 0), stop=(cc == DC - 1), waits=[t])
    t = ctx.dve(lambda e: e.tensor_copy(rsum_sb[0:1, :], ctx.ps_g[2][0:1, 0:7]),
                waits=[rs_t])
    b1 = ctx.mm(ctx.ps_g[2][:, 0:7], c["ones_r"][:, :], rsum_sb[0:1, :],
                start=True, stop=True, waits=[t])
    t = ctx.dve(lambda e: e.tensor_copy(rsbc[:, :], ctx.ps_g[2][:, 0:7]), waits=[b1])
    b2 = ctx.mm(ctx.ps_g[2][:, 0:7], c["ones_r"][:, :], rbias_sb[0:1, :],
                start=True, stop=True, waits=[t])
    t = ctx.dve(lambda e: e.tensor_copy(rbbc[:, :], ctx.ps_g[2][:, 0:7]), waits=[b2])
    s.barrier()
    ctx.war = {}

    for tile in range(NTOK // NTILE):
        col0 = tile * NTILE
        for sub in range(2):
            def dst_fn(cc, sub=sub):
                return yh2[:, cc * NTILE + sub * 512: cc * NTILE + sub * 512 + 512]
            emit_ln(ctx, ysrc, col0 + sub * 512, 512,
                    _Slice1024(y1, sub), sqbuf, rowbuf, tmpbuf, dst_fn,
                    gain_col=lambda cc: ln2v_sb[:, cc:cc + 1])
            a_t = ctx.dve(lambda e, o=arow[0:1, sub * 512:(sub + 1) * 512]:
                          e.tensor_copy(o, rowbuf[0:1, 0:512]))
            b_t = ctx.dve(lambda e, o=brow[0:1, sub * 512:(sub + 1) * 512]:
                          e.tensor_copy(o, rowbuf[0:1, 512:1024]))
            ctx.war["rowA"] = a_t
            ctx.war["rowB"] = b_t
        yh_tok = s.now("Vn")
        for sub in range(2):
            t = None
            for cc in range(DC):
                w = ([yh_tok, ctx.war.get("rps")] if cc == 0 else [])
                t = ctx.mm(ctx.ps_g[2][0:7, 0:512], rp_sb[:, cc * 7:(cc + 1) * 7],
                           y1[:, cc * NTILE + sub * 512: cc * NTILE + sub * 512 + 512],
                           start=(cc == 0), stop=(cc == DC - 1), waits=w)
            ev = ctx.dve(lambda e, o=raw_sb[0:7, sub * 512:(sub + 1) * 512],
                         i=ctx.ps_g[2][0:7, 0:512]: e.tensor_copy(o, i), waits=[t])
            ctx.war["rps"] = ev
        for bb in range(8):
            t1 = ctx.s.emit(
                "tensor",
                lambda e, o=ctx.ps_g[2][0:128, 0:7],
                i=raw_sb[0:7, bb * 128:(bb + 1) * 128]:
                e.transpose(o, i, identf[0:7, 0:7]),
                waits=[ctx.war.get("rps"), ctx.war.get("rtr")], inc="Tn")
            ta = ctx.mm(ctx.ps_g[0][0:128, 0:1], arow[0:1, bb * 128:(bb + 1) * 128],
                        c["ones_r"][0:1, 0:1], start=True, stop=True,
                        waits=[ctx.war.get("rta")])
            tb = ctx.mm(ctx.ps_g[0][0:128, 1:2], brow[0:1, bb * 128:(bb + 1) * 128],
                        c["ones_r"][0:1, 0:1], start=True, stop=True)
            e1 = ctx.dve(lambda e, o=ltm[:, bb * 7:(bb + 1) * 7],
                         i=ctx.ps_g[2][0:128, 0:7]: e.tensor_copy(o, i), waits=[t1])
            e2 = ctx.dve(lambda e, o=ab_sb[:, bb * 2:bb * 2 + 2],
                         i=ctx.ps_g[0][0:128, 0:2]: e.tensor_copy(o, i), waits=[tb, ta])
            ctx.war["rtr"] = e1
            ctx.war["rta"] = e2
            l1 = ctx.dve(lambda e, o=rsc[:, bb * 7:(bb + 1) * 7],
                         b_=ab_sb[:, bb * 2 + 1:bb * 2 + 2]:
                         e.scalar_tensor_tensor(o, rsbc[:, :], b_, rbbc[:, :],
                                                ALU.mult, ALU.add), waits=[e2])
            l2 = ctx.dve(lambda e, o=ltm[:, bb * 7:(bb + 1) * 7],
                         a_=ab_sb[:, bb * 2:bb * 2 + 1],
                         t2=rsc[:, bb * 7:(bb + 1) * 7]:
                         e.scalar_tensor_tensor(o, o, a_, t2, ALU.mult, ALU.add),
                         waits=[l1, e1])
            sg = ctx.act(lambda e, o=prob[:, bb * 7:(bb + 1) * 7],
                         i=ltm[:, bb * 7:(bb + 1) * 7]:
                         e.activation(o, i, AF.Sigmoid), waits=[l2])
            bm = bb * 7
            m1c = m12[:, bb * 3:bb * 3 + 1]
            m2c = m12[:, bb * 3 + 1:bb * 3 + 2]
            sc_ = m12[:, bb * 3 + 2:bb * 3 + 3]
            r1 = ctx.dve(lambda e, o=m1c, i=prob[:, bm:bm + 7]:
                         e.reduce_max(o, i, AX.X), waits=[sg])
            r2 = ctx.dve(lambda e, o=rsc[:, bm:bm + 7], i=prob[:, bm:bm + 7], m=m1c:
                         e.tensor_scalar(o, i, m, None, ALU.is_ge), waits=[r1])
            r3 = ctx.dve(lambda e, o=rsc[:, bm:bm + 7], i=rsc[:, bm:bm + 7],
                         p=prob[:, bm:bm + 7]:
                         e.scalar_tensor_tensor(o, i, -1e30, p, ALU.mult, ALU.add),
                         waits=[r2])
            r4 = ctx.dve(lambda e, o=m2c, i=rsc[:, bm:bm + 7]:
                         e.reduce_max(o, i, AX.X), waits=[r3])
            r5 = ctx.dve(lambda e, o=rsc[:, bm:bm + 7], i=prob[:, bm:bm + 7], m=m2c:
                         e.tensor_scalar(o, i, m, None, ALU.is_ge), waits=[r4])
            r6 = ctx.dve(lambda e, o=sc_, i=m1c, i2=m2c:
                         e.tensor_tensor(o, i, i2, ALU.add), waits=[r4])
            r7 = ctx.dve(lambda e, o=sc_, i=sc_: e.reciprocal(o, i), waits=[r6])
            r8 = ctx.dve(lambda e, o=wgt_tm[:, bm:bm + 7], i=prob[:, bm:bm + 7],
                         i2=rsc[:, bm:bm + 7]:
                         e.tensor_tensor(o, i, i2, ALU.mult), waits=[r5])
            r9 = ctx.dve(lambda e, o=wgt_tm[:, bm:bm + 7], i=wgt_tm[:, bm:bm + 7],
                         m=sc_: e.tensor_scalar_mul(o, i, m), waits=[r7, r8])
        wgt_tok = s.now("Vn")
        ctx.war["dps0"] = wgt_tok
        ctx.war["dps1"] = wgt_tok
        ctx.war["wbcp"] = wgt_tok
        for ei in range(8):
            if ei > 0:
                wtoks = []
                for bb in range(8):
                    dg = ctx.dve(lambda e, o=diag[:, :],
                                 m=wgt_tm[:, bb * 7 + ei - 1: bb * 7 + ei]:
                                 e.tensor_scalar_mul(o, identf[:, :], m),
                                 waits=[wgt_tok, ctx.war.get("diag")])
                    bt = ctx.mm(ctx.ps_g[2][:, 0:128], onesq[:, :], diag[:, :],
                                start=True, stop=True,
                                waits=[dg, ctx.war.get("wbcp")])
                    ctx.war["diag"] = bt
                    ev = ctx.dve(lambda e, o=wbc[:, bb * 128:(bb + 1) * 128],
                                 i=ctx.ps_g[2][:, 0:128]: e.tensor_copy(o, i),
                                 waits=[bt, ctx.war.get("wbcr")])
                    ctx.war["wbcp"] = ev
                    wtoks.append(ev)
                wbc_tok = wtoks[-1]
            else:
                wbc_tok = None
            for mf in range(16):
                gslot = gstage[mf % 2]
                gt = ctx.load(gslot[:, :], ctx.dram["GB"][ei, mf],
                              waits=[ctx.war.get("gsl%d" % (mf % 2))])
                uslot = gstage[2 + mf % 2]
                ut = ctx.load(uslot[:, :], ctx.dram["UB"][ei, mf],
                              waits=[ctx.war.get("usl%d" % (mf % 2))])
                for nt in range(2):
                    ps_gm = ctx.st[nt]
                    t = None
                    for cc in range(DC):
                        w = ([yh_tok, gt, ctx.war.get("gps%d" % nt)]
                             if cc == 0 else [])
                        t = ctx.mm(ps_gm[:, 0:512],
                                   gslot[:, cc * 128:(cc + 1) * 128],
                                   yh2[:, cc * NTILE + nt * 512: cc * NTILE + nt * 512 + 512],
                                   start=(cc == 0), stop=(cc == DC - 1), waits=w)
                    sl = ctx.act(lambda e, o=glbuf[:, nt * 512:(nt + 1) * 512],
                                 i=ps_gm[:, 0:512]: e.activation(o, i, AF.Silu),
                                 waits=[t, ctx.war.get("glb%d" % nt)])
                    ctx.war["gps%d" % nt] = sl
                for nt in range(2):
                    ps_um = ctx.st[2 + nt]
                    t = None
                    for cc in range(DC):
                        w = ([ut, ctx.war.get("ups%d" % nt)] if cc == 0 else [])
                        t = ctx.mm(ps_um[:, 0:512],
                                   uslot[:, cc * 128:(cc + 1) * 128],
                                   yh2[:, cc * NTILE + nt * 512: cc * NTILE + nt * 512 + 512],
                                   start=(cc == 0), stop=(cc == DC - 1), waits=w)
                    ue = ctx.act(lambda e, o=ubuf[:, nt * 512:(nt + 1) * 512],
                                 i=ps_um[:, 0:512]: e.activation(o, i, AF.Copy),
                                 waits=[t, ctx.war.get("ubf%d" % nt)])
                    ctx.war["ups%d" % nt] = ue
                    hm = ctx.pool(lambda e, o=h_sb[:, mf * NTILE + nt * 512:
                                                   mf * NTILE + nt * 512 + 512],
                                  i=glbuf[:, nt * 512:(nt + 1) * 512],
                                  i2=ubuf[:, nt * 512:(nt + 1) * 512]:
                                  e.tensor_tensor(o, i, i2, ALU.mult),
                                  waits=[ue, ctx.war.get("gps%d" % nt),
                                         ctx.war.get("hsb%d" % nt)])
                    ctx.war["glb%d" % nt] = hm
                    ctx.war["ubf%d" % nt] = hm
                ctx.war["gsl%d" % (mf % 2)] = s.now("Tn")
                ctx.war["usl%d" % (mf % 2)] = s.now("Tn")
            h_tok = s.now("Pn")
            for m2 in range(8):
                dslot = dstage[m2 % 2]
                dt = ctx.load(dslot[:, :], ctx.dram["DB"][ei, m2],
                              waits=[ctx.war.get("dsl%d" % (m2 % 2))])
                for nt in range(2):
                    ps_d = ctx.ps_g[nt]
                    t = None
                    for k in range(16):
                        w = ([h_tok, dt, ctx.war.get("dps%d" % nt)]
                             if k == 0 else [])
                        t = ctx.mm(ps_d[:, 0:512],
                                   dslot[:, k * 128:(k + 1) * 128],
                                   h_sb[:, k * NTILE + nt * 512: k * NTILE + nt * 512 + 512],
                                   start=(k == 0), stop=(k == 15), waits=w)
                    col = m2 * NTILE + nt * 512
                    if ei == 0:
                        ev = ctx.dve(lambda e, o=acc[1][:, col:col + 512],
                                     i=ps_d[:, 0:512]: e.tensor_copy(o, i),
                                     waits=[t, ctx.war.get("accw")])
                    else:
                        ev = ctx.dve(lambda e, o=tmpo[:, :], i=ps_d[:, 0:512],
                                     wb=wbc[:, nt * 512:(nt + 1) * 512]:
                                     e.tensor_tensor(o, i, wb, ALU.mult),
                                     waits=[t, wbc_tok, ctx.war.get("tmpo")])
                        ctx.war["wbcr"] = ev
                        ev = ctx.dve(lambda e, o=acc[(ei + 1) % 2][:, col:col + 512],
                                     i=acc[ei % 2][:, col:col + 512], t2=tmpo[:, :]:
                                     e.tensor_tensor(o, i, t2, ALU.add),
                                     waits=[ev, ctx.war.get("accw")])
                        ctx.war["tmpo"] = ev
                    ctx.war["dps%d" % nt] = ev
                ctx.war["dsl%d" % (m2 % 2)] = s.now("Tn")
            ctx.war["hsb0"] = s.now("Tn")
            ctx.war["hsb1"] = s.now("Tn")
        acc_fin = acc[0]
        acc_tok = s.now("Vn")
        for m2 in range(8):
            for nt in range(2):
                col = m2 * NTILE + nt * 512
                xr = xres[nt]
                xt = ctx.load(xr[:, :],
                              xsrc[m2 * 128:(m2 + 1) * 128,
                                   col0 + nt * 512: col0 + nt * 512 + 512],
                              waits=[ctx.war.get("xres%d" % nt)])
                o1 = ctx.dve(lambda e, o=ostg[nt][:, :], i=acc_fin[:, col:col + 512],
                             y=y1[:, col:col + 512]:
                             e.tensor_tensor(o, i, y, ALU.add),
                             waits=[acc_tok, ctx.war.get("ost%d" % nt)])
                o2 = ctx.dve(lambda e, o=ostg[nt][:, :], i=ostg[nt][:, :], x=xr[:, :]:
                             e.tensor_tensor(o, i, x, ALU.add), waits=[o1, xt])
                ctx.war["xres%d" % nt] = o2
                st_ = ctx.store(dst[m2 * 128:(m2 + 1) * 128,
                                    col0 + nt * 512: col0 + nt * 512 + 512],
                                ostg[nt][:, :], waits=[o2])
                ctx.war["ost%d" % nt] = st_
        ctx.war["accw"] = s.now("Vn")
        ctx.war["lnW"] = s.now("Tn")
        s.barrier()
        ctx.war = {}
    sa.reset(mark)


class _Slice1024:
    def __init__(self, y1, sub):
        self.y1 = y1
        self.sub = sub

    def __getitem__(self, key):
        p, f = key
        lo, hi = f.start, f.stop
        cc = lo // 512
        return self.y1[p, cc * 1024 + self.sub * 512:
                       cc * 1024 + self.sub * 512 + (hi - lo)]


# version 11
# speedup vs baseline: 1.2306x; 1.0601x over previous
import numpy as np
import ml_dtypes

import concourse.bass as bass
import concourse.mybir as mybir
from concourse.bass_utils import run_bass_kernel_spmd

F32 = mybir.dt.float32
BF16 = mybir.dt.bfloat16
AF = mybir.ActivationFunctionType
ALU = mybir.AluOpType
AX = mybir.AxisListType
BF = ml_dtypes.bfloat16

D = 1024
DC = 8
T = 2048
NSEQ = 2
NTOK = NSEQ * T
EPS = 1e-5
MAGIC = 1.5 * (2.0 ** 23)

ENGS = ("sync", "gpsimd", "tensor", "vector", "scalar")

N_LD = 8
N_GL = 4
N_ST = 4


class Sched:
    def __init__(self, nc):
        self.nc = nc
        self.q = {e: [] for e in ENGS}
        self.sem = {}
        self.cnt = {}
        self.last_wait = {e: {} for e in ENGS}

    def new_sem(self, name):
        self.sem[name] = self.nc.alloc_semaphore(name=name)
        self.cnt[name] = 0
        return name

    def emit(self, eng, fn, waits=(), inc=None, inc_by=1):
        lw = self.last_wait[eng]
        for w in waits:
            if w is None:
                continue
            sname, v = w
            if v > 0 and lw.get(sname, -1) < v:
                self.q[eng].append(("w", sname, v))
                lw[sname] = v
        self.q[eng].append(("o", fn, inc, inc_by))
        if inc is not None:
            self.cnt[inc] += inc_by
            return (inc, self.cnt[inc])
        return None

    def now(self, sem):
        return (sem, self.cnt[sem])

    def barrier(self):
        toks = [(n, c) for n, c in self.cnt.items() if c > 0]
        for e in ENGS:
            lw = self.last_wait[e]
            for sname, v in toks:
                if lw.get(sname, -1) < v:
                    self.q[e].append(("w", sname, v))
                    lw[sname] = v

    def build(self):
        with self.nc.Block() as block:
            for eng in ENGS:
                stream = self.q[eng]
                sem = self.sem

                def body(engine, stream=stream):
                    for item in stream:
                        if item[0] == "w":
                            engine.wait_ge(sem[item[1]], item[2])
                        else:
                            _, fn, inc, n = item
                            inst = fn(engine)
                            if inc is not None:
                                inst.then_inc(sem[inc], n)
                getattr(block, eng)(body)


class SA:
    def __init__(self, nc):
        self.nc = nc
        self.base = (nc.sbuf_base + 31) & ~31
        self.off = self.base
        self.top = nc.sbuf_top
        self.uid = 0

    def alloc(self, name, shape, dtype):
        per = int(np.prod(shape[1:])) * mybir.dt.size(dtype)
        per = (per + 31) & ~31
        assert self.off + per <= self.top, (
            f"SBUF overflow at {name}: {self.off}+{per} > {self.top}")
        self.uid += 1
        t = self.nc.alloc_sbuf_tensor_at(f"{name}_{self.uid}", list(shape), dtype,
                                         offset=self.off)
        self.off += per
        return t

    def mark(self):
        return self.off

    def reset(self, m):
        self.off = m


class Ctx:
    def __init__(self, inputs, scratch, outputs):
        nc = bass.Bass("TRN2", target_bir_lowering=False, debug=False)
        self.nc = nc
        self.s = Sched(nc)
        self.sa = SA(nc)
        s = self.s
        self.ld_sems = [s.new_sem(f"ld{i}") for i in range(N_LD)]
        self.gl_sems = [s.new_sem(f"gl{i}") for i in range(N_GL)]
        self.st_sems = [s.new_sem(f"st{i}") for i in range(N_ST)]
        self._ld_i = 0
        self._gl_i = 0
        self._st_i = 0
        for n in ("Tn", "Vn", "Cn", "Pn"):
            s.new_sem(n)
        self.dram = {}
        for name, shape, dt in inputs:
            self.dram[name] = nc.dram_tensor(name, list(shape), dt,
                                             kind="ExternalInput").ap()
        for name, shape, dt in scratch:
            self.dram[name] = nc.dram_tensor(name, list(shape), dt,
                                             kind="Internal").ap()
        for name, shape, dt in outputs:
            self.dram[name] = nc.dram_tensor(name, list(shape), dt,
                                             kind="ExternalOutput").ap()
        self.st = [nc.alloc_psum_tensor(f"ps_st{i}", [128, 512], F32)
                   for i in range(4)]
        self.ps_g = [nc.alloc_psum_tensor(f"ps_g{i}", [128, 512], F32)
                     for i in range(3)]
        self.ps_bf = nc.alloc_psum_tensor("ps_bf", [128, 1024], BF16)
        self.war = {}
        self.precast = {}
        self._make_consts()
        self.s.barrier()

    def _make_consts(self):
        nc, s, sa = self.nc, self.s, self.sa
        tri = np.triu(np.ones((128, 128), np.float32)).astype(BF)
        RD = 32
        inv_freq = 1.0 / (10000.0 ** (np.arange(0, RD, 2, dtype=np.float64) / RD))
        tpos = np.arange(T, dtype=np.float64)
        freqs = tpos[:, None] * inv_freq[None, :]
        emb = np.concatenate([freqs, freqs], axis=-1)
        cos = np.cos(emb).astype(np.float32)
        sin = np.sin(emb).astype(np.float32)
        sgn = np.ones(32, np.float32)
        sgn[:16] = -1.0
        consts = {
            "ident": np.eye(128, dtype=BF),
            "triu": tri,
            "ones_b": np.ones((128, 1), BF),
            "ones_f": np.ones((128, 1), np.float32),
            "ones_r": np.ones((1, 128), np.float32),
            "eps_c": np.full((128, 1), EPS, np.float32),
        }
        self.rope_cos = np.ascontiguousarray(np.tile(cos.T, (4, 1)))
        self.rope_sin = np.ascontiguousarray(np.tile((sin * sgn[None, :]).T, (4, 1)))
        self.c = {}
        for name, arr in consts.items():
            dt = BF16 if arr.dtype == BF else F32
            dr = nc.inline_tensor(np.ascontiguousarray(arr), name=f"c_{name}").ap()
            sb = sa.alloc(f"c_{name}", list(arr.shape), dt)
            self.load(sb[:, :], dr[:, :])
            self.c[name] = sb

    def _dma(self, eng, sems, idx_attr, dst_ap, src_ap, waits):
        i = getattr(self, idx_attr)
        setattr(self, idx_attr, i + 1)
        sem = sems[i % len(sems)]
        pre = self.s.now(sem)
        return self.s.emit(eng, lambda e: e.dma_start(out=dst_ap, in_=src_ap),
                           waits=list(waits) + [pre], inc=sem, inc_by=16)

    def load(self, dst_ap, src_ap, waits=()):
        return self._dma("sync", self.ld_sems, "_ld_i", dst_ap, src_ap, waits)

    def cast_load(self, dst_ap, src_ap, waits=()):
        return self._dma("gpsimd", self.gl_sems, "_gl_i", dst_ap, src_ap, waits)

    def store(self, dst_ap, src_ap, waits=()):
        return self._dma("gpsimd", self.st_sems, "_st_i", dst_ap, src_ap, waits)

    def dma_tr(self, dst_ap, src_ap, waits=()):
        i = self._ld_i
        self._ld_i = i + 1
        sem = self.ld_sems[i % N_LD]
        pre = self.s.now(sem)
        return self.s.emit("sync",
                           lambda e: e.dma_start_transpose(dst_ap, src_ap),
                           waits=list(waits) + [pre], inc=sem, inc_by=16)

    def mm(self, out_ap, lhsT, rhs, start, stop, waits=(), force_inc=False):
        return self.s.emit(
            "tensor",
            lambda e: e.matmul(out_ap, lhsT, rhs, start=start, stop=stop),
            waits=waits, inc=("Tn" if (stop or force_inc) else None))

    def tr(self, out_ap, in_ap, waits=()):
        ident = self.c["ident"]
        return self.s.emit(
            "tensor", lambda e: e.transpose(out_ap, in_ap, ident[:, :]),
            waits=waits, inc="Tn")

    def act(self, fn, waits=()):
        return self.s.emit("scalar", fn, waits=waits, inc="Cn")

    def dve(self, fn, waits=()):
        return self.s.emit("vector", fn, waits=waits, inc="Vn")

    def pool(self, fn, waits=()):
        return self.s.emit("gpsimd", fn, waits=waits, inc="Pn")



def emit_ln(ctx, src, col0, n, xbuf, sqbuf, rowbuf, tmpbuf, dst_fn,
            gain_col=None, dst_war=None):
    s, c = ctx.s, ctx.c
    ps_sum, ps_sq = ctx.ps_g[0], ctx.ps_g[1]
    x_toks = []
    sum_tok = None
    sq_act_tok = None
    sq_tok = ctx.war.get("lnsq")
    for cc in range(DC):
        xa = xbuf[:, cc * n:(cc + 1) * n]
        xt = ctx.load(xa, src[cc * 128:(cc + 1) * 128, col0:col0 + n],
                      waits=[ctx.war.get(f"lnx{cc}"), ctx.war.get("lnxT")])
        x_toks.append(xt)
        sum_tok = ctx.mm(ps_sum[0:1, 0:n], c["ones_f"][:, :], xa,
                         start=(cc == 0), stop=(cc == DC - 1),
                         waits=[xt, ctx.war.get("g0")] if cc == 0 else [xt])
        sq_act_tok = ctx.act(
            lambda e, o=sqbuf[:, 0:n], i=xa: e.activation(o, i, AF.Square),
            waits=[xt, sq_tok])
        sq_tok = ctx.mm(ps_sq[0:1, 0:n], c["ones_b"][:, :], sqbuf[:, 0:n],
                        start=(cc == 0), stop=(cc == DC - 1), force_inc=True,
                        waits=([sq_act_tok, ctx.war.get("g1")]
                               if cc == 0 else [sq_act_tok]))
    ctx.war["lnsq"] = sq_tok
    A = rowbuf[0:1, 0:n]
    B = rowbuf[0:1, n:2 * n]
    M = rowbuf[0:1, 2 * n:3 * n]
    T1 = rowbuf[0:1, 3 * n:4 * n]
    T2 = rowbuf[0:1, 4 * n:5 * n]
    t1 = ctx.dve(lambda e: e.tensor_scalar_mul(M, ps_sum[0:1, 0:n], 1.0 / D),
                 waits=[sum_tok])
    t2 = ctx.dve(lambda e: e.tensor_tensor(T1, M, M, ALU.mult), waits=[t1])
    t3 = ctx.dve(lambda e: e.scalar_tensor_tensor(T2, ps_sq[0:1, 0:n], 1.0 / D,
                                                  T1, ALU.mult, ALU.subtract),
                 waits=[t2, sq_tok])
    t4 = ctx.act(lambda e: e.activation(T1, T2, AF.Sqrt, bias=c["eps_c"][0:1, :]),
                 waits=[t3])
    t5 = ctx.dve(lambda e: e.reciprocal(A, T1),
                 waits=[t4, ctx.war.get("rowA")])
    t6 = ctx.dve(lambda e: e.scalar_tensor_tensor(B, M, -1.0, A, ALU.mult, ALU.mult),
                 waits=[t5, ctx.war.get("rowB")])
    bA = ctx.mm(ps_sum[:, 0:n], c["ones_r"][:, :], A, start=True, stop=True,
                waits=[t5, t1])
    bB = ctx.mm(ps_sq[:, 0:n], c["ones_r"][:, :], B, start=True, stop=True,
                waits=[t6, t3])
    ctx.war["rowA"] = bA
    ctx.war["rowB"] = bB
    xhat_toks = []
    tp = None
    for cc in range(DC):
        xa = xbuf[:, cc * n:(cc + 1) * n]
        tp = ctx.dve(lambda e, o=tmpbuf[:, 0:n], i=xa:
                     e.tensor_tensor(o, i, ps_sum[:, 0:n], ALU.mult),
                     waits=[bA, tp])
        if gain_col is None:
            tw = ctx.dve(lambda e, o=dst_fn(cc), i=tmpbuf[:, 0:n]:
                         e.tensor_tensor(o, i, ps_sq[:, 0:n], ALU.add),
                         waits=[bB, tp, dst_war(cc) if dst_war else None])
        else:
            tw = ctx.dve(lambda e, o=tmpbuf[:, 0:n], i=tmpbuf[:, 0:n]:
                         e.tensor_tensor(o, i, ps_sq[:, 0:n], ALU.add),
                         waits=[bB, tp])
            tw = ctx.dve(lambda e, o=dst_fn(cc), i=tmpbuf[:, 0:n], g=gain_col(cc):
                         e.tensor_scalar_mul(o, i, g),
                         waits=[tw, dst_war(cc) if dst_war else None])
        tp = tw
        ctx.war[f"lnx{cc}"] = tw
        xhat_toks.append(tw)
    ctx.war["g0"] = xhat_toks[-1]
    ctx.war["g1"] = xhat_toks[-1]
    return xhat_toks, x_toks



def emit_attn_core(ctx, nh, dh, Q_sb, K_sb, V_sb, scale, PT_sb, dmask_sb,
                   otmn_sb, recip_sb, ofm4_sb, qk_tok, v_tok, oproj_fn):
    s, c = ctx.s, ctx.c
    hpc = 128 // dh
    DCk = nh // hpc
    ng64 = dh // 64
    hw = ng64 * 65
    gw = nh * hw
    nh64 = nh * ng64
    exp_tok = [None] * 4
    pv_par = [None, None]
    mask_tok = None
    tr_toks = []
    hh = 0
    for j in range(16):
        obank_used = []
        for h in range(nh):
            rb = (h % hpc) * dh
            cc = h // hpc
            par = hh % 2
            hh += 1
            PT = PT_sb[par]
            dmask = dmask_sb[par]
            for b in range(0, j + 1, 4):
                mm_t = None
                bank = ctx.st[b // 4]
                for c4 in range(b, min(b + 4, j + 1)):
                    w = [qk_tok, v_tok, exp_tok[c4 // 4]] if c4 == b else []
                    mm_t = ctx.mm(
                        bank[:, (c4 - b) * 128:(c4 - b + 1) * 128],
                        K_sb[rb:rb + dh, cc * T + c4 * 128: cc * T + (c4 + 1) * 128],
                        Q_sb[rb:rb + dh, cc * T + j * 128: cc * T + (j + 1) * 128],
                        start=True, stop=True, waits=w)
                hi = min(b + 4, j + 1) * 128
                o_ap = PT[:, b * 128:hi]
                i_ap = bank[:, 0:hi - b * 128]
                exp_tok[b // 4] = ctx.act(
                    lambda e, o=o_ap, i=i_ap: e.activation(o, i, AF.Exp, scale=scale),
                    waits=[mm_t, pv_par[par]])
            dm_in = PT[:, j * 128:(j + 1) * 128]
            mask_tok = ctx.dve(
                lambda e, o=dmask[:, 0:128], i=dm_in:
                e.tensor_tensor(o, i, c["triu"][:, :], ALU.mult),
                waits=[exp_tok[j // 4], pv_par[par]])
            pv_tok = None
            for g in range(ng64):
                h64 = h * ng64 + g
                bk = h64 // 7
                ocol = (h64 % 7) * 65
                if bk not in obank_used:
                    obank_used.append(bk)
                for c4 in range(j + 1):
                    lt = (dmask[:, 0:128] if c4 == j
                          else PT[:, c4 * 128:(c4 + 1) * 128])
                    if c4 == 0:
                        w = [v_tok, mask_tok, ctx.war.get(f"ob{bk}")]
                    elif c4 == j:
                        w = [mask_tok]
                    else:
                        w = [exp_tok[c4 // 4]]
                    vs = V_sb[:, c4 * gw + h * hw + g * 65:
                              c4 * gw + h * hw + g * 65 + 65]
                    pv_tok = ctx.mm(ctx.ps_g[bk][:, ocol:ocol + 65], lt, vs,
                                    start=(c4 == 0), stop=(c4 == j), waits=w)
            pv_par[par] = pv_tok
        rc_tok = {}
        for bk in obank_used:
            ngrp = min(7, nh64 - bk * 7)
            src = ctx.ps_g[bk][:, 0:ngrp * 65].rearrange(
                "p (g k) -> p g k", k=65)[:, :, 64]
            dst = recip_sb[:, bk * 7:bk * 7 + ngrp]
            rc_tok[bk] = ctx.dve(lambda e, o=dst, i=src: e.reciprocal(o, i),
                                 waits=[pv_par[0], pv_par[1]])
        nt = None
        for h in range(nh):
            for g in range(ng64):
                h64 = h * ng64 + g
                bk = h64 // 7
                ocol = (h64 % 7) * 65
                o_ap = otmn_sb[:, h64 * 64:(h64 + 1) * 64]
                i_ap = ctx.ps_g[bk][:, ocol:ocol + 64]
                r_ap = recip_sb[:, h64:h64 + 1]
                nt = ctx.dve(
                    lambda e, o=o_ap, i=i_ap, r=r_ap: e.tensor_scalar_mul(o, i, r),
                    waits=[rc_tok[bk], ctx.war.get(f"otm{h64 // 2}")])
            if (h % hpc == hpc - 1) or dh == 128:
                cc = h // hpc
                src = otmn_sb[:, cc * 128:(cc + 1) * 128]
                dst = ofm4_sb[:, cc * 512 + (j % 4) * 128:
                              cc * 512 + (j % 4) * 128 + 128]
                tt = ctx.dma_tr(dst, src, waits=[nt, ctx.war.get("ofm4")])
                ctx.war[f"otm{cc}"] = tt
                tr_toks.append(tt)
        for bk in obank_used:
            ctx.war[f"ob{bk}"] = s.now("Vn")
        if j % 4 == 3:
            ctx.war["opx0"] = exp_tok[0]
            ctx.war["opx1"] = exp_tok[1]
            op_tok = oproj_fn(j // 4, tr_toks + [s.now("Vn")])
            exp_tok[0] = op_tok
            exp_tok[1] = op_tok
            ctx.war["ofm4"] = op_tok
            tr_toks = []


def emit_proj_to(ctx, w_sb, x_sb, nchunk, x_off, n, mcols, out_fn, waits=()):
    for m in range(mcols // 128):
        bank = ctx.ps_g[2] if (m % 2 == 0) else ctx.ps_st[:, 0:512]
        ps = bank[:, 0:n] if (m % 2 == 0) else ctx.ps_st[:, 0:n]
        t = None
        for cc in range(nchunk):
            w = list(waits) + [ctx.war.get("proj%d" % (m % 2))] if cc == 0 else []
            t = ctx.mm(ps, w_sb[:, cc * mcols + m * 128: cc * mcols + (m + 1) * 128],
                       x_sb[:, cc * n + x_off: cc * n + x_off + n],
                       start=(cc == 0), stop=(cc == nchunk - 1), waits=w)
        ev = out_fn(m, ps, t)
        ctx.war["proj%d" % (m % 2)] = ev


def emit_l0_attn(ctx, src_name, dst_name):
    s, c, sa = ctx.s, ctx.c, ctx.sa
    src = ctx.dram[src_name]
    dst = ctx.dram[dst_name]
    mark = sa.mark()
    xbuf = sa.alloc("p1_x", [128, DC * 512], F32)
    sqbuf = sa.alloc("p1_sq", [128, 512], BF16)
    rowbuf = sa.alloc("p1_row", [1, 5 * 512], F32)
    tmpbuf = sa.alloc("p1_tmp", [128, 512], F32)
    xhat = sa.alloc("p1_xhat", [128, DC * T], BF16)
    wq_sb = sa.alloc("p1_wq", [128, DC * 128], BF16)
    wk_sb = sa.alloc("p1_wk", [128, DC * 128], BF16)
    wv_sb = sa.alloc("p1_wv", [128, DC * 128], BF16)
    wo_sb = sa.alloc("p1_wo", [128, 1024], BF16)
    Q_sb = sa.alloc("p1_Q", [128, T], BF16)
    K_sb = sa.alloc("p1_K", [128, T], BF16)
    V_sb = sa.alloc("p1_V", [128, 16 * 130], BF16)
    PT_sb = [sa.alloc("p1_PT%d" % i, [128, 2048], BF16) for i in range(2)]
    dmask = [sa.alloc("p1_dm%d" % i, [128, 128], BF16) for i in range(2)]
    otmn = sa.alloc("p1_otmn", [128, 128], BF16)
    recip = sa.alloc("p1_rc", [128, 16], F32)
    ofm4 = sa.alloc("p1_ofm4", [128, 512], BF16)
    xres = [sa.alloc(f"p1_xres{i}", [128, 512], F32) for i in range(2)]
    ostg = [sa.alloc(f"p1_ostg{i}", [128, 512], F32) for i in range(4)]

    wq_t = ctx.dram["wq0"].rearrange("(c p) m -> p c m", p=128)
    wk_t = ctx.dram["wk0"].rearrange("(c p) m -> p c m", p=128)
    wv_t = ctx.dram["wv0"].rearrange("(c p) m -> p c m", p=128)
    wtok = [ctx.cast_load(wq_sb[:, :].rearrange("p (c m) -> p c m", m=128), wq_t),
            ctx.cast_load(wk_sb[:, :].rearrange("p (c m) -> p c m", m=128), wk_t),
            ctx.cast_load(wv_sb[:, :].rearrange("p (c m) -> p c m", m=128), wv_t),
            ctx.cast_load(wo_sb[:, :], ctx.dram["wo0"][:, :])]
    ones_tok = ctx.dve(lambda e: e.memset(V_sb[:, :], 1.0))

    for sq in range(NSEQ):
        base = sq * T
        xh_toks = []
        for tile in range(4):
            def dst_fn(cc, tile=tile):
                return xhat[:, cc * T + tile * 512: cc * T + tile * 512 + 512]
            ts_, _ = emit_ln(ctx, src, base + tile * 512, 512,
                             xbuf, sqbuf, rowbuf, tmpbuf, dst_fn,
                             dst_war=lambda cc: ctx.war.get("seqg"))
            xh_toks += ts_
        xh_tok = s.now("Vn")
        qk_evs = []
        for (w_sb, dst_sb) in ((wq_sb, Q_sb), (wk_sb, K_sb)):
            for tile in range(4):
                ps = ctx.ps_g[2][:, 0:512] if tile % 2 == 0 else ctx.st[0][:, 0:512]
                t = None
                for cc in range(DC):
                    w = ([xh_tok, wtok[0], wtok[1], wtok[2],
                          ctx.war.get("qk%d" % (tile % 2))] if cc == 0 else [])
                    t = ctx.mm(ps, w_sb[:, cc * 128:(cc + 1) * 128],
                               xhat[:, cc * T + tile * 512: cc * T + tile * 512 + 512],
                               start=(cc == 0), stop=(cc == DC - 1), waits=w)
                o_ap = dst_sb[:, tile * 512: tile * 512 + 512]
                ev = ctx.act(lambda e, o=o_ap, i=ps: e.activation(o, i, AF.Copy),
                             waits=[t, ctx.war.get("seqg")])
                ctx.war["qk%d" % (tile % 2)] = ev
                qk_evs.append(ev)
        v_evs = []
        for tc in range(16):
            ps = ctx.st[1 + (tc % 3)][:, 0:128]
            t = None
            for cc in range(DC):
                w = ([xh_tok, ctx.war.get("v%d" % (tc % 3))] if cc == 0 else [])
                t = ctx.mm(ps, xhat[:, cc * T + tc * 128: cc * T + (tc + 1) * 128],
                           wv_sb[:, cc * 128:(cc + 1) * 128],
                           start=(cc == 0), stop=(cc == DC - 1), waits=w)
            o_ap = V_sb[:, tc * 130: tc * 130 + 130].rearrange(
                "p (g k) -> p g k", k=65)[:, :, 0:64]
            i_ap = ps.rearrange("p (g k) -> p g k", k=64)
            ev = ctx.dve(lambda e, o=o_ap, i=i_ap: e.tensor_copy(o, i),
                         waits=[t, ones_tok, ctx.war.get("seqg")])
            ctx.war["v%d" % (tc % 3)] = ev
            v_evs.append(ev)
        qk_tok = s.now("Cn")
        v_tok = s.now("Vn")

        def oproj(jg, ofm_toks, base=base):
            last = None
            for m in range(8):
                ps = ctx.st[m % 2][:, 0:512]
                t = ctx.mm(ps, wo_sb[:, m * 128:(m + 1) * 128], ofm4[:, 0:512],
                           start=True, stop=True,
                           waits=list(ofm_toks) + [wtok[3],
                                  ctx.war.get("op%d" % (m % 2)),
                                  ctx.war.get("opx%d" % (m % 2))])
                xr = xres[m % 2]
                xt = ctx.load(xr[:, :], src[m * 128:(m + 1) * 128,
                                            base + jg * 512: base + jg * 512 + 512],
                              waits=[ctx.war.get("xres%d" % (m % 2))])
                og = ostg[(jg * 8 + m) % 4]
                ev = ctx.dve(lambda e, o=og[:, :], i=ps, x=xr[:, :]:
                             e.tensor_tensor(o, i, x, ALU.add),
                             waits=[t, xt, ctx.war.get("ostg%d" % ((jg * 8 + m) % 4))])
                st = ctx.store(dst[m * 128:(m + 1) * 128,
                                   base + jg * 512: base + jg * 512 + 512],
                               og[:, :], waits=[ev])
                ctx.war["ostg%d" % ((jg * 8 + m) % 4)] = st
                ctx.war["op%d" % (m % 2)] = ev
                ctx.war["xres%d" % (m % 2)] = ev
                last = ev
            return last

        emit_attn_core(ctx, 1, 128, Q_sb, K_sb, V_sb, 128.0 ** -0.5,
                       PT_sb, dmask, otmn, recip, ofm4, qk_tok, v_tok, oproj)
        ctx.war["seqg"] = s.now("Tn")
        ctx.war["lnxT"] = s.now("Tn")
    s.barrier()
    ctx.war = {}
    sa.reset(mark)



def _prep_weights(inp):
    w = {}
    f32c = lambda a: np.ascontiguousarray(np.asarray(a), dtype=np.float32)
    w["wq0"] = f32c(inp["L0_attn_Wq"])
    w["wk0"] = f32c(inp["L0_attn_Wk"])
    w["wv0"] = f32c(inp["L0_attn_Wv"])
    w["wo0"] = f32c(inp["L0_attn_Wo"])
    w["w1t"] = f32c(np.asarray(inp["L0_ffn_W1"]).T)
    w["b1v"] = f32c(np.asarray(inp["L0_ffn_b1"]).reshape(32, 128).T)
    w["w2t"] = f32c(np.asarray(inp["L0_ffn_W2"]).T)
    w["b2v"] = f32c(np.asarray(inp["L0_ffn_b2"]).reshape(8, 128).T)
    HEADS = 16
    hd = D // HEADS
    perm = np.arange(D).reshape(hd, HEADS).T.reshape(-1)
    w["qw1"] = f32c(np.asarray(inp["L1_attn_qw"])[:, perm])
    w["kw1"] = f32c(np.asarray(inp["L1_attn_kw"])[:, perm])
    w["vw1"] = f32c(np.asarray(inp["L1_attn_vw"])[:, perm])
    w["ow1"] = f32c(np.asarray(inp["L1_attn_ow"])[perm, :])
    w["ln1v"] = f32c(np.asarray(inp["L1_ffn_ln1_w"]).reshape(8, 128).T)
    w["ln2v"] = f32c(np.asarray(inp["L1_ffn_ln2_w"]).reshape(8, 128).T)
    w["kvdt"] = f32c(np.asarray(inp["L1_ffn_kv_d"]).T)
    w["qdt"] = f32c(np.asarray(inp["L1_ffn_q_d"]).T)
    w["vut"] = f32c(np.asarray(inp["L1_ffn_v_u"]).T)
    rq = np.asarray(inp["L1_ffn_rope_q_w"], np.float32)
    rk = np.asarray(inp["L1_ffn_rope_k_w"], np.float32)
    qu_full = np.asarray(inp["L1_ffn_q_u"], np.float32)
    ku_full = np.asarray(inp["L1_ffn_k_u"], np.float32)
    pj = (np.arange(32) + 16) % 32
    qmix = np.zeros((1024, 256), np.float32)
    qmixp = np.zeros((1024, 256), np.float32)
    kmixn = np.zeros((1024, 256), np.float32)
    kmixr = np.zeros((1024, 1024), np.float32)
    kmixrp = np.zeros((1024, 1024), np.float32)
    for h in range(16):
        rows = slice(h * 64, h * 64 + 32)
        qmix[h * 64:h * 64 + 32] = rq[h * 32:h * 32 + 32]
        qmix[h * 64 + 32:h * 64 + 64] = qu_full[h * 64 + 32:h * 64 + 64]
        qmixp[h * 64:h * 64 + 32] = rq[h * 32 + pj]
        kmixn[h * 64 + 32:h * 64 + 64] = ku_full[h * 64 + 32:h * 64 + 64]
        kmixr[h * 64:h * 64 + 32] = rk[h * 32:h * 32 + 32]
        kmixrp[h * 64:h * 64 + 32] = rk[h * 32 + pj]
    w["qmix"] = f32c(qmix.T)
    w["qmixp"] = f32c(qmixp.T)
    w["kmixn"] = f32c(kmixn.T)
    w["kmixr"] = f32c(kmixr.T)
    w["kmixrp"] = f32c(kmixrp.T)
    w["owft"] = f32c(np.asarray(inp["L1_ffn_o_w"]).T)
    w["rwt"] = f32c(np.asarray(inp["L1_ffn_router_w"]).T)
    w["rbias"] = f32c(np.asarray(inp["L1_ffn_routing_bias"]).reshape(1, 7))
    ga = np.concatenate([np.asarray(inp["L1_ffn_shared_gate"]),
                         np.asarray(inp["L1_ffn_routed_gate"])], 0)
    ua = np.concatenate([np.asarray(inp["L1_ffn_shared_up"]),
                         np.asarray(inp["L1_ffn_routed_up"])], 0)
    da = np.concatenate([np.asarray(inp["L1_ffn_shared_down"]),
                         np.asarray(inp["L1_ffn_routed_down"])], 0)
    w["gall"] = f32c(ga.reshape(8, 16, 128, 8, 128).transpose(0, 1, 4, 3, 2))
    w["uall"] = f32c(ua.reshape(8, 16, 128, 8, 128).transpose(0, 1, 4, 3, 2))
    w["dall"] = f32c(da.reshape(8, 8, 128, 16, 128).transpose(0, 1, 4, 3, 2))

    def tile4(mat):
        return f32c(np.ascontiguousarray(
            np.asarray(mat, np.float32).reshape(8, 128, 8, 128).transpose(2, 1, 0, 3)))
    w["qw1"] = tile4(w["qw1"])
    w["kw1"] = tile4(w["kw1"])
    w["ow1"] = tile4(w["ow1"])
    w["kmixr"] = tile4(w["kmixr"])
    w["kmixrp"] = tile4(w["kmixrp"])
    w["owft"] = tile4(w["owft"])
    w["vw1"] = f32c(np.asarray(w["vw1"]).reshape(8, 128, 2, 512).transpose(2, 1, 0, 3))
    return w


WSHAPES = {
    "wq0": ([1024, 128], F32), "wk0": ([1024, 128], F32), "wv0": ([1024, 128], F32),
    "wo0": ([128, 1024], F32), "w1t": ([1024, 4096], F32), "b1v": ([128, 32], F32),
    "w2t": ([4096, 1024], F32), "b2v": ([128, 8], F32),
    "qw1": ([8, 128, 8, 128], F32), "kw1": ([8, 128, 8, 128], F32),
    "vw1": ([2, 128, 8, 512], F32), "ow1": ([8, 128, 8, 128], F32),
    "ln1v": ([128, 8], F32), "ln2v": ([128, 8], F32),
    "kvdt": ([1024, 256], F32), "qdt": ([1024, 256], F32),
    "vut": ([256, 1024], F32),
    "qmix": ([256, 1024], F32), "qmixp": ([256, 1024], F32),
    "kmixn": ([256, 1024], F32),
    "kmixr": ([8, 128, 8, 128], F32), "kmixrp": ([8, 128, 8, 128], F32),
    "owft": ([8, 128, 8, 128], F32), "rwt": ([1024, 7], F32), "rbias": ([1, 7], F32),
    "gall": ([8, 16, 128, 8, 128], F32), "uall": ([8, 16, 128, 8, 128], F32),
    "dall": ([8, 8, 128, 16, 128], F32),
}

ACT_BUFS = {
    "XT": None, "RX1": None, "RX2": None, "RX3": None, "Y": None, "Y1": None,
    "OUT": None,
}

PHASES = {
    "P0c": (lambda ctx: emit_precast(ctx),
            ["gall", "uall", "dall", "qw1", "kw1", "vw1", "ow1",
             "kmixr", "kmixrp", "owft"], [], []),
    "P1": (lambda ctx: emit_l0_attn(ctx, "XT", "RX1"),
           ["wq0", "wk0", "wv0", "wo0"], ["XT"], ["RX1"]),
    "P2": (lambda ctx: emit_l0_ffn(ctx, "RX1", "RX2"),
           ["w1t", "b1v", "w2t", "b2v"], ["RX1"], ["RX2"]),
    "P3": (lambda ctx: emit_l1_attn(ctx, "RX2", "RX3"),
           [], ["RX2"], ["RX3"]),
    "P4": (lambda ctx: emit_outer_ln(ctx, "RX3", "Y"),
           [], ["RX3"], ["Y"]),
    "P5": (lambda ctx: emit_mla(ctx, "Y", "Y1"),
           ["ln1v", "qmix", "qmixp", "kmixn", "kvdt", "qdt", "vut"],
           ["Y"], ["Y1"]),
    "P6": (lambda ctx: emit_moe(ctx, "RX3", "Y1", "OUT"),
           ["ln2v", "rwt", "rbias"], ["RX3", "Y1"], ["OUT"]),
}

SCRATCH_BUFS = {
    "WQ1B": ([1024, 4096], BF16),
    "WQ2B": ([8, 32, 128, 128], BF16),
    "GB": ([8, 16, 128, 1024], BF16), "UB": ([8, 16, 128, 1024], BF16),
    "DB": ([8, 8, 128, 2048], BF16),
    "QWB": ([8, 128, 1024], BF16), "KWB": ([8, 128, 1024], BF16),
    "VWB": ([2, 128, 4096], BF16), "OWB": ([8, 128, 1024], BF16),
    "KRB": ([8, 128, 1024], BF16), "KRPB": ([8, 128, 1024], BF16),
    "OWFB": ([8, 128, 1024], BF16),
}
SCRATCH_OF = {"P2": ["WQ1B", "WQ2B"],
              "P0c": ["GB", "UB", "DB", "QWB", "KWB", "VWB", "OWB",
                      "KRB", "KRPB", "OWFB"],
              "P3": ["QWB", "KWB", "VWB", "OWB"],
              "P5": ["KRB", "KRPB", "OWFB"],
              "P6": ["GB", "UB", "DB"]}


def build_launch(phases, ext_in_bufs, ext_out_bufs):
    wnames = sorted({w for p in phases for w in PHASES[p][1]})
    bufs = sorted({b for p in phases for b in PHASES[p][2] + PHASES[p][3]})
    inputs = [(n,) + tuple(WSHAPES[n]) for n in wnames]
    inputs = [(n, WSHAPES[n][0], WSHAPES[n][1]) for n in wnames]
    for b in bufs:
        if b in ext_in_bufs:
            inputs.append((b, [D, NTOK], F32))
    outputs = [(b, [D, NTOK], F32) for b in bufs if b in ext_out_bufs]
    scratch = [(b, [D, NTOK], F32) for b in bufs
               if b not in ext_in_bufs and b not in ext_out_bufs]
    seen = set()
    for p in phases:
        for sb in SCRATCH_OF.get(p, []):
            if sb not in seen:
                seen.add(sb)
                scratch.append((sb,) + SCRATCH_BUFS[sb])
    ctx = Ctx(inputs, scratch, outputs)
    for p in phases:
        PHASES[p][0](ctx)
        ctx.s.barrier()
        ctx.war = {}
    ctx.s.build()
    return ctx, wnames


def _run_launch(phases, ext_inputs_per_core, out_names):
    ctx, wnames = build_launch(
        phases,
        ext_in_bufs={n for n in ext_inputs_per_core[0]
                     if n in ACT_BUFS},
        ext_out_bufs=set(out_names))
    res = run_bass_kernel_spmd(ctx.nc, ext_inputs_per_core,
                               core_ids=list(range(len(ext_inputs_per_core))))
    return res.results


ALL_PHASES = ["P0c", "P1", "P2", "P3", "P4", "P5", "P6"]


def kernel(**inputs):
    w = _prep_weights(inputs)
    x = np.asarray(inputs["L0_attn_x"], dtype=np.float32)
    n_cores = 8
    in_maps = []
    for cc in range(n_cores):
        shard = x[cc * NSEQ:(cc + 1) * NSEQ]
        xt = np.ascontiguousarray(shard.reshape(NTOK, D).T)
        m = dict(w)
        m["XT"] = xt
        in_maps.append(m)
    results = _run_launch(ALL_PHASES, in_maps, ["OUT"])
    outs = []
    for cc in range(n_cores):
        o = results[cc]["OUT"]
        outs.append(np.ascontiguousarray(o.T).reshape(NSEQ, T, D))
    return np.concatenate(outs, 0)



def emit_quantize(ctx, w_name, rows, cols, store_fn):
    s, c, sa = ctx.s, ctx.c, ctx.sa
    src = ctx.dram[w_name]
    nch = rows // 128
    mark = sa.mark()
    wbuf = sa.alloc("qz_w", [128, cols], F32)
    abuf = sa.alloc("qz_a", [128, cols], F32)
    grow = sa.alloc("qz_g", [1, 8], F32)
    gvec = sa.alloc("qz_gv", [128, 1], F32)
    qbuf = sa.alloc("qz_q", [128, cols], BF16)
    ps = ctx.ps_g[0]
    mm_tok = None
    ld_toks = []
    for ch in range(nch):
        lt = ctx.load(wbuf[:, :], src[ch * 128:(ch + 1) * 128, :],
                      waits=[ctx.war.get("qzw")])
        ab = ctx.act(lambda e, o=abuf[:, :], i=wbuf[:, :]:
                     e.activation(o, i, AF.Abs), waits=[lt, mm_tok])
        ctx.war["qzw"] = ab
        for sub in range(cols // 512):
            mm_tok = ctx.mm(ps[0:1, 0:512], c["ones_f"][:, :],
                            abuf[:, sub * 512:(sub + 1) * 512],
                            start=(ch == 0 and sub == 0),
                            stop=(ch == nch - 1 and sub == cols // 512 - 1),
                            force_inc=True, waits=[ab])
    t1 = ctx.dve(lambda e: e.reduce_sum(grow[0:1, 0:1], ps[0:1, 0:512], AX.X),
                 waits=[mm_tok])
    t2 = ctx.dve(lambda e: e.tensor_scalar(grow[0:1, 1:2], grow[0:1, 0:1],
                                           1.0 / (rows * cols), EPS,
                                           ALU.mult, ALU.add), waits=[t1])
    t3 = ctx.dve(lambda e: e.reciprocal(grow[0:1, 2:3], grow[0:1, 1:2]),
                 waits=[t2])
    bc = ctx.mm(ctx.ps_g[1][0:128, 0:1], c["ones_r"][:, :], grow[0:1, 2:3],
                start=True, stop=True, waits=[t3, ctx.war.get("g1")])
    t4 = ctx.dve(lambda e: e.tensor_copy(gvec[:, :], ctx.ps_g[1][:, 0:1]),
                 waits=[bc])
    prev_q = None
    for ch in range(nch):
        lt = ctx.load(wbuf[:, :], src[ch * 128:(ch + 1) * 128, :],
                      waits=[ctx.war.get("qzw")])
        a1 = ctx.dve(lambda e, o=abuf[:, :], i=wbuf[:, :]:
                     e.tensor_scalar(o, i, gvec[:, :], MAGIC, ALU.mult, ALU.add),
                     waits=[lt, t4, prev_q])
        ctx.war["qzw"] = a1
        a2 = ctx.dve(lambda e, o=abuf[:, :], i=abuf[:, :]:
                     e.tensor_scalar(o, i, MAGIC, None, ALU.subtract),
                     waits=[a1])
        a3 = ctx.dve(lambda e, o=qbuf[:, :], i=abuf[:, :]:
                     e.tensor_scalar(o, i, 1.0, -1.0, ALU.min, ALU.max),
                     waits=[a2, prev_q])
        prev_q = store_fn(ch, qbuf, a3)
    s.barrier()
    ctx.war = {}
    sa.reset(mark)


def emit_l0_ffn(ctx, src_name, dst_name):
    s, c, sa = ctx.s, ctx.c, ctx.sa
    src = ctx.dram[src_name]
    dst = ctx.dram[dst_name]
    wq1d = ctx.dram["WQ1B"]
    wq2d = ctx.dram["WQ2B"]

    def store1(ch, qb, tok):
        return ctx.store(wq1d[ch * 128:(ch + 1) * 128, :], qb[:, :], waits=[tok])

    def store2(ch, qb, tok):
        d = wq2d[:, ch, :, :].rearrange("m p j -> p m j")
        sp = qb[:, :].rearrange("p (m j) -> p m j", j=128)
        return ctx.store(d, sp, waits=[tok])

    emit_quantize(ctx, "w1t", 1024, 4096, store1)
    emit_quantize(ctx, "w2t", 4096, 1024, store2)

    mark = sa.mark()
    xbuf = sa.alloc("p2_x", [128, DC * 512], F32)
    sqbuf = sa.alloc("p2_sq", [128, 512], BF16)
    rowbuf = sa.alloc("p2_row", [1, 5 * 512], F32)
    tmpbuf = sa.alloc("p2_tmp", [128, 512], F32)
    xhat = sa.alloc("p2_xhat", [128, DC * 512], BF16)
    wq1_sb = sa.alloc("p2_wq1", [128, DC * 4096], BF16)
    h_sb = sa.alloc("p2_h", [128, 32 * 512], BF16)
    wq2_sb = [sa.alloc(f"p2_wq2_{i}", [128, 32 * 128], BF16) for i in range(2)]
    b1_sb = sa.alloc("p2_b1", [128, 32], F32)
    b2_sb = sa.alloc("p2_b2", [128, 8], F32)
    ostg = [sa.alloc(f"p2_ostg{i}", [128, 512], F32) for i in range(4)]

    for ch in range(DC):
        ctx.load(wq1_sb[:, ch * 4096:(ch + 1) * 4096],
                 wq1d[ch * 128:(ch + 1) * 128, :])
    ctx.load(b1_sb[:, :], ctx.dram["b1v"][:, :])
    ctx.load(b2_sb[:, :], ctx.dram["b2v"][:, :])
    s.barrier()

    for tile in range(NTOK // 512):
        col0 = tile * 512

        def dst_fn(cc):
            return xhat[:, cc * 512:(cc + 1) * 512]
        emit_ln(ctx, src, col0, 512, xbuf, sqbuf, rowbuf, tmpbuf, dst_fn)
        xh_tok = s.now("Vn")
        for mf in range(32):
            ps = ctx.st[mf % 2]
            t = None
            for cc in range(DC):
                w = ([xh_tok, ctx.war.get("h%d" % (mf % 2))] if cc == 0 else [])
                t = ctx.mm(ps[:, 0:512],
                           wq1_sb[:, cc * 4096 + mf * 128: cc * 4096 + (mf + 1) * 128],
                           xhat[:, cc * 512:(cc + 1) * 512],
                           start=(cc == 0), stop=(cc == DC - 1), waits=w)
            o_ap = h_sb[:, mf * 512:(mf + 1) * 512]
            ev = ctx.act(lambda e, o=o_ap, i=ps[:, 0:512], b=b1_sb[:, mf:mf + 1]:
                         e.activation(o, i, AF.Relu, bias=b), waits=[t])
            ctx.war["h%d" % (mf % 2)] = ev
        h_tok = s.now("Cn")
        for m in range(8):
            slot = wq2_sb[m % 2]
            d = wq2d[m].rearrange("k p j -> p k j")
            wt = ctx.load(slot[:, :].rearrange("p (k j) -> p k j", j=128), d,
                          waits=[ctx.war.get("wq2%d" % (m % 2))])
            ps = ctx.st[2 + m % 2]
            t = None
            for k in range(32):
                w = ([h_tok, wt, ctx.war.get("o%d" % (m % 2))] if k == 0 else [])
                t = ctx.mm(ps[:, 0:512], slot[:, k * 128:(k + 1) * 128],
                           h_sb[:, k * 512:(k + 1) * 512],
                           start=(k == 0), stop=(k == 31), waits=w)
            ctx.war["wq2%d" % (m % 2)] = t
            og = ostg[m % 4]
            ev = ctx.dve(lambda e, o=og[:, :], i=ps[:, 0:512],
                         b=b2_sb[:, m:m + 1], x=xbuf[:, m * 512:(m + 1) * 512]:
                         e.scalar_tensor_tensor(o, i, b, x, ALU.add, ALU.add),
                         waits=[t, ctx.war.get("ostg%d" % (m % 4))])
            ctx.war["o%d" % (m % 2)] = ev
            ctx.war[f"lnx{m}"] = ev
            st = ctx.store(dst[m * 128:(m + 1) * 128, col0:col0 + 512],
                           og[:, :], waits=[ev])
            ctx.war["ostg%d" % (m % 4)] = st
        ctx.war["h0"] = s.now("Tn")
        ctx.war["h1"] = s.now("Tn")
    s.barrier()
    ctx.war = {}
    sa.reset(mark)



def emit_l1_attn(ctx, src_name, dst_name):
    s, c, sa = ctx.s, ctx.c, ctx.sa
    src = ctx.dram[src_name]
    dst = ctx.dram[dst_name]
    mark = sa.mark()
    xbuf = sa.alloc("p3_x", [128, DC * 512], F32)
    sqbuf = sa.alloc("p3_sq", [128, 512], BF16)
    rowbuf = sa.alloc("p3_row", [1, 5 * 512], F32)
    tmpbuf = sa.alloc("p3_tmp", [128, 512], F32)
    xhat = sa.alloc("p3_xhat", [128, DC * T], BF16)
    Q_sb = sa.alloc("p3_Q", [128, DC * T], BF16)
    K_sb = sa.alloc("p3_K", [128, DC * T], BF16)
    V_sb = sa.alloc("p3_V", [128, 16 * 1040], BF16)
    vw_sb = sa.alloc("p3_vw", [128, DC * 512], BF16)
    qkw_sb = [sa.alloc(f"p3_qkw{i}", [128, DC * 128], BF16) for i in range(2)]
    ow_sb = [sa.alloc(f"p3_ow{i}", [128, DC * 128], BF16) for i in range(2)]
    ofm4 = sa.alloc("p3_ofm4", [128, DC * 512], BF16)
    PT_sb = [sa.alloc("p3_PT%d" % i, [128, 2048], BF16) for i in range(2)]
    dmask = [sa.alloc("p3_dm%d" % i, [128, 128], BF16) for i in range(2)]
    otmn = sa.alloc("p3_otmn", [128, 1024], BF16)
    recip = sa.alloc("p3_rc", [128, 16], F32)
    xres = [sa.alloc(f"p3_xres{i}", [128, 512], F32) for i in range(2)]
    ostg = [sa.alloc(f"p3_ostg{i}", [128, 512], F32) for i in range(2)]

    qw_d = ctx.dram["QWB"]
    kw_d = ctx.dram["KWB"]
    vw_d = ctx.dram["VWB"]
    ow_d = ctx.dram["OWB"]
    ones_tok = ctx.dve(lambda e: e.memset(V_sb[:, :], 1.0))

    for sq in range(NSEQ):
        base = sq * T
        for tile in range(4):
            def dst_fn(cc, tile=tile):
                return xhat[:, cc * T + tile * 512: cc * T + tile * 512 + 512]
            emit_ln(ctx, src, base + tile * 512, 512,
                    xbuf, sqbuf, rowbuf, tmpbuf, dst_fn,
                    dst_war=lambda cc: ctx.war.get("seqg"))
        xh_tok = s.now("Vn")
        for wi, (wd, dst_sb) in enumerate(((qw_d, Q_sb), (kw_d, K_sb))):
            for mc in range(DC):
                slot = qkw_sb[mc % 2]
                wt = ctx.load(slot[:, :], wd[mc],
                              waits=[ctx.war.get("qkw%d" % (mc % 2)),
                                     ctx.precast.get("QWB"), ctx.precast.get("KWB")])
                for tile in range(4):
                    pi = (mc * 4 + tile) % 2
                    ps = ctx.st[pi]
                    t = None
                    for cc in range(DC):
                        w = ([xh_tok, wt, ctx.war.get("qkps%d" % pi)]
                             if cc == 0 else [])
                        t = ctx.mm(ps[:, 0:512], slot[:, cc * 128:(cc + 1) * 128],
                                   xhat[:, cc * T + tile * 512: cc * T + tile * 512 + 512],
                                   start=(cc == 0), stop=(cc == DC - 1), waits=w)
                    o_ap = dst_sb[:, mc * T + tile * 512: mc * T + tile * 512 + 512]
                    ev = ctx.act(lambda e, o=o_ap, i=ps[:, 0:512]:
                                 e.activation(o, i, AF.Copy),
                                 waits=[t, ctx.war.get("seqg")])
                    ctx.war["qkps%d" % pi] = ev
                ctx.war["qkw%d" % (mc % 2)] = s.now("Tn")
        for dvs in range(2):
            vt = ctx.load(vw_sb[:, :], vw_d[dvs], waits=[ctx.war.get("vw"), ctx.precast.get("VWB")])
            for tc in range(16):
                ps = ctx.st[2 + tc % 2]
                t = None
                for cc in range(DC):
                    w = ([xh_tok, vt, ctx.war.get("vps%d" % (tc % 2))]
                         if cc == 0 else [])
                    t = ctx.mm(ps[:, 0:512],
                               xhat[:, cc * T + tc * 128: cc * T + (tc + 1) * 128],
                               vw_sb[:, cc * 512:(cc + 1) * 512],
                               start=(cc == 0), stop=(cc == DC - 1), waits=w)
                o_ap = V_sb[:, tc * 1040 + dvs * 520: tc * 1040 + dvs * 520 + 520]
                o_ap = o_ap.rearrange("p (g k) -> p g k", k=65)[:, :, 0:64]
                i_ap = ps[:, 0:512].rearrange("p (g k) -> p g k", k=64)
                ev = ctx.dve(lambda e, o=o_ap, i=i_ap: e.tensor_copy(o, i),
                             waits=[t, ones_tok, ctx.war.get("seqg")])
                ctx.war["vps%d" % (tc % 2)] = ev
            ctx.war["vw"] = s.now("Tn")
        qk_tok = s.now("Cn")
        v_tok = s.now("Vn")

        def oproj(jg, ofm_toks, base=base):
            last = None
            for m in range(8):
                slot = ow_sb[m % 2]
                wt = ctx.load(slot[:, :], ow_d[m],
                              waits=[ctx.war.get("oww%d" % (m % 2)),
                                     ctx.precast.get("OWB"), ctx.precast.get("OWFB")])
                ps = ctx.st[m % 2]
                t = None
                for cc in range(DC):
                    w = (list(ofm_toks) + [wt, ctx.war.get("op%d" % (m % 2)),
                          ctx.war.get("opx%d" % (m % 2))] if cc == 0 else [])
                    t = ctx.mm(ps[:, 0:512], slot[:, cc * 128:(cc + 1) * 128],
                               ofm4[:, cc * 512:(cc + 1) * 512],
                               start=(cc == 0), stop=(cc == DC - 1), waits=w)
                ctx.war["oww%d" % (m % 2)] = t
                xr = xres[m % 2]
                xt = ctx.load(xr[:, :], src[m * 128:(m + 1) * 128,
                                            base + jg * 512: base + jg * 512 + 512],
                              waits=[ctx.war.get("xres%d" % (m % 2))])
                og = ostg[m % 2]
                ev = ctx.dve(lambda e, o=og[:, :], i=ps[:, 0:512], x=xr[:, :]:
                             e.tensor_tensor(o, i, x, ALU.add),
                             waits=[t, xt, ctx.war.get("ostg%d" % (m % 2))])
                st = ctx.store(dst[m * 128:(m + 1) * 128,
                                   base + jg * 512: base + jg * 512 + 512],
                               og[:, :], waits=[ev])
                ctx.war["ostg%d" % (m % 2)] = st
                ctx.war["op%d" % (m % 2)] = ev
                ctx.war["xres%d" % (m % 2)] = ev
                last = ev
            return last

        emit_attn_core(ctx, 16, 64, Q_sb, K_sb, V_sb, float(D) ** -0.5,
                       PT_sb, dmask, otmn, recip, ofm4, qk_tok, v_tok, oproj)
        ctx.war["seqg"] = s.now("Tn")
        ctx.war["lnxT"] = s.now("Tn")
    s.barrier()
    ctx.war = {}
    sa.reset(mark)



def emit_outer_ln(ctx, src_name, dst_name):
    s, sa = ctx.s, ctx.sa
    src = ctx.dram[src_name]
    dst = ctx.dram[dst_name]
    mark = sa.mark()
    xbuf = sa.alloc("p4_x", [128, DC * 512], F32)
    sqbuf = sa.alloc("p4_sq", [128, 512], BF16)
    rowbuf = sa.alloc("p4_row", [1, 5 * 512], F32)
    tmpbuf = sa.alloc("p4_tmp", [128, 512], F32)
    ybuf = sa.alloc("p4_y", [128, DC * 512], F32)
    for tile in range(NTOK // 512):
        col0 = tile * 512

        def dst_fn(cc):
            return ybuf[:, cc * 512:(cc + 1) * 512]
        toks, _ = emit_ln(ctx, src, col0, 512, xbuf, sqbuf, rowbuf, tmpbuf,
                          dst_fn, dst_war=lambda cc: ctx.war.get(f"p4y{cc}"))
        for cc in range(DC):
            st = ctx.store(dst[cc * 128:(cc + 1) * 128, col0:col0 + 512],
                           ybuf[:, cc * 512:(cc + 1) * 512], waits=[toks[cc]])
            ctx.war[f"p4y{cc}"] = st
    s.barrier()
    ctx.war = {}
    sa.reset(mark)



def emit_mla(ctx, src_name, dst_name):
    s, c, sa = ctx.s, ctx.c, ctx.sa
    src = ctx.dram[src_name]
    dst = ctx.dram[dst_name]
    mark = sa.mark()
    K_sb = sa.alloc("p5_K", [128, DC * T], BF16)
    Q_sb = sa.alloc("p5_Q", [128, DC * T], BF16)
    V_sb = sa.alloc("p5_V", [128, 16 * 1040], BF16)
    qmix_sb = sa.alloc("p5_qmix", [128, 2 * 1024], BF16)
    qmixp_sb = sa.alloc("p5_qmixp", [128, 2 * 1024], BF16)
    kmixn_sb = sa.alloc("p5_kmixn", [128, 2 * 1024], BF16)
    kr_sb = [sa.alloc(f"p5_kr{i}", [128, DC * 128], BF16) for i in range(2)]
    krp_sb = [sa.alloc(f"p5_krp{i}", [128, DC * 128], BF16) for i in range(2)]
    vut_sb = sa.alloc("p5_vut", [128, 2 * 512], BF16)
    cmask = sa.alloc("p5_cm", [128, T], BF16)
    smask = sa.alloc("p5_sm", [128, T], BF16)
    ln1v_sb = sa.alloc("p5_ln1", [128, 8], F32)
    seq_mark = sa.mark()

    cm = np.ones((128, T), np.float32)
    sm = np.zeros((128, T), np.float32)
    cm[0:32] = ctx.rope_cos[0:32]
    cm[64:96] = ctx.rope_cos[0:32]
    sm[0:32] = ctx.rope_sin[0:32]
    sm[64:96] = ctx.rope_sin[0:32]
    cm_d = ctx.nc.inline_tensor(cm.astype(BF), name="p5cm").ap()
    sm_d = ctx.nc.inline_tensor(sm.astype(BF), name="p5sm").ap()
    ctx.load(cmask[:, :], cm_d[:, :])
    ctx.load(smask[:, :], sm_d[:, :])
    ctx.load(ln1v_sb[:, :], ctx.dram["ln1v"][:, :])
    for (sb, nm, m) in ((qmix_sb, "qmix", 1024), (qmixp_sb, "qmixp", 1024),
                        (kmixn_sb, "kmixn", 1024)):
        ctx.cast_load(sb[:, :].rearrange("p (cc m) -> p cc m", m=m),
                      ctx.dram[nm].rearrange("(cc p) m -> p cc m", p=128))
    kvd_d = ctx.dram["kvdt"].rearrange("(cc p) m -> p cc m", p=128)
    qd_d = ctx.dram["qdt"].rearrange("(cc p) m -> p cc m", p=128)
    kr_d = ctx.dram["KRB"]
    krp_d = ctx.dram["KRPB"]
    vut_d = ctx.dram["vut"].rearrange("(cc p) m -> p cc m", p=128)
    ow_d = ctx.dram["OWFB"]
    s.barrier()

    yh = sa.alloc("p5_yh", [128, DC * T], BF16)
    kvlat = sa.alloc("p5_kvl", [128, 2 * T], BF16)
    qlat = sa.alloc("p5_ql", [128, 2 * T], BF16)
    ln_mark = sa.mark()
    xbuf = sa.alloc("p5_x", [128, DC * 512], F32)
    sqbuf = sa.alloc("p5_sq", [128, 512], BF16)
    rowbuf = sa.alloc("p5_row", [1, 5 * 512], F32)
    tmpbuf = sa.alloc("p5_tmp", [128, 512], F32)
    sa.reset(ln_mark)
    rt1 = sa.alloc("p5_rt1", [128, 512], F32)
    rt2 = sa.alloc("p5_rt2", [128, 512], F32)
    rt3 = sa.alloc("p5_rt3", [128, 512], F32)
    rt4 = sa.alloc("p5_rt4", [128, 512], F32)
    sa.reset(ln_mark)
    PT_sb = [sa.alloc("p5_PT%d" % i, [128, 2048], BF16) for i in range(2)]
    ofm4 = sa.alloc("p5_ofm4", [128, DC * 512], BF16)
    dmask = [sa.alloc("p5_dm%d" % i, [128, 128], BF16) for i in range(2)]
    otmn = sa.alloc("p5_otmn", [128, 1024], BF16)
    recip = sa.alloc("p5_rc", [128, 16], F32)
    xres = [sa.alloc(f"p5_xres{i}", [128, 512], F32) for i in range(2)]
    ostg = [sa.alloc(f"p5_ostg{i}", [128, 512], F32) for i in range(2)]

    for sq in range(NSEQ):
        base = sq * T
        for tile in range(4):
            def dst_fn(cc, tile=tile):
                return yh[:, cc * T + tile * 512: cc * T + tile * 512 + 512]
            emit_ln(ctx, src, base + tile * 512, 512, xbuf, sqbuf, rowbuf,
                    tmpbuf, dst_fn,
                    gain_col=lambda cc: ln1v_sb[:, cc:cc + 1],
                    dst_war=lambda cc: ctx.war.get("seqg"))
        yh_tok = s.now("Vn")
        for wi, (wd, dst_sb) in enumerate(((kvd_d, kvlat), (qd_d, qlat))):
            for m2 in range(2):
                slot = kr_sb[(wi * 2 + m2) % 2]
                wt = ctx.cast_load(
                    slot[:, :].rearrange("p (cc m) -> p cc m", m=128),
                    wd[:, :, m2 * 128:(m2 + 1) * 128],
                    waits=[ctx.war.get("krw%d" % ((wi * 2 + m2) % 2))])
                for tile in range(4):
                    pi = tile % 2
                    ps = ctx.st[pi]
                    t = None
                    for cc in range(DC):
                        w = ([yh_tok, wt, ctx.war.get("lat%d" % pi)]
                             if cc == 0 else [])
                        t = ctx.mm(ps[:, 0:512],
                                   slot[:, cc * 128:(cc + 1) * 128],
                                   yh[:, cc * T + tile * 512: cc * T + tile * 512 + 512],
                                   start=(cc == 0), stop=(cc == DC - 1), waits=w)
                    o_ap = dst_sb[:, m2 * T + tile * 512: m2 * T + tile * 512 + 512]
                    ev = ctx.act(lambda e, o=o_ap, i=ps[:, 0:512]:
                                 e.activation(o, i, AF.Copy),
                                 waits=[t, ctx.war.get("seqg")])
                    ctx.war["lat%d" % pi] = ev
                ctx.war["krw%d" % ((wi * 2 + m2) % 2)] = s.now("Tn")
        lat_tok = s.now("Cn")
        asm_guard = s.now("Vn")
        for cc in range(DC):
            krt = ctx.load(kr_sb[cc % 2][:, :], kr_d[cc],
                           waits=[ctx.war.get("krw%d" % (cc % 2)),
                                  ctx.precast.get("KRB")])
            krpt = ctx.load(krp_sb[cc % 2][:, :], krp_d[cc],
                            waits=[ctx.war.get("krpw%d" % (cc % 2)),
                                   ctx.war.get("oww%d" % (cc % 2)),
                                   ctx.precast.get("KRPB")])
            for tile in range(4):
                ps1 = ctx.st[0]
                ps2 = ctx.st[1]
                qs1 = ctx.st[2]
                qs2 = ctx.st[3]
                t1 = None
                for k8 in range(DC):
                    w = ([yh_tok, lat_tok, krt, ctx.war.get("kps")]
                         if k8 == 0 else [])
                    t1 = ctx.mm(ps1[:, 0:512],
                                kr_sb[cc % 2][:, k8 * 128:(k8 + 1) * 128],
                                yh[:, k8 * T + tile * 512: k8 * T + tile * 512 + 512],
                                start=(k8 == 0), stop=False, waits=w)
                for lc in range(2):
                    t1 = ctx.mm(ps1[:, 0:512],
                                kmixn_sb[:, lc * 1024 + cc * 128: lc * 1024 + (cc + 1) * 128],
                                kvlat[:, lc * T + tile * 512: lc * T + tile * 512 + 512],
                                start=False, stop=(lc == 1), waits=[])
                t2 = None
                for k8 in range(DC):
                    w = ([krpt, ctx.war.get("kps2")] if k8 == 0 else [])
                    t2 = ctx.mm(ps2[:, 0:512],
                                krp_sb[cc % 2][:, k8 * 128:(k8 + 1) * 128],
                                yh[:, k8 * T + tile * 512: k8 * T + tile * 512 + 512],
                                start=(k8 == 0), stop=(k8 == DC - 1), waits=w)
                m1 = ctx.dve(lambda e, o=rt1[:, :], i=ps1[:, 0:512], t=tile:
                             e.tensor_tensor(o, i, cmask[:, t * 512:(t + 1) * 512],
                                             ALU.mult),
                             waits=[t1, asm_guard, ctx.war.get("rt1")])
                m2_ = ctx.dve(lambda e, o=rt2[:, :], i=ps2[:, 0:512], t=tile:
                              e.tensor_tensor(o, i, smask[:, t * 512:(t + 1) * 512],
                                              ALU.mult),
                              waits=[t2, asm_guard, ctx.war.get("rt2")])
                kd = K_sb[:, cc * T + tile * 512: cc * T + tile * 512 + 512]
                m3 = ctx.dve(lambda e, o=kd: e.tensor_tensor(o, rt1[:, :], rt2[:, :],
                                                             ALU.add),
                             waits=[m1, m2_, ctx.war.get("seqg")])
                ctx.war["kps"] = m1
                ctx.war["kps2"] = m2_
                ctx.war["rt1"] = m3
                ctx.war["rt2"] = m3
                t1 = None
                t2 = None
                for lc in range(2):
                    w = ([lat_tok, ctx.war.get("qps")] if lc == 0 else [])
                    t1 = ctx.mm(qs1[:, 0:512],
                                qmix_sb[:, lc * 1024 + cc * 128: lc * 1024 + (cc + 1) * 128],
                                qlat[:, lc * T + tile * 512: lc * T + tile * 512 + 512],
                                start=(lc == 0), stop=(lc == 1), waits=w)
                    w2 = ([ctx.war.get("qps2")] if lc == 0 else [])
                    t2 = ctx.mm(qs2[:, 0:512],
                                qmixp_sb[:, lc * 1024 + cc * 128: lc * 1024 + (cc + 1) * 128],
                                qlat[:, lc * T + tile * 512: lc * T + tile * 512 + 512],
                                start=(lc == 0), stop=(lc == 1), waits=w2)
                q1 = ctx.dve(lambda e, o=rt3[:, :], i=qs1[:, 0:512], t=tile:
                             e.tensor_tensor(o, i, cmask[:, t * 512:(t + 1) * 512],
                                             ALU.mult),
                             waits=[t1, asm_guard, ctx.war.get("rt3")])
                q2 = ctx.dve(lambda e, o=rt4[:, :], i=qs2[:, 0:512], t=tile:
                              e.tensor_tensor(o, i, smask[:, t * 512:(t + 1) * 512],
                                              ALU.mult),
                              waits=[t2, asm_guard, ctx.war.get("rt4")])
                qd = Q_sb[:, cc * T + tile * 512: cc * T + tile * 512 + 512]
                q3 = ctx.dve(lambda e, o=qd: e.tensor_tensor(o, rt3[:, :], rt4[:, :],
                                                             ALU.add),
                             waits=[q1, q2, ctx.war.get("seqg")])
                ctx.war["qps"] = q1
                ctx.war["qps2"] = q2
                ctx.war["rt3"] = q3
                ctx.war["rt4"] = q3
            ctx.war["krw%d" % (cc % 2)] = s.now("Tn")
            ctx.war["krpw%d" % (cc % 2)] = s.now("Tn")
        ones_tok = ctx.dve(lambda e: e.memset(V_sb[:, :], 1.0),
                           waits=[ctx.war.get("vmem")])
        for dvs in range(2):
            vt = ctx.cast_load(vut_sb[:, :].rearrange("p (k m) -> p k m", m=512),
                               vut_d[:, :, dvs * 512:(dvs + 1) * 512],
                               waits=[ctx.war.get("vw")])
            for tc in range(16):
                ps = ctx.st[tc % 2]
                t = None
                for lc in range(2):
                    w = ([lat_tok, vt, ctx.war.get("kps"), ctx.war.get("kps2"),
                          ctx.war.get("qps"), ctx.war.get("qps2"),
                          ctx.war.get("vps%d" % (tc % 2))] if lc == 0 else [])
                    t = ctx.mm(ps[:, 0:512],
                               kvlat[:, lc * T + tc * 128: lc * T + (tc + 1) * 128],
                               vut_sb[:, lc * 512:(lc + 1) * 512],
                               start=(lc == 0), stop=(lc == 1), waits=w)
                o_ap = V_sb[:, tc * 1040 + dvs * 520: tc * 1040 + dvs * 520 + 520]
                o_ap = o_ap.rearrange("p (g k) -> p g k", k=65)[:, :, 0:64]
                i_ap = ps[:, 0:512].rearrange("p (g k) -> p g k", k=64)
                ev = ctx.dve(lambda e, o=o_ap, i=i_ap: e.tensor_copy(o, i),
                             waits=[t, ones_tok, ctx.war.get("seqg")])
                ctx.war["vps%d" % (tc % 2)] = ev
            ctx.war["vw"] = s.now("Tn")
        qk_tok = s.now("Vn")
        v_tok = s.now("Vn")
        ctx.war["oww0"] = s.now("Tn")
        ctx.war["oww1"] = s.now("Tn")
        ctx.war["xres0"] = qk_tok
        ctx.war["xres1"] = qk_tok

        def oproj(jg, ofm_toks, base=base):
            last = None
            for m in range(8):
                slot = krp_sb[m % 2]
                wt = ctx.load(slot[:, :], ow_d[m],
                              waits=[ctx.war.get("oww%d" % (m % 2)),
                                     ctx.precast.get("OWB"), ctx.precast.get("OWFB")])
                ps = ctx.st[m % 2]
                t = None
                for cc in range(DC):
                    w = (list(ofm_toks) + [wt, ctx.war.get("op%d" % (m % 2)),
                          ctx.war.get("opx%d" % (m % 2))] if cc == 0 else [])
                    t = ctx.mm(ps[:, 0:512], slot[:, cc * 128:(cc + 1) * 128],
                               ofm4[:, cc * 512:(cc + 1) * 512],
                               start=(cc == 0), stop=(cc == DC - 1), waits=w)
                ctx.war["oww%d" % (m % 2)] = t
                xr = xres[m % 2]
                xt = ctx.load(xr[:, :], src[m * 128:(m + 1) * 128,
                                            base + jg * 512: base + jg * 512 + 512],
                              waits=[ctx.war.get("xres%d" % (m % 2))])
                og = ostg[m % 2]
                ev = ctx.dve(lambda e, o=og[:, :], i=ps[:, 0:512], x=xr[:, :]:
                             e.tensor_tensor(o, i, x, ALU.add),
                             waits=[t, xt, ctx.war.get("ostg%d" % (m % 2))])
                st = ctx.store(dst[m * 128:(m + 1) * 128,
                                   base + jg * 512: base + jg * 512 + 512],
                               og[:, :], waits=[ev])
                ctx.war["ostg%d" % (m % 2)] = st
                ctx.war["op%d" % (m % 2)] = ev
                ctx.war["xres%d" % (m % 2)] = ev
                last = ev
            return last

        emit_attn_core(ctx, 16, 64, Q_sb, K_sb, V_sb, 64.0 ** -0.5,
                       PT_sb, dmask, otmn, recip, ofm4, qk_tok, v_tok, oproj)
        ctx.war["seqg"] = s.now("Tn")
        ctx.war["lnxT"] = s.now("Tn")
        ctx.war["vmem"] = s.now("Tn")
    s.barrier()
    ctx.war = {}
    sa.reset(mark)



def emit_precast(ctx):
    ctx.precast = {}
    pairs = [
        ("gall", "GB", 8), ("uall", "UB", 8), ("dall", "DB", 8),
        ("qw1", "QWB", 1), ("kw1", "KWB", 1), ("vw1", "VWB", 1),
        ("ow1", "OWB", 1), ("kmixr", "KRB", 1), ("kmixrp", "KRPB", 1),
        ("owft", "OWFB", 1),
    ]
    for src, dstn, nsplit in pairs:
        sd = ctx.dram[src]
        dd = ctx.dram[dstn]
        tok = None
        if nsplit == 1:
            tok = ctx.cast_load(dd.rearrange("a p c -> (a p) c"),
                                sd.rearrange("a p b c -> (a p) (b c)"))
        else:
            for e in range(nsplit):
                tok = ctx.cast_load(dd[e].rearrange("m p c -> (m p) c"),
                                    sd[e].rearrange("m p b c -> (m p) (b c)"))
        ctx.precast[dstn] = tok



def emit_moe(ctx, x_name, y1_name, dst_name):
    s, c, sa = ctx.s, ctx.c, ctx.sa
    xsrc = ctx.dram[x_name]
    ysrc = ctx.dram[y1_name]
    dst = ctx.dram[dst_name]
    NTILE = 1024
    mark = sa.mark()
    y1 = sa.alloc("p6_y1", [128, DC * NTILE], F32)
    yh2 = sa.alloc("p6_yh2", [128, DC * NTILE], BF16)
    h_sb = sa.alloc("p6_h", [128, 16 * NTILE], BF16)
    acc = [sa.alloc(f"p6_acc{i}", [128, DC * NTILE], F32) for i in range(2)]
    wbc = sa.alloc("p6_wbc", [128, NTILE], F32)
    gstage = [sa.alloc(f"p6_gs{i}", [128, DC * 128], BF16) for i in range(4)]
    dstage = [sa.alloc(f"p6_ds{i}", [128, 16 * 128], BF16) for i in range(2)]
    glbuf = sa.alloc("p6_gl", [128, 2 * 512], BF16)
    ubuf = sa.alloc("p6_ub", [128, 2 * 512], BF16)
    tmpo = sa.alloc("p6_tmpo", [128, 512], F32)
    rp_sb = sa.alloc("p6_rp", [128, DC * 7], F32)
    ln2v_sb = sa.alloc("p6_ln2", [128, 8], F32)
    rbias_sb = sa.alloc("p6_rb", [1, 7], F32)
    rsum_sb = sa.alloc("p6_rs", [1, 7], F32)
    rsbc = sa.alloc("p6_rsbc", [128, 7], F32)
    rbbc = sa.alloc("p6_rbbc", [128, 7], F32)
    arow = sa.alloc("p6_ar", [1, NTILE], F32)
    brow = sa.alloc("p6_br", [1, NTILE], F32)
    raw_sb = sa.alloc("p6_raw", [7, NTILE], F32)
    ltm = sa.alloc("p6_ltm", [128, 8 * 7], F32)
    ab_sb = sa.alloc("p6_ab", [128, 2 * 8], F32)
    prob = sa.alloc("p6_prob", [128, 8 * 7], F32)
    rsc = sa.alloc("p6_rsc", [128, 8 * 7], F32)
    m12 = sa.alloc("p6_m12", [128, 3 * 8], F32)
    wgt_tm = sa.alloc("p6_wgt", [128, 8 * 7], F32)
    identf = sa.alloc("p6_idf", [128, 128], F32)
    diag = sa.alloc("p6_diag", [128, 128], F32)
    onesq = sa.alloc("p6_onsq", [128, 128], F32)
    xbuf_alias = sa.mark()
    sqbuf = sa.alloc("p6_sq", [128, 512], BF16)
    rowbuf = sa.alloc("p6_row", [1, 5 * 512], F32)
    tmpbuf = sa.alloc("p6_tmp", [128, 512], F32)
    xres = [sa.alloc(f"p6_xres{i}", [128, 512], F32) for i in range(2)]
    ostg = [sa.alloc(f"p6_ostg{i}", [128, 512], F32) for i in range(2)]

    idf_d = ctx.nc.inline_tensor(np.eye(128, dtype=np.float32), name="p6idf").ap()
    onsq_d = ctx.nc.inline_tensor(np.ones((128, 128), np.float32), name="p6ones").ap()
    ctx.load(identf[:, :], idf_d[:, :])
    ctx.load(onesq[:, :], onsq_d[:, :])
    ctx.load(ln2v_sb[:, :], ctx.dram["ln2v"][:, :])
    ctx.load(rbias_sb[:, :], ctx.dram["rbias"][:, :])
    rw_tok = ctx.load(rp_sb[:, :].rearrange("p (cc m) -> p cc m", m=7),
                      ctx.dram["rwt"].rearrange("(cc p) m -> p cc m", p=128))
    s.barrier()
    t = None
    for cc in range(DC):
        t = ctx.dve(lambda e, o=rp_sb[:, cc * 7:(cc + 1) * 7], g=ln2v_sb[:, cc:cc + 1]:
                    e.tensor_scalar_mul(o, o, g), waits=[])
    rs_t = None
    for cc in range(DC):
        rs_t = ctx.mm(ctx.ps_g[2][0:1, 0:7], c["ones_f"][:, :],
                      rp_sb[:, cc * 7:(cc + 1) * 7],
                      start=(cc == 0), stop=(cc == DC - 1), waits=[t])
    t = ctx.dve(lambda e: e.tensor_copy(rsum_sb[0:1, :], ctx.ps_g[2][0:1, 0:7]),
                waits=[rs_t])
    b1 = ctx.mm(ctx.ps_g[2][:, 0:7], c["ones_r"][:, :], rsum_sb[0:1, :],
                start=True, stop=True, waits=[t])
    t = ctx.dve(lambda e: e.tensor_copy(rsbc[:, :], ctx.ps_g[2][:, 0:7]), waits=[b1])
    b2 = ctx.mm(ctx.ps_g[2][:, 0:7], c["ones_r"][:, :], rbias_sb[0:1, :],
                start=True, stop=True, waits=[t])
    t = ctx.dve(lambda e: e.tensor_copy(rbbc[:, :], ctx.ps_g[2][:, 0:7]), waits=[b2])
    s.barrier()
    ctx.war = {}

    for tile in range(NTOK // NTILE):
        col0 = tile * NTILE
        for sub in range(2):
            def dst_fn(cc, sub=sub):
                return yh2[:, cc * NTILE + sub * 512: cc * NTILE + sub * 512 + 512]
            emit_ln(ctx, ysrc, col0 + sub * 512, 512,
                    _Slice1024(y1, sub), sqbuf, rowbuf, tmpbuf, dst_fn,
                    gain_col=lambda cc: ln2v_sb[:, cc:cc + 1])
            a_t = ctx.dve(lambda e, o=arow[0:1, sub * 512:(sub + 1) * 512]:
                          e.tensor_copy(o, rowbuf[0:1, 0:512]))
            b_t = ctx.dve(lambda e, o=brow[0:1, sub * 512:(sub + 1) * 512]:
                          e.tensor_copy(o, rowbuf[0:1, 512:1024]))
            ctx.war["rowA"] = a_t
            ctx.war["rowB"] = b_t
        yh_tok = s.now("Vn")
        for sub in range(2):
            t = None
            for cc in range(DC):
                w = ([yh_tok, ctx.war.get("rps")] if cc == 0 else [])
                t = ctx.mm(ctx.ps_g[2][0:7, 0:512], rp_sb[:, cc * 7:(cc + 1) * 7],
                           y1[:, cc * NTILE + sub * 512: cc * NTILE + sub * 512 + 512],
                           start=(cc == 0), stop=(cc == DC - 1), waits=w)
            ev = ctx.dve(lambda e, o=raw_sb[0:7, sub * 512:(sub + 1) * 512],
                         i=ctx.ps_g[2][0:7, 0:512]: e.tensor_copy(o, i), waits=[t])
            ctx.war["rps"] = ev
        for bb in range(8):
            t1 = ctx.s.emit(
                "tensor",
                lambda e, o=ctx.ps_g[2][0:128, 0:7],
                i=raw_sb[0:7, bb * 128:(bb + 1) * 128]:
                e.transpose(o, i, identf[0:7, 0:7]),
                waits=[ctx.war.get("rps"), ctx.war.get("rtr")], inc="Tn")
            ta = ctx.mm(ctx.ps_g[0][0:128, 0:1], arow[0:1, bb * 128:(bb + 1) * 128],
                        c["ones_r"][0:1, 0:1], start=True, stop=True,
                        waits=[ctx.war.get("rta")])
            tb = ctx.mm(ctx.ps_g[0][0:128, 1:2], brow[0:1, bb * 128:(bb + 1) * 128],
                        c["ones_r"][0:1, 0:1], start=True, stop=True)
            e1 = ctx.dve(lambda e, o=ltm[:, bb * 7:(bb + 1) * 7],
                         i=ctx.ps_g[2][0:128, 0:7]: e.tensor_copy(o, i), waits=[t1])
            e2 = ctx.dve(lambda e, o=ab_sb[:, bb * 2:bb * 2 + 2],
                         i=ctx.ps_g[0][0:128, 0:2]: e.tensor_copy(o, i), waits=[tb, ta])
            ctx.war["rtr"] = e1
            ctx.war["rta"] = e2
            l1 = ctx.dve(lambda e, o=rsc[:, bb * 7:(bb + 1) * 7],
                         b_=ab_sb[:, bb * 2 + 1:bb * 2 + 2]:
                         e.scalar_tensor_tensor(o, rsbc[:, :], b_, rbbc[:, :],
                                                ALU.mult, ALU.add), waits=[e2])
            l2 = ctx.dve(lambda e, o=ltm[:, bb * 7:(bb + 1) * 7],
                         a_=ab_sb[:, bb * 2:bb * 2 + 1],
                         t2=rsc[:, bb * 7:(bb + 1) * 7]:
                         e.scalar_tensor_tensor(o, o, a_, t2, ALU.mult, ALU.add),
                         waits=[l1, e1])
            sg = ctx.act(lambda e, o=prob[:, bb * 7:(bb + 1) * 7],
                         i=ltm[:, bb * 7:(bb + 1) * 7]:
                         e.activation(o, i, AF.Sigmoid), waits=[l2])
            bm = bb * 7
            m1c = m12[:, bb * 3:bb * 3 + 1]
            m2c = m12[:, bb * 3 + 1:bb * 3 + 2]
            sc_ = m12[:, bb * 3 + 2:bb * 3 + 3]
            r1 = ctx.dve(lambda e, o=m1c, i=prob[:, bm:bm + 7]:
                         e.reduce_max(o, i, AX.X), waits=[sg])
            r2 = ctx.dve(lambda e, o=rsc[:, bm:bm + 7], i=prob[:, bm:bm + 7], m=m1c:
                         e.tensor_scalar(o, i, m, None, ALU.is_ge), waits=[r1])
            r3 = ctx.dve(lambda e, o=rsc[:, bm:bm + 7], i=rsc[:, bm:bm + 7],
                         p=prob[:, bm:bm + 7]:
                         e.scalar_tensor_tensor(o, i, -1e30, p, ALU.mult, ALU.add),
                         waits=[r2])
            r4 = ctx.dve(lambda e, o=m2c, i=rsc[:, bm:bm + 7]:
                         e.reduce_max(o, i, AX.X), waits=[r3])
            r5 = ctx.dve(lambda e, o=rsc[:, bm:bm + 7], i=prob[:, bm:bm + 7], m=m2c:
                         e.tensor_scalar(o, i, m, None, ALU.is_ge), waits=[r4])
            r6 = ctx.dve(lambda e, o=sc_, i=m1c, i2=m2c:
                         e.tensor_tensor(o, i, i2, ALU.add), waits=[r4])
            r7 = ctx.dve(lambda e, o=sc_, i=sc_: e.reciprocal(o, i), waits=[r6])
            r8 = ctx.dve(lambda e, o=wgt_tm[:, bm:bm + 7], i=prob[:, bm:bm + 7],
                         i2=rsc[:, bm:bm + 7]:
                         e.tensor_tensor(o, i, i2, ALU.mult), waits=[r5])
            r9 = ctx.dve(lambda e, o=wgt_tm[:, bm:bm + 7], i=wgt_tm[:, bm:bm + 7],
                         m=sc_: e.tensor_scalar_mul(o, i, m), waits=[r7, r8])
        wgt_tok = s.now("Vn")
        ctx.war["dps0"] = wgt_tok
        ctx.war["dps1"] = wgt_tok
        ctx.war["wbcp"] = wgt_tok
        for ei in range(8):
            if ei > 0:
                wtoks = []
                for bb in range(8):
                    dg = ctx.dve(lambda e, o=diag[:, :],
                                 m=wgt_tm[:, bb * 7 + ei - 1: bb * 7 + ei]:
                                 e.tensor_scalar_mul(o, identf[:, :], m),
                                 waits=[wgt_tok, ctx.war.get("diag")])
                    bt = ctx.mm(ctx.ps_g[2][:, 0:128], onesq[:, :], diag[:, :],
                                start=True, stop=True,
                                waits=[dg, ctx.war.get("wbcp")])
                    ctx.war["diag"] = bt
                    ev = ctx.dve(lambda e, o=wbc[:, bb * 128:(bb + 1) * 128],
                                 i=ctx.ps_g[2][:, 0:128]: e.tensor_copy(o, i),
                                 waits=[bt, ctx.war.get("wbcr")])
                    ctx.war["wbcp"] = ev
                    wtoks.append(ev)
                wbc_tok = wtoks[-1]
            else:
                wbc_tok = None
            for mf in range(16):
                gslot = gstage[mf % 2]
                gt = ctx.load(gslot[:, :], ctx.dram["GB"][ei, mf],
                              waits=[ctx.war.get("gsl%d" % (mf % 2)),
                                     ctx.precast.get("GB")])
                uslot = gstage[2 + mf % 2]
                ut = ctx.load(uslot[:, :], ctx.dram["UB"][ei, mf],
                              waits=[ctx.war.get("usl%d" % (mf % 2)),
                                     ctx.precast.get("UB")])
                for nt in range(2):
                    ps_gm = ctx.st[nt]
                    t = None
                    for cc in range(DC):
                        w = ([yh_tok, gt, ctx.war.get("gps%d" % nt)]
                             if cc == 0 else [])
                        t = ctx.mm(ps_gm[:, 0:512],
                                   gslot[:, cc * 128:(cc + 1) * 128],
                                   yh2[:, cc * NTILE + nt * 512: cc * NTILE + nt * 512 + 512],
                                   start=(cc == 0), stop=(cc == DC - 1), waits=w)
                    sl = ctx.act(lambda e, o=glbuf[:, nt * 512:(nt + 1) * 512],
                                 i=ps_gm[:, 0:512]: e.activation(o, i, AF.Silu),
                                 waits=[t, ctx.war.get("glb%d" % nt)])
                    ctx.war["gps%d" % nt] = sl
                for nt in range(2):
                    ps_um = ctx.st[2 + nt]
                    t = None
                    for cc in range(DC):
                        w = ([ut, ctx.war.get("ups%d" % nt)] if cc == 0 else [])
                        t = ctx.mm(ps_um[:, 0:512],
                                   uslot[:, cc * 128:(cc + 1) * 128],
                                   yh2[:, cc * NTILE + nt * 512: cc * NTILE + nt * 512 + 512],
                                   start=(cc == 0), stop=(cc == DC - 1), waits=w)
                    ue = ctx.act(lambda e, o=ubuf[:, nt * 512:(nt + 1) * 512],
                                 i=ps_um[:, 0:512]: e.activation(o, i, AF.Copy),
                                 waits=[t, ctx.war.get("ubf%d" % nt)])
                    ctx.war["ups%d" % nt] = ue
                    hm = ctx.pool(lambda e, o=h_sb[:, mf * NTILE + nt * 512:
                                                   mf * NTILE + nt * 512 + 512],
                                  i=glbuf[:, nt * 512:(nt + 1) * 512],
                                  i2=ubuf[:, nt * 512:(nt + 1) * 512]:
                                  e.tensor_tensor(o, i, i2, ALU.mult),
                                  waits=[ue, ctx.war.get("gps%d" % nt),
                                         ctx.war.get("hsb%d" % nt)])
                    ctx.war["glb%d" % nt] = hm
                    ctx.war["ubf%d" % nt] = hm
                ctx.war["gsl%d" % (mf % 2)] = s.now("Tn")
                ctx.war["usl%d" % (mf % 2)] = s.now("Tn")
            h_tok = s.now("Pn")
            for m2 in range(8):
                dslot = dstage[m2 % 2]
                dt = ctx.load(dslot[:, :], ctx.dram["DB"][ei, m2],
                              waits=[ctx.war.get("dsl%d" % (m2 % 2)),
                                     ctx.precast.get("DB")])
                for nt in range(2):
                    ps_d = ctx.ps_g[nt]
                    t = None
                    for k in range(16):
                        w = ([h_tok, dt, ctx.war.get("dps%d" % nt)]
                             if k == 0 else [])
                        t = ctx.mm(ps_d[:, 0:512],
                                   dslot[:, k * 128:(k + 1) * 128],
                                   h_sb[:, k * NTILE + nt * 512: k * NTILE + nt * 512 + 512],
                                   start=(k == 0), stop=(k == 15), waits=w)
                    col = m2 * NTILE + nt * 512
                    if ei == 0:
                        ev = ctx.dve(lambda e, o=acc[1][:, col:col + 512],
                                     i=ps_d[:, 0:512]: e.tensor_copy(o, i),
                                     waits=[t, ctx.war.get("accw")])
                    else:
                        ev = ctx.dve(lambda e, o=tmpo[:, :], i=ps_d[:, 0:512],
                                     wb=wbc[:, nt * 512:(nt + 1) * 512]:
                                     e.tensor_tensor(o, i, wb, ALU.mult),
                                     waits=[t, wbc_tok, ctx.war.get("tmpo")])
                        ctx.war["wbcr"] = ev
                        ev = ctx.dve(lambda e, o=acc[(ei + 1) % 2][:, col:col + 512],
                                     i=acc[ei % 2][:, col:col + 512], t2=tmpo[:, :]:
                                     e.tensor_tensor(o, i, t2, ALU.add),
                                     waits=[ev, ctx.war.get("accw")])
                        ctx.war["tmpo"] = ev
                    ctx.war["dps%d" % nt] = ev
                ctx.war["dsl%d" % (m2 % 2)] = s.now("Tn")
            ctx.war["hsb0"] = s.now("Tn")
            ctx.war["hsb1"] = s.now("Tn")
        acc_fin = acc[0]
        acc_tok = s.now("Vn")
        for m2 in range(8):
            for nt in range(2):
                col = m2 * NTILE + nt * 512
                xr = xres[nt]
                xt = ctx.load(xr[:, :],
                              xsrc[m2 * 128:(m2 + 1) * 128,
                                   col0 + nt * 512: col0 + nt * 512 + 512],
                              waits=[ctx.war.get("xres%d" % nt)])
                o1 = ctx.dve(lambda e, o=ostg[nt][:, :], i=acc_fin[:, col:col + 512],
                             y=y1[:, col:col + 512]:
                             e.tensor_tensor(o, i, y, ALU.add),
                             waits=[acc_tok, ctx.war.get("ost%d" % nt)])
                o2 = ctx.dve(lambda e, o=ostg[nt][:, :], i=ostg[nt][:, :], x=xr[:, :]:
                             e.tensor_tensor(o, i, x, ALU.add), waits=[o1, xt])
                ctx.war["xres%d" % nt] = o2
                st_ = ctx.store(dst[m2 * 128:(m2 + 1) * 128,
                                    col0 + nt * 512: col0 + nt * 512 + 512],
                                ostg[nt][:, :], waits=[o2])
                ctx.war["ost%d" % nt] = st_
        ctx.war["accw"] = s.now("Vn")
        ctx.war["lnW"] = s.now("Tn")
        s.barrier()
        ctx.war = {}
    sa.reset(mark)


class _Slice1024:
    def __init__(self, y1, sub):
        self.y1 = y1
        self.sub = sub

    def __getitem__(self, key):
        p, f = key
        lo, hi = f.start, f.stop
        cc = lo // 512
        return self.y1[p, cc * 1024 + self.sub * 512:
                       cc * 1024 + self.sub * 512 + (hi - lo)]


# version 12
# speedup vs baseline: 1.3332x; 1.0834x over previous
import numpy as np
import ml_dtypes

import concourse.bass as bass
import concourse.mybir as mybir
from concourse.bass_utils import run_bass_kernel_spmd

F32 = mybir.dt.float32
BF16 = mybir.dt.bfloat16
AF = mybir.ActivationFunctionType
ALU = mybir.AluOpType
AX = mybir.AxisListType
BF = ml_dtypes.bfloat16

D = 1024
DC = 8
T = 2048
NSEQ = 2
NTOK = NSEQ * T
EPS = 1e-5
MAGIC = 1.5 * (2.0 ** 23)

ENGS = ("sync", "gpsimd", "tensor", "vector", "scalar")

N_LD = 8
N_GL = 4
N_ST = 4


class Sched:
    def __init__(self, nc):
        self.nc = nc
        self.q = {e: [] for e in ENGS}
        self.sem = {}
        self.cnt = {}
        self.last_wait = {e: {} for e in ENGS}

    def new_sem(self, name):
        self.sem[name] = self.nc.alloc_semaphore(name=name)
        self.cnt[name] = 0
        return name

    def emit(self, eng, fn, waits=(), inc=None, inc_by=1):
        lw = self.last_wait[eng]
        for w in waits:
            if w is None:
                continue
            sname, v = w
            if v > 0 and lw.get(sname, -1) < v:
                self.q[eng].append(("w", sname, v))
                lw[sname] = v
        self.q[eng].append(("o", fn, inc, inc_by))
        if inc is not None:
            self.cnt[inc] += inc_by
            return (inc, self.cnt[inc])
        return None

    def now(self, sem):
        return (sem, self.cnt[sem])

    def barrier(self):
        toks = [(n, c) for n, c in self.cnt.items() if c > 0]
        for e in ENGS:
            lw = self.last_wait[e]
            for sname, v in toks:
                if lw.get(sname, -1) < v:
                    self.q[e].append(("w", sname, v))
                    lw[sname] = v

    def build(self):
        with self.nc.Block() as block:
            for eng in ENGS:
                stream = self.q[eng]
                sem = self.sem

                def body(engine, stream=stream):
                    for item in stream:
                        if item[0] == "w":
                            engine.wait_ge(sem[item[1]], item[2])
                        else:
                            _, fn, inc, n = item
                            inst = fn(engine)
                            if inc is not None:
                                inst.then_inc(sem[inc], n)
                getattr(block, eng)(body)


class SA:
    def __init__(self, nc):
        self.nc = nc
        self.base = (nc.sbuf_base + 31) & ~31
        self.off = self.base
        self.top = nc.sbuf_top
        self.uid = 0

    def alloc(self, name, shape, dtype):
        per = int(np.prod(shape[1:])) * mybir.dt.size(dtype)
        per = (per + 31) & ~31
        assert self.off + per <= self.top, (
            f"SBUF overflow at {name}: {self.off}+{per} > {self.top}")
        self.uid += 1
        t = self.nc.alloc_sbuf_tensor_at(f"{name}_{self.uid}", list(shape), dtype,
                                         offset=self.off)
        self.off += per
        return t

    def mark(self):
        return self.off

    def reset(self, m):
        self.off = m


class Ctx:
    def __init__(self, inputs, scratch, outputs):
        nc = bass.Bass("TRN2", target_bir_lowering=False, debug=False)
        self.nc = nc
        self.s = Sched(nc)
        self.sa = SA(nc)
        s = self.s
        self.ld_sems = [s.new_sem(f"ld{i}") for i in range(N_LD)]
        self.gl_sems = [s.new_sem(f"gl{i}") for i in range(N_GL)]
        self.st_sems = [s.new_sem(f"st{i}") for i in range(N_ST)]
        self._ld_i = 0
        self._gl_i = 0
        self._st_i = 0
        for n in ("Tn", "Vn", "Cn", "Pn"):
            s.new_sem(n)
        self.dram = {}
        for name, shape, dt in inputs:
            self.dram[name] = nc.dram_tensor(name, list(shape), dt,
                                             kind="ExternalInput").ap()
        for name, shape, dt in scratch:
            self.dram[name] = nc.dram_tensor(name, list(shape), dt,
                                             kind="Internal").ap()
        for name, shape, dt in outputs:
            self.dram[name] = nc.dram_tensor(name, list(shape), dt,
                                             kind="ExternalOutput").ap()
        self.st = [nc.alloc_psum_tensor(f"ps_st{i}", [128, 512], F32)
                   for i in range(4)]
        self.ps_g = [nc.alloc_psum_tensor(f"ps_g{i}", [128, 512], F32)
                     for i in range(3)]
        self.ps_bf = nc.alloc_psum_tensor("ps_bf", [128, 1024], BF16)
        self.war = {}
        self.precast = {}
        self._make_consts()
        self.s.barrier()

    def _make_consts(self):
        nc, s, sa = self.nc, self.s, self.sa
        tri = np.triu(np.ones((128, 128), np.float32)).astype(BF)
        RD = 32
        inv_freq = 1.0 / (10000.0 ** (np.arange(0, RD, 2, dtype=np.float64) / RD))
        tpos = np.arange(T, dtype=np.float64)
        freqs = tpos[:, None] * inv_freq[None, :]
        emb = np.concatenate([freqs, freqs], axis=-1)
        cos = np.cos(emb).astype(np.float32)
        sin = np.sin(emb).astype(np.float32)
        sgn = np.ones(32, np.float32)
        sgn[:16] = -1.0
        consts = {
            "ident": np.eye(128, dtype=BF),
            "triu": tri,
            "ones_b": np.ones((128, 1), BF),
            "ones_f": np.ones((128, 1), np.float32),
            "ones_r": np.ones((1, 128), np.float32),
            "eps_c": np.full((128, 1), EPS, np.float32),
        }
        self.rope_cos = np.ascontiguousarray(np.tile(cos.T, (4, 1)))
        self.rope_sin = np.ascontiguousarray(np.tile((sin * sgn[None, :]).T, (4, 1)))
        self.c = {}
        for name, arr in consts.items():
            dt = BF16 if arr.dtype == BF else F32
            dr = nc.inline_tensor(np.ascontiguousarray(arr), name=f"c_{name}").ap()
            sb = sa.alloc(f"c_{name}", list(arr.shape), dt)
            self.load(sb[:, :], dr[:, :])
            self.c[name] = sb

    def _dma(self, eng, sems, idx_attr, dst_ap, src_ap, waits):
        i = getattr(self, idx_attr)
        setattr(self, idx_attr, i + 1)
        sem = sems[i % len(sems)]
        pre = self.s.now(sem)
        return self.s.emit(eng, lambda e: e.dma_start(out=dst_ap, in_=src_ap),
                           waits=list(waits) + [pre], inc=sem, inc_by=16)

    def load(self, dst_ap, src_ap, waits=()):
        return self._dma("sync", self.ld_sems, "_ld_i", dst_ap, src_ap, waits)

    def cast_load(self, dst_ap, src_ap, waits=()):
        return self._dma("gpsimd", self.gl_sems, "_gl_i", dst_ap, src_ap, waits)

    def store(self, dst_ap, src_ap, waits=()):
        return self._dma("gpsimd", self.st_sems, "_st_i", dst_ap, src_ap, waits)

    def dma_tr(self, dst_ap, src_ap, waits=()):
        i = self._ld_i
        self._ld_i = i + 1
        sem = self.ld_sems[i % N_LD]
        pre = self.s.now(sem)
        return self.s.emit("sync",
                           lambda e: e.dma_start_transpose(dst_ap, src_ap),
                           waits=list(waits) + [pre], inc=sem, inc_by=16)

    def mm(self, out_ap, lhsT, rhs, start, stop, waits=(), force_inc=False):
        return self.s.emit(
            "tensor",
            lambda e: e.matmul(out_ap, lhsT, rhs, start=start, stop=stop),
            waits=waits, inc=("Tn" if (stop or force_inc) else None))

    def tr(self, out_ap, in_ap, waits=()):
        ident = self.c["ident"]
        return self.s.emit(
            "tensor", lambda e: e.transpose(out_ap, in_ap, ident[:, :]),
            waits=waits, inc="Tn")

    def act(self, fn, waits=()):
        return self.s.emit("scalar", fn, waits=waits, inc="Cn")

    def dve(self, fn, waits=()):
        return self.s.emit("vector", fn, waits=waits, inc="Vn")

    def pool(self, fn, waits=()):
        return self.s.emit("gpsimd", fn, waits=waits, inc="Pn")



def emit_ln(ctx, src, col0, n, xbuf, sqbuf, rowbuf, tmpbuf, dst_fn,
            gain_col=None, dst_war=None):
    s, c = ctx.s, ctx.c
    ps_sum, ps_sq = ctx.ps_g[0], ctx.ps_g[1]
    x_toks = []
    sum_tok = None
    sq_act_tok = None
    sq_tok = ctx.war.get("lnsq")
    for cc in range(DC):
        xa = xbuf[:, cc * n:(cc + 1) * n]
        xt = ctx.load(xa, src[cc * 128:(cc + 1) * 128, col0:col0 + n],
                      waits=[ctx.war.get(f"lnx{cc}"), ctx.war.get("lnxT")])
        x_toks.append(xt)
        sum_tok = ctx.mm(ps_sum[0:1, 0:n], c["ones_f"][:, :], xa,
                         start=(cc == 0), stop=(cc == DC - 1),
                         waits=[xt, ctx.war.get("g0")] if cc == 0 else [xt])
        sq_act_tok = ctx.act(
            lambda e, o=sqbuf[:, 0:n], i=xa: e.activation(o, i, AF.Square),
            waits=[xt, sq_tok])
        sq_tok = ctx.mm(ps_sq[0:1, 0:n], c["ones_b"][:, :], sqbuf[:, 0:n],
                        start=(cc == 0), stop=(cc == DC - 1), force_inc=True,
                        waits=([sq_act_tok, ctx.war.get("g1")]
                               if cc == 0 else [sq_act_tok]))
    ctx.war["lnsq"] = sq_tok
    A = rowbuf[0:1, 0:n]
    B = rowbuf[0:1, n:2 * n]
    M = rowbuf[0:1, 2 * n:3 * n]
    T1 = rowbuf[0:1, 3 * n:4 * n]
    T2 = rowbuf[0:1, 4 * n:5 * n]
    t1 = ctx.dve(lambda e: e.tensor_scalar_mul(M, ps_sum[0:1, 0:n], 1.0 / D),
                 waits=[sum_tok])
    t2 = ctx.dve(lambda e: e.tensor_tensor(T1, M, M, ALU.mult), waits=[t1])
    t3 = ctx.dve(lambda e: e.scalar_tensor_tensor(T2, ps_sq[0:1, 0:n], 1.0 / D,
                                                  T1, ALU.mult, ALU.subtract),
                 waits=[t2, sq_tok])
    t4 = ctx.act(lambda e: e.activation(T1, T2, AF.Sqrt, bias=c["eps_c"][0:1, :]),
                 waits=[t3])
    t5 = ctx.dve(lambda e: e.reciprocal(A, T1),
                 waits=[t4, ctx.war.get("rowA")])
    t6 = ctx.dve(lambda e: e.scalar_tensor_tensor(B, M, -1.0, A, ALU.mult, ALU.mult),
                 waits=[t5, ctx.war.get("rowB")])
    bA = ctx.mm(ps_sum[:, 0:n], c["ones_r"][:, :], A, start=True, stop=True,
                waits=[t5, t1])
    bB = ctx.mm(ps_sq[:, 0:n], c["ones_r"][:, :], B, start=True, stop=True,
                waits=[t6, t3])
    ctx.war["rowA"] = bA
    ctx.war["rowB"] = bB
    xhat_toks = []
    tp = None
    for cc in range(DC):
        xa = xbuf[:, cc * n:(cc + 1) * n]
        tp = ctx.dve(lambda e, o=tmpbuf[:, 0:n], i=xa:
                     e.tensor_tensor(o, i, ps_sum[:, 0:n], ALU.mult),
                     waits=[bA, tp])
        if gain_col is None:
            tw = ctx.dve(lambda e, o=dst_fn(cc), i=tmpbuf[:, 0:n]:
                         e.tensor_tensor(o, i, ps_sq[:, 0:n], ALU.add),
                         waits=[bB, tp, dst_war(cc) if dst_war else None])
        else:
            tw = ctx.dve(lambda e, o=tmpbuf[:, 0:n], i=tmpbuf[:, 0:n]:
                         e.tensor_tensor(o, i, ps_sq[:, 0:n], ALU.add),
                         waits=[bB, tp])
            tw = ctx.dve(lambda e, o=dst_fn(cc), i=tmpbuf[:, 0:n], g=gain_col(cc):
                         e.tensor_scalar_mul(o, i, g),
                         waits=[tw, dst_war(cc) if dst_war else None])
        tp = tw
        ctx.war[f"lnx{cc}"] = tw
        xhat_toks.append(tw)
    ctx.war["g0"] = xhat_toks[-1]
    ctx.war["g1"] = xhat_toks[-1]
    return xhat_toks, x_toks



def emit_attn_core(ctx, nh, dh, Q_sb, K_sb, V_sb, scale, PT_sb, dmask_sb,
                   otmn_sb, recip_sb, ofm4_sb, qk_tok, v_tok, oproj_fn):
    s, c = ctx.s, ctx.c
    hpc = 128 // dh
    DCk = nh // hpc
    ng64 = dh // 64
    hw = ng64 * 65
    gw = nh * hw
    nh64 = nh * ng64
    exp_tok = [None] * 4
    pv_par = [None, None]
    mask_tok = None
    tr_toks = []
    hh = 0
    for j in range(16):
        obank_used = []
        prev = None

        def emit_pv(item):
            nonlocal pv_par
            h_, par_, mk_ = item
            pv_t = None
            for g in range(ng64):
                h64 = h_ * ng64 + g
                bk = h64 // 7
                ocol = (h64 % 7) * 65
                if bk not in obank_used:
                    obank_used.append(bk)
                for c4 in range(j + 1):
                    lt = (dmask_sb[par_][:, 0:128] if c4 == j
                          else PT_sb[par_][:, c4 * 128:(c4 + 1) * 128])
                    gb = 2 * (h_ % 2) + (c4 % 8) // 4
                    if c4 == 0:
                        w = [v_tok, mk_, ctx.war.get(f"ob{bk}")]
                    elif c4 == j:
                        w = [mk_]
                    else:
                        w = [exp_tok[gb]]
                    vs = V_sb[:, c4 * gw + h_ * hw + g * 65:
                              c4 * gw + h_ * hw + g * 65 + 65]
                    pv_t = ctx.mm(ctx.ps_g[bk][:, ocol:ocol + 65], lt, vs,
                                  start=(c4 == 0), stop=(c4 == j), waits=w)
            pv_par[par_] = pv_t

        for h in range(nh):
            rb = (h % hpc) * dh
            cc = h // hpc
            par = hh % 2
            hh += 1
            bset = 2 * (h % 2)
            PT = PT_sb[par]
            for lo in range(0, j + 1, 8):
                hi8 = min(lo + 8, j + 1)
                for bq in range(lo, hi8, 4):
                    gb = bset + (bq % 8) // 4
                    bank = ctx.st[gb]
                    mm_t = None
                    for c4 in range(bq, min(bq + 4, hi8)):
                        w = ([qk_tok, v_tok, exp_tok[gb]] if c4 == bq else [])
                        mm_t = ctx.mm(
                            bank[:, (c4 % 4) * 128:(c4 % 4 + 1) * 128],
                            K_sb[rb:rb + dh, cc * T + c4 * 128: cc * T + (c4 + 1) * 128],
                            Q_sb[rb:rb + dh, cc * T + j * 128: cc * T + (j + 1) * 128],
                            start=True, stop=True, waits=w)
                    hi = min(bq + 4, hi8)
                    o_ap = PT[:, bq * 128:hi * 128]
                    i_ap = bank[:, 0:(hi - bq) * 128]
                    exp_tok[gb] = ctx.act(
                        lambda e, o=o_ap, i=i_ap:
                        e.activation(o, i, AF.Exp, scale=scale),
                        waits=[mm_t, pv_par[par]])
            dm_in = PT[:, j * 128:(j + 1) * 128]
            gb_diag = bset + (j % 8) // 4
            mask_tok = ctx.dve(
                lambda e, o=dmask_sb[par][:, 0:128], i=dm_in:
                e.tensor_tensor(o, i, c["triu"][:, :], ALU.mult),
                waits=[exp_tok[gb_diag], pv_par[par]])
            if prev is not None:
                emit_pv(prev)
            prev = (h, par, mask_tok)
        emit_pv(prev)
        rc_tok = {}
        for bk in obank_used:
            ngrp = min(7, nh64 - bk * 7)
            src = ctx.ps_g[bk][:, 0:ngrp * 65].rearrange(
                "p (g k) -> p g k", k=65)[:, :, 64]
            dst = recip_sb[:, bk * 7:bk * 7 + ngrp]
            rc_tok[bk] = ctx.dve(lambda e, o=dst, i=src: e.reciprocal(o, i),
                                 waits=[pv_par[0], pv_par[1]])
        nt = None
        for h in range(nh):
            for g in range(ng64):
                h64 = h * ng64 + g
                bk = h64 // 7
                ocol = (h64 % 7) * 65
                o_ap = otmn_sb[:, h64 * 64:(h64 + 1) * 64]
                i_ap = ctx.ps_g[bk][:, ocol:ocol + 64]
                r_ap = recip_sb[:, h64:h64 + 1]
                nt = ctx.dve(
                    lambda e, o=o_ap, i=i_ap, r=r_ap: e.tensor_scalar_mul(o, i, r),
                    waits=[rc_tok[bk], ctx.war.get(f"otm{h64 // 2}")])
            if (h % hpc == hpc - 1) or dh == 128:
                cc2 = h // hpc
                src2 = otmn_sb[:, cc2 * 128:(cc2 + 1) * 128]
                dst2 = ofm4_sb[:, cc2 * 512 + (j % 4) * 128:
                               cc2 * 512 + (j % 4) * 128 + 128]
                tt = ctx.dma_tr(dst2, src2, waits=[nt, ctx.war.get("ofm4")])
                ctx.war[f"otm{cc2}"] = tt
                tr_toks.append(tt)
        for bk in obank_used:
            ctx.war[f"ob{bk}"] = s.now("Vn")
        if j % 4 == 3:
            ctx.war["opx0"] = exp_tok[0]
            ctx.war["opx1"] = exp_tok[1]
            op_tok = oproj_fn(j // 4, tr_toks + [s.now("Vn")])
            exp_tok[0] = op_tok
            exp_tok[1] = op_tok
            ctx.war["ofm4"] = op_tok
            tr_toks = []


def emit_proj_to(ctx, w_sb, x_sb, nchunk, x_off, n, mcols, out_fn, waits=()):
    for m in range(mcols // 128):
        bank = ctx.ps_g[2] if (m % 2 == 0) else ctx.ps_st[:, 0:512]
        ps = bank[:, 0:n] if (m % 2 == 0) else ctx.ps_st[:, 0:n]
        t = None
        for cc in range(nchunk):
            w = list(waits) + [ctx.war.get("proj%d" % (m % 2))] if cc == 0 else []
            t = ctx.mm(ps, w_sb[:, cc * mcols + m * 128: cc * mcols + (m + 1) * 128],
                       x_sb[:, cc * n + x_off: cc * n + x_off + n],
                       start=(cc == 0), stop=(cc == nchunk - 1), waits=w)
        ev = out_fn(m, ps, t)
        ctx.war["proj%d" % (m % 2)] = ev


def emit_l0_attn(ctx, src_name, dst_name):
    s, c, sa = ctx.s, ctx.c, ctx.sa
    src = ctx.dram[src_name]
    dst = ctx.dram[dst_name]
    mark = sa.mark()
    xbuf = sa.alloc("p1_x", [128, DC * 512], F32)
    sqbuf = sa.alloc("p1_sq", [128, 512], BF16)
    rowbuf = sa.alloc("p1_row", [1, 5 * 512], F32)
    tmpbuf = sa.alloc("p1_tmp", [128, 512], F32)
    xhat = sa.alloc("p1_xhat", [128, DC * T], BF16)
    wq_sb = sa.alloc("p1_wq", [128, DC * 128], BF16)
    wk_sb = sa.alloc("p1_wk", [128, DC * 128], BF16)
    wv_sb = sa.alloc("p1_wv", [128, DC * 128], BF16)
    wo_sb = sa.alloc("p1_wo", [128, 1024], BF16)
    Q_sb = sa.alloc("p1_Q", [128, T], BF16)
    K_sb = sa.alloc("p1_K", [128, T], BF16)
    V_sb = sa.alloc("p1_V", [128, 16 * 130], BF16)
    PT_sb = [sa.alloc("p1_PT%d" % i, [128, 2048], BF16) for i in range(2)]
    dmask = [sa.alloc("p1_dm%d" % i, [128, 128], BF16) for i in range(2)]
    otmn = sa.alloc("p1_otmn", [128, 128], BF16)
    recip = sa.alloc("p1_rc", [128, 16], F32)
    ofm4 = sa.alloc("p1_ofm4", [128, 512], BF16)
    xres = [sa.alloc(f"p1_xres{i}", [128, 512], F32) for i in range(2)]
    ostg = [sa.alloc(f"p1_ostg{i}", [128, 512], F32) for i in range(4)]

    wq_t = ctx.dram["wq0"].rearrange("(c p) m -> p c m", p=128)
    wk_t = ctx.dram["wk0"].rearrange("(c p) m -> p c m", p=128)
    wv_t = ctx.dram["wv0"].rearrange("(c p) m -> p c m", p=128)
    wtok = [ctx.cast_load(wq_sb[:, :].rearrange("p (c m) -> p c m", m=128), wq_t),
            ctx.cast_load(wk_sb[:, :].rearrange("p (c m) -> p c m", m=128), wk_t),
            ctx.cast_load(wv_sb[:, :].rearrange("p (c m) -> p c m", m=128), wv_t),
            ctx.cast_load(wo_sb[:, :], ctx.dram["wo0"][:, :])]
    ones_tok = ctx.dve(lambda e: e.memset(V_sb[:, :], 1.0))

    for sq in range(NSEQ):
        base = sq * T
        xh_toks = []
        for tile in range(4):
            def dst_fn(cc, tile=tile):
                return xhat[:, cc * T + tile * 512: cc * T + tile * 512 + 512]
            ts_, _ = emit_ln(ctx, src, base + tile * 512, 512,
                             xbuf, sqbuf, rowbuf, tmpbuf, dst_fn,
                             dst_war=lambda cc: ctx.war.get("seqg"))
            xh_toks += ts_
        xh_tok = s.now("Vn")
        qk_evs = []
        for (w_sb, dst_sb) in ((wq_sb, Q_sb), (wk_sb, K_sb)):
            for tile in range(4):
                ps = ctx.ps_g[2][:, 0:512] if tile % 2 == 0 else ctx.st[0][:, 0:512]
                t = None
                for cc in range(DC):
                    w = ([xh_tok, wtok[0], wtok[1], wtok[2],
                          ctx.war.get("qk%d" % (tile % 2))] if cc == 0 else [])
                    t = ctx.mm(ps, w_sb[:, cc * 128:(cc + 1) * 128],
                               xhat[:, cc * T + tile * 512: cc * T + tile * 512 + 512],
                               start=(cc == 0), stop=(cc == DC - 1), waits=w)
                o_ap = dst_sb[:, tile * 512: tile * 512 + 512]
                ev = ctx.act(lambda e, o=o_ap, i=ps: e.activation(o, i, AF.Copy),
                             waits=[t, ctx.war.get("seqg")])
                ctx.war["qk%d" % (tile % 2)] = ev
                qk_evs.append(ev)
        v_evs = []
        for tc in range(16):
            ps = ctx.st[1 + (tc % 3)][:, 0:128]
            t = None
            for cc in range(DC):
                w = ([xh_tok, ctx.war.get("v%d" % (tc % 3))] if cc == 0 else [])
                t = ctx.mm(ps, xhat[:, cc * T + tc * 128: cc * T + (tc + 1) * 128],
                           wv_sb[:, cc * 128:(cc + 1) * 128],
                           start=(cc == 0), stop=(cc == DC - 1), waits=w)
            o_ap = V_sb[:, tc * 130: tc * 130 + 130].rearrange(
                "p (g k) -> p g k", k=65)[:, :, 0:64]
            i_ap = ps.rearrange("p (g k) -> p g k", k=64)
            ev = ctx.dve(lambda e, o=o_ap, i=i_ap: e.tensor_copy(o, i),
                         waits=[t, ones_tok, ctx.war.get("seqg")])
            ctx.war["v%d" % (tc % 3)] = ev
            v_evs.append(ev)
        qk_tok = s.now("Cn")
        v_tok = s.now("Vn")

        def oproj(jg, ofm_toks, base=base):
            last = None
            for m in range(8):
                ps = ctx.st[m % 2][:, 0:512]
                t = ctx.mm(ps, wo_sb[:, m * 128:(m + 1) * 128], ofm4[:, 0:512],
                           start=True, stop=True,
                           waits=list(ofm_toks) + [wtok[3],
                                  ctx.war.get("op%d" % (m % 2)),
                                  ctx.war.get("opx%d" % (m % 2))])
                xr = xres[m % 2]
                xt = ctx.load(xr[:, :], src[m * 128:(m + 1) * 128,
                                            base + jg * 512: base + jg * 512 + 512],
                              waits=[ctx.war.get("xres%d" % (m % 2))])
                og = ostg[(jg * 8 + m) % 4]
                ev = ctx.dve(lambda e, o=og[:, :], i=ps, x=xr[:, :]:
                             e.tensor_tensor(o, i, x, ALU.add),
                             waits=[t, xt, ctx.war.get("ostg%d" % ((jg * 8 + m) % 4))])
                st = ctx.store(dst[m * 128:(m + 1) * 128,
                                   base + jg * 512: base + jg * 512 + 512],
                               og[:, :], waits=[ev])
                ctx.war["ostg%d" % ((jg * 8 + m) % 4)] = st
                ctx.war["op%d" % (m % 2)] = ev
                ctx.war["xres%d" % (m % 2)] = ev
                last = ev
            return last

        emit_attn_core(ctx, 1, 128, Q_sb, K_sb, V_sb, 128.0 ** -0.5,
                       PT_sb, dmask, otmn, recip, ofm4, qk_tok, v_tok, oproj)
        ctx.war["seqg"] = s.now("Tn")
        ctx.war["lnxT"] = s.now("Tn")
    s.barrier()
    ctx.war = {}
    sa.reset(mark)



def _prep_weights(inp):
    w = {}
    f32c = lambda a: np.ascontiguousarray(np.asarray(a), dtype=np.float32)
    w["wq0"] = f32c(inp["L0_attn_Wq"])
    w["wk0"] = f32c(inp["L0_attn_Wk"])
    w["wv0"] = f32c(inp["L0_attn_Wv"])
    w["wo0"] = f32c(inp["L0_attn_Wo"])
    w["w1t"] = f32c(np.asarray(inp["L0_ffn_W1"]).T)
    w["b1v"] = f32c(np.asarray(inp["L0_ffn_b1"]).reshape(32, 128).T)
    w["w2t"] = f32c(np.asarray(inp["L0_ffn_W2"]).T)
    w["b2v"] = f32c(np.asarray(inp["L0_ffn_b2"]).reshape(8, 128).T)
    HEADS = 16
    hd = D // HEADS
    perm = np.arange(D).reshape(hd, HEADS).T.reshape(-1)
    w["qw1"] = f32c(np.asarray(inp["L1_attn_qw"])[:, perm])
    w["kw1"] = f32c(np.asarray(inp["L1_attn_kw"])[:, perm])
    w["vw1"] = f32c(np.asarray(inp["L1_attn_vw"])[:, perm])
    w["ow1"] = f32c(np.asarray(inp["L1_attn_ow"])[perm, :])
    w["ln1v"] = f32c(np.asarray(inp["L1_ffn_ln1_w"]).reshape(8, 128).T)
    w["ln2v"] = f32c(np.asarray(inp["L1_ffn_ln2_w"]).reshape(8, 128).T)
    w["kvdt"] = f32c(np.asarray(inp["L1_ffn_kv_d"]).T)
    w["qdt"] = f32c(np.asarray(inp["L1_ffn_q_d"]).T)
    w["vut"] = f32c(np.asarray(inp["L1_ffn_v_u"]).T)
    rq = np.asarray(inp["L1_ffn_rope_q_w"], np.float32)
    rk = np.asarray(inp["L1_ffn_rope_k_w"], np.float32)
    qu_full = np.asarray(inp["L1_ffn_q_u"], np.float32)
    ku_full = np.asarray(inp["L1_ffn_k_u"], np.float32)
    pj = (np.arange(32) + 16) % 32
    qmix = np.zeros((1024, 256), np.float32)
    qmixp = np.zeros((1024, 256), np.float32)
    kmixn = np.zeros((1024, 256), np.float32)
    kmixr = np.zeros((1024, 1024), np.float32)
    kmixrp = np.zeros((1024, 1024), np.float32)
    for h in range(16):
        rows = slice(h * 64, h * 64 + 32)
        qmix[h * 64:h * 64 + 32] = rq[h * 32:h * 32 + 32]
        qmix[h * 64 + 32:h * 64 + 64] = qu_full[h * 64 + 32:h * 64 + 64]
        qmixp[h * 64:h * 64 + 32] = rq[h * 32 + pj]
        kmixn[h * 64 + 32:h * 64 + 64] = ku_full[h * 64 + 32:h * 64 + 64]
        kmixr[h * 64:h * 64 + 32] = rk[h * 32:h * 32 + 32]
        kmixrp[h * 64:h * 64 + 32] = rk[h * 32 + pj]
    w["qmix"] = f32c(qmix.T)
    w["qmixp"] = f32c(qmixp.T)
    w["kmixn"] = f32c(kmixn.T)
    w["kmixr"] = f32c(kmixr.T)
    w["kmixrp"] = f32c(kmixrp.T)
    w["owft"] = f32c(np.asarray(inp["L1_ffn_o_w"]).T)
    w["rwt"] = f32c(np.asarray(inp["L1_ffn_router_w"]).T)
    w["rbias"] = f32c(np.asarray(inp["L1_ffn_routing_bias"]).reshape(1, 7))
    ga = np.concatenate([np.asarray(inp["L1_ffn_shared_gate"]),
                         np.asarray(inp["L1_ffn_routed_gate"])], 0)
    ua = np.concatenate([np.asarray(inp["L1_ffn_shared_up"]),
                         np.asarray(inp["L1_ffn_routed_up"])], 0)
    da = np.concatenate([np.asarray(inp["L1_ffn_shared_down"]),
                         np.asarray(inp["L1_ffn_routed_down"])], 0)
    w["gall"] = f32c(ga.reshape(8, 16, 128, 8, 128).transpose(0, 1, 4, 3, 2))
    w["uall"] = f32c(ua.reshape(8, 16, 128, 8, 128).transpose(0, 1, 4, 3, 2))
    w["dall"] = f32c(da.reshape(8, 8, 128, 16, 128).transpose(0, 1, 4, 3, 2))

    def tile4(mat):
        return f32c(np.ascontiguousarray(
            np.asarray(mat, np.float32).reshape(8, 128, 8, 128).transpose(2, 1, 0, 3)))
    w["qw1"] = tile4(w["qw1"])
    w["kw1"] = tile4(w["kw1"])
    w["ow1"] = tile4(w["ow1"])
    w["kmixr"] = tile4(w["kmixr"])
    w["kmixrp"] = tile4(w["kmixrp"])
    w["owft"] = tile4(w["owft"])
    w["vw1"] = f32c(np.asarray(w["vw1"]).reshape(8, 128, 2, 512).transpose(2, 1, 0, 3))
    return w


WSHAPES = {
    "wq0": ([1024, 128], F32), "wk0": ([1024, 128], F32), "wv0": ([1024, 128], F32),
    "wo0": ([128, 1024], F32), "w1t": ([1024, 4096], F32), "b1v": ([128, 32], F32),
    "w2t": ([4096, 1024], F32), "b2v": ([128, 8], F32),
    "qw1": ([8, 128, 8, 128], F32), "kw1": ([8, 128, 8, 128], F32),
    "vw1": ([2, 128, 8, 512], F32), "ow1": ([8, 128, 8, 128], F32),
    "ln1v": ([128, 8], F32), "ln2v": ([128, 8], F32),
    "kvdt": ([1024, 256], F32), "qdt": ([1024, 256], F32),
    "vut": ([256, 1024], F32),
    "qmix": ([256, 1024], F32), "qmixp": ([256, 1024], F32),
    "kmixn": ([256, 1024], F32),
    "kmixr": ([8, 128, 8, 128], F32), "kmixrp": ([8, 128, 8, 128], F32),
    "owft": ([8, 128, 8, 128], F32), "rwt": ([1024, 7], F32), "rbias": ([1, 7], F32),
    "gall": ([8, 16, 128, 8, 128], F32), "uall": ([8, 16, 128, 8, 128], F32),
    "dall": ([8, 8, 128, 16, 128], F32),
}

ACT_BUFS = {
    "XT": None, "RX1": None, "RX2": None, "RX3": None, "Y": None, "Y1": None,
    "OUT": None,
}

PHASES = {
    "P0c": (lambda ctx: emit_precast(ctx),
            ["gall", "uall", "dall", "qw1", "kw1", "vw1", "ow1",
             "kmixr", "kmixrp", "owft"], [], []),
    "P1": (lambda ctx: emit_l0_attn(ctx, "XT", "RX1"),
           ["wq0", "wk0", "wv0", "wo0"], ["XT"], ["RX1"]),
    "P2": (lambda ctx: emit_l0_ffn(ctx, "RX1", "RX2"),
           ["w1t", "b1v", "w2t", "b2v"], ["RX1"], ["RX2"]),
    "P3": (lambda ctx: emit_l1_attn(ctx, "RX2", "RX3"),
           [], ["RX2"], ["RX3"]),
    "P4": (lambda ctx: emit_outer_ln(ctx, "RX3", "Y"),
           [], ["RX3"], ["Y"]),
    "P5": (lambda ctx: emit_mla(ctx, "Y", "Y1"),
           ["ln1v", "qmix", "qmixp", "kmixn", "kvdt", "qdt", "vut"],
           ["Y"], ["Y1"]),
    "P6": (lambda ctx: emit_moe(ctx, "RX3", "Y1", "OUT"),
           ["ln2v", "rwt", "rbias"], ["RX3", "Y1"], ["OUT"]),
}

SCRATCH_BUFS = {
    "WQ1B": ([1024, 4096], BF16),
    "WQ2B": ([8, 32, 128, 128], BF16),
    "GB": ([8, 16, 128, 1024], BF16), "UB": ([8, 16, 128, 1024], BF16),
    "DB": ([8, 8, 128, 2048], BF16),
    "QWB": ([8, 128, 1024], BF16), "KWB": ([8, 128, 1024], BF16),
    "VWB": ([2, 128, 4096], BF16), "OWB": ([8, 128, 1024], BF16),
    "KRB": ([8, 128, 1024], BF16), "KRPB": ([8, 128, 1024], BF16),
    "OWFB": ([8, 128, 1024], BF16),
}
SCRATCH_OF = {"P2": ["WQ1B", "WQ2B"],
              "P0c": ["GB", "UB", "DB", "QWB", "KWB", "VWB", "OWB",
                      "KRB", "KRPB", "OWFB"],
              "P3": ["QWB", "KWB", "VWB", "OWB"],
              "P5": ["KRB", "KRPB", "OWFB"],
              "P6": ["GB", "UB", "DB"]}


def build_launch(phases, ext_in_bufs, ext_out_bufs):
    wnames = sorted({w for p in phases for w in PHASES[p][1]})
    bufs = sorted({b for p in phases for b in PHASES[p][2] + PHASES[p][3]})
    inputs = [(n,) + tuple(WSHAPES[n]) for n in wnames]
    inputs = [(n, WSHAPES[n][0], WSHAPES[n][1]) for n in wnames]
    for b in bufs:
        if b in ext_in_bufs:
            inputs.append((b, [D, NTOK], F32))
    outputs = [(b, [D, NTOK], F32) for b in bufs if b in ext_out_bufs]
    scratch = [(b, [D, NTOK], F32) for b in bufs
               if b not in ext_in_bufs and b not in ext_out_bufs]
    seen = set()
    for p in phases:
        for sb in SCRATCH_OF.get(p, []):
            if sb not in seen:
                seen.add(sb)
                scratch.append((sb,) + SCRATCH_BUFS[sb])
    ctx = Ctx(inputs, scratch, outputs)
    for p in phases:
        PHASES[p][0](ctx)
        ctx.s.barrier()
        ctx.war = {}
    ctx.s.build()
    return ctx, wnames


def _run_launch(phases, ext_inputs_per_core, out_names):
    ctx, wnames = build_launch(
        phases,
        ext_in_bufs={n for n in ext_inputs_per_core[0]
                     if n in ACT_BUFS},
        ext_out_bufs=set(out_names))
    res = run_bass_kernel_spmd(ctx.nc, ext_inputs_per_core,
                               core_ids=list(range(len(ext_inputs_per_core))))
    return res.results


ALL_PHASES = ["P0c", "P1", "P2", "P3", "P4", "P5", "P6"]


def kernel(**inputs):
    w = _prep_weights(inputs)
    x = np.asarray(inputs["L0_attn_x"], dtype=np.float32)
    n_cores = 8
    in_maps = []
    for cc in range(n_cores):
        shard = x[cc * NSEQ:(cc + 1) * NSEQ]
        xt = np.ascontiguousarray(shard.reshape(NTOK, D).T)
        m = dict(w)
        m["XT"] = xt
        in_maps.append(m)
    results = _run_launch(ALL_PHASES, in_maps, ["OUT"])
    outs = []
    for cc in range(n_cores):
        o = results[cc]["OUT"]
        outs.append(np.ascontiguousarray(o.T).reshape(NSEQ, T, D))
    return np.concatenate(outs, 0)



def emit_quantize(ctx, w_name, rows, cols, store_fn):
    s, c, sa = ctx.s, ctx.c, ctx.sa
    src = ctx.dram[w_name]
    nch = rows // 128
    mark = sa.mark()
    wbuf = sa.alloc("qz_w", [128, cols], F32)
    abuf = sa.alloc("qz_a", [128, cols], F32)
    grow = sa.alloc("qz_g", [1, 8], F32)
    gvec = sa.alloc("qz_gv", [128, 1], F32)
    qbuf = sa.alloc("qz_q", [128, cols], BF16)
    ps = ctx.ps_g[0]
    mm_tok = None
    ld_toks = []
    for ch in range(nch):
        lt = ctx.load(wbuf[:, :], src[ch * 128:(ch + 1) * 128, :],
                      waits=[ctx.war.get("qzw")])
        ab = ctx.act(lambda e, o=abuf[:, :], i=wbuf[:, :]:
                     e.activation(o, i, AF.Abs), waits=[lt, mm_tok])
        ctx.war["qzw"] = ab
        for sub in range(cols // 512):
            mm_tok = ctx.mm(ps[0:1, 0:512], c["ones_f"][:, :],
                            abuf[:, sub * 512:(sub + 1) * 512],
                            start=(ch == 0 and sub == 0),
                            stop=(ch == nch - 1 and sub == cols // 512 - 1),
                            force_inc=True, waits=[ab])
    t1 = ctx.dve(lambda e: e.reduce_sum(grow[0:1, 0:1], ps[0:1, 0:512], AX.X),
                 waits=[mm_tok])
    t2 = ctx.dve(lambda e: e.tensor_scalar(grow[0:1, 1:2], grow[0:1, 0:1],
                                           1.0 / (rows * cols), EPS,
                                           ALU.mult, ALU.add), waits=[t1])
    t3 = ctx.dve(lambda e: e.reciprocal(grow[0:1, 2:3], grow[0:1, 1:2]),
                 waits=[t2])
    bc = ctx.mm(ctx.ps_g[1][0:128, 0:1], c["ones_r"][:, :], grow[0:1, 2:3],
                start=True, stop=True, waits=[t3, ctx.war.get("g1")])
    t4 = ctx.dve(lambda e: e.tensor_copy(gvec[:, :], ctx.ps_g[1][:, 0:1]),
                 waits=[bc])
    prev_q = None
    for ch in range(nch):
        lt = ctx.load(wbuf[:, :], src[ch * 128:(ch + 1) * 128, :],
                      waits=[ctx.war.get("qzw")])
        a1 = ctx.dve(lambda e, o=abuf[:, :], i=wbuf[:, :]:
                     e.tensor_scalar(o, i, gvec[:, :], MAGIC, ALU.mult, ALU.add),
                     waits=[lt, t4, prev_q])
        ctx.war["qzw"] = a1
        a2 = ctx.dve(lambda e, o=abuf[:, :], i=abuf[:, :]:
                     e.tensor_scalar(o, i, MAGIC, None, ALU.subtract),
                     waits=[a1])
        a3 = ctx.dve(lambda e, o=qbuf[:, :], i=abuf[:, :]:
                     e.tensor_scalar(o, i, 1.0, -1.0, ALU.min, ALU.max),
                     waits=[a2, prev_q])
        prev_q = store_fn(ch, qbuf, a3)
    s.barrier()
    ctx.war = {}
    sa.reset(mark)


def emit_l0_ffn(ctx, src_name, dst_name):
    s, c, sa = ctx.s, ctx.c, ctx.sa
    src = ctx.dram[src_name]
    dst = ctx.dram[dst_name]
    wq1d = ctx.dram["WQ1B"]
    wq2d = ctx.dram["WQ2B"]

    def store1(ch, qb, tok):
        return ctx.store(wq1d[ch * 128:(ch + 1) * 128, :], qb[:, :], waits=[tok])

    def store2(ch, qb, tok):
        d = wq2d[:, ch, :, :].rearrange("m p j -> p m j")
        sp = qb[:, :].rearrange("p (m j) -> p m j", j=128)
        return ctx.store(d, sp, waits=[tok])

    emit_quantize(ctx, "w1t", 1024, 4096, store1)
    emit_quantize(ctx, "w2t", 4096, 1024, store2)

    mark = sa.mark()
    xbuf = sa.alloc("p2_x", [128, DC * 512], F32)
    sqbuf = sa.alloc("p2_sq", [128, 512], BF16)
    rowbuf = sa.alloc("p2_row", [1, 5 * 512], F32)
    tmpbuf = sa.alloc("p2_tmp", [128, 512], F32)
    xhat = sa.alloc("p2_xhat", [128, DC * 512], BF16)
    wq1_sb = sa.alloc("p2_wq1", [128, DC * 4096], BF16)
    h_sb = sa.alloc("p2_h", [128, 32 * 512], BF16)
    wq2_sb = [sa.alloc(f"p2_wq2_{i}", [128, 32 * 128], BF16) for i in range(2)]
    b1_sb = sa.alloc("p2_b1", [128, 32], F32)
    b2_sb = sa.alloc("p2_b2", [128, 8], F32)
    ostg = [sa.alloc(f"p2_ostg{i}", [128, 512], F32) for i in range(4)]

    for ch in range(DC):
        ctx.load(wq1_sb[:, ch * 4096:(ch + 1) * 4096],
                 wq1d[ch * 128:(ch + 1) * 128, :])
    ctx.load(b1_sb[:, :], ctx.dram["b1v"][:, :])
    ctx.load(b2_sb[:, :], ctx.dram["b2v"][:, :])
    s.barrier()

    for tile in range(NTOK // 512):
        col0 = tile * 512

        def dst_fn(cc):
            return xhat[:, cc * 512:(cc + 1) * 512]
        emit_ln(ctx, src, col0, 512, xbuf, sqbuf, rowbuf, tmpbuf, dst_fn)
        xh_tok = s.now("Vn")
        for mf in range(32):
            ps = ctx.st[mf % 2]
            t = None
            for cc in range(DC):
                w = ([xh_tok, ctx.war.get("h%d" % (mf % 2))] if cc == 0 else [])
                t = ctx.mm(ps[:, 0:512],
                           wq1_sb[:, cc * 4096 + mf * 128: cc * 4096 + (mf + 1) * 128],
                           xhat[:, cc * 512:(cc + 1) * 512],
                           start=(cc == 0), stop=(cc == DC - 1), waits=w)
            o_ap = h_sb[:, mf * 512:(mf + 1) * 512]
            ev = ctx.act(lambda e, o=o_ap, i=ps[:, 0:512], b=b1_sb[:, mf:mf + 1]:
                         e.activation(o, i, AF.Relu, bias=b), waits=[t])
            ctx.war["h%d" % (mf % 2)] = ev
        h_tok = s.now("Cn")
        for m in range(8):
            slot = wq2_sb[m % 2]
            d = wq2d[m].rearrange("k p j -> p k j")
            wt = ctx.load(slot[:, :].rearrange("p (k j) -> p k j", j=128), d,
                          waits=[ctx.war.get("wq2%d" % (m % 2))])
            ps = ctx.st[2 + m % 2]
            t = None
            for k in range(32):
                w = ([h_tok, wt, ctx.war.get("o%d" % (m % 2))] if k == 0 else [])
                t = ctx.mm(ps[:, 0:512], slot[:, k * 128:(k + 1) * 128],
                           h_sb[:, k * 512:(k + 1) * 512],
                           start=(k == 0), stop=(k == 31), waits=w)
            ctx.war["wq2%d" % (m % 2)] = t
            og = ostg[m % 4]
            ev = ctx.dve(lambda e, o=og[:, :], i=ps[:, 0:512],
                         b=b2_sb[:, m:m + 1], x=xbuf[:, m * 512:(m + 1) * 512]:
                         e.scalar_tensor_tensor(o, i, b, x, ALU.add, ALU.add),
                         waits=[t, ctx.war.get("ostg%d" % (m % 4))])
            ctx.war["o%d" % (m % 2)] = ev
            ctx.war[f"lnx{m}"] = ev
            st = ctx.store(dst[m * 128:(m + 1) * 128, col0:col0 + 512],
                           og[:, :], waits=[ev])
            ctx.war["ostg%d" % (m % 4)] = st
        ctx.war["h0"] = s.now("Tn")
        ctx.war["h1"] = s.now("Tn")
    s.barrier()
    ctx.war = {}
    sa.reset(mark)



def emit_l1_attn(ctx, src_name, dst_name):
    s, c, sa = ctx.s, ctx.c, ctx.sa
    src = ctx.dram[src_name]
    dst = ctx.dram[dst_name]
    mark = sa.mark()
    xbuf = sa.alloc("p3_x", [128, DC * 512], F32)
    sqbuf = sa.alloc("p3_sq", [128, 512], BF16)
    rowbuf = sa.alloc("p3_row", [1, 5 * 512], F32)
    tmpbuf = sa.alloc("p3_tmp", [128, 512], F32)
    xhat = sa.alloc("p3_xhat", [128, DC * T], BF16)
    Q_sb = sa.alloc("p3_Q", [128, DC * T], BF16)
    K_sb = sa.alloc("p3_K", [128, DC * T], BF16)
    V_sb = sa.alloc("p3_V", [128, 16 * 1040], BF16)
    vw_sb = sa.alloc("p3_vw", [128, DC * 512], BF16)
    qkw_sb = [sa.alloc(f"p3_qkw{i}", [128, DC * 128], BF16) for i in range(2)]
    ow_sb = [sa.alloc(f"p3_ow{i}", [128, DC * 128], BF16) for i in range(2)]
    ofm4 = sa.alloc("p3_ofm4", [128, DC * 512], BF16)
    PT_sb = [sa.alloc("p3_PT%d" % i, [128, 2048], BF16) for i in range(2)]
    dmask = [sa.alloc("p3_dm%d" % i, [128, 128], BF16) for i in range(2)]
    otmn = sa.alloc("p3_otmn", [128, 1024], BF16)
    recip = sa.alloc("p3_rc", [128, 16], F32)
    xres = [sa.alloc(f"p3_xres{i}", [128, 512], F32) for i in range(2)]
    ostg = [sa.alloc(f"p3_ostg{i}", [128, 512], F32) for i in range(2)]

    qw_d = ctx.dram["QWB"]
    kw_d = ctx.dram["KWB"]
    vw_d = ctx.dram["VWB"]
    ow_d = ctx.dram["OWB"]
    ones_tok = ctx.dve(lambda e: e.memset(V_sb[:, :], 1.0))

    for sq in range(NSEQ):
        base = sq * T
        for tile in range(4):
            def dst_fn(cc, tile=tile):
                return xhat[:, cc * T + tile * 512: cc * T + tile * 512 + 512]
            emit_ln(ctx, src, base + tile * 512, 512,
                    xbuf, sqbuf, rowbuf, tmpbuf, dst_fn,
                    dst_war=lambda cc: ctx.war.get("seqg"))
        xh_tok = s.now("Vn")
        for wi, (wd, dst_sb) in enumerate(((qw_d, Q_sb), (kw_d, K_sb))):
            for mc in range(DC):
                slot = qkw_sb[mc % 2]
                wt = ctx.load(slot[:, :], wd[mc],
                              waits=[ctx.war.get("qkw%d" % (mc % 2)),
                                     ctx.precast.get("QWB"), ctx.precast.get("KWB")])
                for tile in range(4):
                    pi = (mc * 4 + tile) % 2
                    ps = ctx.st[pi]
                    t = None
                    for cc in range(DC):
                        w = ([xh_tok, wt, ctx.war.get("qkps%d" % pi)]
                             if cc == 0 else [])
                        t = ctx.mm(ps[:, 0:512], slot[:, cc * 128:(cc + 1) * 128],
                                   xhat[:, cc * T + tile * 512: cc * T + tile * 512 + 512],
                                   start=(cc == 0), stop=(cc == DC - 1), waits=w)
                    o_ap = dst_sb[:, mc * T + tile * 512: mc * T + tile * 512 + 512]
                    ev = ctx.act(lambda e, o=o_ap, i=ps[:, 0:512]:
                                 e.activation(o, i, AF.Copy),
                                 waits=[t, ctx.war.get("seqg")])
                    ctx.war["qkps%d" % pi] = ev
                ctx.war["qkw%d" % (mc % 2)] = s.now("Tn")
        for dvs in range(2):
            vt = ctx.load(vw_sb[:, :], vw_d[dvs], waits=[ctx.war.get("vw"), ctx.precast.get("VWB")])
            for tc in range(16):
                ps = ctx.st[2 + tc % 2]
                t = None
                for cc in range(DC):
                    w = ([xh_tok, vt, ctx.war.get("vps%d" % (tc % 2))]
                         if cc == 0 else [])
                    t = ctx.mm(ps[:, 0:512],
                               xhat[:, cc * T + tc * 128: cc * T + (tc + 1) * 128],
                               vw_sb[:, cc * 512:(cc + 1) * 512],
                               start=(cc == 0), stop=(cc == DC - 1), waits=w)
                o_ap = V_sb[:, tc * 1040 + dvs * 520: tc * 1040 + dvs * 520 + 520]
                o_ap = o_ap.rearrange("p (g k) -> p g k", k=65)[:, :, 0:64]
                i_ap = ps[:, 0:512].rearrange("p (g k) -> p g k", k=64)
                ev = ctx.dve(lambda e, o=o_ap, i=i_ap: e.tensor_copy(o, i),
                             waits=[t, ones_tok, ctx.war.get("seqg")])
                ctx.war["vps%d" % (tc % 2)] = ev
            ctx.war["vw"] = s.now("Tn")
        qk_tok = s.now("Cn")
        v_tok = s.now("Vn")

        def oproj(jg, ofm_toks, base=base):
            last = None
            for m in range(8):
                slot = ow_sb[m % 2]
                wt = ctx.load(slot[:, :], ow_d[m],
                              waits=[ctx.war.get("oww%d" % (m % 2)),
                                     ctx.precast.get("OWB"), ctx.precast.get("OWFB")])
                ps = ctx.st[m % 2]
                t = None
                for cc in range(DC):
                    w = (list(ofm_toks) + [wt, ctx.war.get("op%d" % (m % 2)),
                          ctx.war.get("opx%d" % (m % 2))] if cc == 0 else [])
                    t = ctx.mm(ps[:, 0:512], slot[:, cc * 128:(cc + 1) * 128],
                               ofm4[:, cc * 512:(cc + 1) * 512],
                               start=(cc == 0), stop=(cc == DC - 1), waits=w)
                ctx.war["oww%d" % (m % 2)] = t
                xr = xres[m % 2]
                xt = ctx.load(xr[:, :], src[m * 128:(m + 1) * 128,
                                            base + jg * 512: base + jg * 512 + 512],
                              waits=[ctx.war.get("xres%d" % (m % 2))])
                og = ostg[m % 2]
                ev = ctx.dve(lambda e, o=og[:, :], i=ps[:, 0:512], x=xr[:, :]:
                             e.tensor_tensor(o, i, x, ALU.add),
                             waits=[t, xt, ctx.war.get("ostg%d" % (m % 2))])
                st = ctx.store(dst[m * 128:(m + 1) * 128,
                                   base + jg * 512: base + jg * 512 + 512],
                               og[:, :], waits=[ev])
                ctx.war["ostg%d" % (m % 2)] = st
                ctx.war["op%d" % (m % 2)] = ev
                ctx.war["xres%d" % (m % 2)] = ev
                last = ev
            return last

        emit_attn_core(ctx, 16, 64, Q_sb, K_sb, V_sb, float(D) ** -0.5,
                       PT_sb, dmask, otmn, recip, ofm4, qk_tok, v_tok, oproj)
        ctx.war["seqg"] = s.now("Tn")
        ctx.war["lnxT"] = s.now("Tn")
    s.barrier()
    ctx.war = {}
    sa.reset(mark)



def emit_outer_ln(ctx, src_name, dst_name):
    s, sa = ctx.s, ctx.sa
    src = ctx.dram[src_name]
    dst = ctx.dram[dst_name]
    mark = sa.mark()
    xbuf = sa.alloc("p4_x", [128, DC * 512], F32)
    sqbuf = sa.alloc("p4_sq", [128, 512], BF16)
    rowbuf = sa.alloc("p4_row", [1, 5 * 512], F32)
    tmpbuf = sa.alloc("p4_tmp", [128, 512], F32)
    ybuf = sa.alloc("p4_y", [128, DC * 512], F32)
    for tile in range(NTOK // 512):
        col0 = tile * 512

        def dst_fn(cc):
            return ybuf[:, cc * 512:(cc + 1) * 512]
        toks, _ = emit_ln(ctx, src, col0, 512, xbuf, sqbuf, rowbuf, tmpbuf,
                          dst_fn, dst_war=lambda cc: ctx.war.get(f"p4y{cc}"))
        for cc in range(DC):
            st = ctx.store(dst[cc * 128:(cc + 1) * 128, col0:col0 + 512],
                           ybuf[:, cc * 512:(cc + 1) * 512], waits=[toks[cc]])
            ctx.war[f"p4y{cc}"] = st
    s.barrier()
    ctx.war = {}
    sa.reset(mark)



def emit_mla(ctx, src_name, dst_name):
    s, c, sa = ctx.s, ctx.c, ctx.sa
    src = ctx.dram[src_name]
    dst = ctx.dram[dst_name]
    mark = sa.mark()
    K_sb = sa.alloc("p5_K", [128, DC * T], BF16)
    Q_sb = sa.alloc("p5_Q", [128, DC * T], BF16)
    V_sb = sa.alloc("p5_V", [128, 16 * 1040], BF16)
    qmix_sb = sa.alloc("p5_qmix", [128, 2 * 1024], BF16)
    qmixp_sb = sa.alloc("p5_qmixp", [128, 2 * 1024], BF16)
    kmixn_sb = sa.alloc("p5_kmixn", [128, 2 * 1024], BF16)
    kr_sb = [sa.alloc(f"p5_kr{i}", [128, DC * 128], BF16) for i in range(2)]
    krp_sb = [sa.alloc(f"p5_krp{i}", [128, DC * 128], BF16) for i in range(2)]
    vut_sb = sa.alloc("p5_vut", [128, 2 * 512], BF16)
    cmask = sa.alloc("p5_cm", [128, T], BF16)
    smask = sa.alloc("p5_sm", [128, T], BF16)
    ln1v_sb = sa.alloc("p5_ln1", [128, 8], F32)
    seq_mark = sa.mark()

    cm = np.ones((128, T), np.float32)
    sm = np.zeros((128, T), np.float32)
    cm[0:32] = ctx.rope_cos[0:32]
    cm[64:96] = ctx.rope_cos[0:32]
    sm[0:32] = ctx.rope_sin[0:32]
    sm[64:96] = ctx.rope_sin[0:32]
    cm_d = ctx.nc.inline_tensor(cm.astype(BF), name="p5cm").ap()
    sm_d = ctx.nc.inline_tensor(sm.astype(BF), name="p5sm").ap()
    ctx.load(cmask[:, :], cm_d[:, :])
    ctx.load(smask[:, :], sm_d[:, :])
    ctx.load(ln1v_sb[:, :], ctx.dram["ln1v"][:, :])
    for (sb, nm, m) in ((qmix_sb, "qmix", 1024), (qmixp_sb, "qmixp", 1024),
                        (kmixn_sb, "kmixn", 1024)):
        ctx.cast_load(sb[:, :].rearrange("p (cc m) -> p cc m", m=m),
                      ctx.dram[nm].rearrange("(cc p) m -> p cc m", p=128))
    kvd_d = ctx.dram["kvdt"].rearrange("(cc p) m -> p cc m", p=128)
    qd_d = ctx.dram["qdt"].rearrange("(cc p) m -> p cc m", p=128)
    kr_d = ctx.dram["KRB"]
    krp_d = ctx.dram["KRPB"]
    vut_d = ctx.dram["vut"].rearrange("(cc p) m -> p cc m", p=128)
    ow_d = ctx.dram["OWFB"]
    s.barrier()

    yh = sa.alloc("p5_yh", [128, DC * T], BF16)
    kvlat = sa.alloc("p5_kvl", [128, 2 * T], BF16)
    qlat = sa.alloc("p5_ql", [128, 2 * T], BF16)
    ln_mark = sa.mark()
    xbuf = sa.alloc("p5_x", [128, DC * 512], F32)
    sqbuf = sa.alloc("p5_sq", [128, 512], BF16)
    rowbuf = sa.alloc("p5_row", [1, 5 * 512], F32)
    tmpbuf = sa.alloc("p5_tmp", [128, 512], F32)
    sa.reset(ln_mark)
    rt1 = sa.alloc("p5_rt1", [128, 512], F32)
    rt2 = sa.alloc("p5_rt2", [128, 512], F32)
    rt3 = sa.alloc("p5_rt3", [128, 512], F32)
    rt4 = sa.alloc("p5_rt4", [128, 512], F32)
    sa.reset(ln_mark)
    PT_sb = [sa.alloc("p5_PT%d" % i, [128, 2048], BF16) for i in range(2)]
    ofm4 = sa.alloc("p5_ofm4", [128, DC * 512], BF16)
    dmask = [sa.alloc("p5_dm%d" % i, [128, 128], BF16) for i in range(2)]
    otmn = sa.alloc("p5_otmn", [128, 1024], BF16)
    recip = sa.alloc("p5_rc", [128, 16], F32)
    xres = [sa.alloc(f"p5_xres{i}", [128, 512], F32) for i in range(2)]
    ostg = [sa.alloc(f"p5_ostg{i}", [128, 512], F32) for i in range(2)]

    for sq in range(NSEQ):
        base = sq * T
        for tile in range(4):
            def dst_fn(cc, tile=tile):
                return yh[:, cc * T + tile * 512: cc * T + tile * 512 + 512]
            emit_ln(ctx, src, base + tile * 512, 512, xbuf, sqbuf, rowbuf,
                    tmpbuf, dst_fn,
                    gain_col=lambda cc: ln1v_sb[:, cc:cc + 1],
                    dst_war=lambda cc: ctx.war.get("seqg"))
        yh_tok = s.now("Vn")
        for wi, (wd, dst_sb) in enumerate(((kvd_d, kvlat), (qd_d, qlat))):
            for m2 in range(2):
                slot = kr_sb[(wi * 2 + m2) % 2]
                wt = ctx.cast_load(
                    slot[:, :].rearrange("p (cc m) -> p cc m", m=128),
                    wd[:, :, m2 * 128:(m2 + 1) * 128],
                    waits=[ctx.war.get("krw%d" % ((wi * 2 + m2) % 2))])
                for tile in range(4):
                    pi = tile % 2
                    ps = ctx.st[pi]
                    t = None
                    for cc in range(DC):
                        w = ([yh_tok, wt, ctx.war.get("lat%d" % pi)]
                             if cc == 0 else [])
                        t = ctx.mm(ps[:, 0:512],
                                   slot[:, cc * 128:(cc + 1) * 128],
                                   yh[:, cc * T + tile * 512: cc * T + tile * 512 + 512],
                                   start=(cc == 0), stop=(cc == DC - 1), waits=w)
                    o_ap = dst_sb[:, m2 * T + tile * 512: m2 * T + tile * 512 + 512]
                    ev = ctx.act(lambda e, o=o_ap, i=ps[:, 0:512]:
                                 e.activation(o, i, AF.Copy),
                                 waits=[t, ctx.war.get("seqg")])
                    ctx.war["lat%d" % pi] = ev
                ctx.war["krw%d" % ((wi * 2 + m2) % 2)] = s.now("Tn")
        lat_tok = s.now("Cn")
        asm_guard = s.now("Vn")
        for cc in range(DC):
            krt = ctx.load(kr_sb[cc % 2][:, :], kr_d[cc],
                           waits=[ctx.war.get("krw%d" % (cc % 2)),
                                  ctx.precast.get("KRB")])
            krpt = ctx.load(krp_sb[cc % 2][:, :], krp_d[cc],
                            waits=[ctx.war.get("krpw%d" % (cc % 2)),
                                   ctx.war.get("oww%d" % (cc % 2)),
                                   ctx.precast.get("KRPB")])
            for tile in range(4):
                ps1 = ctx.st[0]
                ps2 = ctx.st[1]
                qs1 = ctx.st[2]
                qs2 = ctx.st[3]
                t1 = None
                for k8 in range(DC):
                    w = ([yh_tok, lat_tok, krt, ctx.war.get("kps")]
                         if k8 == 0 else [])
                    t1 = ctx.mm(ps1[:, 0:512],
                                kr_sb[cc % 2][:, k8 * 128:(k8 + 1) * 128],
                                yh[:, k8 * T + tile * 512: k8 * T + tile * 512 + 512],
                                start=(k8 == 0), stop=False, waits=w)
                for lc in range(2):
                    t1 = ctx.mm(ps1[:, 0:512],
                                kmixn_sb[:, lc * 1024 + cc * 128: lc * 1024 + (cc + 1) * 128],
                                kvlat[:, lc * T + tile * 512: lc * T + tile * 512 + 512],
                                start=False, stop=(lc == 1), waits=[])
                t2 = None
                for k8 in range(DC):
                    w = ([krpt, ctx.war.get("kps2")] if k8 == 0 else [])
                    t2 = ctx.mm(ps2[:, 0:512],
                                krp_sb[cc % 2][:, k8 * 128:(k8 + 1) * 128],
                                yh[:, k8 * T + tile * 512: k8 * T + tile * 512 + 512],
                                start=(k8 == 0), stop=(k8 == DC - 1), waits=w)
                m1 = ctx.dve(lambda e, o=rt1[:, :], i=ps1[:, 0:512], t=tile:
                             e.tensor_tensor(o, i, cmask[:, t * 512:(t + 1) * 512],
                                             ALU.mult),
                             waits=[t1, asm_guard, ctx.war.get("rt1")])
                m2_ = ctx.dve(lambda e, o=rt2[:, :], i=ps2[:, 0:512], t=tile:
                              e.tensor_tensor(o, i, smask[:, t * 512:(t + 1) * 512],
                                              ALU.mult),
                              waits=[t2, asm_guard, ctx.war.get("rt2")])
                kd = K_sb[:, cc * T + tile * 512: cc * T + tile * 512 + 512]
                m3 = ctx.dve(lambda e, o=kd: e.tensor_tensor(o, rt1[:, :], rt2[:, :],
                                                             ALU.add),
                             waits=[m1, m2_, ctx.war.get("seqg")])
                ctx.war["kps"] = m1
                ctx.war["kps2"] = m2_
                ctx.war["rt1"] = m3
                ctx.war["rt2"] = m3
                t1 = None
                t2 = None
                for lc in range(2):
                    w = ([lat_tok, ctx.war.get("qps")] if lc == 0 else [])
                    t1 = ctx.mm(qs1[:, 0:512],
                                qmix_sb[:, lc * 1024 + cc * 128: lc * 1024 + (cc + 1) * 128],
                                qlat[:, lc * T + tile * 512: lc * T + tile * 512 + 512],
                                start=(lc == 0), stop=(lc == 1), waits=w)
                    w2 = ([ctx.war.get("qps2")] if lc == 0 else [])
                    t2 = ctx.mm(qs2[:, 0:512],
                                qmixp_sb[:, lc * 1024 + cc * 128: lc * 1024 + (cc + 1) * 128],
                                qlat[:, lc * T + tile * 512: lc * T + tile * 512 + 512],
                                start=(lc == 0), stop=(lc == 1), waits=w2)
                q1 = ctx.dve(lambda e, o=rt3[:, :], i=qs1[:, 0:512], t=tile:
                             e.tensor_tensor(o, i, cmask[:, t * 512:(t + 1) * 512],
                                             ALU.mult),
                             waits=[t1, asm_guard, ctx.war.get("rt3")])
                q2 = ctx.dve(lambda e, o=rt4[:, :], i=qs2[:, 0:512], t=tile:
                              e.tensor_tensor(o, i, smask[:, t * 512:(t + 1) * 512],
                                              ALU.mult),
                              waits=[t2, asm_guard, ctx.war.get("rt4")])
                qd = Q_sb[:, cc * T + tile * 512: cc * T + tile * 512 + 512]
                q3 = ctx.dve(lambda e, o=qd: e.tensor_tensor(o, rt3[:, :], rt4[:, :],
                                                             ALU.add),
                             waits=[q1, q2, ctx.war.get("seqg")])
                ctx.war["qps"] = q1
                ctx.war["qps2"] = q2
                ctx.war["rt3"] = q3
                ctx.war["rt4"] = q3
            ctx.war["krw%d" % (cc % 2)] = s.now("Tn")
            ctx.war["krpw%d" % (cc % 2)] = s.now("Tn")
        ones_tok = ctx.dve(lambda e: e.memset(V_sb[:, :], 1.0),
                           waits=[ctx.war.get("vmem")])
        for dvs in range(2):
            vt = ctx.cast_load(vut_sb[:, :].rearrange("p (k m) -> p k m", m=512),
                               vut_d[:, :, dvs * 512:(dvs + 1) * 512],
                               waits=[ctx.war.get("vw")])
            for tc in range(16):
                ps = ctx.st[tc % 2]
                t = None
                for lc in range(2):
                    w = ([lat_tok, vt, ctx.war.get("kps"), ctx.war.get("kps2"),
                          ctx.war.get("qps"), ctx.war.get("qps2"),
                          ctx.war.get("vps%d" % (tc % 2))] if lc == 0 else [])
                    t = ctx.mm(ps[:, 0:512],
                               kvlat[:, lc * T + tc * 128: lc * T + (tc + 1) * 128],
                               vut_sb[:, lc * 512:(lc + 1) * 512],
                               start=(lc == 0), stop=(lc == 1), waits=w)
                o_ap = V_sb[:, tc * 1040 + dvs * 520: tc * 1040 + dvs * 520 + 520]
                o_ap = o_ap.rearrange("p (g k) -> p g k", k=65)[:, :, 0:64]
                i_ap = ps[:, 0:512].rearrange("p (g k) -> p g k", k=64)
                ev = ctx.dve(lambda e, o=o_ap, i=i_ap: e.tensor_copy(o, i),
                             waits=[t, ones_tok, ctx.war.get("seqg")])
                ctx.war["vps%d" % (tc % 2)] = ev
            ctx.war["vw"] = s.now("Tn")
        qk_tok = s.now("Vn")
        v_tok = s.now("Vn")
        ctx.war["oww0"] = s.now("Tn")
        ctx.war["oww1"] = s.now("Tn")
        ctx.war["xres0"] = qk_tok
        ctx.war["xres1"] = qk_tok

        def oproj(jg, ofm_toks, base=base):
            last = None
            for m in range(8):
                slot = krp_sb[m % 2]
                wt = ctx.load(slot[:, :], ow_d[m],
                              waits=[ctx.war.get("oww%d" % (m % 2)),
                                     ctx.precast.get("OWB"), ctx.precast.get("OWFB")])
                ps = ctx.st[m % 2]
                t = None
                for cc in range(DC):
                    w = (list(ofm_toks) + [wt, ctx.war.get("op%d" % (m % 2)),
                          ctx.war.get("opx%d" % (m % 2))] if cc == 0 else [])
                    t = ctx.mm(ps[:, 0:512], slot[:, cc * 128:(cc + 1) * 128],
                               ofm4[:, cc * 512:(cc + 1) * 512],
                               start=(cc == 0), stop=(cc == DC - 1), waits=w)
                ctx.war["oww%d" % (m % 2)] = t
                xr = xres[m % 2]
                xt = ctx.load(xr[:, :], src[m * 128:(m + 1) * 128,
                                            base + jg * 512: base + jg * 512 + 512],
                              waits=[ctx.war.get("xres%d" % (m % 2))])
                og = ostg[m % 2]
                ev = ctx.dve(lambda e, o=og[:, :], i=ps[:, 0:512], x=xr[:, :]:
                             e.tensor_tensor(o, i, x, ALU.add),
                             waits=[t, xt, ctx.war.get("ostg%d" % (m % 2))])
                st = ctx.store(dst[m * 128:(m + 1) * 128,
                                   base + jg * 512: base + jg * 512 + 512],
                               og[:, :], waits=[ev])
                ctx.war["ostg%d" % (m % 2)] = st
                ctx.war["op%d" % (m % 2)] = ev
                ctx.war["xres%d" % (m % 2)] = ev
                last = ev
            return last

        emit_attn_core(ctx, 16, 64, Q_sb, K_sb, V_sb, 64.0 ** -0.5,
                       PT_sb, dmask, otmn, recip, ofm4, qk_tok, v_tok, oproj)
        ctx.war["seqg"] = s.now("Tn")
        ctx.war["lnxT"] = s.now("Tn")
        ctx.war["vmem"] = s.now("Tn")
    s.barrier()
    ctx.war = {}
    sa.reset(mark)



def emit_precast(ctx):
    ctx.precast = {}
    pairs = [
        ("gall", "GB", 8), ("uall", "UB", 8), ("dall", "DB", 8),
        ("qw1", "QWB", 1), ("kw1", "KWB", 1), ("vw1", "VWB", 1),
        ("ow1", "OWB", 1), ("kmixr", "KRB", 1), ("kmixrp", "KRPB", 1),
        ("owft", "OWFB", 1),
    ]
    for src, dstn, nsplit in pairs:
        sd = ctx.dram[src]
        dd = ctx.dram[dstn]
        tok = None
        if nsplit == 1:
            tok = ctx.cast_load(dd.rearrange("a p c -> (a p) c"),
                                sd.rearrange("a p b c -> (a p) (b c)"))
        else:
            for e in range(nsplit):
                tok = ctx.cast_load(dd[e].rearrange("m p c -> (m p) c"),
                                    sd[e].rearrange("m p b c -> (m p) (b c)"))
        ctx.precast[dstn] = tok



def emit_moe(ctx, x_name, y1_name, dst_name):
    s, c, sa = ctx.s, ctx.c, ctx.sa
    xsrc = ctx.dram[x_name]
    ysrc = ctx.dram[y1_name]
    dst = ctx.dram[dst_name]
    NTILE = 1024
    mark = sa.mark()
    y1 = sa.alloc("p6_y1", [128, DC * NTILE], F32)
    yh2 = sa.alloc("p6_yh2", [128, DC * NTILE], BF16)
    h_sb = sa.alloc("p6_h", [128, 16 * NTILE], BF16)
    acc = [sa.alloc(f"p6_acc{i}", [128, DC * NTILE], F32) for i in range(2)]
    wbc = sa.alloc("p6_wbc", [128, NTILE], F32)
    gstage = [sa.alloc(f"p6_gs{i}", [128, DC * 128], BF16) for i in range(4)]
    dstage = [sa.alloc(f"p6_ds{i}", [128, 16 * 128], BF16) for i in range(2)]
    glbuf = sa.alloc("p6_gl", [128, 2 * 512], BF16)
    ubuf = sa.alloc("p6_ub", [128, 2 * 512], BF16)
    tmpo = sa.alloc("p6_tmpo", [128, 512], F32)
    rp_sb = sa.alloc("p6_rp", [128, DC * 7], F32)
    ln2v_sb = sa.alloc("p6_ln2", [128, 8], F32)
    rbias_sb = sa.alloc("p6_rb", [1, 7], F32)
    rsum_sb = sa.alloc("p6_rs", [1, 7], F32)
    rsbc = sa.alloc("p6_rsbc", [128, 7], F32)
    rbbc = sa.alloc("p6_rbbc", [128, 7], F32)
    arow = sa.alloc("p6_ar", [1, NTILE], F32)
    brow = sa.alloc("p6_br", [1, NTILE], F32)
    raw_sb = sa.alloc("p6_raw", [7, NTILE], F32)
    ltm = sa.alloc("p6_ltm", [128, 8 * 7], F32)
    ab_sb = sa.alloc("p6_ab", [128, 2 * 8], F32)
    prob = sa.alloc("p6_prob", [128, 8 * 7], F32)
    rsc = sa.alloc("p6_rsc", [128, 8 * 7], F32)
    m12 = sa.alloc("p6_m12", [128, 3 * 8], F32)
    wgt_tm = sa.alloc("p6_wgt", [128, 8 * 7], F32)
    identf = sa.alloc("p6_idf", [128, 128], F32)
    diag = sa.alloc("p6_diag", [128, 128], F32)
    onesq = sa.alloc("p6_onsq", [128, 128], F32)
    xbuf_alias = sa.mark()
    sqbuf = sa.alloc("p6_sq", [128, 512], BF16)
    rowbuf = sa.alloc("p6_row", [1, 5 * 512], F32)
    tmpbuf = sa.alloc("p6_tmp", [128, 512], F32)
    xres = [sa.alloc(f"p6_xres{i}", [128, 512], F32) for i in range(2)]
    ostg = [sa.alloc(f"p6_ostg{i}", [128, 512], F32) for i in range(2)]

    idf_d = ctx.nc.inline_tensor(np.eye(128, dtype=np.float32), name="p6idf").ap()
    onsq_d = ctx.nc.inline_tensor(np.ones((128, 128), np.float32), name="p6ones").ap()
    ctx.load(identf[:, :], idf_d[:, :])
    ctx.load(onesq[:, :], onsq_d[:, :])
    ctx.load(ln2v_sb[:, :], ctx.dram["ln2v"][:, :])
    ctx.load(rbias_sb[:, :], ctx.dram["rbias"][:, :])
    rw_tok = ctx.load(rp_sb[:, :].rearrange("p (cc m) -> p cc m", m=7),
                      ctx.dram["rwt"].rearrange("(cc p) m -> p cc m", p=128))
    s.barrier()
    t = None
    for cc in range(DC):
        t = ctx.dve(lambda e, o=rp_sb[:, cc * 7:(cc + 1) * 7], g=ln2v_sb[:, cc:cc + 1]:
                    e.tensor_scalar_mul(o, o, g), waits=[])
    rs_t = None
    for cc in range(DC):
        rs_t = ctx.mm(ctx.ps_g[2][0:1, 0:7], c["ones_f"][:, :],
                      rp_sb[:, cc * 7:(cc + 1) * 7],
                      start=(cc == 0), stop=(cc == DC - 1), waits=[t])
    t = ctx.dve(lambda e: e.tensor_copy(rsum_sb[0:1, :], ctx.ps_g[2][0:1, 0:7]),
                waits=[rs_t])
    b1 = ctx.mm(ctx.ps_g[2][:, 0:7], c["ones_r"][:, :], rsum_sb[0:1, :],
                start=True, stop=True, waits=[t])
    t = ctx.dve(lambda e: e.tensor_copy(rsbc[:, :], ctx.ps_g[2][:, 0:7]), waits=[b1])
    b2 = ctx.mm(ctx.ps_g[2][:, 0:7], c["ones_r"][:, :], rbias_sb[0:1, :],
                start=True, stop=True, waits=[t])
    t = ctx.dve(lambda e: e.tensor_copy(rbbc[:, :], ctx.ps_g[2][:, 0:7]), waits=[b2])
    s.barrier()
    ctx.war = {}

    for tile in range(NTOK // NTILE):
        col0 = tile * NTILE
        for sub in range(2):
            def dst_fn(cc, sub=sub):
                return yh2[:, cc * NTILE + sub * 512: cc * NTILE + sub * 512 + 512]
            emit_ln(ctx, ysrc, col0 + sub * 512, 512,
                    _Slice1024(y1, sub), sqbuf, rowbuf, tmpbuf, dst_fn,
                    gain_col=lambda cc: ln2v_sb[:, cc:cc + 1])
            a_t = ctx.dve(lambda e, o=arow[0:1, sub * 512:(sub + 1) * 512]:
                          e.tensor_copy(o, rowbuf[0:1, 0:512]))
            b_t = ctx.dve(lambda e, o=brow[0:1, sub * 512:(sub + 1) * 512]:
                          e.tensor_copy(o, rowbuf[0:1, 512:1024]))
            ctx.war["rowA"] = a_t
            ctx.war["rowB"] = b_t
        yh_tok = s.now("Vn")
        for sub in range(2):
            t = None
            for cc in range(DC):
                w = ([yh_tok, ctx.war.get("rps")] if cc == 0 else [])
                t = ctx.mm(ctx.ps_g[2][0:7, 0:512], rp_sb[:, cc * 7:(cc + 1) * 7],
                           y1[:, cc * NTILE + sub * 512: cc * NTILE + sub * 512 + 512],
                           start=(cc == 0), stop=(cc == DC - 1), waits=w)
            ev = ctx.dve(lambda e, o=raw_sb[0:7, sub * 512:(sub + 1) * 512],
                         i=ctx.ps_g[2][0:7, 0:512]: e.tensor_copy(o, i), waits=[t])
            ctx.war["rps"] = ev
        for bb in range(8):
            t1 = ctx.s.emit(
                "tensor",
                lambda e, o=ctx.ps_g[2][0:128, 0:7],
                i=raw_sb[0:7, bb * 128:(bb + 1) * 128]:
                e.transpose(o, i, identf[0:7, 0:7]),
                waits=[ctx.war.get("rps"), ctx.war.get("rtr")], inc="Tn")
            ta = ctx.mm(ctx.ps_g[0][0:128, 0:1], arow[0:1, bb * 128:(bb + 1) * 128],
                        c["ones_r"][0:1, 0:1], start=True, stop=True,
                        waits=[ctx.war.get("rta")])
            tb = ctx.mm(ctx.ps_g[0][0:128, 1:2], brow[0:1, bb * 128:(bb + 1) * 128],
                        c["ones_r"][0:1, 0:1], start=True, stop=True)
            e1 = ctx.dve(lambda e, o=ltm[:, bb * 7:(bb + 1) * 7],
                         i=ctx.ps_g[2][0:128, 0:7]: e.tensor_copy(o, i), waits=[t1])
            e2 = ctx.dve(lambda e, o=ab_sb[:, bb * 2:bb * 2 + 2],
                         i=ctx.ps_g[0][0:128, 0:2]: e.tensor_copy(o, i), waits=[tb, ta])
            ctx.war["rtr"] = e1
            ctx.war["rta"] = e2
            l1 = ctx.dve(lambda e, o=rsc[:, bb * 7:(bb + 1) * 7],
                         b_=ab_sb[:, bb * 2 + 1:bb * 2 + 2]:
                         e.scalar_tensor_tensor(o, rsbc[:, :], b_, rbbc[:, :],
                                                ALU.mult, ALU.add), waits=[e2])
            l2 = ctx.dve(lambda e, o=ltm[:, bb * 7:(bb + 1) * 7],
                         a_=ab_sb[:, bb * 2:bb * 2 + 1],
                         t2=rsc[:, bb * 7:(bb + 1) * 7]:
                         e.scalar_tensor_tensor(o, o, a_, t2, ALU.mult, ALU.add),
                         waits=[l1, e1])
            sg = ctx.act(lambda e, o=prob[:, bb * 7:(bb + 1) * 7],
                         i=ltm[:, bb * 7:(bb + 1) * 7]:
                         e.activation(o, i, AF.Sigmoid), waits=[l2])
            bm = bb * 7
            m1c = m12[:, bb * 3:bb * 3 + 1]
            m2c = m12[:, bb * 3 + 1:bb * 3 + 2]
            sc_ = m12[:, bb * 3 + 2:bb * 3 + 3]
            r1 = ctx.dve(lambda e, o=m1c, i=prob[:, bm:bm + 7]:
                         e.reduce_max(o, i, AX.X), waits=[sg])
            r2 = ctx.dve(lambda e, o=rsc[:, bm:bm + 7], i=prob[:, bm:bm + 7], m=m1c:
                         e.tensor_scalar(o, i, m, None, ALU.is_ge), waits=[r1])
            r3 = ctx.dve(lambda e, o=rsc[:, bm:bm + 7], i=rsc[:, bm:bm + 7],
                         p=prob[:, bm:bm + 7]:
                         e.scalar_tensor_tensor(o, i, -1e30, p, ALU.mult, ALU.add),
                         waits=[r2])
            r4 = ctx.dve(lambda e, o=m2c, i=rsc[:, bm:bm + 7]:
                         e.reduce_max(o, i, AX.X), waits=[r3])
            r5 = ctx.dve(lambda e, o=rsc[:, bm:bm + 7], i=prob[:, bm:bm + 7], m=m2c:
                         e.tensor_scalar(o, i, m, None, ALU.is_ge), waits=[r4])
            r6 = ctx.dve(lambda e, o=sc_, i=m1c, i2=m2c:
                         e.tensor_tensor(o, i, i2, ALU.add), waits=[r4])
            r7 = ctx.dve(lambda e, o=sc_, i=sc_: e.reciprocal(o, i), waits=[r6])
            r8 = ctx.dve(lambda e, o=wgt_tm[:, bm:bm + 7], i=prob[:, bm:bm + 7],
                         i2=rsc[:, bm:bm + 7]:
                         e.tensor_tensor(o, i, i2, ALU.mult), waits=[r5])
            r9 = ctx.dve(lambda e, o=wgt_tm[:, bm:bm + 7], i=wgt_tm[:, bm:bm + 7],
                         m=sc_: e.tensor_scalar_mul(o, i, m), waits=[r7, r8])
        wgt_tok = s.now("Vn")
        ctx.war["dps0"] = wgt_tok
        ctx.war["dps1"] = wgt_tok
        ctx.war["wbcp"] = wgt_tok
        for ei in range(8):
            if ei > 0:
                wtoks = []
                for bb in range(8):
                    dg = ctx.dve(lambda e, o=diag[:, :],
                                 m=wgt_tm[:, bb * 7 + ei - 1: bb * 7 + ei]:
                                 e.tensor_scalar_mul(o, identf[:, :], m),
                                 waits=[wgt_tok, ctx.war.get("diag")])
                    bt = ctx.mm(ctx.ps_g[2][:, 0:128], onesq[:, :], diag[:, :],
                                start=True, stop=True,
                                waits=[dg, ctx.war.get("wbcp")])
                    ctx.war["diag"] = bt
                    ev = ctx.dve(lambda e, o=wbc[:, bb * 128:(bb + 1) * 128],
                                 i=ctx.ps_g[2][:, 0:128]: e.tensor_copy(o, i),
                                 waits=[bt, ctx.war.get("wbcr")])
                    ctx.war["wbcp"] = ev
                    wtoks.append(ev)
                wbc_tok = wtoks[-1]
            else:
                wbc_tok = None
            for mf in range(16):
                gslot = gstage[mf % 2]
                gt = ctx.load(gslot[:, :], ctx.dram["GB"][ei, mf],
                              waits=[ctx.war.get("gsl%d" % (mf % 2)),
                                     ctx.precast.get("GB")])
                uslot = gstage[2 + mf % 2]
                ut = ctx.load(uslot[:, :], ctx.dram["UB"][ei, mf],
                              waits=[ctx.war.get("usl%d" % (mf % 2)),
                                     ctx.precast.get("UB")])
                for nt in range(2):
                    ps_gm = ctx.st[nt]
                    t = None
                    for cc in range(DC):
                        w = ([yh_tok, gt, ctx.war.get("gps%d" % nt)]
                             if cc == 0 else [])
                        t = ctx.mm(ps_gm[:, 0:512],
                                   gslot[:, cc * 128:(cc + 1) * 128],
                                   yh2[:, cc * NTILE + nt * 512: cc * NTILE + nt * 512 + 512],
                                   start=(cc == 0), stop=(cc == DC - 1), waits=w)
                    sl = ctx.act(lambda e, o=glbuf[:, nt * 512:(nt + 1) * 512],
                                 i=ps_gm[:, 0:512]: e.activation(o, i, AF.Silu),
                                 waits=[t, ctx.war.get("glb%d" % nt)])
                    ctx.war["gps%d" % nt] = sl
                for nt in range(2):
                    ps_um = ctx.st[2 + nt]
                    t = None
                    for cc in range(DC):
                        w = ([ut, ctx.war.get("ups%d" % nt)] if cc == 0 else [])
                        t = ctx.mm(ps_um[:, 0:512],
                                   uslot[:, cc * 128:(cc + 1) * 128],
                                   yh2[:, cc * NTILE + nt * 512: cc * NTILE + nt * 512 + 512],
                                   start=(cc == 0), stop=(cc == DC - 1), waits=w)
                    ue = ctx.act(lambda e, o=ubuf[:, nt * 512:(nt + 1) * 512],
                                 i=ps_um[:, 0:512]: e.activation(o, i, AF.Copy),
                                 waits=[t, ctx.war.get("ubf%d" % nt)])
                    ctx.war["ups%d" % nt] = ue
                    hm = ctx.pool(lambda e, o=h_sb[:, mf * NTILE + nt * 512:
                                                   mf * NTILE + nt * 512 + 512],
                                  i=glbuf[:, nt * 512:(nt + 1) * 512],
                                  i2=ubuf[:, nt * 512:(nt + 1) * 512]:
                                  e.tensor_tensor(o, i, i2, ALU.mult),
                                  waits=[ue, ctx.war.get("gps%d" % nt),
                                         ctx.war.get("hsb%d" % nt)])
                    ctx.war["glb%d" % nt] = hm
                    ctx.war["ubf%d" % nt] = hm
                ctx.war["gsl%d" % (mf % 2)] = s.now("Tn")
                ctx.war["usl%d" % (mf % 2)] = s.now("Tn")
            h_tok = s.now("Pn")
            for m2 in range(8):
                dslot = dstage[m2 % 2]
                dt = ctx.load(dslot[:, :], ctx.dram["DB"][ei, m2],
                              waits=[ctx.war.get("dsl%d" % (m2 % 2)),
                                     ctx.precast.get("DB")])
                for nt in range(2):
                    ps_d = ctx.ps_g[nt]
                    t = None
                    for k in range(16):
                        w = ([h_tok, dt, ctx.war.get("dps%d" % nt)]
                             if k == 0 else [])
                        t = ctx.mm(ps_d[:, 0:512],
                                   dslot[:, k * 128:(k + 1) * 128],
                                   h_sb[:, k * NTILE + nt * 512: k * NTILE + nt * 512 + 512],
                                   start=(k == 0), stop=(k == 15), waits=w)
                    col = m2 * NTILE + nt * 512
                    if ei == 0:
                        ev = ctx.dve(lambda e, o=acc[1][:, col:col + 512],
                                     i=ps_d[:, 0:512]: e.tensor_copy(o, i),
                                     waits=[t, ctx.war.get("accw")])
                    else:
                        ev = ctx.dve(lambda e, o=tmpo[:, :], i=ps_d[:, 0:512],
                                     wb=wbc[:, nt * 512:(nt + 1) * 512]:
                                     e.tensor_tensor(o, i, wb, ALU.mult),
                                     waits=[t, wbc_tok, ctx.war.get("tmpo")])
                        ctx.war["wbcr"] = ev
                        ev = ctx.dve(lambda e, o=acc[(ei + 1) % 2][:, col:col + 512],
                                     i=acc[ei % 2][:, col:col + 512], t2=tmpo[:, :]:
                                     e.tensor_tensor(o, i, t2, ALU.add),
                                     waits=[ev, ctx.war.get("accw")])
                        ctx.war["tmpo"] = ev
                    ctx.war["dps%d" % nt] = ev
                ctx.war["dsl%d" % (m2 % 2)] = s.now("Tn")
            ctx.war["hsb0"] = s.now("Tn")
            ctx.war["hsb1"] = s.now("Tn")
        acc_fin = acc[0]
        acc_tok = s.now("Vn")
        for m2 in range(8):
            for nt in range(2):
                col = m2 * NTILE + nt * 512
                xr = xres[nt]
                xt = ctx.load(xr[:, :],
                              xsrc[m2 * 128:(m2 + 1) * 128,
                                   col0 + nt * 512: col0 + nt * 512 + 512],
                              waits=[ctx.war.get("xres%d" % nt)])
                o1 = ctx.dve(lambda e, o=ostg[nt][:, :], i=acc_fin[:, col:col + 512],
                             y=y1[:, col:col + 512]:
                             e.tensor_tensor(o, i, y, ALU.add),
                             waits=[acc_tok, ctx.war.get("ost%d" % nt)])
                o2 = ctx.dve(lambda e, o=ostg[nt][:, :], i=ostg[nt][:, :], x=xr[:, :]:
                             e.tensor_tensor(o, i, x, ALU.add), waits=[o1, xt])
                ctx.war["xres%d" % nt] = o2
                st_ = ctx.store(dst[m2 * 128:(m2 + 1) * 128,
                                    col0 + nt * 512: col0 + nt * 512 + 512],
                                ostg[nt][:, :], waits=[o2])
                ctx.war["ost%d" % nt] = st_
        ctx.war["accw"] = s.now("Vn")
        ctx.war["lnW"] = s.now("Tn")
        s.barrier()
        ctx.war = {}
    sa.reset(mark)


class _Slice1024:
    def __init__(self, y1, sub):
        self.y1 = y1
        self.sub = sub

    def __getitem__(self, key):
        p, f = key
        lo, hi = f.start, f.stop
        cc = lo // 512
        return self.y1[p, cc * 1024 + self.sub * 512:
                       cc * 1024 + self.sub * 512 + (hi - lo)]
